# version 24
# baseline (speedup 1.0000x reference)
import contextlib
import numpy as np
import concourse.bass as bass
import concourse.mybir as mybir
from concourse.bass_utils import run_bass_kernel_spmd

F32 = mybir.dt.float32
BF16 = mybir.dt.bfloat16
U32 = mybir.dt.uint32
ALU = mybir.AluOpType
AF = mybir.ActivationFunctionType
AX = mybir.AxisListType

T = 4096
D = 1024
NT = 32
CAP = 640
NE = 8
NS = NE * CAP
H, HD = 8, 64
FF = 4096
EPS = 1e-5
ETA = 0.1
DTC = 0.01
GAMMA_FLOOR = 0.1
INV_D = 1.0 / D
INV_SQ_D = float(1.0 / np.sqrt(D))
LN_ETA = float(np.log(ETA))


def _host_consts():
    import ml_dtypes
    c = {}
    c["c_idn"] = np.eye(128, dtype=np.float32)
    c["c_idnb"] = np.eye(128, dtype=np.float32).astype(ml_dtypes.bfloat16)
    tril = (np.arange(128)[:, None] <= np.arange(128)[None, :]).astype(np.float32)
    c["c_trilT"] = tril
    c["c_iota8"] = np.tile(np.arange(8, dtype=np.float32), (128, 1))
    colpack = np.zeros((128, 2), np.float32)
    for b in (0, 32, 64):
        colpack[b:b + 8, 0] = np.arange(8)
        colpack[b:b + 8, 1] = np.arange(8) * CAP
    c["c_colpack"] = colpack
    return c




class Prog:
    def __init__(self, nc, es):
        self.nc = nc
        self.blk = es.enter_context(nc.Block())
        self.sem = {}
        self.cnt = {}
        for s in ("qS", "qG", "qA", "qC", "qD", "eT", "eV", "eA"):
            self.sem[s] = es.enter_context(nc.semaphore(s))
            self.cnt[s] = 0
        self.bank_free = {}

    def on(self, eng, f):
        getattr(self.blk, eng)(f)

    def inc(self, instr, s, n=1):
        instr.then_inc(self.sem[s], n)
        self.cnt[s] += n
        return self.cnt[s]

    def wait(self, eng, s, v):
        eng.wait_ge(self.sem[s], v)

    def wait_all(self, eng, *ss):
        if not ss:
            ss = tuple(self.sem)
        for s in ss:
            eng.wait_ge(self.sem[s], self.cnt[s])

    def ch(self, eng, s, instr):
        tok = self.inc(instr, s)
        eng.wait_ge(self.sem[s], tok)
        return tok

    def bank_wait(self, eng, name):
        t = self.bank_free.get(name)
        if t is not None:
            self.wait(eng, t[0], t[1])

    def bank_set(self, name, s):
        self.bank_free[name] = (s, self.cnt[s])

    def scope(self, name):
        if getattr(self, "_scope", None) is not None:
            self.nc.leave_named_scope(self._scope[0], self._scope[1], False)
        sid, _ = self.nc.enter_named_scope(name, False)
        self._scope = (name, sid)

    def scope_end(self):
        if getattr(self, "_scope", None) is not None:
            self.nc.leave_named_scope(self._scope[0], self._scope[1], False)
            self._scope = None

    def barrier(self):
        snap = dict(self.cnt)
        for ename in ("sync", "gpsimd", "tensor", "vector", "scalar"):
            def w(e, snap=snap):
                for s, v in snap.items():
                    e.wait_ge(self.sem[s], v)
            self.on(ename, w)


def build(dbg=0, upto=None, noaff=False):
    nc = bass.Bass("TRN2", target_bir_lowering=False)

    def di(name, shape, dt=F32):
        return nc.dram_tensor(name, list(shape), dt, kind="ExternalInput").ap()

    x_d = di("x", [T, D])
    ln1g_d = di("ln1_g", [8, 128]); ln1b_d = di("ln1_b", [8, 128])
    ln2g_d = di("ln2_g", [8, 128]); ln2b_d = di("ln2_b", [8, 128])
    ln3g_d = di("ln3_g", [8, 128]); ln3b_d = di("ln3_b", [8, 128])
    wpotT_d = di("w_potT", [2, D])
    bpot_d = di("b_pot", [1, 2])
    wbk_d = di("w_bk", [2, D])
    bbk_d = di("b_bk", [1, D])
    wqkv_d = di("w_qkv", [D, 3 * H * HD])
    bqkv_d = di("b_qkv", [1, 3 * H * HD])
    wout_d = di("w_out", [H * HD, D])
    bout_d = di("b_out", [1, D])
    wff1_d = di("w_ff1", [D, FF])
    bff1_d = di("b_ff1", [32, 128])
    wff2_d = di("w_ff2", [FF, D])
    bff2_d = di("b_ff2", [1, D])
    cent_d = di("cent", [NE, D])
    mlng_d = di("m_ln_g", [NE, 8, 128])
    mlnb_d = di("m_ln_b", [NE, 8, 128])
    mw1_d = di("m_w1", [NE, D, FF])
    mb1_d = di("m_b1", [NE, 32, 128])
    mw2_d = di("m_w2", [NE, FF, D])
    mb2_d = di("m_b2", [NE, 1, D])
    idn_d = di("c_idn", [128, 128])
    idnb_d = di("c_idnb", [128, 128], BF16)
    trilT_d = di("c_trilT", [128, 128])
    iota8_d = di("c_iota8", [128, 8])
    colpack_d = di("c_colpack", [128, 2])

    out_d = nc.dram_tensor("out", [T, D], F32, kind="ExternalOutput").ap()
    y_d = nc.dram_tensor("ybuf", [T, D], F32, kind="Internal").ap()
    xs_d = nc.dram_tensor("xln_sorted", [NS, D], BF16, kind="Internal").ap()
    y2s_d = nc.dram_tensor("y2_sorted", [NS, D], F32, kind="Internal").ap()
    xn3_d = nc.dram_tensor("xn3fm", [D, T], BF16, kind="Internal").ap()
    row_d = nc.dram_tensor("rowscratch", [48, T], F32, kind="Internal").ap()
    wff1b_d = nc.dram_tensor("wff1b", [D, FF], BF16, kind="Internal").ap()
    wff2b_d = nc.dram_tensor("wff2b", [FF, D], BF16, kind="Internal").ap()

    dbg_outs = {}
    def dout(name, shape, dt=F32):
        tns = nc.dram_tensor(name, list(shape), dt, kind="ExternalOutput").ap()
        dbg_outs[name] = tns
        return tns

    es = contextlib.ExitStack()
    with es:
        def sb(name, shape, dt=F32, st=None):
            return (st or es).enter_context(nc.sbuf_tensor(name, list(shape), dt))

        P = Prog(nc, es)
        on, inc, wait, wait_all, ch = P.on, P.inc, P.wait, P.wait_all, P.ch
        pb = [es.enter_context(nc.psum_tensor(f"pb{k}", [128, 512], F32)) for k in range(8)]

        m0a = sb("m0a", [128, NT]); q0a = sb("q0a", [128, NT]); r0a = sb("r0a", [128, NT])
        m1a = sb("m1a", [128, NT]); q1a = sb("q1a", [128, NT]); r1a = sb("r1a", [128, NT])
        m1r1a = sb("m1r1a", [128, NT])
        alpha_a = sb("alpha_a", [128, NT]); amax_a = sb("amax_a", [128, NT])
        eid_a = sb("eid_a", [128, NT])
        potraw = sb("potraw", [128, 2 * NT])
        V_a = sb("V_a", [128, NT]); gam_a = sb("gam_a", [128, NT])
        slotu = sb("slotu", [128, NT], U32)
        st1 = sb("st1", [128, NT]); st2 = sb("st2", [128, NT])
        junkA = sb("junkA", [128, D]); junkA2 = sb("junkA2", [128, D])
        junkV = sb("junkV", [128, D])
        idn = sb("idn", [128, 128]); idnb = sb("idnb", [128, 128], BF16)
        trilT = sb("trilT", [128, 128])
        iota8 = sb("iota8", [128, 8]); colpack = sb("colpack", [128, 2])
        ones1f = sb("ones1f", [1, 128]); onescol = sb("onescol", [128, 1])
        ones8c = sb("ones8c", [8, 1])
        epscol = sb("epscol", [128, 1])
        lnecol = sb("lnecol", [128, 1])
        centT = sb("centT", [128, 8 * 10])
        csneg = sb("csneg", [128, 2]); beffp = sb("beffp", [128, 2])
        beff_row = sb("beff_row", [1, 2])
        g1c = sb("g1c", [128, 8])
        vsm = sb("vsm", [8, 128]); vsm32 = sb("vsm32", [32, 128])
        rowA = sb("rowA", [1, T])
        e1 = sb("e1", [128, 40]); e2 = sb("e2", [128, 40]); e3 = sb("e3", [128, 40])

        esA = contextlib.ExitStack()
        cent8 = sb("cent8", [8, D], st=esA)
        wpotT = sb("wpotT", [2, D], st=esA)

        P.scope("prep")
        def s_prep(s):
            for dst, src in ((idn, idn_d), (idnb, idnb_d), (trilT, trilT_d),
                             (iota8, iota8_d), (colpack, colpack_d),
                             (cent8, cent_d), (wpotT, wpotT_d), (vsm, ln1g_d),
                             (beff_row, bpot_d)):
                inc(s.dma_start(out=dst[:], in_=src), "qS", 16)
        on("sync", s_prep)
        tok_ld0 = P.cnt["qS"]
        BREG = [None]
        def g_breg(gp):
            BREG[0] = gp.alloc_register(name="breg")
            gp.reg_mov(BREG[0], NS - 1)
        on("gpsimd", g_breg)

        def v_prep(v):
            wait(v, "qS", tok_ld0)
            v.memset(ones1f[:], 1.0)
            v.memset(onescol[:], 1.0)
            v.memset(epscol[:], EPS)
            v.memset(lnecol[:], LN_ETA)
            inc(v.memset(ones8c[:], 1.0), "eV")
        on("vector", v_prep)

        def t_g1(t, tok=P.cnt["eV"]):
            wait(t, "eV", tok)
            inc(t.transpose(pb[0][0:128, 0:8], vsm[:], idn[0:8, 0:8]), "eT")
        on("tensor", t_g1)
        def v_g1(v, tok=P.cnt["eT"]):
            wait(v, "eT", tok)
            inc(v.tensor_copy(g1c[:], pb[0][0:128, 0:8]), "eV")
        on("vector", v_g1)
        P.bank_set("pb0", "eV")

        for c in range(8):
            bk = c % 2
            def t_c(t, c=c, bk=bk):
                P.bank_wait(t, f"pb{bk}")
                t.transpose(pb[bk][0:128, 0:8], cent8[:, 128 * c:128 * (c + 1)], idn[0:8, 0:8])
                inc(t.transpose(pb[bk][0:128, 8:10], wpotT[:, 128 * c:128 * (c + 1)], idn[0:2, 0:2]), "eT")
            on("tensor", t_c)
            def v_c(v, c=c, bk=bk, tok=P.cnt["eT"]):
                wait(v, "eT", tok)
                v.tensor_copy(centT[:, 10 * c:10 * c + 8], pb[bk][0:128, 0:8])
                inc(v.tensor_scalar(centT[:, 10 * c + 8:10 * c + 10], pb[bk][0:128, 8:10],
                                    g1c[:, c:c + 1], None, ALU.mult), "eV")
            on("vector", v_c)
            P.bank_set(f"pb{bk}", "eV")

        def t_cs(t):
            wait_all(t, "eV")
            P.bank_wait(t, "pb2")
            for c in range(8):
                ins = t.matmul(pb[2][0:2, 0:1], centT[:, 10 * c + 8:10 * c + 10], onescol[:],
                               start=(c == 0), stop=(c == 7))
            inc(ins, "eT")
        on("tensor", t_cs)
        def v_cs(v, tok=P.cnt["eT"]):
            wait(v, "eT", tok)
            inc(v.tensor_scalar(e1[0:2, 0:1], pb[2][0:2, 0:1], -1.0, None, ALU.mult), "eV")
        on("vector", v_cs)
        P.bank_set("pb2", "eV")
        def t_cs2(t, tok=P.cnt["eV"]):
            wait(t, "eV", tok)
            P.bank_wait(t, "pb3")
            inc(t.transpose(pb[3][0:1, 0:2], e1[0:2, 0:1], idn[0:2, 0:2]), "eT")
        on("tensor", t_cs2)
        def v_cs2(v, tok=P.cnt["eT"]):
            wait(v, "eT", tok)
            inc(v.tensor_copy(e1[0:1, 4:6], pb[3][0:1, 0:2]), "eV")
        on("vector", v_cs2)
        P.bank_set("pb3", "eV")

        def s_lb(s):
            wait(s, "eT", P.cnt["eT"])
            inc(s.dma_start(out=vsm[:], in_=ln1b_d), "qS", 16)
        on("sync", s_lb)
        def t_lb(t, tok=P.cnt["qS"]):
            wait(t, "qS", tok)
            P.bank_wait(t, "pb0")
            inc(t.transpose(pb[0][0:128, 16:24], vsm[:], idn[0:8, 0:8]), "eT")
        on("tensor", t_lb)
        def v_lb(v, tok=P.cnt["eT"]):
            wait(v, "eT", tok)
            inc(v.tensor_copy(e2[:, 0:8], pb[0][0:128, 16:24]), "eV")
        on("vector", v_lb)
        P.bank_set("pb0", "eV")
        for c in range(8):
            bk = c % 2
            def t_wp(t, c=c, bk=bk):
                if c < 2:
                    wait_all(t, "eV")
                P.bank_wait(t, f"pb{bk}")
                inc(t.transpose(pb[bk][0:128, 12:14], wpotT[:, 128 * c:128 * (c + 1)], idn[0:2, 0:2]), "eT")
            on("tensor", t_wp)
            def v_wp(v, c=c, bk=bk, tok=P.cnt["eT"]):
                wait(v, "eT", tok)
                inc(v.tensor_copy(e3[:, 2 * (c % 4):2 * (c % 4) + 2], pb[bk][0:128, 12:14]), "eV")
            on("vector", v_wp)
            P.bank_set(f"pb{bk}", "eV")
            def t_be(t, c=c, tok=P.cnt["eV"]):
                wait(t, "eV", tok)
                if c == 0:
                    P.bank_wait(t, "pb4")
                inc(t.matmul(pb[4][0:1, 0:2], e2[:, c:c + 1], e3[:, 2 * (c % 4):2 * (c % 4) + 2],
                             start=(c == 0), stop=(c == 7)), "eT")
            on("tensor", t_be)
        def v_be(v, tok=P.cnt["eT"]):
            wait(v, "eT", tok)
            inc(v.tensor_tensor(beff_row[:], pb[4][0:1, 0:2], beff_row[:], ALU.add), "eV")
        on("vector", v_be)
        P.bank_set("pb4", "eV")
        def t_bc(t, tok=P.cnt["eV"]):
            wait(t, "eV", tok)
            P.bank_wait(t, "pb5")
            t.matmul(pb[5][0:128, 0:2], ones1f[:], e1[0:1, 4:6], start=True, stop=True)
            inc(t.matmul(pb[5][0:128, 2:4], ones1f[:], beff_row[:], start=True, stop=True), "eT")
        on("tensor", t_bc)
        def v_bc(v, tok=P.cnt["eT"]):
            wait(v, "eT", tok)
            v.tensor_copy(csneg[:], pb[5][0:128, 0:2])
            inc(v.tensor_copy(beffp[:], pb[5][0:128, 2:4]), "eV")
        on("vector", v_bc)
        P.bank_set("pb5", "eV")

        if upto == "prep":
            dc1 = dout("d_centT", [128, 80])
            dc2 = dout("d_csneg", [128, 2])
            dc3 = dout("d_beffp", [128, 2])
            dc4 = dout("d_g1c", [128, 8])
            dc5 = dout("d_e1row", [128, 40])
            dc6 = dout("d_beffrow", [128, 2])
            def g_cpb(gp):
                wait_all(gp)
                inc(gp.dma_start(out=dc6[0:1, :], in_=beff_row[:]), "qG", 16)
            on("gpsimd", g_cpb)
            def g_cp(gp):
                wait_all(gp)
                inc(gp.dma_start(out=dc1, in_=centT[:]), "qG", 16)
                inc(gp.dma_start(out=dc2, in_=csneg[:]), "qG", 16)
                inc(gp.dma_start(out=dc3, in_=beffp[:]), "qG", 16)
                inc(gp.dma_start(out=dc4, in_=g1c[:]), "qG", 16)
                inc(gp.dma_start(out=dc5, in_=e1[:]), "qG", 16)
                wait_all(gp, "qG")
            on("gpsimd", g_cp)
            P.barrier()
            return nc, dbg_outs

        P.scope("phaseA")
        xr = sb("xr", [128, 2 * D], st=esA)
        xr2 = sb("xr2", [128, 2 * D], st=esA)
        xnb = sb("xnb", [128, NT * D], BF16, st=esA)
        xn_t = sb("xn_t", [128, 2 * D], st=esA)
        out_t = sb("out_t", [128, 2 * D], st=esA)
        xnT = sb("xnT", [128, 8 * 512], st=esA)
        afffm = sb("afffm", [10, 512], st=esA)
        affs = sb("affs", [128, 16], st=esA)

        atok = {}
        xtok = {}
        def emit_a0_tile(i):
            sl = i % 2
            def s_x(s, i=i, sl=sl, tk0=P.cnt["qS"]):
                if i >= 2:
                    wait(s, "eA", atok[i - 2])
                else:
                    wait(s, "qS", tk0)
                inc(s.dma_start(out=xr[:, D * sl:D * (sl + 1)], in_=x_d[128 * i:128 * (i + 1), :]), "qS", 16)
            on("sync", s_x)
            xtok[i] = P.cnt["qS"]
            def a_st(a, i=i, sl=sl, tok=P.cnt["qS"]):
                wait(a, "qS", tok)
                a.activation(junkA[:], xr[:, D * sl:D * (sl + 1)], AF.Copy,
                             scale=INV_D, accum_out=m0a[:, i:i + 1])
                inc(a.activation(junkA2[:], xr[:, D * sl:D * (sl + 1)], AF.Square,
                                 scale=INV_SQ_D, accum_out=q0a[:, i:i + 1]), "eA")
            on("scalar", a_st)
            atok[i] = P.cnt["eA"]

        tokR0h = {}
        def emit_fin0(h):
            c0, c1 = 16 * h, 16 * (h + 1)
            def v_f0(v, c0=c0, c1=c1, tok=P.cnt["eA"]):
                wait(v, "eA", tok)
                ch(v, "eV", v.tensor_tensor(st1[:, c0:c1], m0a[:, c0:c1], m0a[:, c0:c1], ALU.mult))
                inc(v.tensor_tensor(st2[:, c0:c1], q0a[:, c0:c1], st1[:, c0:c1], ALU.subtract), "eV")
            on("vector", v_f0)
            def a_f0(a, c0=c0, c1=c1, tok=P.cnt["eV"]):
                wait(a, "eV", tok)
                inc(a.activation(st1[:, c0:c1], st2[:, c0:c1], AF.Sqrt, bias=epscol[:, 0:1]), "eA")
            on("scalar", a_f0)
            def v_f0b(v, c0=c0, c1=c1, tok=P.cnt["eA"]):
                wait(v, "eA", tok)
                inc(v.reciprocal(r0a[:, c0:c1], st1[:, c0:c1]), "eV")
            on("vector", v_f0b)
            tokR0h[h] = P.cnt["eV"]

        for i in range(16):
            emit_a0_tile(i)
        emit_fin0(0)

        if upto == "a0":
            for ii in range(16, NT):
                emit_a0_tile(ii)
            emit_fin0(1)
            dm0 = dout("d_m0", [128, NT]); dq0 = dout("d_q0", [128, NT]); dr0 = dout("d_r0", [128, NT])
            def g_cp0(gp):
                wait_all(gp)
                inc(gp.dma_start(out=dm0, in_=m0a[:]), "qG", 16)
                inc(gp.dma_start(out=dq0, in_=q0a[:]), "qG", 16)
                inc(gp.dma_start(out=dr0, in_=r0a[:]), "qG", 16)
                wait_all(gp, "qG")
            on("gpsimd", g_cp0)
            P.barrier()
            esA.close()
            return nc, dbg_outs

        qg_tok = {}
        a1_tok = {}
        def emit_a1_group(g):
            for j in range(4):
                i = 4 * g + j
                sl = i % 2
                xs = xn_t[:, D * sl:D * (sl + 1)]
                os_ = out_t[:, D * sl:D * (sl + 1)]
                xv = xr2[:, D * sl:D * (sl + 1)]
                def s_x1(s, i=i, sl=sl):
                    if i >= 2:
                        wait(s, "eA", a1_tok[i - 2])
                    inc(s.dma_start(out=xr2[:, D * sl:D * (sl + 1)],
                                    in_=x_d[128 * i:128 * (i + 1), :]), "qS", 16)
                on("sync", s_x1)
                tok_x = P.cnt["qS"]
                def v_ap(v, i=i, xs=xs, os_=os_, xv=xv, tok_x=tok_x):
                    if i in (0, 16):
                        wait(v, "eV", tokR0h[i // 16])
                    wait(v, "qS", tok_x)
                    if i >= 2:
                        wait(v, "qG", qg_tok[i - 2])
                    v.tensor_scalar(xs, xv, m0a[:, i:i + 1], r0a[:, i:i + 1],
                                    ALU.subtract, ALU.mult)
                    v.tensor_tensor(os_, xv, xs, ALU.add)
                    inc(v.tensor_copy(xnb[:, D * i:D * (i + 1)], xs), "eV")
                on("vector", v_ap)
                tokXN = P.cnt["eV"]
                def g_st(gp, i=i, xs=xs, os_=os_, tok=tokXN):
                    wait(gp, "eV", tok)
                    inc(gp.dma_start(out=out_d[128 * i:128 * (i + 1), :], in_=os_), "qG", 16)
                    inc(gp.dma_start(out=y_d[128 * i:128 * (i + 1), :], in_=xs), "qG", 16)
                on("gpsimd", g_st)
                qg_tok[i] = P.cnt["qG"]
                def a_s1(a, i=i, xs=xs, tok=tokXN):
                    wait(a, "eV", tok)
                    a.activation(junkA[:], xs, AF.Copy, scale=INV_D, accum_out=m1a[:, i:i + 1])
                    inc(a.activation(junkA2[:], xs, AF.Square, scale=INV_SQ_D,
                                     accum_out=q1a[:, i:i + 1]), "eA")
                on("scalar", a_s1)
                a1_tok[i] = P.cnt["eA"]
                for c in range(8):
                    bk = c % 4
                    def t_tp(t, c=c, bk=bk, xs=xs, tok=tokXN):
                        if c == 0:
                            wait(t, "eV", tok)
                        P.bank_wait(t, f"pb{bk}")
                        inc(t.transpose(pb[bk][:, 0:128], xs[:, 128 * c:128 * (c + 1)], idn[:]), "eT")
                    on("tensor", t_tp)
                    def v_ev(v, c=c, bk=bk, j=j, tok=P.cnt["eT"]):
                        wait(v, "eT", tok)
                        inc(v.tensor_copy(xnT[:, 512 * c + 128 * j:512 * c + 128 * (j + 1)],
                                          pb[bk][:, 0:128]), "eV")
                    on("vector", v_ev)
                    P.bank_set(f"pb{bk}", "eV")
            tokGrp = P.cnt["eV"]
            if noaff:
                return
            def t_aff(t, tok=tokGrp):
                wait(t, "eV", tok)
                P.bank_wait(t, "pb4")
                for c in range(8):
                    ins = t.matmul(pb[4][0:10, 0:512], centT[:, 10 * c:10 * (c + 1)],
                                   xnT[:, 512 * c:512 * (c + 1)], start=(c == 0), stop=(c == 7))
                inc(ins, "eT")
            on("tensor", t_aff)
            def v_affev(v, tok=P.cnt["eT"]):
                wait(v, "eT", tok)
                inc(v.tensor_copy(afffm[:], pb[4][0:10, 0:512]), "eV")
            on("vector", v_affev)
            P.bank_set("pb4", "eV")
            for j in range(4):
                bk = 4 + j
                def t_at(t, j=j, bk=bk, tok=P.cnt["eV"]):
                    if j == 0:
                        wait(t, "eV", tok)
                    P.bank_wait(t, f"pb{bk}")
                    inc(t.transpose(pb[bk][0:128, 0:10], afffm[0:10, 128 * j:128 * (j + 1)],
                                    idn[0:10, 0:10]), "eT")
                on("tensor", t_at)
            def v_ep4(v, g=g, tok=P.cnt["eT"]):
                wait(v, "eT", tok)
                for j in range(4):
                    v.tensor_copy(e2[:, 10 * j:10 * j + 10], pb[4 + j][0:128, 0:10])
                for j in range(4):
                    i = 4 * g + j
                    v.tensor_copy(potraw[:, 2 * i:2 * i + 2], e2[:, 10 * j + 8:10 * j + 10])
                for j in range(4):
                    i = 4 * g + j
                    v.reduce_max(amax_a[:, i:i + 1], e2[:, 10 * j:10 * j + 8], axis=AX.X)
                for j in range(4):
                    i = 4 * g + j
                    v.tensor_scalar(e1[:, 8 * j:8 * (j + 1)], e2[:, 10 * j:10 * j + 8],
                                    amax_a[:, i:i + 1], None, ALU.is_equal)
                for j in range(4):
                    v.tensor_scalar(e1[:, 8 * j:8 * (j + 1)], e1[:, 8 * j:8 * (j + 1)],
                                    -1e9, 1e9, ALU.mult, ALU.add)
                for j in range(4):
                    v.tensor_tensor(e1[:, 8 * j:8 * (j + 1)], e1[:, 8 * j:8 * (j + 1)],
                                    iota8[:], ALU.add)
                ins = None
                for j in range(4):
                    i = 4 * g + j
                    ins = v.tensor_reduce(eid_a[:, i:i + 1], e1[:, 8 * j:8 * (j + 1)],
                                          axis=AX.X, op=ALU.min)
                inc(ins, "eV")
            on("vector", v_ep4)
            for j in range(4):
                P.bank_set(f"pb{4 + j}", "eV")

        for g in range(4):
            for ii in range(16 + 4 * g, 20 + 4 * g):
                emit_a0_tile(ii)
            emit_a1_group(g)
        emit_fin0(1)
        for g in range(4, 8):
            emit_a1_group(g)

        def a_al(a, tok=P.cnt["eV"]):
            wait(a, "eV", tok)
            inc(a.activation(alpha_a[:], amax_a[:], AF.Sigmoid), "eA")
        on("scalar", a_al)

        def v_f1(v, tok=P.cnt["eA"]):
            wait(v, "eA", tok)
            ch(v, "eV", v.tensor_tensor(st1[:], m1a[:], m1a[:], ALU.mult))
            inc(v.tensor_tensor(st2[:], q1a[:], st1[:], ALU.subtract), "eV")
        on("vector", v_f1)
        def a_f1(a, tok=P.cnt["eV"]):
            wait(a, "eV", tok)
            inc(a.activation(st1[:], st2[:], AF.Sqrt, bias=epscol[:, 0:1]), "eA")
        on("scalar", a_f1)
        def v_f1b(v, tok=P.cnt["eA"]):
            wait(v, "eA", tok)
            ch(v, "eV", v.reciprocal(r1a[:], st1[:]))
            inc(v.tensor_tensor(m1r1a[:], m1a[:], r1a[:], ALU.mult), "eV")
        on("vector", v_f1b)

        def v_pot(v, tok=P.cnt["eV"]):
            wait(v, "eV", tok)
            for cpos, dst in ((0, V_a), (1, e3)):
                raw = potraw.rearrange("p (i c) -> p c i", c=2)[:, cpos, :]
                ch(v, "eV", v.tensor_tensor(e1[:, 0:NT], raw, r1a[:], ALU.mult))
                ch(v, "eV", v.scalar_tensor_tensor(e2[:, 0:NT], m1r1a[:], csneg[:, cpos:cpos + 1],
                                                   e1[:, 0:NT], ALU.mult, ALU.add))
                inc(v.tensor_scalar(dst[:, 0:NT], e2[:, 0:NT], beffp[:, cpos:cpos + 1],
                                    None, ALU.add), "eV")
        on("vector", v_pot)
        def a_g1(a, tok=P.cnt["eV"]):
            wait(a, "eV", tok)
            ch(a, "eA", a.activation(e1[:, 0:NT], e3[:, 0:NT], AF.Abs))
            ch(a, "eA", a.activation(e1[:, 0:NT], e1[:, 0:NT], AF.Exp, scale=-1.0))
            inc(a.activation(e1[:, 0:NT], e1[:, 0:NT], AF.Ln, bias=1.0), "eA")
        on("scalar", a_g1)
        def v_g2(v, tok=P.cnt["eA"]):
            wait(v, "eA", tok)
            ch(v, "eV", v.tensor_scalar(e2[:, 0:NT], e3[:, 0:NT], 0.0, None, ALU.max))
            ch(v, "eV", v.tensor_tensor(gam_a[:], e1[:, 0:NT], e2[:, 0:NT], ALU.add))
            inc(v.tensor_scalar(gam_a[:], gam_a[:], GAMMA_FLOOR, None, ALU.add), "eV")
        on("vector", v_g2)

        if upto == "a1":
            deid = dout("d_eid", [128, NT]); dalpha = dout("d_alpha", [128, NT])
            dgam = dout("d_gam", [128, NT]); dV = dout("d_V", [128, NT])
            dxnT = dout("d_xnT", [128, 8 * 512])
            dxnb = dout("d_xnb", [128, 4 * D], BF16)
            dafffm = dout("d_afffm", [128, 512])
            dpraw = dout("d_praw", [128, 2 * NT])
            daffs = dout("d_affs", [128, 16])
            def g_cp1(gp):
                wait_all(gp)
                inc(gp.dma_start(out=dxnT, in_=xnT[:]), "qG", 16)
                inc(gp.dma_start(out=dxnb, in_=xnb[:, 28 * D:32 * D]), "qG", 16)
                inc(gp.dma_start(out=dpraw, in_=potraw[:]), "qG", 16)
                inc(gp.dma_start(out=daffs, in_=affs[:]), "qG", 16)
                def v_c1(v):
                    v.memset(junkV[:, 0:512], 0.0)
                    inc(v.tensor_copy(junkV[0:10, 0:512], afffm[:]), "eV")
                on("vector", v_c1)
                wait_all(gp, "eV")
                inc(gp.dma_start(out=dafffm, in_=junkV[:, 0:512]), "qG", 16)
                inc(gp.dma_start(out=deid, in_=eid_a[:]), "qG", 16)
                inc(gp.dma_start(out=dalpha, in_=alpha_a[:]), "qG", 16)
                inc(gp.dma_start(out=dgam, in_=gam_a[:]), "qG", 16)
                inc(gp.dma_start(out=dV, in_=V_a[:]), "qG", 16)
                wait_all(gp, "qG")
            on("gpsimd", g_cp1)
            P.barrier()
            esA.close()
            return nc, dbg_outs

        P.scope("rank")
        esR = contextlib.ExitStack()
        masks8 = sb("masks8", [8, T], st=esR)
        cum8 = sb("cum8", [8, T], st=esR)
        s8b = sb("s8b", [8, T], st=esR)
        slotrow = rowA

        def t_eT(t, tok=P.cnt["eV"]):
            wait(t, "eV", tok)
            P.bank_wait(t, "pb0")
            inc(t.transpose(pb[0][0:32, 0:128], eid_a[:], idn[:]), "eT")
        on("tensor", t_eT)
        def v_eT(v, tok=P.cnt["eT"]):
            wait(v, "eT", tok)
            inc(v.tensor_copy(vsm32[:], pb[0][0:32, 0:128]), "eV")
        on("vector", v_eT)
        P.bank_set("pb0", "eV")
        def g_eT(gp, tok=P.cnt["eV"]):
            wait(gp, "eV", tok)
            inc(gp.dma_start(out=row_d[0, :].rearrange("(a b) -> a b", a=32), in_=vsm32[:]), "qG", 16)
        on("gpsimd", g_eT)
        def s_er(s, tok=P.cnt["qG"]):
            wait(s, "qG", tok)
            inc(s.dma_start(out=rowA[:], in_=row_d[0:1, :]), "qS", 16)
        on("sync", s_er)
        tok_eidrow = P.cnt["qS"]
        for n in range(8):
            bk = n % 2
            def t_b8(t, n=n, bk=bk):
                if n == 0:
                    wait(t, "qS", tok_eidrow)
                P.bank_wait(t, f"pb{bk}")
                inc(t.matmul(pb[bk][0:8, 0:512], ones1f[0:1, 0:8],
                             rowA[0:1, 512 * n:512 * (n + 1)], start=True, stop=True), "eT")
            on("tensor", t_b8)
            def v_m8(v, n=n, bk=bk, tok=P.cnt["eT"]):
                wait(v, "eT", tok)
                inc(v.tensor_scalar(masks8[:, 512 * n:512 * (n + 1)], pb[bk][0:8, 0:512],
                                    colpack[0:8, 0:1], None, ALU.is_equal), "eV")
            on("vector", v_m8)
            P.bank_set(f"pb{bk}", "eV")
        def v_rk(v, tok=P.cnt["eV"]):
            wait(v, "eV", tok)
            ch(v, "eV", v.tensor_tensor_scan(cum8[:], masks8[:], masks8[:], 0.0, ALU.add, ALU.bypass))
            ch(v, "eV", v.tensor_scalar(s8b[:], cum8[:], CAP + 0.5, None, ALU.is_gt))
            ch(v, "eV", v.tensor_scalar(cum8[:], cum8[:], colpack[0:8, 1:2], -1.0, ALU.add, ALU.add))
            ch(v, "eV", v.scalar_tensor_tensor(s8b[:], s8b[:], 1e7, cum8[:], ALU.mult, ALU.add))
            inc(v.tensor_tensor(masks8[:], s8b[:], masks8[:], ALU.mult), "eV")
        on("vector", v_rk)
        for n in range(8):
            bk = n % 2
            def t_sl(t, n=n, bk=bk, tok=P.cnt["eV"]):
                if n == 0:
                    wait(t, "eV", tok)
                P.bank_wait(t, f"pb{bk}")
                inc(t.matmul(pb[bk][0:1, 0:512], ones8c[:], masks8[:, 512 * n:512 * (n + 1)],
                             start=True, stop=True), "eT")
            on("tensor", t_sl)
            def v_sl(v, n=n, bk=bk, tok=P.cnt["eT"]):
                wait(v, "eT", tok)
                inc(v.tensor_copy(slotrow[0:1, 512 * n:512 * (n + 1)], pb[bk][0:1, 0:512]), "eV")
            on("vector", v_sl)
            P.bank_set(f"pb{bk}", "eV")
        for i in range(NT):
            bk = 2 + (i % 2)
            def t_st(t, i=i, bk=bk, tok=P.cnt["eV"]):
                if i == 0:
                    wait(t, "eV", tok)
                P.bank_wait(t, f"pb{bk}")
                inc(t.transpose(pb[bk][0:128, 0:1], slotrow[0:1, 128 * i:128 * (i + 1)],
                                idn[0:1, 0:1]), "eT")
            on("tensor", t_st)
            def v_su(v, i=i, bk=bk, tok=P.cnt["eT"]):
                wait(v, "eT", tok)
                inc(v.tensor_copy(slotu[:, i:i + 1], pb[bk][0:128, 0:1]), "eV")
            on("vector", v_su)
            P.bank_set(f"pb{bk}", "eV")
        tokSlots = P.cnt["eV"]

        if upto == "a2a":
            dslot = dout("d_slot", [128, NT], U32)
            def g_cp2(gp):
                wait_all(gp)
                inc(gp.dma_start(out=dslot, in_=slotu[:]), "qG", 16)
                wait_all(gp, "qG")
            on("gpsimd", g_cp2)
            P.barrier()
            esR.close()
            esA.close()
            return nc, dbg_outs

        xlnr = xn_t[:].bitcast(BF16)[:, 0:2 * D]
        sc_tok = {}
        for i in range(NT):
            sl = i % 2
            def v_xl(v, i=i, sl=sl):
                if i == 0:
                    wait(v, "eV", tokSlots)
                    wait(v, "qG", qg_tok[NT - 1])
                    wait(v, "eA", a1_tok[NT - 1])
                if i >= 2:
                    wait(v, "qG", sc_tok[i - 2])
                inc(v.tensor_scalar(xlnr[:, D * sl:D * (sl + 1)], xnb[:, D * i:D * (i + 1)],
                                    m1a[:, i:i + 1], r1a[:, i:i + 1],
                                    ALU.subtract, ALU.mult), "eV")
            on("vector", v_xl)
            def g_sc(gp, i=i, sl=sl, tok=P.cnt["eV"]):
                wait(gp, "eV", tok)
                inc(gp.indirect_dma_start(
                    out=xs_d, out_offset=bass.IndirectOffsetOnAxis(ap=slotu[:, i:i + 1], axis=0),
                    in_=xlnr[:, D * sl:D * (sl + 1)], in_offset=None,
                    bounds_check=BREG[0], oob_is_err=False), "qG", 16)
            on("gpsimd", g_sc)
            sc_tok[i] = P.cnt["qG"]

        if dbg >= 1:
            deid = dout("d_eid", [128, NT])
            dalpha = dout("d_alpha", [128, NT])
            dgam = dout("d_gam", [128, NT])
            dV = dout("d_V", [128, NT])
            dslot = dout("d_slot", [128, NT], U32)
            dxs = dout("d_xs", [NS, D], BF16)
            dxnb = dout("d_xnb", [T, D], BF16)
            def g_d3(gp):
                wait_all(gp)
                inc(gp.dma_start(out=deid, in_=eid_a[:]), "qG", 16)
                inc(gp.dma_start(out=dalpha, in_=alpha_a[:]), "qG", 16)
                inc(gp.dma_start(out=dgam, in_=gam_a[:]), "qG", 16)
                inc(gp.dma_start(out=dV, in_=V_a[:]), "qG", 16)
                inc(gp.dma_start(out=dslot, in_=slotu[:]), "qG", 16)
                inc(gp.dma_start(out=dxs, in_=xs_d), "qG", 16)
                for i in range(NT):
                    inc(gp.dma_start(out=dxnb[128 * i:128 * (i + 1), :],
                                     in_=xnb[:, D * i:D * (i + 1)]), "qG", 16)
                wait_all(gp, "qG")
            on("gpsimd", g_d3)
            P.barrier()
            esR.close()
            esA.close()
            return nc, dbg_outs


        P.scope("scan")
        P.barrier()
        esR.close()
        esA.close()
        esB = contextlib.ExitStack()
        esBH = contextlib.ExitStack()
        esB2 = contextlib.ExitStack()
        esB1 = contextlib.ExitStack()

        R_DRE, R_DIM, R_DRER, R_DIMR, R_T1, R_T2, R_T3, R_T4 = 1, 2, 3, 4, 5, 6, 7, 8
        R_GR, R_GI, R_EM, R_EP, R_GC = 9, 10, 11, 12, 13

        gfm = sb("gfm", [2, T], BF16, st=esB)
        gch = sb("gch", [32, 128], st=esB)
        ghat = sb("ghat", [32, 128], st=esB)
        epx = sb("epx", [32, 128], st=esB)
        emx = sb("emx", [32, 128], st=esB)
        epT = sb("epT", [128, 32], st=esB)
        emT = sb("emT", [128, 32], st=esB)
        gcbc = sb("gcbc", [128, 32], st=esB)
        m2a = sb("m2a", [128, NT], st=esB); q2a = sb("q2a", [128, NT], st=esB)
        r2a = sb("r2a", [128, NT], st=esB)
        m3a = sb("m3a", [128, NT], st=esB); q3a = sb("q3a", [128, NT], st=esB)
        r3a = sb("r3a", [128, NT], st=esB)
        hebfm = sb("hebfm", [128, 4 * T], BF16, st=esBH)
        wbkb = sb("wbkb", [2, D], BF16, st=esB2)
        bbk_bc = sb("bbk_bc", [128, D], st=esB2)
        bbk_row = sb("bbk_row", [1, D], st=esB2)
        embc = sb("embc", [128, T], BF16, st=esB2)
        epbc = sb("epbc", [128, T], BF16, st=esB2)
        yring = sb("yring", [128, 3 * D], st=esB2)

        dreF = sb("dreF", [128, 32], st=esB1); dimF = sb("dimF", [128, 32], st=esB1)
        ngiF = sb("ngiF", [128, 32], st=esB1)
        dreB = sb("dreB", [128, 32], st=esB1); dimB = sb("dimB", [128, 32], st=esB1)
        ngiB = sb("ngiB", [128, 32], st=esB1)
        s_re = [sb(f"s_re{k}", [128, 2], st=esB1) for k in range(3)]
        s_im = [sb(f"s_im{k}", [128, 2], st=esB1) for k in range(3)]
        c_re = [sb(f"c_re{k}", [128, 2], st=esB1) for k in range(3)]
        c_im = [sb(f"c_im{k}", [128, 2], st=esB1) for k in range(3)]
        tcomp = sb("tcomp", [128, 8], st=esB1)
        sb8 = sb("sb8", [16, 64], st=esB1); sb8n = sb("sb8n", [16, 64], st=esB1)
        rst = [sb(f"rst{k}", [16, 8], st=esB1) for k in range(2)]
        srow = sb("srow", [1, 128], st=esB1); srown = sb("srown", [1, 128], st=esB1)
        psup = sb("psup", [1, 64], st=esB1)
        ps16 = [sb(f"ps16_{k}", [16, 4], st=esB1) for k in range(2)]
        pblk = sb("pblk", [16, 32], st=esB1)
        pb128 = sb("pb128", [128, 4], st=esB1)
        NRf = sb("NRf", [128, 33], st=esB1); NIf = sb("NIf", [128, 33], st=esB1)
        DRf = sb("DRf", [128, 33], st=esB1); DIf = sb("DIf", [128, 33], st=esB1)
        NRb = sb("NRb", [128, 33], st=esB1); NIb = sb("NIb", [128, 33], st=esB1)
        DRb = sb("DRb", [128, 33], st=esB1); DIb = sb("DIb", [128, 33], st=esB1)
        bre = sb("bre", [128, 32], st=esB1); bim = sb("bim", [128, 32], st=esB1)
        bre2 = sb("bre2", [128, 32], st=esB1); bim2 = sb("bim2", [128, 32], st=esB1)
        gsc = [sb(f"gsc{k}", [128, 32], st=esB1) for k in range(6)]
        tinyc = sb("tinyc", [128, 1], st=esB1)
        rowB = sb("rowB", [1, T], st=esB1)
        s_reB = [sb(f"s_reB{k}", [128, 2], st=esB1) for k in range(2)]
        s_imB = [sb(f"s_imB{k}", [128, 2], st=esB1) for k in range(2)]
        c_reB = [sb(f"c_reB{k}", [128, 2], st=esB1) for k in range(2)]
        c_imB = [sb(f"c_imB{k}", [128, 2], st=esB1) for k in range(2)]
        tcompB = sb("tcompB", [128, 8], st=esB1)
        sb8B = sb("sb8B", [16, 64], st=esB1); sb8nB = sb("sb8nB", [16, 64], st=esB1)
        rstB = [sb(f"rstB{k}", [16, 8], st=esB1) for k in range(2)]
        srowB = sb("srowB", [1, 128], st=esB1); srownB = sb("srownB", [1, 128], st=esB1)
        psupB = sb("psupB", [1, 64], st=esB1)
        ps16B = [sb(f"ps16B_{k}", [16, 4], st=esB1) for k in range(2)]
        pblkB = sb("pblkB", [16, 32], st=esB1)
        pb128B = sb("pb128B", [128, 4], st=esB1)

        def v_db(v):
            wait_all(v, "qG", "eA")
            ch(v, "eV", v.tensor_scalar(e1[:, 0:NT], V_a[:], 2.0, None, ALU.add))
            ch(v, "eV", v.tensor_scalar(e2[:, 0:NT], gam_a[:], -1.0, None, ALU.mult))
        on("vector", v_db)
        for (til, row) in ((e1, R_DRE), (e2, R_DIM)):
            def t_tr(t, til=til, tok=P.cnt["eV"]):
                wait(t, "eV", tok)
                P.bank_wait(t, "pb7")
                inc(t.transpose(pb[7][0:32, 0:128], til[:, 0:NT], idn[:]), "eT")
            on("tensor", t_tr)
            def v_tr(v, tok=P.cnt["eT"], qg=P.cnt["qG"]):
                wait(v, "eT", tok)
                wait(v, "qG", qg)
                inc(v.tensor_copy(vsm32[:], pb[7][0:32, 0:128]), "eV")
            on("vector", v_tr)
            P.bank_set("pb7", "eV")
            def g_tr(gp, row=row, tok=P.cnt["eV"]):
                wait(gp, "eV", tok)
                inc(gp.dma_start(out=row_d[row, :].rearrange("(a b) -> a b", a=32), in_=vsm32[:]), "qG", 16)
            on("gpsimd", g_tr)
        for (row, rrow) in ((R_DRE, R_DRER), (R_DIM, R_DIMR)):
            def s_rv(s, row=row, tok=P.cnt["qG"]):
                wait(s, "qG", tok)
                inc(s.dma_start(out=rowA[:], in_=row_d[row:row + 1, :]), "qS", 16)
            on("sync", s_rv)
            def v_rv(v, tok=P.cnt["qS"]):
                wait(v, "qS", tok)
                inc(v.tensor_copy(rowB[:], rowA[0:1, ::-1]), "eV")
            on("vector", v_rv)
            def g_rv(gp, rrow=rrow, tok=P.cnt["eV"]):
                wait(gp, "eV", tok)
                inc(gp.dma_start(out=row_d[rrow:rrow + 1, :], in_=rowB[:]), "qG", 16)
            on("gpsimd", g_rv)
        def s_bl(s, tok=P.cnt["qG"]):
            wait(s, "qG", tok)
            inc(s.dma_start(out=dreF[:], in_=row_d[R_DRE, :].rearrange("(b j) -> b j", b=128)), "qS", 16)
            inc(s.dma_start(out=dimF[:], in_=row_d[R_DIM, :].rearrange("(b j) -> b j", b=128)), "qS", 16)
            inc(s.dma_start(out=dreB[:], in_=row_d[R_DRER, :].rearrange("(b j) -> b j", b=128)), "qS", 16)
            inc(s.dma_start(out=dimB[:], in_=row_d[R_DIMR, :].rearrange("(b j) -> b j", b=128)), "qS", 16)
            inc(s.dma_start(out=gch[:], in_=row_d[R_DIM, :].rearrange("(c p) -> c p", c=32)), "qS", 16)
        on("sync", s_bl)
        def v_ng(v, tok=P.cnt["qS"]):
            wait(v, "qS", tok)
            ch(v, "eV", v.tensor_scalar(ngiF[:], dimF[:], -1.0, None, ALU.mult))
            ch(v, "eV", v.tensor_scalar(ngiB[:], dimB[:], -1.0, None, ALU.mult))
            ch(v, "eV", v.tensor_scalar(gch[:], gch[:], -DTC, None, ALU.mult))
        on("vector", v_ng)

        def cf_scan2(cfgs):
            def v_i0(v):
                wait_all(v, "eV")
                for cf in cfgs:
                    for k in range(2):
                        v.memset(cf["s_re"][k][:], 0.0); v.memset(cf["s_im"][k][:], 0.0)
                        v.memset(cf["c_re"][k][:], 0.0); v.memset(cf["c_im"][k][:], 0.0)
                    v.memset(cf["s_re"][0][:, 0:1], 1.0)
                    ins = v.memset(cf["c_re"][0][:, 1:2], 1.0)
                inc(ins, "eV")
            on("vector", v_i0)
            ab = 0
            for k in range(32):
                fr = 1 - ab
                def v_st(v, k=k, ab=ab, fr=fr):
                    for cf in cfgs:
                        s_re, s_im = cf["s_re"], cf["s_im"]
                        dcol = cf["dre"][:, k:k + 1]
                        v.scalar_tensor_tensor(s_re[fr][:], s_re[ab][:], dcol, cf["c_re"][ab][:], ALU.mult, ALU.subtract)
                        ins = v.scalar_tensor_tensor(s_im[fr][:], s_im[ab][:], dcol, cf["c_im"][ab][:], ALU.mult, ALU.subtract)
                    for cf in cfgs:
                        s_re, s_im = cf["s_re"], cf["s_im"]
                        icol = cf["dim"][:, k:k + 1]; ncol = cf["ngi"][:, k:k + 1]
                        v.scalar_tensor_tensor(s_re[fr][:], s_im[ab][:], ncol, s_re[fr][:], ALU.mult, ALU.add)
                        ins = v.scalar_tensor_tensor(s_im[fr][:], s_re[ab][:], icol, s_im[fr][:], ALU.mult, ALU.add)
                    for cf in cfgs:
                        v.tensor_copy(cf["c_re"][fr][:], cf["s_re"][ab][:])
                        ins = v.tensor_copy(cf["c_im"][fr][:], cf["s_im"][ab][:])
                    if k % 8 == 7:
                        for ci, cf in enumerate(cfgs):
                            e1c = e1[:, 4 * ci:4 * ci + 4]
                            v.tensor_tensor(e1c[:, 0:1], cf["s_re"][fr][:, 0:1], cf["s_re"][fr][:, 0:1], ALU.mult)
                            ins = v.tensor_tensor(e1c[:, 1:2], cf["s_im"][fr][:, 0:1], cf["s_im"][fr][:, 0:1], ALU.mult)
                        for ci, cf in enumerate(cfgs):
                            e1c = e1[:, 4 * ci:4 * ci + 4]
                            ins = v.scalar_tensor_tensor(e1c[:, 1:2], e1c[:, 1:2], 1e-30, e1c[:, 0:1], ALU.add, ALU.add)
                        for ci, cf in enumerate(cfgs):
                            e1c = e1[:, 4 * ci:4 * ci + 4]
                            ins = v.reciprocal(e1c[:, 2:3], e1c[:, 1:2])
                        for ci, cf in enumerate(cfgs):
                            e1c = e1[:, 4 * ci:4 * ci + 4]
                            for z in (cf["s_re"][fr], cf["s_im"][fr], cf["c_re"][fr], cf["c_im"][fr]):
                                ins = v.tensor_scalar(z[:], z[:], e1c[:, 2:3], None, ALU.mult)
                on("vector", v_st)
                ab = fr
            def v_tc(v, ab=ab):
                wait(v, "eV", P.cnt["eV"])
                for cf in cfgs:
                    v.tensor_copy(cf["tcomp"][:, 0:2], cf["s_re"][ab][:])
                    v.tensor_copy(cf["tcomp"][:, 2:4], cf["s_im"][ab][:])
                    v.tensor_copy(cf["tcomp"][:, 4:6], cf["c_re"][ab][:])
                    ins = v.tensor_copy(cf["tcomp"][:, 6:8], cf["c_im"][ab][:])
                inc(ins, "eV")
            on("vector", v_tc)
            def g_tc(gp, tok=P.cnt["eV"]):
                wait(gp, "eV", tok)
                for ci, cf in enumerate(cfgs):
                    inc(gp.dma_start(out=row_d[R_T1 + ci, 0:1024].rearrange("(a b) -> a b", a=128),
                                     in_=cf["tcomp"][:]), "qG", 16)
            on("gpsimd", g_tc)
            def s_tc(s, tok=P.cnt["qG"]):
                wait(s, "qG", tok)
                for ci, cf in enumerate(cfgs):
                    inc(s.dma_start(out=cf["sb8"][:], in_=row_d[R_T1 + ci, 0:1024].rearrange("(s k) -> s k", s=16)), "qS", 16)
            on("sync", s_tc)
            def v_n8(v, tok=P.cnt["qS"]):
                wait(v, "qS", tok)
                for cf in cfgs:
                    ins = v.tensor_scalar(cf["sb8n"][:], cf["sb8"][:], -1.0, None, ALU.mult)
                for cf in cfgs:
                    v.memset(cf["rst"][0][:], 0.0)
                    v.memset(cf["rst"][0][:, 0:1], 1.0)
                    ins = v.memset(cf["rst"][0][:, 5:6], 1.0)
                inc(ins, "eV")
            on("vector", v_n8)
            cur = 0
            for k in range(8):
                def v_l2(v, k=k, cur=cur):
                    o = 8 * k
                    for slot in range(4):
                        for cf in cfgs:
                            sb8_, sb8n_ = cf["sb8"], cf["sb8n"]
                            R = cf["rst"][cur]; O = cf["rst"][1 - cur]
                            a_r = sb8_[:, o + 0:o + 1]; b_r = sb8_[:, o + 1:o + 2]
                            a_i = sb8_[:, o + 2:o + 3]; b_i = sb8_[:, o + 3:o + 4]
                            c_r = sb8_[:, o + 4:o + 5]; d_r = sb8_[:, o + 5:o + 6]
                            c_i = sb8_[:, o + 6:o + 7]; d_i = sb8_[:, o + 7:o + 8]
                            na_r = sb8n_[:, o + 2:o + 3]; nb_r = sb8n_[:, o + 3:o + 4]
                            nc_r = sb8n_[:, o + 6:o + 7]; nd_r = sb8n_[:, o + 7:o + 8]
                            for sub in range(4):
                                oc = (0, 1, 4, 5)[sub]
                                m1r, m1i, n1i = ((a_r, a_i, na_r), (a_r, a_i, na_r),
                                                 (c_r, c_i, nc_r), (c_r, c_i, nc_r))[sub]
                                m2r, m2i, n2i = ((b_r, b_i, nb_r), (b_r, b_i, nb_r),
                                                 (d_r, d_i, nd_r), (d_r, d_i, nd_r))[sub]
                                x_r, x_i = ((R[:, 0:1], R[:, 2:3]), (R[:, 1:2], R[:, 3:4]),
                                            (R[:, 0:1], R[:, 2:3]), (R[:, 1:2], R[:, 3:4]))[sub]
                                y_r, y_i = ((R[:, 4:5], R[:, 6:7]), (R[:, 5:6], R[:, 7:8]),
                                            (R[:, 4:5], R[:, 6:7]), (R[:, 5:6], R[:, 7:8]))[sub]
                                dst = O[:, oc:oc + 1]; dsti = O[:, oc + 2:oc + 3]
                                if slot == 0:
                                    v.tensor_scalar(dst, x_r, m1r, None, ALU.mult)
                                    ins = v.tensor_scalar(dsti, x_i, m1r, None, ALU.mult)
                                elif slot == 1:
                                    v.scalar_tensor_tensor(dst, x_i, n1i, dst, ALU.mult, ALU.add)
                                    ins = v.scalar_tensor_tensor(dsti, x_r, m1i, dsti, ALU.mult, ALU.add)
                                elif slot == 2:
                                    v.scalar_tensor_tensor(dst, y_r, m2r, dst, ALU.mult, ALU.add)
                                    ins = v.scalar_tensor_tensor(dsti, y_i, m2r, dsti, ALU.mult, ALU.add)
                                else:
                                    v.scalar_tensor_tensor(dst, y_i, n2i, dst, ALU.mult, ALU.add)
                                    ins = v.scalar_tensor_tensor(dsti, y_r, m2i, dsti, ALU.mult, ALU.add)
                    for ci, cf in enumerate(cfgs):
                        O = cf["rst"][1 - cur]
                        e1c = e1[0:16, 8 + 4 * ci:12 + 4 * ci]
                        v.tensor_tensor(e1c[:, 0:1], O[:, 0:1], O[:, 0:1], ALU.mult)
                        ins = v.tensor_tensor(e1c[:, 1:2], O[:, 2:3], O[:, 2:3], ALU.mult)
                    for ci, cf in enumerate(cfgs):
                        e1c = e1[0:16, 8 + 4 * ci:12 + 4 * ci]
                        ins = v.scalar_tensor_tensor(e1c[:, 1:2], e1c[:, 1:2], 1e-30, e1c[:, 0:1], ALU.add, ALU.add)
                    for ci, cf in enumerate(cfgs):
                        e1c = e1[0:16, 8 + 4 * ci:12 + 4 * ci]
                        ins = v.reciprocal(e1c[:, 2:3], e1c[:, 1:2])
                    for ci, cf in enumerate(cfgs):
                        O = cf["rst"][1 - cur]
                        e1c = e1[0:16, 8 + 4 * ci:12 + 4 * ci]
                        ins = v.tensor_scalar(O[:], O[:], e1c[:, 2:3], None, ALU.mult)
                    if k == 7:
                        inc(ins, "eV")
                on("vector", v_l2)
                cur = 1 - cur
            def g_sc(gp, cur=cur, tok=P.cnt["eV"]):
                wait(gp, "eV", tok)
                for ci, cf in enumerate(cfgs):
                    inc(gp.dma_start(out=row_d[R_T1 + ci, 0:128].rearrange("(a b) -> a b", a=16),
                                     in_=cf["rst"][cur][:]), "qG", 16)
            on("gpsimd", g_sc)
            def s_sc(s, tok=P.cnt["qG"]):
                wait(s, "qG", tok)
                for ci, cf in enumerate(cfgs):
                    inc(s.dma_start(out=cf["srow"][:], in_=row_d[R_T1 + ci:R_T1 + ci + 1, 0:128]), "qS", 16)
            on("sync", s_sc)
            def v_sn(v, tok=P.cnt["qS"]):
                wait(v, "qS", tok)
                for cf in cfgs:
                    ins = v.tensor_scalar(cf["srown"][:], cf["srow"][:], -1.0, None, ALU.mult)
                for ci in range(len(cfgs)):
                    v.memset(e2[0:1, 12 * ci:12 * ci + 12], 0.0)
                    ins = v.memset(e2[0:1, 12 * ci:12 * ci + 1], 1.0)
                inc(ins, "eV")
            on("vector", v_sn)
            for s_ in range(16):
                def v_ap2(v, s_=s_):
                    o = 8 * s_
                    for slot in range(4):
                        for ci, cf in enumerate(cfgs):
                            srow_, srown_ = cf["srow"], cf["srown"]
                            st = e2[0:1, 12 * ci:12 * ci + 4]; nw = e2[0:1, 12 * ci + 4:12 * ci + 8]
                            a_r = srow_[0:1, o + 0:o + 1]; b_r = srow_[0:1, o + 1:o + 2]
                            a_i = srow_[0:1, o + 2:o + 3]; b_i = srow_[0:1, o + 3:o + 4]
                            c_r = srow_[0:1, o + 4:o + 5]; d_r = srow_[0:1, o + 5:o + 6]
                            c_i = srow_[0:1, o + 6:o + 7]; d_i = srow_[0:1, o + 7:o + 8]
                            na_i = srown_[0:1, o + 2:o + 3]; nb_i = srown_[0:1, o + 3:o + 4]
                            nc_i = srown_[0:1, o + 6:o + 7]; nd_i = srown_[0:1, o + 7:o + 8]
                            n_r, n_i, d_rr, d_ii = st[:, 0:1], st[:, 1:2], st[:, 2:3], st[:, 3:4]
                            if slot == 0:
                                v.tensor_copy(cf["psup"][0:1, 4 * s_:4 * s_ + 4], st)
                                v.tensor_scalar(nw[:, 0:1], n_r, a_r, None, ALU.mult)
                                v.tensor_scalar(nw[:, 1:2], n_i, a_r, None, ALU.mult)
                                v.tensor_scalar(nw[:, 2:3], n_r, c_r, None, ALU.mult)
                                ins = v.tensor_scalar(nw[:, 3:4], n_i, c_r, None, ALU.mult)
                            elif slot == 1:
                                v.scalar_tensor_tensor(nw[:, 0:1], n_i, na_i, nw[:, 0:1], ALU.mult, ALU.add)
                                v.scalar_tensor_tensor(nw[:, 1:2], n_r, a_i, nw[:, 1:2], ALU.mult, ALU.add)
                                v.scalar_tensor_tensor(nw[:, 2:3], n_i, nc_i, nw[:, 2:3], ALU.mult, ALU.add)
                                ins = v.scalar_tensor_tensor(nw[:, 3:4], n_r, c_i, nw[:, 3:4], ALU.mult, ALU.add)
                            elif slot == 2:
                                v.scalar_tensor_tensor(nw[:, 0:1], d_rr, b_r, nw[:, 0:1], ALU.mult, ALU.add)
                                v.scalar_tensor_tensor(nw[:, 1:2], d_ii, b_r, nw[:, 1:2], ALU.mult, ALU.add)
                                v.scalar_tensor_tensor(nw[:, 2:3], d_rr, d_r, nw[:, 2:3], ALU.mult, ALU.add)
                                ins = v.scalar_tensor_tensor(nw[:, 3:4], d_ii, d_r, nw[:, 3:4], ALU.mult, ALU.add)
                            else:
                                v.scalar_tensor_tensor(nw[:, 0:1], d_ii, nb_i, nw[:, 0:1], ALU.mult, ALU.add)
                                v.scalar_tensor_tensor(nw[:, 1:2], d_rr, b_i, nw[:, 1:2], ALU.mult, ALU.add)
                                v.scalar_tensor_tensor(nw[:, 2:3], d_ii, nd_i, nw[:, 2:3], ALU.mult, ALU.add)
                                ins = v.scalar_tensor_tensor(nw[:, 3:4], d_rr, d_i, nw[:, 3:4], ALU.mult, ALU.add)
                    for ci, cf in enumerate(cfgs):
                        nw = e2[0:1, 12 * ci + 4:12 * ci + 8]; sc_ = e2[0:1, 12 * ci + 8:12 * ci + 10]
                        v.tensor_tensor(sc_[:, 0:1], nw[:, 0:1], nw[:, 0:1], ALU.mult)
                        ins = v.tensor_tensor(sc_[:, 1:2], nw[:, 1:2], nw[:, 1:2], ALU.mult)
                    for ci, cf in enumerate(cfgs):
                        sc_ = e2[0:1, 12 * ci + 8:12 * ci + 10]
                        ins = v.scalar_tensor_tensor(sc_[:, 0:1], sc_[:, 0:1], 1e-30, sc_[:, 1:2], ALU.add, ALU.add)
                    for ci, cf in enumerate(cfgs):
                        sc_ = e2[0:1, 12 * ci + 8:12 * ci + 10]
                        ins = v.reciprocal(sc_[:, 1:2], sc_[:, 0:1])
                    for ci, cf in enumerate(cfgs):
                        st = e2[0:1, 12 * ci:12 * ci + 4]; nw = e2[0:1, 12 * ci + 4:12 * ci + 8]
                        sc_ = e2[0:1, 12 * ci + 8:12 * ci + 10]
                        ins = v.tensor_scalar(st, nw[:], sc_[:, 1:2], None, ALU.mult)
                    if s_ == 15:
                        inc(ins, "eV")
                on("vector", v_ap2)
            def g_ps(gp, tok=P.cnt["eV"]):
                wait(gp, "eV", tok)
                for ci, cf in enumerate(cfgs):
                    inc(gp.dma_start(out=row_d[R_T1 + ci, 0:64].rearrange("(a b) -> a b", a=1),
                                     in_=cf["psup"][:]), "qG", 16)
            on("gpsimd", g_ps)
            def s_ps(s, tok=P.cnt["qG"]):
                wait(s, "qG", tok)
                for ci, cf in enumerate(cfgs):
                    inc(s.dma_start(out=cf["ps16"][0][:], in_=row_d[R_T1 + ci, 0:64].rearrange("(s c) -> s c", s=16)), "qS", 16)
            on("sync", s_ps)
            cur = 0
            for k in range(8):
                def v_l2c(v, k=k, cur=cur):
                    if k == 0:
                        wait(v, "qS", P.cnt["qS"])
                    o = 8 * k
                    for slot in range(4):
                        for ci, cf in enumerate(cfgs):
                            sb8_, sb8n_ = cf["sb8"], cf["sb8n"]
                            St = cf["ps16"][cur]; Nw = cf["ps16"][1 - cur]
                            a_r = sb8_[:, o + 0:o + 1]; b_r = sb8_[:, o + 1:o + 2]
                            a_i = sb8_[:, o + 2:o + 3]; b_i = sb8_[:, o + 3:o + 4]
                            c_r = sb8_[:, o + 4:o + 5]; d_r = sb8_[:, o + 5:o + 6]
                            c_i = sb8_[:, o + 6:o + 7]; d_i = sb8_[:, o + 7:o + 8]
                            na_i = sb8n_[:, o + 2:o + 3]; nb_i = sb8n_[:, o + 3:o + 4]
                            nc_i = sb8n_[:, o + 6:o + 7]; nd_i = sb8n_[:, o + 7:o + 8]
                            n_r, n_i, d_rr, d_ii = St[:, 0:1], St[:, 1:2], St[:, 2:3], St[:, 3:4]
                            if slot == 0:
                                v.tensor_copy(cf["pblk"][:, 4 * k:4 * k + 4], St[:])
                            for (oc, m_r, m_i, nm_i, q_r, q_i, nq_i) in (
                                (0, a_r, a_i, na_i, b_r, b_i, nb_i),
                                (2, c_r, c_i, nc_i, d_r, d_i, nd_i),
                            ):
                                dst = Nw[:, oc:oc + 1]; dsti = Nw[:, oc + 1:oc + 2]
                                if slot == 0:
                                    v.tensor_scalar(dst, n_r, m_r, None, ALU.mult)
                                    ins = v.tensor_scalar(dsti, n_i, m_r, None, ALU.mult)
                                elif slot == 1:
                                    v.scalar_tensor_tensor(dst, n_i, nm_i, dst, ALU.mult, ALU.add)
                                    ins = v.scalar_tensor_tensor(dsti, n_r, m_i, dsti, ALU.mult, ALU.add)
                                elif slot == 2:
                                    v.scalar_tensor_tensor(dst, d_rr, q_r, dst, ALU.mult, ALU.add)
                                    ins = v.scalar_tensor_tensor(dsti, d_ii, q_r, dsti, ALU.mult, ALU.add)
                                else:
                                    v.scalar_tensor_tensor(dst, d_ii, nq_i, dst, ALU.mult, ALU.add)
                                    ins = v.scalar_tensor_tensor(dsti, d_rr, q_i, dsti, ALU.mult, ALU.add)
                    for ci, cf in enumerate(cfgs):
                        Nw = cf["ps16"][1 - cur]
                        e1c = e1[0:16, 16 + 4 * ci:20 + 4 * ci]
                        v.tensor_tensor(e1c[:, 0:1], Nw[:, 0:1], Nw[:, 0:1], ALU.mult)
                        ins = v.tensor_tensor(e1c[:, 1:2], Nw[:, 1:2], Nw[:, 1:2], ALU.mult)
                    for ci, cf in enumerate(cfgs):
                        e1c = e1[0:16, 16 + 4 * ci:20 + 4 * ci]
                        ins = v.scalar_tensor_tensor(e1c[:, 0:1], e1c[:, 0:1], 1e-30, e1c[:, 1:2], ALU.add, ALU.add)
                    for ci, cf in enumerate(cfgs):
                        e1c = e1[0:16, 16 + 4 * ci:20 + 4 * ci]
                        ins = v.reciprocal(e1c[:, 2:3], e1c[:, 0:1])
                    for ci, cf in enumerate(cfgs):
                        Nw = cf["ps16"][1 - cur]
                        e1c = e1[0:16, 16 + 4 * ci:20 + 4 * ci]
                        ins = v.tensor_scalar(Nw[:], Nw[:], e1c[:, 2:3], None, ALU.mult)
                    if k == 7:
                        inc(ins, "eV")
                on("vector", v_l2c)
                cur = 1 - cur
            def g_pb(gp, tok=P.cnt["eV"]):
                wait(gp, "eV", tok)
                for ci, cf in enumerate(cfgs):
                    inc(gp.dma_start(out=row_d[R_T1 + ci, 0:512].rearrange("(a b) -> a b", a=16),
                                     in_=cf["pblk"][:]), "qG", 16)
            on("gpsimd", g_pb)
            def s_pb(s, tok=P.cnt["qG"]):
                wait(s, "qG", tok)
                for ci, cf in enumerate(cfgs):
                    inc(s.dma_start(out=cf["pb128"][:], in_=row_d[R_T1 + ci, 0:512].rearrange("(b c) -> b c", b=128)), "qS", 16)
            on("sync", s_pb)
            def v_l3i(v, tok=P.cnt["qS"]):
                wait(v, "qS", tok)
                for cf in cfgs:
                    v.tensor_copy(cf["NR"][:, 0:1], cf["pb128"][:, 0:1])
                    v.tensor_copy(cf["NI"][:, 0:1], cf["pb128"][:, 1:2])
                    v.tensor_copy(cf["DR"][:, 0:1], cf["pb128"][:, 2:3])
                    ins = v.tensor_copy(cf["DI"][:, 0:1], cf["pb128"][:, 3:4])
                inc(ins, "eV")
            on("vector", v_l3i)
            for k in range(32):
                def v_l3(v, k=k):
                    for cf in cfgs:
                        NR, NI, DR, DI = cf["NR"], cf["NI"], cf["DR"], cf["DI"]
                        dcol = cf["dre"][:, k:k + 1]
                        v.scalar_tensor_tensor(NR[:, k + 1:k + 2], NR[:, k:k + 1], dcol, DR[:, k:k + 1], ALU.mult, ALU.subtract)
                        ins = v.scalar_tensor_tensor(NI[:, k + 1:k + 2], NI[:, k:k + 1], dcol, DI[:, k:k + 1], ALU.mult, ALU.subtract)
                    for cf in cfgs:
                        NR, NI = cf["NR"], cf["NI"]
                        icol = cf["dim"][:, k:k + 1]; ncol = cf["ngi"][:, k:k + 1]
                        v.scalar_tensor_tensor(NR[:, k + 1:k + 2], NI[:, k:k + 1], ncol, NR[:, k + 1:k + 2], ALU.mult, ALU.add)
                        ins = v.scalar_tensor_tensor(NI[:, k + 1:k + 2], NR[:, k:k + 1], icol, NI[:, k + 1:k + 2], ALU.mult, ALU.add)
                    for cf in cfgs:
                        v.tensor_copy(cf["DR"][:, k + 1:k + 2], cf["NR"][:, k:k + 1])
                        ins = v.tensor_copy(cf["DI"][:, k + 1:k + 2], cf["NI"][:, k:k + 1])
                    if k % 8 == 7:
                        for ci, cf in enumerate(cfgs):
                            NR, NI = cf["NR"], cf["NI"]
                            e1c = e1[:, 16 + 4 * ci:20 + 4 * ci]
                            v.tensor_tensor(e1c[:, 0:1], NR[:, k + 1:k + 2], NR[:, k + 1:k + 2], ALU.mult)
                            ins = v.tensor_tensor(e1c[:, 1:2], NI[:, k + 1:k + 2], NI[:, k + 1:k + 2], ALU.mult)
                        for ci, cf in enumerate(cfgs):
                            e1c = e1[:, 16 + 4 * ci:20 + 4 * ci]
                            ins = v.scalar_tensor_tensor(e1c[:, 0:1], e1c[:, 0:1], 1e-30, e1c[:, 1:2], ALU.add, ALU.add)
                        for ci, cf in enumerate(cfgs):
                            e1c = e1[:, 16 + 4 * ci:20 + 4 * ci]
                            ins = v.reciprocal(e1c[:, 2:3], e1c[:, 0:1])
                        for ci, cf in enumerate(cfgs):
                            e1c = e1[:, 16 + 4 * ci:20 + 4 * ci]
                            for z in (cf["NR"], cf["NI"], cf["DR"], cf["DI"]):
                                ins = v.tensor_scalar(z[:, k + 1:k + 2], z[:, k + 1:k + 2], e1c[:, 2:3], None, ALU.mult)
                        if k == 31:
                            inc(ins, "eV")
                on("vector", v_l3)

        wcv = sb("wcv", [128, 4 * 4096], BF16, st=esB1)
        cv_tok = {}
        for j in range(16):
            slc = j % 4
            def g_cv(gp, j=j, slc=slc):
                if j >= 4:
                    wait(gp, "qD", cv_tok[j - 4])
                if j < 8:
                    inc(gp.dma_start(out=wcv[:, 4096 * slc:4096 * (slc + 1)],
                                     in_=wff1_d[128 * j:128 * (j + 1), :]), "qC", 16)
                else:
                    jj = j - 8
                    inc(gp.dma_start(out=wcv.rearrange("p (s a b) -> p s a b", s=4, a=4)[:, slc],
                                     in_=wff2_d.rearrange("(a p) b -> p a b", p=128)[:, 4 * jj:4 * (jj + 1), :]), "qC", 16)
            on("gpsimd", g_cv)
            def s_cv(s, j=j, slc=slc, tok=P.cnt["qC"]):
                wait(s, "qC", tok)
                if j < 8:
                    inc(s.dma_start(out=wff1b_d[128 * j:128 * (j + 1), :],
                                    in_=wcv[:, 4096 * slc:4096 * (slc + 1)]), "qD", 16)
                else:
                    jj = j - 8
                    inc(s.dma_start(out=wff2b_d.rearrange("(a p) b -> p a b", p=128)[:, 4 * jj:4 * (jj + 1), :],
                                    in_=wcv.rearrange("p (s a b) -> p s a b", s=4, a=4)[:, slc]), "qD", 16)
            on("sync", s_cv)
            cv_tok[j] = P.cnt["qD"]

        if upto == "wcv1":
            dw1 = dout("d_w1b", [128, FF], BF16)
            dw2 = dout("d_w2b", [128, D], BF16)
            def g_dw1(gp):
                wait_all(gp)
                inc(gp.dma_start(out=dw1, in_=wff1b_d[0:128, :]), "qG", 16)
                inc(gp.dma_start(out=dw2, in_=wff2b_d[256:384, :]), "qG", 16)
                wait_all(gp, "qG")
            on("gpsimd", g_dw1)
            P.barrier()
            esB1.close()
            esB2.close()
            esBH.close()
            esB.close()
            return nc, dbg_outs

        cfgF = dict(dre=dreF, dim=dimF, ngi=ngiF, NR=NRf, NI=NIf, DR=DRf, DI=DIf,
                    s_re=s_re, s_im=s_im, c_re=c_re, c_im=c_im, tcomp=tcomp,
                    sb8=sb8, sb8n=sb8n, rst=rst, srow=srow, srown=srown,
                    psup=psup, ps16=ps16, pblk=pblk, pb128=pb128)
        cfgB = dict(dre=dreB, dim=dimB, ngi=ngiB, NR=NRb, NI=NIb, DR=DRb, DI=DIb,
                    s_re=s_reB, s_im=s_imB, c_re=c_reB, c_im=c_imB, tcomp=tcompB,
                    sb8=sb8B, sb8n=sb8nB, rst=rstB, srow=srowB, srown=srownB,
                    psup=psupB, ps16=ps16B, pblk=pblkB, pb128=pb128B)
        cf_scan2([cfgF, cfgB])

        for (til, dst, r1, r2) in ((NRb, bre, R_T1, R_T2), (NIb, bim, R_T3, R_T4),
                                   (DRb, bre2, R_T1, R_T2), (DIb, bim2, R_T3, R_T4)):
            def g_u1(gp, til=til, r1=r1, tok=P.cnt["eV"]):
                wait(gp, "eV", tok)
                wait(gp, "qS", P.cnt["qS"])
                inc(gp.dma_start(out=row_d[r1, :].rearrange("(a b) -> a b", a=128), in_=til[:, 1:33]), "qG", 16)
            on("gpsimd", g_u1)
            def s_u1(s, r1=r1, tok=P.cnt["qG"]):
                wait(s, "qG", tok)
                inc(s.dma_start(out=rowA[:], in_=row_d[r1:r1 + 1, :]), "qS", 16)
            on("sync", s_u1)
            def v_u1(v, tok=P.cnt["qS"]):
                wait(v, "qS", tok)
                inc(v.tensor_copy(rowB[:], rowA[0:1, ::-1]), "eV")
            on("vector", v_u1)
            def g_u2(gp, r2=r2, tok=P.cnt["eV"]):
                wait(gp, "eV", tok)
                inc(gp.dma_start(out=row_d[r2:r2 + 1, :], in_=rowB[:]), "qG", 16)
            on("gpsimd", g_u2)
            def s_u2(s, dst=dst, r2=r2, tok=P.cnt["qG"]):
                wait(s, "qG", tok)
                inc(s.dma_start(out=dst[:], in_=row_d[r2, :].rearrange("(b j) -> b j", b=128)), "qS", 16)
            on("sync", s_u2)

        def v_gc(v):
            wait_all(v, "qS", "eV")
            na_r, na_i = NRf[:, 1:33], NIf[:, 1:33]
            da_r, da_i = DRf[:, 1:33], DIf[:, 1:33]
            nb_r, nb_i = bre[:], bim[:]
            db_r, db_i = bre2[:], bim2[:]
            numr, numi, t1r, t1i, denr, deni = (gsc[0], gsc[1], gsc[2], gsc[3], gsc[4], gsc[5])
            for mi, (pr, pi, qr, qi) in ((4, (na_r, na_i, da_r, da_i)), (5, (nb_r, nb_i, db_r, db_i))):
                ch(v, "eV", v.tensor_tensor(gsc[mi][:], pr, pr, ALU.mult))
                ch(v, "eV", v.tensor_tensor(t1r[:], pi, pi, ALU.mult))
                ch(v, "eV", v.tensor_tensor(gsc[mi][:], gsc[mi][:], t1r[:], ALU.add))
                ch(v, "eV", v.tensor_tensor(t1r[:], qr, qr, ALU.mult))
                ch(v, "eV", v.tensor_tensor(gsc[mi][:], gsc[mi][:], t1r[:], ALU.add))
                ch(v, "eV", v.tensor_tensor(t1r[:], qi, qi, ALU.mult))
                ch(v, "eV", v.tensor_tensor(gsc[mi][:], gsc[mi][:], t1r[:], ALU.add))
                inc(v.tensor_scalar(gsc[mi][:], gsc[mi][:], 1e-30, None, ALU.add), "eV")
            inc(v.memset(tinyc[:], 0.0), "eV")
        on("vector", v_gc)
        def a_gcs(a, tok=P.cnt["eV"]):
            wait(a, "eA", P.cnt["eA"])
            wait(a, "eV", tok)
            ch(a, "eA", a.activation(gsc[4][:], gsc[4][:], AF.Sqrt))
            inc(a.activation(gsc[5][:], gsc[5][:], AF.Sqrt), "eA")
        on("scalar", a_gcs)
        def v_gc2b(v, tok=P.cnt["eA"]):
            wait(v, "eA", tok)
            na_r, na_i = NRf[:, 1:33], NIf[:, 1:33]
            da_r, da_i = DRf[:, 1:33], DIf[:, 1:33]
            nb_r, nb_i = bre[:], bim[:]
            db_r, db_i = bre2[:], bim2[:]
            numr, numi, t1r, t1i, denr, deni = (gsc[0], gsc[1], gsc[2], gsc[3], gsc[4], gsc[5])
            ch(v, "eV", v.reciprocal(gsc[4][:], gsc[4][:]))
            ch(v, "eV", v.reciprocal(gsc[5][:], gsc[5][:]))
            for mi, (pr, pi, qr, qi) in ((4, (na_r, na_i, da_r, da_i)), (5, (nb_r, nb_i, db_r, db_i))):
                for z in (pr, pi, qr, qi):
                    ch(v, "eV", v.tensor_tensor(z, z, gsc[mi][:], ALU.mult))
            ch(v, "eV", v.tensor_tensor(numr[:], da_r, db_r, ALU.mult))
            ch(v, "eV", v.tensor_tensor(t1r[:], da_i, db_i, ALU.mult))
            ch(v, "eV", v.tensor_tensor(numr[:], numr[:], t1r[:], ALU.subtract))
            ch(v, "eV", v.tensor_tensor(numi[:], da_r, db_i, ALU.mult))
            ch(v, "eV", v.tensor_tensor(t1r[:], da_i, db_r, ALU.mult))
            ch(v, "eV", v.tensor_tensor(numi[:], numi[:], t1r[:], ALU.add))
            ch(v, "eV", v.tensor_tensor(denr[:], na_r, db_r, ALU.mult))
            ch(v, "eV", v.tensor_tensor(t1r[:], na_i, db_i, ALU.mult))
            ch(v, "eV", v.tensor_tensor(denr[:], denr[:], t1r[:], ALU.subtract))
            ch(v, "eV", v.tensor_tensor(t1r[:], nb_r, da_r, ALU.mult))
            ch(v, "eV", v.tensor_tensor(denr[:], denr[:], t1r[:], ALU.add))
            ch(v, "eV", v.tensor_tensor(t1r[:], nb_i, da_i, ALU.mult))
            ch(v, "eV", v.tensor_tensor(denr[:], denr[:], t1r[:], ALU.subtract))
            ch(v, "eV", v.tensor_tensor(t1r[:], dreF[:], numr[:], ALU.mult))
            ch(v, "eV", v.tensor_tensor(denr[:], denr[:], t1r[:], ALU.subtract))
            ch(v, "eV", v.tensor_tensor(t1r[:], dimF[:], numi[:], ALU.mult))
            ch(v, "eV", v.tensor_tensor(denr[:], denr[:], t1r[:], ALU.add))
            ch(v, "eV", v.tensor_tensor(deni[:], na_r, db_i, ALU.mult))
            ch(v, "eV", v.tensor_tensor(t1r[:], na_i, db_r, ALU.mult))
            ch(v, "eV", v.tensor_tensor(deni[:], deni[:], t1r[:], ALU.add))
            ch(v, "eV", v.tensor_tensor(t1r[:], nb_r, da_i, ALU.mult))
            ch(v, "eV", v.tensor_tensor(deni[:], deni[:], t1r[:], ALU.add))
            ch(v, "eV", v.tensor_tensor(t1r[:], nb_i, da_r, ALU.mult))
            ch(v, "eV", v.tensor_tensor(deni[:], deni[:], t1r[:], ALU.add))
            ch(v, "eV", v.tensor_tensor(t1r[:], dreF[:], numi[:], ALU.mult))
            ch(v, "eV", v.tensor_tensor(deni[:], deni[:], t1r[:], ALU.subtract))
            ch(v, "eV", v.tensor_tensor(t1r[:], dimF[:], numr[:], ALU.mult))
            ch(v, "eV", v.tensor_tensor(deni[:], deni[:], t1r[:], ALU.subtract))
            ch(v, "eV", v.tensor_tensor(t1i[:], denr[:], denr[:], ALU.mult))
            ch(v, "eV", v.tensor_tensor(t1r[:], deni[:], deni[:], ALU.mult))
            ch(v, "eV", v.tensor_tensor(t1i[:], t1i[:], t1r[:], ALU.add))
            ch(v, "eV", v.reciprocal(t1i[:], t1i[:]))
            ch(v, "eV", v.tensor_tensor(bre[:], numr[:], denr[:], ALU.mult))
            ch(v, "eV", v.tensor_tensor(bim[:], numi[:], deni[:], ALU.mult))
            ch(v, "eV", v.tensor_tensor(bre[:], bre[:], bim[:], ALU.add))
            ch(v, "eV", v.tensor_tensor(bim[:], numi[:], denr[:], ALU.mult))
            ch(v, "eV", v.tensor_tensor(bre2[:], numr[:], deni[:], ALU.mult))
            ch(v, "eV", v.tensor_tensor(bim[:], bim[:], bre2[:], ALU.subtract))
            ch(v, "eV", v.tensor_tensor(gsc[2][:], bre[:], t1i[:], ALU.mult))
            inc(v.tensor_tensor(gsc[3][:], bim[:], t1i[:], ALU.mult), "eV")
        on("vector", v_gc2b)
        def g_g1(gp, tok=P.cnt["eV"]):
            wait(gp, "eV", tok)
            inc(gp.dma_start(out=row_d[R_GR, :].rearrange("(a b) -> a b", a=128), in_=gsc[2][:]), "qG", 16)
            inc(gp.dma_start(out=row_d[R_GI, :].rearrange("(a b) -> a b", a=128), in_=gsc[3][:]), "qG", 16)
        on("gpsimd", g_g1)
        def g_g2(gp, tok=P.cnt["qG"]):
            wait(gp, "qG", tok)
            inc(gp.dma_start(out=gfm[0:1, :], in_=row_d[R_GR:R_GR + 1, :]), "qG", 16)
            inc(gp.dma_start(out=gfm[1:2, :], in_=row_d[R_GI:R_GI + 1, :]), "qG", 16)
        on("gpsimd", g_g2)

        def v_gh(v):
            wait_all(v, "qS", "eV")
            inc(v.tensor_tensor_scan(ghat[:], gch[:], gch[:], 0.0, ALU.add, ALU.bypass), "eV")
        on("vector", v_gh)
        def a_eh(a, tok=P.cnt["eV"]):
            wait(a, "eV", tok)
            ch(a, "eA", a.activation(epx[:], ghat[:], AF.Exp, bias=lnecol[0:32, 0:1]))
            inc(a.activation(emx[:], ghat[:], AF.Exp, scale=-1.0), "eA")
        on("scalar", a_eh)
        for (srcT, dstT) in ((epx, epT), (emx, emT)):
            def t_eT2(t, srcT=srcT, tok=P.cnt["eA"]):
                wait(t, "eA", tok)
                P.bank_wait(t, "pb7")
                inc(t.transpose(pb[7][0:128, 0:32], srcT[:], idn[0:32, 0:32]), "eT")
            on("tensor", t_eT2)
            def v_eT2(v, dstT=dstT, tok=P.cnt["eT"]):
                wait(v, "eT", tok)
                inc(v.tensor_copy(dstT[:], pb[7][0:128, 0:32]), "eV")
            on("vector", v_eT2)
            P.bank_set("pb7", "eV")
        def g_gc(gp, tok=P.cnt["eA"]):
            wait(gp, "eA", tok)
            inc(gp.dma_start(out=row_d[R_GC, 0:32].rearrange("(a b) -> a b", a=32), in_=emx[:, 127:128]), "qG", 16)
        on("gpsimd", g_gc)
        def s_gc(s, tok=P.cnt["qG"]):
            wait(s, "qG", tok)
            inc(s.dma_start(out=e3[0:1, 0:32], in_=row_d[R_GC:R_GC + 1, 0:32]), "qS", 16)
        on("sync", s_gc)
        def t_gc(t, tok=P.cnt["qS"]):
            wait(t, "qS", tok)
            P.bank_wait(t, "pb7")
            inc(t.matmul(pb[7][0:128, 0:32], ones1f[:], e3[0:1, 0:32], start=True, stop=True), "eT")
        on("tensor", t_gc)
        def v_gc2(v, tok=P.cnt["eT"]):
            wait(v, "eT", tok)
            inc(v.tensor_copy(gcbc[:], pb[7][0:128, 0:32]), "eV")
        on("vector", v_gc2)
        P.bank_set("pb7", "eV")
        def g_er(gp):
            wait_all(gp, "eA", "qS")
            inc(gp.dma_start(out=row_d[R_EM, :].rearrange("(c p) -> c p", c=32), in_=emx[:]), "qG", 16)
            inc(gp.dma_start(out=row_d[R_EP, :].rearrange("(c p) -> c p", c=32), in_=epx[:]), "qG", 16)
        on("gpsimd", g_er)
        for (rr, dstb) in ((R_EM, embc), (R_EP, epbc)):
            def s_er2(s, rr=rr, tok=P.cnt["qG"]):
                wait(s, "qG", tok)
                wait(s, "eT", P.cnt["eT"])
                inc(s.dma_start(out=rowA[:], in_=row_d[rr:rr + 1, :]), "qS", 16)
            on("sync", s_er2)
            tok_row = P.cnt["qS"]
            for c in range(32):
                bk = c % 2
                def t_eb(t, c=c, bk=bk, tok=tok_row):
                    if c == 0:
                        wait(t, "qS", tok)
                    P.bank_wait(t, f"pb{bk}")
                    inc(t.matmul(pb[bk][0:128, 0:128], ones1f[:], rowA[0:1, 128 * c:128 * (c + 1)],
                                 start=True, stop=True), "eT")
                on("tensor", t_eb)
                def v_eb(v, c=c, bk=bk, dstb=dstb, tok=P.cnt["eT"]):
                    wait(v, "eT", tok)
                    inc(v.tensor_copy(dstb[:, 128 * c:128 * (c + 1)], pb[bk][0:128, 0:128]), "eV")
                on("vector", v_eb)
                P.bank_set(f"pb{bk}", "eV")
        def g_wbk(gp):
            inc(gp.dma_start(out=wbkb[:], in_=wbk_d), "qG", 16)
            inc(gp.dma_start(out=bbk_row[:], in_=bbk_d), "qG", 16)
        on("gpsimd", g_wbk)
        def t_bbk(t, tok=P.cnt["qG"]):
            wait(t, "qG", tok)
            P.bank_wait(t, "pb6")
            P.bank_wait(t, "pb5")
            t.matmul(pb[6][0:128, 0:512], ones1f[:], bbk_row[0:1, 0:512], start=True, stop=True)
            inc(t.matmul(pb[5][0:128, 0:512],
                         ones1f[:], bbk_row[0:1, 512:1024], start=True, stop=True), "eT")
        on("tensor", t_bbk)
        def v_bbk(v, tok=P.cnt["eT"]):
            wait(v, "eT", tok)
            v.tensor_copy(bbk_bc[:, 0:512], pb[6][0:128, 0:512])
            inc(v.tensor_copy(bbk_bc[:, 512:1024], pb[5][0:128, 0:512]), "eV")
        on("vector", v_bbk)
        P.bank_set("pb6", "eV"); P.bank_set("pb5", "eV")

        if dbg >= 2 or upto == "b1":
            dG = dout("d_G", [128, 64])
            dghat = dout("d_ghat", [32, 128])
            dept = dout("d_epT", [128, 32])
            demt = dout("d_emT", [128, 32])
            dgc = dout("d_gcbc", [128, 32])
            dprs = dout("d_pairs", [128, 4 * 33])
            dprb = dout("d_pairb", [128, 4 * 32])
            dprbr = dout("d_pairbr", [128, 4 * 33])
            def g_db1(gp):
                wait_all(gp)
                for kk, til in enumerate((NRf, NIf, DRf, DIf)):
                    inc(gp.dma_start(out=dprs[:, 33 * kk:33 * (kk + 1)], in_=til[:]), "qG", 16)
                for kk, til in enumerate((NRb, NIb, DRb, DIb)):
                    inc(gp.dma_start(out=dprbr[:, 33 * kk:33 * (kk + 1)], in_=til[:]), "qG", 16)
                for kk, til in enumerate((bre, bim, bre2, bim2)):
                    inc(gp.dma_start(out=dprb[:, 32 * kk:32 * (kk + 1)], in_=til[:]), "qG", 16)
                inc(gp.dma_start(out=dG[:, 0:32], in_=gsc[2][:]), "qG", 16)
                inc(gp.dma_start(out=dG[:, 32:64], in_=gsc[3][:]), "qG", 16)
                inc(gp.dma_start(out=dghat, in_=ghat[:]), "qG", 16)
                inc(gp.dma_start(out=dept, in_=epT[:]), "qG", 16)
                inc(gp.dma_start(out=demt, in_=emT[:]), "qG", 16)
                inc(gp.dma_start(out=dgc, in_=gcbc[:]), "qG", 16)
                wait_all(gp, "qG")
            on("gpsimd", g_db1)
            if upto == "b1":
                P.barrier()
                esB1.close()
                esB2.close()
                esBH.close()
                esB.close()
                return nc, dbg_outs


        P.scope("b2")
        P.barrier()
        esB1.close()

        b2_tok = {}
        for i in range(NT):
            sl = i % 3
            yv = yring[:, D * sl:D * (sl + 1)]
            bk0, bk1 = (6, 7) if i % 2 == 0 else (4, 5)
            def s_y(s, i=i, yv=yv):
                if i >= 3:
                    wait(s, "eA", b2_tok[i - 3][0])
                    wait(s, "qG", b2_tok[i - 3][1])
                inc(s.dma_start(out=yv, in_=y_d[128 * i:128 * (i + 1), :]), "qS", 16)
            on("sync", s_y)
            tok_y = P.cnt["qS"]
            def t_bk(t, i=i, bk0=bk0, bk1=bk1, tok=tok_y):
                P.bank_wait(t, f"pb{bk0}"); P.bank_wait(t, f"pb{bk1}")
                if i == 0:
                    wait(t, "eV", P.cnt["eV"])
                    wait(t, "qG", P.cnt["qG"])
                t.matmul(pb[bk0][0:128, 0:512], gfm[:, 128 * i:128 * (i + 1)], wbkb[:, 0:512],
                         start=True, stop=True)
                inc(t.matmul(pb[bk1][0:128, 0:512], gfm[:, 128 * i:128 * (i + 1)], wbkb[:, 512:1024],
                             start=True, stop=True), "eT")
            on("tensor", t_bk)
            def v_y1(v, i=i, yv=yv, bk0=bk0, bk1=bk1, tok=P.cnt["eT"], toky=tok_y):
                wait(v, "eT", tok)
                wait(v, "qS", toky)
                v.tensor_tensor(junkV[:, 0:512], pb[bk0][0:128, 0:512], bbk_bc[:, 0:512], ALU.add)
                ch(v, "eV", v.tensor_tensor(junkV[:, 512:1024], pb[bk1][0:128, 0:512], bbk_bc[:, 512:1024], ALU.add))
                v.tensor_tensor(yv[:, 0:512], junkV[:, 0:512], yv[:, 0:512], ALU.add)
                inc(v.tensor_tensor(yv[:, 512:1024], junkV[:, 512:1024], yv[:, 512:1024], ALU.add), "eV")
            on("vector", v_y1)
            P.bank_set(f"pb{bk0}", "eV"); P.bank_set(f"pb{bk1}", "eV")
            tok_y1 = P.cnt["eV"]
            def g_y1(gp, i=i, yv=yv, tok=tok_y1):
                wait(gp, "eV", tok)
                inc(gp.dma_start(out=y_d[128 * i:128 * (i + 1), :], in_=yv), "qG", 16)
            on("gpsimd", g_y1)
            def a_s2(a, i=i, yv=yv, tok=tok_y1):
                wait(a, "eV", tok)
                a.activation(junkA[:], yv, AF.Copy, scale=INV_D, accum_out=m2a[:, i:i + 1])
                inc(a.activation(junkA2[:], yv, AF.Square, scale=INV_SQ_D,
                                 accum_out=q2a[:, i:i + 1]), "eA")
            on("scalar", a_s2)
            b2_tok[i] = (P.cnt["eA"], P.cnt["qG"])

        def v_f2(v, tok=P.cnt["eA"]):
            wait(v, "eA", tok)
            ch(v, "eV", v.tensor_tensor(st1[:], m2a[:], m2a[:], ALU.mult))
            inc(v.tensor_tensor(st2[:], q2a[:], st1[:], ALU.subtract), "eV")
        on("vector", v_f2)
        def a_f2(a, tok=P.cnt["eV"]):
            wait(a, "eV", tok)
            inc(a.activation(st1[:], st2[:], AF.Sqrt, bias=epscol[:, 0:1]), "eA")
        on("scalar", a_f2)
        def v_f2b(v, tok=P.cnt["eA"]):
            wait(v, "eA", tok)
            inc(v.reciprocal(r2a[:], st1[:]), "eV")
        on("vector", v_f2b)
        tokR2 = P.cnt["eV"]

        P.scope("qkvprep")
        esB3 = contextlib.ExitStack()
        wq_sb = sb("wq_sb", [128, 8 * 1536], BF16, st=esB3)
        xn2g = sb("xn2g", [128, 8 * 1024], BF16, st=esB3)
        g2c = sb("g2c", [128, 8], st=esB3)
        b2cb = sb("b2cb", [128, 8], BF16, st=esB3)
        beffq = sb("beffq", [1, 1536], st=esB3)
        beffqT = sb("beffqT", [128, 12], st=esB3)
        kvbc = sb("kvbc", [128, 1024], st=esB3)
        qkfm = sb("qkfm", [128, 8 * 1024], BF16, st=esB3)
        ktok = sb("ktok", [128, 8 * 512], BF16, st=esB3)
        vtok = sb("vtok", [128, 8 * 512], BF16, st=esB3)
        qts = sb("qts", [128, 4 * 128], BF16, st=esB3)
        kts = sb("kts", [128, 4 * 128], BF16, st=esB3)
        khat = sb("khat", [128, 512], BF16, st=esB3)
        vhat = sb("vhat", [128, 512], BF16, st=esB3)
        Wst = sb("Wst", [128, 4 * 64], st=esB3)
        Wbf = [sb(f"Wbf{k}", [128, 4 * 64], BF16, st=esB3) for k in range(2)]
        ones1b = sb("ones1b", [1, 128], BF16, st=esB3)
        sdt = sb("sdt", [128, 512], BF16, st=esB3)

        def g_wq(gp):
            for dc in range(8):
                inc(gp.dma_start(out=wq_sb[:, 1536 * dc:1536 * (dc + 1)],
                                 in_=wqkv_d[128 * dc:128 * (dc + 1), :]), "qG", 16)
            inc(gp.dma_start(out=beffq[:], in_=bqkv_d), "qG", 16)
        on("gpsimd", g_wq)
        tok_wq = P.cnt["qG"]
        def s_g2(s):
            wait(s, "eT", P.cnt["eT"])
            inc(s.dma_start(out=vsm[:], in_=ln2g_d), "qS", 16)
        on("sync", s_g2)
        def t_g2(t, tok=P.cnt["qS"]):
            wait(t, "qS", tok)
            P.bank_wait(t, "pb0")
            inc(t.transpose(pb[0][0:128, 0:8], vsm[:], idn[0:8, 0:8]), "eT")
        on("tensor", t_g2)
        def v_g2c(v, tok=P.cnt["eT"]):
            wait(v, "eT", tok)
            v.memset(ones1b[:], 1.0)
            inc(v.tensor_copy(g2c[:], pb[0][0:128, 0:8]), "eV")
        on("vector", v_g2c)
        P.bank_set("pb0", "eV")
        def s_b2(s):
            wait(s, "eT", P.cnt["eT"])
            inc(s.dma_start(out=vsm[:], in_=ln2b_d), "qS", 16)
        on("sync", s_b2)
        def t_b2(t, tok=P.cnt["qS"]):
            wait(t, "qS", tok)
            P.bank_wait(t, "pb1")
            inc(t.transpose(pb[1][0:128, 0:8], vsm[:], idn[0:8, 0:8]), "eT")
        on("tensor", t_b2)
        def v_b2c(v, tok=P.cnt["eT"]):
            wait(v, "eT", tok)
            inc(v.tensor_copy(b2cb[:], pb[1][0:128, 0:8]), "eV")
        on("vector", v_b2c)
        P.bank_set("pb1", "eV")
        for cc3 in range(3):
            def t_bq(t, cc3=cc3, tokw=tok_wq):
                if cc3 == 0:
                    wait(t, "qG", tokw)
                    wait(t, "eV", P.cnt["eV"])
                P.bank_wait(t, f"pb{2 + cc3}")
                for dc in range(8):
                    ins = t.matmul(pb[2 + cc3][0:1, 0:512], b2cb[:, dc:dc + 1],
                                   wq_sb[:, 1536 * dc + 512 * cc3:1536 * dc + 512 * (cc3 + 1)],
                                   start=(dc == 0), stop=(dc == 7))
                inc(ins, "eT")
            on("tensor", t_bq)
            def v_bq(v, cc3=cc3, tok=P.cnt["eT"]):
                wait(v, "eT", tok)
                inc(v.tensor_tensor(beffq[0:1, 512 * cc3:512 * (cc3 + 1)],
                                    pb[2 + cc3][0:1, 0:512],
                                    beffq[0:1, 512 * cc3:512 * (cc3 + 1)], ALU.add), "eV")
            on("vector", v_bq)
            P.bank_set(f"pb{2 + cc3}", "eV")
        wqpre_d = nc.dram_tensor("d_wqpre", [128, 8 * 1536], BF16, kind="ExternalOutput").ap() if upto == "b4" else None
        if upto == "b4":
            dbg_outs["d_wqpre"] = wqpre_d
            def g_pre(gp):
                wait_all(gp)
                inc(gp.dma_start(out=wqpre_d, in_=wq_sb[:]), "qG", 16)
            on("gpsimd", g_pre)
        def v_fold(v, tok=P.cnt["eV"]):
            if upto == "b4":
                wait(v, "qG", P.cnt["qG"])
            wait(v, "eV", tok)
            for dc in range(8):
                ins = v.tensor_scalar(wq_sb[:, 1536 * dc:1536 * (dc + 1)],
                                      wq_sb[:, 1536 * dc:1536 * (dc + 1)],
                                      g2c[:, dc:dc + 1], None, ALU.mult)
            inc(ins, "eV")
        on("vector", v_fold)
        def g_bq(gp, tok=P.cnt["eV"]):
            wait(gp, "eV", tok)
            inc(gp.dma_start(out=row_d[R_T1, 0:1536].rearrange("(a b) -> a b", a=1), in_=beffq[:]), "qG", 16)
        on("gpsimd", g_bq)
        def s_bqT(s, tok=P.cnt["qG"]):
            wait(s, "qG", tok)
            inc(s.dma_start(out=vsm32[0:12, :], in_=row_d[R_T1, 0:1536].rearrange("(a b) -> a b", a=12)), "qS", 16)
        on("sync", s_bqT)
        def t_bqT(t, tok=P.cnt["qS"]):
            wait(t, "qS", tok)
            P.bank_wait(t, "pb5")
            inc(t.transpose(pb[5][0:128, 0:12], vsm32[0:12, :], idn[0:12, 0:12]), "eT")
        on("tensor", t_bqT)
        def v_bqT(v, tok=P.cnt["eT"]):
            wait(v, "eT", tok)
            inc(v.tensor_copy(beffqT[:], pb[5][0:128, 0:12]), "eV")
        on("vector", v_bqT)
        P.bank_set("pb5", "eV")
        def t_kvb(t, tok=P.cnt["eV"]):
            wait(t, "eV", tok)
            P.bank_wait(t, "pb6"); P.bank_wait(t, "pb7")
            t.matmul(pb[6][0:128, 0:512], ones1f[:], beffq[0:1, 512:1024], start=True, stop=True)
            inc(t.matmul(pb[7][0:128, 0:512], ones1f[:], beffq[0:1, 1024:1536], start=True, stop=True), "eT")
        on("tensor", t_kvb)
        def v_kvb(v, tok=P.cnt["eT"]):
            wait(v, "eT", tok)
            v.tensor_copy(kvbc[:, 0:512], pb[6][0:128, 0:512])
            inc(v.tensor_copy(kvbc[:, 512:1024], pb[7][0:128, 0:512]), "eV")
        on("vector", v_kvb)
        P.bank_set("pb6", "eV"); P.bank_set("pb7", "eV")
        def v_wi(v):
            v.memset(Wst[:], 0.0)
            v.memset(Wbf[0][:], 0.0)
            inc(v.memset(Wbf[1][:], 0.0), "eV")
        on("vector", v_wi)
        P.barrier()

        P.scope("hebbian")
        b2b_tok = {}
        for g in range(4):
            t0 = 1024 * g
            for tt in range(8):
                i = 8 * g + tt
                sl = i % 2
                yv = yring[:, D * sl:D * (sl + 1)]
                def s_y2(s, i=i, yv=yv):
                    if i < 2:
                        wait(s, "eA", P.cnt["eA"])
                        wait(s, "qG", P.cnt["qG"])
                    else:
                        wait(s, "eT", b2b_tok[i - 2])
                    inc(s.dma_start(out=yv, in_=y_d[128 * i:128 * (i + 1), :]), "qS", 16)
                on("sync", s_y2)
                tok_y = P.cnt["qS"]
                def v_x2(v, i=i, yv=yv, tok=tok_y):
                    if i == 0:
                        wait(v, "eV", tokR2)
                    wait(v, "qS", tok)
                    if i >= 1:
                        wait(v, "eT", b2b_tok[i - 1])
                    inc(v.tensor_scalar(junkV[:], yv, m2a[:, i:i + 1], r2a[:, i:i + 1],
                                        ALU.subtract, ALU.mult), "eV")
                on("vector", v_x2)
                tokXN2 = P.cnt["eV"]
                for c in range(8):
                    bk = c % 4
                    def t_t2(t, c=c, bk=bk, tok=tokXN2):
                        if c == 0:
                            wait(t, "eV", tok)
                        P.bank_wait(t, f"pb{bk}")
                        inc(t.transpose(pb[bk][:, 0:128], junkV[:, 128 * c:128 * (c + 1)], idn[:]), "eT")
                    on("tensor", t_t2)
                    def v_e2(v, c=c, bk=bk, tt=tt, tok=P.cnt["eT"]):
                        wait(v, "eT", tok)
                        inc(v.tensor_copy(xn2g[:, 1024 * c + 128 * tt:1024 * c + 128 * (tt + 1)],
                                          pb[bk][:, 0:128]), "eV")
                    on("vector", v_e2)
                    P.bank_set(f"pb{bk}", "eV")
                b2b_tok[i] = P.cnt["eT"]
            tok_grp_x = P.cnt["eV"]
            for cc in range(8):
                for h2 in range(2):
                    bkq = h2
                    def t_qf(t, cc=cc, h2=h2, bkq=bkq, tok=tok_grp_x):
                        if cc == 0 and h2 == 0:
                            wait(t, "eV", tok)
                        P.bank_wait(t, f"pb{bkq}")
                        for dc in range(8):
                            ins = t.matmul(pb[bkq][0:128, 0:512],
                                           wq_sb[:, 1536 * dc + 128 * cc:1536 * dc + 128 * (cc + 1)],
                                           xn2g[:, 1024 * dc + 512 * h2:1024 * dc + 512 * (h2 + 1)],
                                           start=(dc == 0), stop=(dc == 7))
                        inc(ins, "eT")
                    on("tensor", t_qf)
                    def a_qf(a, cc=cc, h2=h2, bkq=bkq, tok=P.cnt["eT"]):
                        wait(a, "eT", tok)
                        inc(a.activation(qkfm[:, 1024 * cc + 512 * h2:1024 * cc + 512 * (h2 + 1)],
                                         pb[bkq][0:128, 0:512], AF.Identity,
                                         bias=beffqT[:, cc:cc + 1]), "eA")
                    on("scalar", a_qf)
                    P.bank_set(f"pb{bkq}", "eA")
            for tt in range(8):
                def t_kv(t, tt=tt):
                    P.bank_wait(t, "pb2"); P.bank_wait(t, "pb3")
                    for dc in range(8):
                        t.matmul(pb[2][0:128, 0:512],
                                 xn2g[:, 1024 * dc + 128 * tt:1024 * dc + 128 * (tt + 1)],
                                 wq_sb[:, 1536 * dc + 512:1536 * dc + 1024],
                                 start=(dc == 0), stop=(dc == 7))
                        ins = t.matmul(pb[3][0:128, 0:512],
                                       xn2g[:, 1024 * dc + 128 * tt:1024 * dc + 128 * (tt + 1)],
                                       wq_sb[:, 1536 * dc + 1024:1536 * dc + 1536],
                                       start=(dc == 0), stop=(dc == 7))
                    inc(ins, "eT")
                on("tensor", t_kv)
                def v_kv(v, tt=tt, tok=P.cnt["eT"]):
                    wait(v, "eT", tok)
                    v.tensor_tensor(ktok[:, 512 * tt:512 * (tt + 1)], pb[2][0:128, 0:512],
                                    kvbc[:, 0:512], ALU.add)
                    inc(v.tensor_tensor(vtok[:, 512 * tt:512 * (tt + 1)], pb[3][0:128, 0:512],
                                        kvbc[:, 512:1024], ALU.add), "eV")
                on("vector", v_kv)
                P.bank_set("pb2", "eV"); P.bank_set("pb3", "eV")
            tok_qkv = (P.cnt["eA"], P.cnt["eV"])
            for tt in range(8):
                c = 8 * g + tt
                par = c % 2
                tok_p = {}
                def v_sc(v, c=c, tt=tt, tok=tok_qkv):
                    wait(v, "eA", tok[0]); wait(v, "eV", tok[1])
                    if c >= 1:
                        v.tensor_copy(Wbf[c % 2][:, 192:256], Wst[:, 192:256])
                    for pp in range(4):
                        v.tensor_tensor(qts[:, 128 * pp:128 * (pp + 1)],
                                        qkfm[:, 1024 * pp + 128 * tt:1024 * pp + 128 * (tt + 1)],
                                        embc[:, 128 * c:128 * (c + 1)], ALU.mult)
                        ins = v.tensor_tensor(kts[:, 128 * pp:128 * (pp + 1)],
                                              qkfm[:, 1024 * (4 + pp) + 128 * tt:1024 * (4 + pp) + 128 * (tt + 1)],
                                              epbc[:, 128 * c:128 * (c + 1)], ALU.mult)
                        tok_p[pp] = inc(ins, "eV")
                    v.tensor_scalar(khat[:], ktok[:, 512 * tt:512 * (tt + 1)], epT[:, c:c + 1], None, ALU.mult)
                    inc(v.tensor_scalar(vhat[:], vtok[:, 512 * tt:512 * (tt + 1)], gcbc[:, c:c + 1], None, ALU.mult), "eV")
                on("vector", v_sc)
                tok_sc = P.cnt["eV"]
                for pp in range(4):
                    pm = pp % 2
                    bse, bso = 2 * pm, 2 * pm + 1
                    bU, bO = 4 + pm, 6 + pm
                    so = 256 * pm
                    def t_s(t, pp=pp, bse=bse, bso=bso, tok_p=tok_p):
                        wait(t, "eV", tok_p[pp])
                        P.bank_wait(t, f"pb{bse}"); P.bank_wait(t, f"pb{bso}")
                        t.matmul(pb[bse][0:128, 0:128], kts[0:64, 128 * pp:128 * (pp + 1)],
                                 qts[0:64, 128 * pp:128 * (pp + 1)], start=True, stop=True)
                        inc(t.matmul(pb[bso][0:128, 0:128], kts[64:128, 128 * pp:128 * (pp + 1)],
                                     qts[64:128, 128 * pp:128 * (pp + 1)], start=True, stop=True), "eT")
                    on("tensor", t_s)
                    def v_m(v, bse=bse, bso=bso, so=so, tok=P.cnt["eT"]):
                        wait(v, "eT", tok)
                        v.tensor_tensor(sdt[:, so:so + 128], pb[bse][0:128, 0:128], trilT[:], ALU.mult)
                        inc(v.tensor_tensor(sdt[:, so + 128:so + 256], pb[bso][0:128, 0:128], trilT[:], ALU.mult), "eV")
                    on("vector", v_m)
                    P.bank_set(f"pb{bse}", "eV"); P.bank_set(f"pb{bso}", "eV")
                    tok_sdt = P.cnt["eV"]
                    def t_uo(t, pp=pp, c=c, tt=tt, bU=bU, bO=bO, so=so, tok=tok_sdt):
                        wait(t, "eV", tok)
                        P.bank_wait(t, f"pb{bU}"); P.bank_wait(t, f"pb{bO}")
                        t.matmul(pb[bU][0:64, 0:64], khat[:, 128 * pp:128 * pp + 64],
                                 vhat[:, 128 * pp:128 * pp + 64], start=True, stop=True)
                        inc(t.matmul(pb[bU][64:128, 0:64], khat[:, 128 * pp + 64:128 * (pp + 1)],
                                     vhat[:, 128 * pp + 64:128 * (pp + 1)], start=True, stop=True,
                                     tile_position=(0, 64)), "eT")
                        t.matmul(pb[bO][0:64, 0:128], vtok[:, 512 * tt + 128 * pp:512 * tt + 128 * pp + 64],
                                 sdt[:, so:so + 128], start=True, stop=False)
                        t.matmul(pb[bO][0:64, 0:128], Wbf[c % 2][0:64, 64 * pp:64 * (pp + 1)],
                                 qts[0:64, 128 * pp:128 * (pp + 1)], start=False, stop=True)
                        t.matmul(pb[bO][64:128, 0:128], vtok[:, 512 * tt + 128 * pp + 64:512 * tt + 128 * (pp + 1)],
                                 sdt[:, so + 128:so + 256], start=True, stop=False, tile_position=(0, 64))
                        inc(t.matmul(pb[bO][64:128, 0:128], Wbf[c % 2][64:128, 64 * pp:64 * (pp + 1)],
                                     qts[64:128, 128 * pp:128 * (pp + 1)], start=False, stop=True,
                                     tile_position=(64, 64)), "eT")
                    on("tensor", t_uo)
                    tok_uo = P.cnt["eT"]
                    def a_hb(a, pp=pp, c=c, bO=bO, tok=tok_uo):
                        wait(a, "eT", tok)
                        inc(a.activation(hebfm[:, T * pp + 128 * c:T * pp + 128 * (c + 1)],
                                         pb[bO][0:128, 0:128], AF.Copy), "eA")
                    on("scalar", a_hb)
                    P.bank_set(f"pb{bO}", "eA")
                    def v_wu(v, pp=pp, c=c, bU=bU, tok=tok_uo):
                        wait(v, "eT", tok)
                        v.scalar_tensor_tensor(Wst[0:64, 64 * pp:64 * (pp + 1)],
                                               Wst[0:64, 64 * pp:64 * (pp + 1)],
                                               gcbc[0:64, c:c + 1],
                                               pb[bU][0:64, 0:64], ALU.mult, ALU.add)
                        ins = v.scalar_tensor_tensor(Wst[64:128, 64 * pp:64 * (pp + 1)],
                                                     Wst[64:128, 64 * pp:64 * (pp + 1)],
                                                     gcbc[64:128, c:c + 1],
                                                     pb[bU][64:128, 0:64], ALU.mult, ALU.add)
                        if pp >= 1:
                            ins = v.tensor_copy(Wbf[(c + 1) % 2][:, 64 * (pp - 1):64 * pp],
                                                Wst[:, 64 * (pp - 1):64 * pp])
                        inc(ins, "eV")
                    on("vector", v_wu)
                    P.bank_set(f"pb{bU}", "eV")

        if upto == "b2":
            dxn2 = dout("d_xn2fm", [128, 8 * 512])
            dy1 = dout("d_y1", [T, D])
            def g_db2(gp):
                wait_all(gp)
                for c in range(8):
                    inc(gp.dma_start(out=dxn2[:, 512 * c:512 * (c + 1)],
                                     in_=xn2g[:, 1024 * c + 512:1024 * (c + 1)]), "qG", 16)
                for i in range(NT):
                    inc(gp.dma_start(out=dy1[128 * i:128 * (i + 1), :],
                                     in_=y_d[128 * i:128 * (i + 1), :]), "qG", 16)
                wait_all(gp, "qG")
            on("gpsimd", g_db2)
            P.barrier()
            esB3.close()
            esB2.close()
            esBH.close()
            esB.close()
            return nc, dbg_outs

        if upto == "b4":
            dheb = dout("d_heb", [128, 4 * T], BF16)
            dqkf = dout("d_qkfm", [128, 8 * 1024], BF16)
            dktok = dout("d_ktok", [128, 8 * 512], BF16)
            dvtok = dout("d_vtok", [128, 8 * 512], BF16)
            dqts = dout("d_qts", [128, 512], BF16)
            dkts = dout("d_kts", [128, 512], BF16)
            dsdt = dout("d_sdt", [128, 256], BF16)
            dwst = dout("d_wst", [128, 256])
            dwq = dout("d_wq", [128, 8 * 1536], BF16)
            dg2c = dout("d_g2c", [128, 8])
            def g_db4(gp):
                wait_all(gp)
                inc(gp.dma_start(out=dheb, in_=hebfm[:]), "qG", 16)
                inc(gp.dma_start(out=dqkf, in_=qkfm[:]), "qG", 16)
                inc(gp.dma_start(out=dktok, in_=ktok[:]), "qG", 16)
                inc(gp.dma_start(out=dvtok, in_=vtok[:]), "qG", 16)
                inc(gp.dma_start(out=dqts, in_=qts[:]), "qG", 16)
                inc(gp.dma_start(out=dkts, in_=kts[:]), "qG", 16)
                inc(gp.dma_start(out=dsdt, in_=sdt[:, 0:256]), "qG", 16)
                inc(gp.dma_start(out=dwst, in_=Wst[:]), "qG", 16)
                inc(gp.dma_start(out=dwq, in_=wq_sb[:]), "qG", 16)
                inc(gp.dma_start(out=dg2c, in_=g2c[:]), "qG", 16)
                wait_all(gp, "qG")
            on("gpsimd", g_db4)
            P.barrier()
            esB3.close()
            esB2.close()
            esBH.close()
            esB.close()
            return nc, dbg_outs

        P.scope("b5a")
        P.barrier()
        esB3.close()
        esB5a = contextlib.ExitStack()
        wout_sb = sb("wout_sb", [128, 4 * D], BF16, st=esB5a)
        boutrow = sb("boutrow", [1, D], st=esB5a)
        boutb = sb("boutb", [1, D], BF16, st=esB5a)
        ones5b = sb("ones5b", [1, 128], BF16, st=esB5a)

        def g_wo(gp):
            for pp in range(4):
                inc(gp.dma_start(out=wout_sb[:, D * pp:D * (pp + 1)],
                                 in_=wout_d[128 * pp:128 * (pp + 1), :]), "qG", 16)
        on("gpsimd", g_wo)
        wo_tok = P.cnt["qG"]
        def s_bo(s):
            inc(s.dma_start(out=boutrow[:], in_=bout_d), "qS", 16)
        on("sync", s_bo)
        def v_cb(v, tok=P.cnt["qS"]):
            wait(v, "qS", tok)
            v.memset(ones5b[:], 1.0)
            inc(v.tensor_copy(boutb[:], boutrow[:]), "eV")
        on("vector", v_cb)
        cb_tok = P.cnt["eV"]
        if upto == "wcv":
            dw1 = dout("d_w1b", [128, FF], BF16)
            dw2 = dout("d_w2b", [128, D], BF16)
            def g_dw(gp):
                wait_all(gp)
                inc(gp.dma_start(out=dw1, in_=wff1b_d[0:128, :]), "qG", 16)
                inc(gp.dma_start(out=dw2, in_=wff2b_d[256:384, :]), "qG", 16)
                wait_all(gp, "qG")
            on("gpsimd", g_dw)
            P.barrier()
            esB5a.close()
            esB2.close()
            esBH.close()
            esB.close()
            return nc, dbg_outs
        b5_tok = {}
        for i in range(NT):
            sl = i % 3
            yv = yring[:, D * sl:D * (sl + 1)]
            bk0, bk1 = (6, 7) if i % 2 == 0 else (4, 5)
            def s_y5(s, i=i, yv=yv):
                if i >= 3:
                    wait(s, "eA", b5_tok[i - 3][0])
                    wait(s, "qG", b5_tok[i - 3][1])
                inc(s.dma_start(out=yv, in_=y_d[128 * i:128 * (i + 1), :]), "qS", 16)
            on("sync", s_y5)
            tok_y = P.cnt["qS"]
            def t_wo(t, i=i, bk0=bk0, bk1=bk1, tok=tok_y):
                if i == 0:
                    wait(t, "qG", wo_tok)
                    wait(t, "eV", cb_tok)
                P.bank_wait(t, "pb%d" % bk0); P.bank_wait(t, "pb%d" % bk1)
                for pp in range(4):
                    t.matmul(pb[bk0][0:128, 0:512],
                             hebfm[:, T * pp + 128 * i:T * pp + 128 * (i + 1)],
                             wout_sb[:, D * pp:D * pp + 512],
                             start=(pp == 0), stop=False)
                    t.matmul(pb[bk1][0:128, 0:512],
                             hebfm[:, T * pp + 128 * i:T * pp + 128 * (i + 1)],
                             wout_sb[:, D * pp + 512:D * pp + 1024],
                             start=(pp == 0), stop=False)
                t.matmul(pb[bk0][0:128, 0:512], ones5b[:],
                         boutb[0:1, 0:512], start=False, stop=True)
                ins = t.matmul(pb[bk1][0:128, 0:512], ones5b[:],
                               boutb[0:1, 512:1024], start=False, stop=True)
                inc(ins, "eT")
            on("tensor", t_wo)
            def v_y2(v, i=i, yv=yv, bk0=bk0, bk1=bk1, tok=P.cnt["eT"], toky=tok_y):
                wait(v, "eT", tok)
                wait(v, "qS", toky)
                v.tensor_tensor(yv[:, 0:512], pb[bk0][0:128, 0:512], yv[:, 0:512], ALU.add)
                inc(v.tensor_tensor(yv[:, 512:1024], pb[bk1][0:128, 0:512], yv[:, 512:1024], ALU.add), "eV")
            on("vector", v_y2)
            P.bank_set("pb%d" % bk0, "eV"); P.bank_set("pb%d" % bk1, "eV")
            tok_y2 = P.cnt["eV"]
            def g_y2(gp, i=i, yv=yv, tok=tok_y2):
                wait(gp, "eV", tok)
                inc(gp.dma_start(out=y_d[128 * i:128 * (i + 1), :], in_=yv), "qG", 16)
            on("gpsimd", g_y2)
            def a_s3(a, i=i, yv=yv, tok=tok_y2):
                wait(a, "eV", tok)
                a.activation(junkA[:], yv, AF.Copy, scale=INV_D, accum_out=m3a[:, i:i + 1])
                inc(a.activation(junkA2[:], yv, AF.Square, scale=INV_SQ_D,
                                 accum_out=q3a[:, i:i + 1]), "eA")
            on("scalar", a_s3)
            b5_tok[i] = (P.cnt["eA"], P.cnt["qG"])

        def v_f3(v, tok=P.cnt["eA"]):
            wait(v, "eA", tok)
            ch(v, "eV", v.tensor_tensor(st1[:], m3a[:], m3a[:], ALU.mult))
            inc(v.tensor_tensor(st2[:], q3a[:], st1[:], ALU.subtract), "eV")
        on("vector", v_f3)
        def a_f3(a, tok=P.cnt["eV"]):
            wait(a, "eV", tok)
            inc(a.activation(st1[:], st2[:], AF.Sqrt, bias=epscol[:, 0:1]), "eA")
        on("scalar", a_f3)
        def v_f3b(v, tok=P.cnt["eA"]):
            wait(v, "eA", tok)
            inc(v.reciprocal(r3a[:], st1[:]), "eV")
        on("vector", v_f3b)

        if upto == "b5":
            dy2 = dout("d_y2", [T, D])
            def g_db5(gp):
                wait_all(gp)
                for i in range(NT):
                    inc(gp.dma_start(out=dy2[128 * i:128 * (i + 1), :],
                                     in_=y_d[128 * i:128 * (i + 1), :]), "qG", 16)
                wait_all(gp, "qG")
            on("gpsimd", g_db5)
            P.barrier()
            esB5a.close()
            esB2.close()
            esBH.close()
            esB.close()
            return nc, dbg_outs

        P.barrier()
        esB5a.close()
        esB2.close()
        esBH.close()

        P.scope("moe")
        esT = contextlib.ExitStack()
        gt = sb("gt", [128, 2 * D], st=esT)
        xstg = sb("xstg", [128, 8 * 128], BF16, st=esT)
        yring2 = sb("yring2", [128, 2 * D], st=esT)
        jxb = sb("jxb", [128, D], BF16, st=esT)
        esT2 = contextlib.ExitStack()
        G3bc = sb("G3bc", [128, D], st=esT2)
        B3bc = sb("B3bc", [128, D], st=esT2)
        lrow = sb("lrow", [1, D], st=esT2)

        P.scope("b5b")
        for (gsrc, dstb) in ((ln3g_d, G3bc), (ln3b_d, B3bc)):
            def s_l3(s, gsrc=gsrc):
                wait(s, "eT", P.cnt["eT"])
                inc(s.dma_start(out=vsm[:], in_=gsrc), "qS", 16)
            on("sync", s_l3)
            def g_l3(gp, tok=P.cnt["qS"]):
                wait(gp, "qS", tok)
                wait(gp, "qS", P.cnt["qS"])
                inc(gp.dma_start(out=row_d[R_T1, 0:D].rearrange("(a b) -> a b", a=8), in_=vsm[:]), "qG", 16)
            on("gpsimd", g_l3)
            def s_l3b(s, tok=P.cnt["qG"]):
                wait(s, "qG", tok)
                inc(s.dma_start(out=lrow[:], in_=row_d[R_T1:R_T1 + 1, 0:D]), "qS", 16)
            on("sync", s_l3b)
            def t_l3(t, tok=P.cnt["qS"]):
                wait(t, "qS", tok)
                P.bank_wait(t, "pb0"); P.bank_wait(t, "pb1")
                t.matmul(pb[0][0:128, 0:512], ones1f[:], lrow[0:1, 0:512], start=True, stop=True)
                inc(t.matmul(pb[1][0:128, 0:512], ones1f[:], lrow[0:1, 512:1024], start=True, stop=True), "eT")
            on("tensor", t_l3)
            def v_l3x(v, dstb=dstb, tok=P.cnt["eT"]):
                wait(v, "eT", tok)
                v.tensor_copy(dstb[:, 0:512], pb[0][0:128, 0:512])
                inc(v.tensor_copy(dstb[:, 512:1024], pb[1][0:128, 0:512]), "eV")
            on("vector", v_l3x)
            P.bank_set("pb0", "eV"); P.bank_set("pb1", "eV")

        b5b_tok = {}
        xstg_war = {}
        xn3st_tok = {}
        def emit_b5b_tile(i):
            sl = i % 2
            yv = yring2[:, D * sl:D * (sl + 1)]
            def s_y6(s, i=i, yv=yv):
                if i >= 2:
                    wait(s, "qA", b5b_tok[i - 2])
                inc(s.dma_start(out=yv, in_=y_d[128 * i:128 * (i + 1), :]), "qS", 16)
            on("sync", s_y6)
            tok_y = P.cnt["qS"]
            def v_x3(v, i=i, yv=yv, tok=tok_y, jw=(b5b_tok.get(i - 1))):
                if i == 0:
                    wait(v, "eV", P.cnt["eV"])
                wait(v, "qS", tok)
                if jw is not None:
                    wait(v, "qA", jw)
                ch(v, "eV", v.tensor_scalar(junkV[:], yv, m3a[:, i:i + 1], r3a[:, i:i + 1],
                                            ALU.subtract, ALU.mult))
                ch(v, "eV", v.tensor_tensor(junkA[:], junkV[:], G3bc[:], ALU.mult))
                inc(v.tensor_tensor(jxb[:], junkA[:], B3bc[:], ALU.add), "eV")
            on("vector", v_x3)
            tokXN3 = P.cnt["eV"]
            for c in range(8):
                def a_t3(a, c=c, i=i, tok=tokXN3, war=xstg_war.get(c)):
                    if c == 0:
                        wait(a, "eV", tok)
                    if war is not None:
                        wait(a, "qG", war)
                    inc(a.dma_start_transpose(out=xstg[:, 128 * c:128 * (c + 1)],
                                              in_=jxb[:, 128 * c:128 * (c + 1)]), "qA", 16)
                on("scalar", a_t3)
                def g_x3s(gp, c=c, i=i, tok=P.cnt["qA"]):
                    wait(gp, "qA", tok)
                    inc(gp.dma_start(out=xn3_d[128 * c:128 * (c + 1), 128 * i:128 * (i + 1)],
                                     in_=xstg[:, 128 * c:128 * (c + 1)]), "qG", 16)
                on("gpsimd", g_x3s)
                xstg_war[c] = P.cnt["qG"]
            b5b_tok[i] = P.cnt["qA"]
            xn3st_tok[i] = P.cnt["qG"]

        P.scope("moe")
        esC = contextlib.ExitStack()
        xlnT = sb("xlnT", [128, 2 * 8 * CAP], BF16, st=esC)
        hbuf = sb("hbuf", [128, 32 * CAP], BF16, st=esC)
        w1sl = sb("w1sl", [128, 2 * 4096], BF16, st=esC)
        w2res = sb("w2res", [128, 32 * 512], BF16, st=esC)
        y2r = sb("y2r", [128, 3 * 512], st=esC)
        glnT = sb("glnT", [128, 16], st=esC)
        blnT = sb("blnT", [128, 16], st=esC)
        b1T = sb("b1T", [128, 64], st=esC)
        b2rowb = sb("b2rowb", [1, 2 * D], BF16, st=esC)
        b2row = sb("b2row", [1, D], st=esC)
        onesCb = sb("onesCb", [1, 128], BF16, st=esC)
        def v_oc(v):
            inc(v.memset(onesCb[:], 1.0), "eV")
        on("vector", v_oc)

        W2PREV = [0]
        vsm_free = [(P.cnt["eT"], P.cnt["eV"])]
        vsm_qg = [P.cnt["qG"]]
        mm1_par = {0: 0, 1: 0}
        mm1_prev = [None]
        rd_prev = [None]

        def emit_prep(e, par):
            glnTp = glnT[:, 8 * par:8 * (par + 1)]
            blnTp = blnT[:, 8 * par:8 * (par + 1)]
            b1Tp = b1T[:, 32 * par:32 * (par + 1)]
            b2rowbp = b2rowb[:, D * par:D * (par + 1)]
            def s_pe(s, e=e, tok=vsm_free[0], tokg=vsm_qg[0]):
                wait(s, "eT", tok[0]); wait(s, "eV", tok[1])
                if tokg is not None:
                    wait(s, "qG", tokg)
                inc(s.dma_start(out=vsm[:], in_=mlng_d[e]), "qS", 16)
                inc(s.dma_start(out=vsm32[:], in_=mb1_d[e]), "qS", 16)
                inc(s.dma_start(out=b2row[:], in_=mb2_d[e]), "qS", 16)
            on("sync", s_pe)
            def t_pe(t, tok=P.cnt["qS"]):
                wait(t, "qS", tok)
                P.bank_wait(t, "pb0")
                t.transpose(pb[0][0:128, 0:8], vsm[:], idn[0:8, 0:8])
                inc(t.transpose(pb[0][0:128, 16:48], vsm32[:], idn[0:32, 0:32]), "eT")
            on("tensor", t_pe)
            def v_pe(v, tok=P.cnt["eT"]):
                wait(v, "eT", tok)
                v.tensor_copy(glnTp[:], pb[0][0:128, 0:8])
                v.tensor_copy(b1Tp[:], pb[0][0:128, 16:48])
                inc(v.tensor_copy(b2rowbp[:], b2row[:]), "eV")
            on("vector", v_pe)
            P.bank_set("pb0", "eV")
            b2_rd = P.cnt["eV"]
            def s_pe2(s, e=e, tok=P.cnt["eT"]):
                wait(s, "eT", tok)
                inc(s.dma_start(out=vsm[:], in_=mlnb_d[e]), "qS", 16)
            on("sync", s_pe2)
            def t_pe2(t, tok=P.cnt["qS"]):
                wait(t, "qS", tok)
                P.bank_wait(t, "pb2")
                inc(t.transpose(pb[2][0:128, 0:8], vsm[:], idn[0:8, 0:8]), "eT")
            on("tensor", t_pe2)
            vsm_free[0] = (P.cnt["eT"], b2_rd)
            def v_pe2(v, tok=P.cnt["eT"]):
                wait(v, "eT", tok)
                inc(v.tensor_copy(blnTp[:], pb[2][0:128, 0:8]), "eV")
            on("vector", v_pe2)
            P.bank_set("pb2", "eV")
            xbase = 8 * CAP * par
            war = mm1_par[par]
            for dc in range(8):
                def s_xT(s, e=e, dc=dc, war=war):
                    if dc == 0:
                        wait(s, "eT", war)
                    inc(s.dma_start_transpose(
                        out=xlnT[:, xbase + CAP * dc:xbase + CAP * (dc + 1)],
                        in_=xs_d[CAP * e:CAP * (e + 1), 128 * dc:128 * (dc + 1)]), "qS", 16)
                on("sync", s_xT)
            tok_xT = P.cnt["qS"]
            def v_aff(v, tok=tok_xT):
                wait(v, "qS", tok)
                for dc in range(8):
                    ins = v.tensor_scalar(xlnT[:, xbase + CAP * dc:xbase + CAP * (dc + 1)],
                                          xlnT[:, xbase + CAP * dc:xbase + CAP * (dc + 1)],
                                          glnTp[:, dc:dc + 1], blnTp[:, dc:dc + 1],
                                          ALU.mult, ALU.add)
                inc(ins, "eV")
            on("vector", v_aff)
            return P.cnt["eV"]

        def emit_mm1(e, par, tok_xlnT):
            xbase = 8 * CAP * par
            b1Tp = b1T[:, 32 * par:32 * (par + 1)]
            prev = mm1_prev[0]
            w1_tok = {}
            for fsl in range(8):
                slw = fsl % 2
                def g_w1(gp, e=e, fsl=fsl, slw=slw, prev=prev):
                    if fsl >= 2:
                        wait(gp, "eT", w1_tok[fsl - 2])
                    elif prev is not None:
                        wait(gp, "eT", prev[fsl + 6])
                    else:
                        wait(gp, "eT", P.cnt["eT"])
                    inc(gp.dma_start(out=w1sl.rearrange("p (s dc f) -> p s dc f", s=2, dc=8)[:, slw],
                                     in_=mw1_d[e].rearrange("(dc p) f -> p dc f", p=128)[:, :, 512 * fsl:512 * (fsl + 1)],
                                     max_dma_last_dim=8192), "qG", 16)
                on("gpsimd", g_w1)
                tok_w1 = P.cnt["qG"]
                for fs4 in range(4):
                    fs = 4 * fsl + fs4
                    def t_h(t, fs=fs, fs4=fs4, slw=slw, tok=tok_w1, tokx=tok_xlnT):
                        if fs == 0:
                            wait(t, "eV", tokx)
                        wait(t, "qG", tok)
                        P.bank_wait(t, "pb6"); P.bank_wait(t, "pb7")
                        for dc in range(8):
                            t.matmul(pb[6][0:128, 0:CAP // 2],
                                     w1sl[:, 4096 * slw + 512 * dc + 128 * fs4:4096 * slw + 512 * dc + 128 * (fs4 + 1)],
                                     xlnT[:, xbase + CAP * dc + 0:xbase + CAP * dc + CAP // 2],
                                     start=(dc == 0), stop=(dc == 7))
                            ins = t.matmul(pb[7][0:128, 0:CAP // 2],
                                           w1sl[:, 4096 * slw + 512 * dc + 128 * fs4:4096 * slw + 512 * dc + 128 * (fs4 + 1)],
                                           xlnT[:, xbase + CAP * dc + CAP // 2:xbase + CAP * dc + CAP],
                                           start=(dc == 0), stop=(dc == 7))
                        inc(ins, "eT")
                    on("tensor", t_h)
                    w1_tok[fsl] = P.cnt["eT"]
                    def a_h(a, fs=fs, tok=P.cnt["eT"]):
                        wait(a, "eT", tok)
                        a.activation(hbuf[:, CAP * fs:CAP * fs + CAP // 2], pb[6][0:128, 0:CAP // 2],
                                     AF.Relu, bias=b1Tp[:, fs:fs + 1])
                        inc(a.activation(hbuf[:, CAP * fs + CAP // 2:CAP * (fs + 1)], pb[7][0:128, 0:CAP // 2],
                                         AF.Relu, bias=b1Tp[:, fs:fs + 1]), "eA")
                    on("scalar", a_h)
                    P.bank_set("pb6", "eA"); P.bank_set("pb7", "eA")
            mm1_prev[0] = w1_tok
            mm1_par[par] = P.cnt["eT"]
            return (P.cnt["eA"], P.cnt["eV"])

        ye_store = [0, 0, 0]
        def emit_mm2(e, par, tok_h):
            b2rowbp = b2rowb[:, D * par:D * (par + 1)]
            rd_new = {}
            for h2 in range(2):
                tokq = {}
                for q_ in range(4):
                    def g_w2(gp, e=e, h2=h2, q_=q_):
                        if h2 == 0:
                            if rd_prev[0] is not None:
                                wait(gp, "eT", rd_prev[0][q_])
                            else:
                                wait(gp, "eT", W2PREV[0])
                        else:
                            wait(gp, "eT", rd_new[(0, q_)])
                        inc(gp.dma_start(
                            out=w2res.rearrange("p (c f) -> p c f", c=32)[:, 8 * q_:8 * (q_ + 1)],
                            in_=mw2_d[e].rearrange("(c p) f -> p c f", p=128)[:, 8 * q_:8 * (q_ + 1),
                                                                             512 * h2:512 * (h2 + 1)],
                            max_dma_last_dim=8192), "qG", 16)
                    on("gpsimd", g_w2)
                    tokq[q_] = P.cnt["qG"]
                for q_ in range(4):
                    def t_y(t, q_=q_, h2=h2, tok=tokq[q_], tokh=tok_h):
                        if q_ == 0 and h2 == 0:
                            wait(t, "eA", tokh[0]); wait(t, "eV", tokh[1])
                        wait(t, "qG", tok)
                        for st_ in range(CAP // 128):
                            if q_ == 0:
                                P.bank_wait(t, f"pb{st_}")
                            for fc in range(8 * q_, 8 * (q_ + 1)):
                                ins = t.matmul(pb[st_][0:128, 0:512],
                                               hbuf[:, CAP * fc + 128 * st_:CAP * fc + 128 * (st_ + 1)],
                                               w2res[:, 512 * fc:512 * (fc + 1)],
                                               start=(fc == 0), stop=False)
                        if q_ == 3:
                            for st_ in range(CAP // 128):
                                ins = t.matmul(pb[st_][0:128, 0:512], onesCb[:],
                                               b2rowbp[0:1, 512 * h2:512 * (h2 + 1)],
                                               start=False, stop=True)
                        inc(ins, "eT")
                    on("tensor", t_y)
                    rd_new[(h2, q_)] = P.cnt["eT"]
                tok_mm2 = P.cnt["eT"]
                for st_ in range(CAP // 128):
                    sl = st_ % 3
                    yslot = y2r[:, 512 * sl:512 * (sl + 1)]
                    if st_ % 2 == 0:
                        def v_ye(v, st_=st_, yslot=yslot, tok=tok_mm2, tokst=ye_store[sl]):
                            wait(v, "eT", tok)
                            wait(v, "qS", tokst)
                            inc(v.tensor_copy(yslot, pb[st_][0:128, 0:512]), "eV")
                        on("vector", v_ye)
                        P.bank_set(f"pb{st_}", "eV")
                        ev_tok = ("eV", P.cnt["eV"])
                    else:
                        def a_ye(a, st_=st_, yslot=yslot, tok=tok_mm2, tokst=ye_store[sl]):
                            wait(a, "eT", tok)
                            wait(a, "qS", tokst)
                            inc(a.activation(yslot, pb[st_][0:128, 0:512], AF.Copy), "eA")
                        on("scalar", a_ye)
                        P.bank_set(f"pb{st_}", "eA")
                        ev_tok = ("eA", P.cnt["eA"])
                    def s_ye(s, e=e, st_=st_, yslot=yslot, h2=h2, tok=ev_tok):
                        wait(s, tok[0], tok[1])
                        inc(s.dma_start(out=y2s_d[CAP * e + 128 * st_:CAP * e + 128 * (st_ + 1),
                                                  512 * h2:512 * (h2 + 1)],
                                        in_=yslot), "qS", 16)
                    on("sync", s_ye)
                    ye_store[sl] = P.cnt["qS"]
            rd_prev[0] = {q_: rd_new[(1, q_)] for q_ in range(4)}
            W2PREV[0] = P.cnt["eT"]

        tok_x = emit_prep(0, 0)
        for e in range(NE):
            par = e % 2
            tok_h = emit_mm1(e, par, tok_x)
            if e + 1 < NE:
                tok_x = emit_prep(e + 1, 1 - par)
            emit_mm2(e, par, tok_h)
        for i5 in range(NT):
            emit_b5b_tile(i5)

        if upto == "c":
            dy2s = dout("d_y2s", [NS, D])
            def g_dbc(gp):
                wait_all(gp)
                for j in range(NS // 128):
                    inc(gp.dma_start(out=dy2s[128 * j:128 * (j + 1), :],
                                     in_=y2s_d[128 * j:128 * (j + 1), :]), "qG", 16)
                wait_all(gp, "qG")
            on("gpsimd", g_dbc)
            P.barrier()
            esC.close()
            esB.close()
            return nc, dbg_outs

        P.barrier()
        esC.close()

        d_tok = {}
        def emit_d_tile(i):
            sl = i % 2
            gv = gt[:, D * sl:D * (sl + 1)]
            def v_gz(v, i=i, gv=gv):
                if i >= 2:
                    wait(v, "qG", d_tok[i - 2])
                inc(v.memset(gv, 0.0), "eV")
            on("vector", v_gz)
            def g_ga(gp, i=i, gv=gv, tok=P.cnt["eV"]):
                wait(gp, "eV", tok)
                inc(gp.indirect_dma_start(
                    out=gv, out_offset=None,
                    in_=y2s_d, in_offset=bass.IndirectOffsetOnAxis(ap=slotu[:, i:i + 1], axis=0),
                    bounds_check=BREG[0], oob_is_err=False), "qG", 16)
            on("gpsimd", g_ga)
            def v_al(v, i=i, gv=gv, tok=P.cnt["qG"]):
                wait(v, "qG", tok)
                inc(v.tensor_scalar(gv, gv, alpha_a[:, i:i + 1], None, ALU.mult), "eV")
            on("vector", v_al)
            def g_ac2(gp, i=i, gv=gv, tok=P.cnt["eV"]):
                wait(gp, "eV", tok)
                inc(gp.dma_start(out=out_d[128 * i:128 * (i + 1), :], in_=gv,
                                 accum_op=ALU.add), "qG", 16)
            on("gpsimd", g_ac2)
            d_tok[i] = P.cnt["qG"]

        P.scope("ff")
        P.barrier()
        esT2.close()
        esB6 = contextlib.ExitStack()
        gelu2 = sb("gelu2", [128, 2 * 32 * 512], BF16, st=esB6)
        wf1sl = sb("wf1sl", [128, 2 * 4096], BF16, st=esB6)
        wf2h = sb("wf2h", [128, 2 * 4 * 1024], BF16, st=esB6)
        xn3buf = sb("xn3buf", [128, 8 * 1024], BF16, st=esB6)
        bff1T = sb("bff1T", [128, 32], st=esB6)
        b2lrow = sb("b2lrow", [1, D], st=esB6)
        b2lrowb = sb("b2lrowb", [1, D], BF16, st=esB6)
        onesFb = sb("onesFb", [1, 128], BF16, st=esB6)

        def s_bf1(s):
            wait(s, "eT", P.cnt["eT"])
            inc(s.dma_start(out=vsm32[:], in_=bff1_d), "qS", 16)
            inc(s.dma_start(out=b2lrow[:], in_=bff2_d), "qS", 16)
        on("sync", s_bf1)
        def t_bf1(t, tok=P.cnt["qS"]):
            wait(t, "qS", tok)
            P.bank_wait(t, "pb0")
            inc(t.transpose(pb[0][0:128, 0:32], vsm32[:], idn[0:32, 0:32]), "eT")
        on("tensor", t_bf1)
        def v_bf1(v, tok=P.cnt["eT"]):
            wait(v, "eT", tok)
            v.tensor_copy(bff1T[:], pb[0][0:128, 0:32])
            v.memset(onesFb[:], 1.0)
            inc(v.tensor_copy(b2lrowb[:], b2lrow[:]), "eV")
        on("vector", v_bf1)
        P.bank_set("pb0", "eV")
        tok_bf1 = P.cnt["eV"]

        F2PREV = [0]
        ACCSL = [{}]
        YRD = [{}]
        b6_acc = {}
        f1_prev = [None]
        x3_rd = [None]
        PF = {}
        def emit_prefetch(pr, gate_x3, gate_w):
            def s_x3(s, pr=pr, prev=gate_x3, st_tok=xn3st_tok[8 * pr + 7]):
                wait(s, "qG", st_tok)
                if prev is not None:
                    wait(s, "eT", prev)
                inc(s.dma_start(out=xn3buf.rearrange("p (dc t) -> p dc t", dc=8),
                                in_=xn3_d.rearrange("(dc p) t -> p dc t", p=128)[:, :, 1024 * pr:1024 * (pr + 1)],
                                max_dma_last_dim=8192), "qS", 16)
            on("sync", s_x3)
            tok_x3 = P.cnt["qS"]
            tw = {}
            for fsl in (0, 1):
                def s_wf1(s, fsl=fsl, gate_w=gate_w):
                    if gate_w is not None:
                        wait(s, "eT", gate_w[fsl + 6])
                    else:
                        wait(s, "eA", P.cnt["eA"])
                    inc(s.dma_start(out=wf1sl.rearrange("p (s dc f) -> p s dc f", s=2, dc=8)[:, fsl],
                                    in_=wff1b_d.rearrange("(dc p) f -> p dc f", p=128)[:, :, 512 * fsl:512 * (fsl + 1)],
                                    max_dma_last_dim=8192), "qS", 16)
                on("sync", s_wf1)
                tw[fsl] = P.cnt["qS"]
            PF[pr] = (tok_x3, tw)

        emit_prefetch(0, None, None)
        for pr in range(4):
            tok_x3, tw01 = PF[pr]
            f1_tok = {}
            for fsl in range(8):
                slw = fsl % 2
                if fsl < 2:
                    tok_w = tw01[fsl]
                else:
                    def s_wf1(s, fsl=fsl, slw=slw):
                        wait(s, "eT", f1_tok[fsl - 2])
                        inc(s.dma_start(out=wf1sl.rearrange("p (s dc f) -> p s dc f", s=2, dc=8)[:, slw],
                                        in_=wff1b_d.rearrange("(dc p) f -> p dc f", p=128)[:, :, 512 * fsl:512 * (fsl + 1)],
                                        max_dma_last_dim=8192), "qS", 16)
                    on("sync", s_wf1)
                    tok_w = P.cnt["qS"]
                for fs4 in range(4):
                    fs = 4 * fsl + fs4
                    bk0, bk1 = (6, 7) if fs % 2 == 0 else (4, 5)
                    def t_f1(t, fs=fs, fs4=fs4, slw=slw, bk0=bk0, bk1=bk1, tok=tok_w, tokx=tok_x3):
                        if fs == 0:
                            wait(t, "qS", tokx)
                        wait(t, "qS", tok)
                        P.bank_wait(t, f"pb{bk0}"); P.bank_wait(t, f"pb{bk1}")
                        for dc in range(8):
                            wsl = wf1sl[:, 4096 * slw + 512 * dc + 128 * fs4:4096 * slw + 512 * dc + 128 * (fs4 + 1)]
                            t.matmul(pb[bk0][0:128, 0:512], wsl,
                                     xn3buf[:, 1024 * dc:1024 * dc + 512],
                                     start=(dc == 0), stop=(dc == 7))
                            ins = t.matmul(pb[bk1][0:128, 0:512], wsl,
                                           xn3buf[:, 1024 * dc + 512:1024 * (dc + 1)],
                                           start=(dc == 0), stop=(dc == 7))
                        inc(ins, "eT")
                    on("tensor", t_f1)
                    f1_tok[fsl] = P.cnt["eT"]
                    def a_g(a, fs=fs, bk0=bk0, bk1=bk1, tok=P.cnt["eT"]):
                        wait(a, "eT", tok)
                        a.activation(gelu2[:, 512 * fs:512 * (fs + 1)], pb[bk0][0:128, 0:512],
                                     AF.Gelu_apprx_tanh, bias=bff1T[:, fs:fs + 1])
                        inc(a.activation(gelu2[:, 16384 + 512 * fs:16384 + 512 * (fs + 1)], pb[bk1][0:128, 0:512],
                                         AF.Gelu_apprx_tanh, bias=bff1T[:, fs:fs + 1]), "eA")
                    on("scalar", a_g)
                    P.bank_set(f"pb{bk0}", "eA"); P.bank_set(f"pb{bk1}", "eA")
            f1_prev[0] = f1_tok
            x3_rd[0] = P.cnt["eT"]
            tok_g = P.cnt["eA"]
            if pr + 1 < 4:
                emit_prefetch(pr + 1, x3_rd[0], f1_tok)
            for half in range(2):
                tc = 2 * pr + half
                gbase = 16384 * half
                tokq = {}
                f2_rd = {}
                def emit_wf2(q_, gate):
                    slq = q_ % 2
                    def g_wf2(gp, q_=q_, slq=slq, gate=gate):
                        wait(gp, "eT", gate)
                        inc(gp.dma_start(
                            out=wf2h.rearrange("p (s c f) -> p s c f", s=2, c=4)[:, slq],
                            in_=wff2b_d.rearrange("(c p) f -> p c f", p=128)[:, 4 * q_:4 * (q_ + 1), :],
                            max_dma_last_dim=8192), "qG", 16)
                    on("gpsimd", g_wf2)
                    tokq[q_] = P.cnt["qG"]
                for q_ in (0, 1):
                    emit_wf2(q_, F2PREV[0][q_ + 6] if isinstance(F2PREV[0], dict) else F2PREV[0])
                for q_ in range(8):
                    slq = q_ % 2
                    def t_f2(t, q_=q_, slq=slq, gbase=gbase, tok=tokq[q_], tokg=tok_g, tokb=tok_bf1):
                        if q_ == 0:
                            wait(t, "eA", tokg)
                            wait(t, "eV", tokb)
                        wait(t, "qG", tok)
                        for tt in range(4):
                            if q_ == 0:
                                P.bank_wait(t, f"pb{2 * tt}"); P.bank_wait(t, f"pb{2 * tt + 1}")
                            for fc in range(4 * q_, 4 * (q_ + 1)):
                                wof = 4096 * slq + 1024 * (fc - 4 * q_)
                                t.matmul(pb[2 * tt][0:128, 0:512],
                                         gelu2[:, gbase + 512 * fc + 128 * tt:gbase + 512 * fc + 128 * (tt + 1)],
                                         wf2h[:, wof:wof + 512],
                                         start=(fc == 0), stop=False)
                                ins = t.matmul(pb[2 * tt + 1][0:128, 0:512],
                                               gelu2[:, gbase + 512 * fc + 128 * tt:gbase + 512 * fc + 128 * (tt + 1)],
                                               wf2h[:, wof + 512:wof + 1024],
                                               start=(fc == 0), stop=False)
                        if q_ == 7:
                            for tt in range(4):
                                t.matmul(pb[2 * tt][0:128, 0:512], onesFb[:],
                                         b2lrowb[0:1, 0:512], start=False, stop=True)
                                ins = t.matmul(pb[2 * tt + 1][0:128, 0:512], onesFb[:],
                                               b2lrowb[0:1, 512:1024], start=False, stop=True)
                        inc(ins, "eT")
                    on("tensor", t_f2)
                    f2_rd[q_] = P.cnt["eT"]
                    if q_ + 2 < 8:
                        emit_wf2(q_ + 2, f2_rd[q_])
                tok_f2 = P.cnt["eT"]
                for tt in range(4):
                    i = 4 * tc + tt
                    yv = yring2[:, D * (tt % 2):D * (tt % 2 + 1)]
                    ytok = YRD[0].get(tt % 2)
                    def s_y7b(s, i=i, yv=yv, ytok=ytok):
                        if ytok is not None:
                            wait(s, "eV", ytok)
                        inc(s.dma_start(out=yv, in_=y_d[128 * i:128 * (i + 1), :]), "qS", 16)
                    on("sync", s_y7b)
                    tok_y = P.cnt["qS"]
                    yf = junkA[:] if tt % 2 == 0 else junkV[:]
                    awar = ACCSL[0].get(tt % 2)
                    def v_yf(v, tt=tt, yv=yv, yf=yf, tok=tok_f2, toky=tok_y, war=awar):
                        wait(v, "eT", tok)
                        wait(v, "qS", toky)
                        if war is not None:
                            wait(v, "qG", war)
                        v.tensor_tensor(yf[:, 0:512], pb[2 * tt][0:128, 0:512], yv[:, 0:512], ALU.add)
                        inc(v.tensor_tensor(yf[:, 512:1024], pb[2 * tt + 1][0:128, 0:512],
                                            yv[:, 512:1024], ALU.add), "eV")
                    on("vector", v_yf)
                    P.bank_set(f"pb{2 * tt}", "eV"); P.bank_set(f"pb{2 * tt + 1}", "eV")
                    YRD[0][tt % 2] = P.cnt["eV"]
                    def g_acc(gp, i=i, yf=yf, tok=P.cnt["eV"]):
                        wait(gp, "eV", tok)
                        inc(gp.dma_start(out=out_d[128 * i:128 * (i + 1), :],
                                         in_=yf, accum_op=ALU.add), "qG", 16)
                    on("gpsimd", g_acc)
                    ACCSL[0][tt % 2] = P.cnt["qG"]
                    b6_acc[i] = P.cnt["qG"]
                F2PREV[0] = {q_: f2_rd[q_] for q_ in range(8)}
                for i5 in range(4 * tc, 4 * (tc + 1)):
                    emit_d_tile(i5)

        if upto == "f6":
            dgel = dout("d_gelu", [128, 32 * 512], BF16)
            dxn3 = dout("d_xn3", [128, 8 * 512], BF16)
            def g_df6(gp):
                wait_all(gp)
                inc(gp.dma_start(out=dgel, in_=gelu2[:, 16384:32768]), "qG", 16)
                for c in range(8):
                    inc(gp.dma_start(out=dxn3[:, 512 * c:512 * (c + 1)],
                                     in_=xn3buf[:, 1024 * c + 512:1024 * (c + 1)]), "qG", 16)
                wait_all(gp, "qG")
            on("gpsimd", g_df6)
            P.barrier()
            esB6.close()
            esT.close()
            esB.close()
            return nc, dbg_outs

        P.barrier()
        esB6.close()
        esT.close()
        esB.close()

        P.scope_end()
        def g_fin(gp):
            wait_all(gp)
        on("gpsimd", g_fin)
        def s_fin(s):
            wait_all(s)
        on("sync", s_fin)
    return nc, dbg_outs


def _in_map(inputs, b):
    c = _host_consts()
    m = {
        "x": np.ascontiguousarray(inputs["L0_attn_x"][b]),
        "ln1_g": inputs["L0_attn_ln1_g"].reshape(8, 128),
        "ln1_b": inputs["L0_attn_ln1_b"].reshape(8, 128),
        "ln2_g": inputs["L0_attn_ln2_g"].reshape(8, 128),
        "ln2_b": inputs["L0_attn_ln2_b"].reshape(8, 128),
        "ln3_g": inputs["L0_attn_ln3_g"].reshape(8, 128),
        "ln3_b": inputs["L0_attn_ln3_b"].reshape(8, 128),
        "w_potT": np.ascontiguousarray(inputs["L0_attn_w_pot"].T),
        "b_pot": inputs["L0_attn_b_pot"].reshape(1, 2),
        "w_bk": inputs["L0_attn_w_bk"],
        "b_bk": inputs["L0_attn_b_bk"].reshape(1, D),
        "w_qkv": inputs["L0_attn_w_qkv"],
        "b_qkv": inputs["L0_attn_b_qkv"].reshape(1, -1),
        "w_out": inputs["L0_attn_w_out"],
        "b_out": inputs["L0_attn_b_out"].reshape(1, D),
        "w_ff1": inputs["L0_attn_w_ff1"],
        "b_ff1": inputs["L0_attn_b_ff1"].reshape(32, 128),
        "w_ff2": inputs["L0_attn_w_ff2"],
        "b_ff2": inputs["L0_attn_b_ff2"].reshape(1, D),
        "cent": inputs["L0_ffn_centroids"],
        "m_ln_g": inputs["L0_ffn_ln_g"].reshape(NE, 8, 128),
        "m_ln_b": inputs["L0_ffn_ln_b"].reshape(NE, 8, 128),
        "m_w1": inputs["L0_ffn_w1"],
        "m_b1": inputs["L0_ffn_b1"].reshape(NE, 32, 128),
        "m_w2": inputs["L0_ffn_w2"],
        "m_b2": inputs["L0_ffn_b2"].reshape(NE, 1, D),
    }
    m.update(c)
    for k, v in list(m.items()):
        if isinstance(v, np.ndarray) and v.dtype == np.float64:
            m[k] = v.astype(np.float32)
    return m


def kernel(**inputs):
    nc, _ = build(dbg=0)
    in_maps = [_in_map(inputs, b) for b in range(8)]
    res = run_bass_kernel_spmd(nc, in_maps, core_ids=list(range(8)))
    return np.stack([r["out"] for r in res.results], axis=0)


# version 25 (final)
# speedup vs baseline: 1.2095x; 1.2095x over previous
import contextlib
import numpy as np
import concourse.bass as bass
import concourse.mybir as mybir
from concourse.bass_utils import run_bass_kernel_spmd

F32 = mybir.dt.float32
BF16 = mybir.dt.bfloat16
U32 = mybir.dt.uint32
ALU = mybir.AluOpType
AF = mybir.ActivationFunctionType
AX = mybir.AxisListType

T = 4096
D = 1024
NT = 32
CAP = 640
NE = 8
NS = NE * CAP
H, HD = 8, 64
FF = 4096
EPS = 1e-5
ETA = 0.1
DTC = 0.01
GAMMA_FLOOR = 0.1
INV_D = 1.0 / D
INV_SQ_D = float(1.0 / np.sqrt(D))
LN_ETA = float(np.log(ETA))


def _host_consts():
    import ml_dtypes
    c = {}
    c["c_idn"] = np.eye(128, dtype=np.float32)
    c["c_idnb"] = np.eye(128, dtype=np.float32).astype(ml_dtypes.bfloat16)
    tril = (np.arange(128)[:, None] <= np.arange(128)[None, :]).astype(np.float32)
    c["c_trilT"] = tril
    c["c_iota8"] = np.tile(np.arange(8, dtype=np.float32), (128, 1))
    colpack = np.zeros((128, 2), np.float32)
    for b in (0, 32, 64):
        colpack[b:b + 8, 0] = np.arange(8)
        colpack[b:b + 8, 1] = np.arange(8) * CAP
    c["c_colpack"] = colpack
    return c




class Prog:
    def __init__(self, nc, es):
        self.nc = nc
        self.blk = es.enter_context(nc.Block())
        self.sem = {}
        self.cnt = {}
        for s in ("qS", "qG", "qA", "qC", "qD", "eT", "eV", "eA"):
            self.sem[s] = es.enter_context(nc.semaphore(s))
            self.cnt[s] = 0
        self.bank_free = {}

    def on(self, eng, f):
        getattr(self.blk, eng)(f)

    def inc(self, instr, s, n=1):
        instr.then_inc(self.sem[s], n)
        self.cnt[s] += n
        return self.cnt[s]

    def wait(self, eng, s, v):
        eng.wait_ge(self.sem[s], v)

    def wait_all(self, eng, *ss):
        if not ss:
            ss = tuple(self.sem)
        for s in ss:
            eng.wait_ge(self.sem[s], self.cnt[s])

    def ch(self, eng, s, instr):
        tok = self.inc(instr, s)
        eng.wait_ge(self.sem[s], tok)
        return tok

    def bank_wait(self, eng, name):
        t = self.bank_free.get(name)
        if t is not None:
            self.wait(eng, t[0], t[1])

    def bank_set(self, name, s):
        self.bank_free[name] = (s, self.cnt[s])

    def scope(self, name):
        if getattr(self, "_scope", None) is not None:
            self.nc.leave_named_scope(self._scope[0], self._scope[1], False)
        sid, _ = self.nc.enter_named_scope(name, False)
        self._scope = (name, sid)

    def scope_end(self):
        if getattr(self, "_scope", None) is not None:
            self.nc.leave_named_scope(self._scope[0], self._scope[1], False)
            self._scope = None

    def barrier(self):
        snap = dict(self.cnt)
        for ename in ("sync", "gpsimd", "tensor", "vector", "scalar"):
            def w(e, snap=snap):
                for s, v in snap.items():
                    e.wait_ge(self.sem[s], v)
            self.on(ename, w)


def build(dbg=0, upto=None, noaff=False):
    nc = bass.Bass("TRN2", target_bir_lowering=False)

    def di(name, shape, dt=F32):
        return nc.dram_tensor(name, list(shape), dt, kind="ExternalInput").ap()

    x_d = di("x", [T, D])
    ln1g_d = di("ln1_g", [8, 128]); ln1b_d = di("ln1_b", [8, 128])
    ln2g_d = di("ln2_g", [8, 128]); ln2b_d = di("ln2_b", [8, 128])
    ln3g_d = di("ln3_g", [8, 128]); ln3b_d = di("ln3_b", [8, 128])
    wpotT_d = di("w_potT", [2, D])
    bpot_d = di("b_pot", [1, 2])
    wbk_d = di("w_bk", [2, D])
    bbk_d = di("b_bk", [1, D])
    wqkv_d = di("w_qkv", [D, 3 * H * HD])
    bqkv_d = di("b_qkv", [1, 3 * H * HD])
    wout_d = di("w_out", [H * HD, D])
    bout_d = di("b_out", [1, D])
    wff1_d = di("w_ff1", [D, FF])
    bff1_d = di("b_ff1", [32, 128])
    wff2_d = di("w_ff2", [FF, D])
    bff2_d = di("b_ff2", [1, D])
    cent_d = di("cent", [NE, D])
    mlng_d = di("m_ln_g", [NE, 8, 128])
    mlnb_d = di("m_ln_b", [NE, 8, 128])
    mw1_d = di("m_w1", [NE, D, FF])
    mb1_d = di("m_b1", [NE, 32, 128])
    mw2_d = di("m_w2", [NE, FF, D])
    mb2_d = di("m_b2", [NE, 1, D])
    idn_d = di("c_idn", [128, 128])
    idnb_d = di("c_idnb", [128, 128], BF16)
    trilT_d = di("c_trilT", [128, 128])
    iota8_d = di("c_iota8", [128, 8])
    colpack_d = di("c_colpack", [128, 2])

    out_d = nc.dram_tensor("out", [T, D], F32, kind="ExternalOutput").ap()
    y_d = nc.dram_tensor("ybuf", [T, D], F32, kind="Internal").ap()
    xs_d = nc.dram_tensor("xln_sorted", [NS, D], BF16, kind="Internal").ap()
    y2s_d = nc.dram_tensor("y2_sorted", [NS, D], F32, kind="Internal").ap()
    xn3_d = nc.dram_tensor("xn3fm", [D, T], BF16, kind="Internal").ap()
    row_d = nc.dram_tensor("rowscratch", [48, T], F32, kind="Internal").ap()
    wff1b_d = nc.dram_tensor("wff1b", [D, FF], BF16, kind="Internal").ap()
    wff2b_d = nc.dram_tensor("wff2b", [FF, D], BF16, kind="Internal").ap()

    dbg_outs = {}
    def dout(name, shape, dt=F32):
        tns = nc.dram_tensor(name, list(shape), dt, kind="ExternalOutput").ap()
        dbg_outs[name] = tns
        return tns

    es = contextlib.ExitStack()
    with es:
        def sb(name, shape, dt=F32, st=None):
            return (st or es).enter_context(nc.sbuf_tensor(name, list(shape), dt))

        P = Prog(nc, es)
        on, inc, wait, wait_all, ch = P.on, P.inc, P.wait, P.wait_all, P.ch
        pb = [es.enter_context(nc.psum_tensor(f"pb{k}", [128, 512], F32)) for k in range(8)]

        m0a = sb("m0a", [128, NT]); q0a = sb("q0a", [128, NT]); r0a = sb("r0a", [128, NT])
        m1a = sb("m1a", [128, NT]); q1a = sb("q1a", [128, NT]); r1a = sb("r1a", [128, NT])
        m1r1a = sb("m1r1a", [128, NT])
        alpha_a = sb("alpha_a", [128, NT]); amax_a = sb("amax_a", [128, NT])
        eid_a = sb("eid_a", [128, NT])
        potraw = sb("potraw", [128, 2 * NT])
        V_a = sb("V_a", [128, NT]); gam_a = sb("gam_a", [128, NT])
        slotu = sb("slotu", [128, NT], U32)
        st1 = sb("st1", [128, NT]); st2 = sb("st2", [128, NT])
        junkA = sb("junkA", [128, D]); junkA2 = sb("junkA2", [128, D])
        junkV = sb("junkV", [128, D])
        idn = sb("idn", [128, 128]); idnb = sb("idnb", [128, 128], BF16)
        trilT = sb("trilT", [128, 128])
        iota8 = sb("iota8", [128, 8]); colpack = sb("colpack", [128, 2])
        ones1f = sb("ones1f", [1, 128]); onescol = sb("onescol", [128, 1])
        ones8c = sb("ones8c", [8, 1])
        epscol = sb("epscol", [128, 1])
        lnecol = sb("lnecol", [128, 1])
        centT = sb("centT", [128, 8 * 10])
        csneg = sb("csneg", [128, 2]); beffp = sb("beffp", [128, 2])
        beff_row = sb("beff_row", [1, 2])
        g1c = sb("g1c", [128, 8])
        vsm = sb("vsm", [8, 128]); vsm32 = sb("vsm32", [32, 128])
        rowA = sb("rowA", [1, T])
        e1 = sb("e1", [128, 40]); e2 = sb("e2", [128, 40]); e3 = sb("e3", [128, 40])

        esA = contextlib.ExitStack()
        cent8 = sb("cent8", [8, D], st=esA)
        wpotT = sb("wpotT", [2, D], st=esA)

        P.scope("prep")
        def s_prep(s):
            for dst, src in ((idn, idn_d), (idnb, idnb_d), (trilT, trilT_d),
                             (iota8, iota8_d), (colpack, colpack_d),
                             (cent8, cent_d), (wpotT, wpotT_d), (vsm, ln1g_d),
                             (beff_row, bpot_d)):
                inc(s.dma_start(out=dst[:], in_=src), "qS", 16)
        on("sync", s_prep)
        tok_ld0 = P.cnt["qS"]
        BREG = [None]
        def g_breg(gp):
            BREG[0] = gp.alloc_register(name="breg")
            gp.reg_mov(BREG[0], NS - 1)
        on("gpsimd", g_breg)

        def v_prep(v):
            wait(v, "qS", tok_ld0)
            v.memset(ones1f[:], 1.0)
            v.memset(onescol[:], 1.0)
            v.memset(epscol[:], EPS)
            v.memset(lnecol[:], LN_ETA)
            inc(v.memset(ones8c[:], 1.0), "eV")
        on("vector", v_prep)

        def t_g1(t, tok=P.cnt["eV"]):
            wait(t, "eV", tok)
            inc(t.transpose(pb[0][0:128, 0:8], vsm[:], idn[0:8, 0:8]), "eT")
        on("tensor", t_g1)
        def v_g1(v, tok=P.cnt["eT"]):
            wait(v, "eT", tok)
            inc(v.tensor_copy(g1c[:], pb[0][0:128, 0:8]), "eV")
        on("vector", v_g1)
        P.bank_set("pb0", "eV")

        for c in range(8):
            bk = c % 2
            def t_c(t, c=c, bk=bk):
                P.bank_wait(t, f"pb{bk}")
                t.transpose(pb[bk][0:128, 0:8], cent8[:, 128 * c:128 * (c + 1)], idn[0:8, 0:8])
                inc(t.transpose(pb[bk][0:128, 8:10], wpotT[:, 128 * c:128 * (c + 1)], idn[0:2, 0:2]), "eT")
            on("tensor", t_c)
            def v_c(v, c=c, bk=bk, tok=P.cnt["eT"]):
                wait(v, "eT", tok)
                v.tensor_copy(centT[:, 10 * c:10 * c + 8], pb[bk][0:128, 0:8])
                inc(v.tensor_scalar(centT[:, 10 * c + 8:10 * c + 10], pb[bk][0:128, 8:10],
                                    g1c[:, c:c + 1], None, ALU.mult), "eV")
            on("vector", v_c)
            P.bank_set(f"pb{bk}", "eV")

        def t_cs(t):
            wait_all(t, "eV")
            P.bank_wait(t, "pb2")
            for c in range(8):
                ins = t.matmul(pb[2][0:2, 0:1], centT[:, 10 * c + 8:10 * c + 10], onescol[:],
                               start=(c == 0), stop=(c == 7))
            inc(ins, "eT")
        on("tensor", t_cs)
        def v_cs(v, tok=P.cnt["eT"]):
            wait(v, "eT", tok)
            inc(v.tensor_scalar(e1[0:2, 0:1], pb[2][0:2, 0:1], -1.0, None, ALU.mult), "eV")
        on("vector", v_cs)
        P.bank_set("pb2", "eV")
        def t_cs2(t, tok=P.cnt["eV"]):
            wait(t, "eV", tok)
            P.bank_wait(t, "pb3")
            inc(t.transpose(pb[3][0:1, 0:2], e1[0:2, 0:1], idn[0:2, 0:2]), "eT")
        on("tensor", t_cs2)
        def v_cs2(v, tok=P.cnt["eT"]):
            wait(v, "eT", tok)
            inc(v.tensor_copy(e1[0:1, 4:6], pb[3][0:1, 0:2]), "eV")
        on("vector", v_cs2)
        P.bank_set("pb3", "eV")

        def s_lb(s):
            wait(s, "eT", P.cnt["eT"])
            inc(s.dma_start(out=vsm[:], in_=ln1b_d), "qS", 16)
        on("sync", s_lb)
        def t_lb(t, tok=P.cnt["qS"]):
            wait(t, "qS", tok)
            P.bank_wait(t, "pb0")
            inc(t.transpose(pb[0][0:128, 16:24], vsm[:], idn[0:8, 0:8]), "eT")
        on("tensor", t_lb)
        def v_lb(v, tok=P.cnt["eT"]):
            wait(v, "eT", tok)
            inc(v.tensor_copy(e2[:, 0:8], pb[0][0:128, 16:24]), "eV")
        on("vector", v_lb)
        P.bank_set("pb0", "eV")
        for c in range(8):
            bk = c % 2
            def t_wp(t, c=c, bk=bk):
                if c < 2:
                    wait_all(t, "eV")
                P.bank_wait(t, f"pb{bk}")
                inc(t.transpose(pb[bk][0:128, 12:14], wpotT[:, 128 * c:128 * (c + 1)], idn[0:2, 0:2]), "eT")
            on("tensor", t_wp)
            def v_wp(v, c=c, bk=bk, tok=P.cnt["eT"]):
                wait(v, "eT", tok)
                inc(v.tensor_copy(e3[:, 2 * (c % 4):2 * (c % 4) + 2], pb[bk][0:128, 12:14]), "eV")
            on("vector", v_wp)
            P.bank_set(f"pb{bk}", "eV")
            def t_be(t, c=c, tok=P.cnt["eV"]):
                wait(t, "eV", tok)
                if c == 0:
                    P.bank_wait(t, "pb4")
                inc(t.matmul(pb[4][0:1, 0:2], e2[:, c:c + 1], e3[:, 2 * (c % 4):2 * (c % 4) + 2],
                             start=(c == 0), stop=(c == 7)), "eT")
            on("tensor", t_be)
        def v_be(v, tok=P.cnt["eT"]):
            wait(v, "eT", tok)
            inc(v.tensor_tensor(beff_row[:], pb[4][0:1, 0:2], beff_row[:], ALU.add), "eV")
        on("vector", v_be)
        P.bank_set("pb4", "eV")
        def t_bc(t, tok=P.cnt["eV"]):
            wait(t, "eV", tok)
            P.bank_wait(t, "pb5")
            t.matmul(pb[5][0:128, 0:2], ones1f[:], e1[0:1, 4:6], start=True, stop=True)
            inc(t.matmul(pb[5][0:128, 2:4], ones1f[:], beff_row[:], start=True, stop=True), "eT")
        on("tensor", t_bc)
        def v_bc(v, tok=P.cnt["eT"]):
            wait(v, "eT", tok)
            v.tensor_copy(csneg[:], pb[5][0:128, 0:2])
            inc(v.tensor_copy(beffp[:], pb[5][0:128, 2:4]), "eV")
        on("vector", v_bc)
        P.bank_set("pb5", "eV")

        if upto == "prep":
            dc1 = dout("d_centT", [128, 80])
            dc2 = dout("d_csneg", [128, 2])
            dc3 = dout("d_beffp", [128, 2])
            dc4 = dout("d_g1c", [128, 8])
            dc5 = dout("d_e1row", [128, 40])
            dc6 = dout("d_beffrow", [128, 2])
            def g_cpb(gp):
                wait_all(gp)
                inc(gp.dma_start(out=dc6[0:1, :], in_=beff_row[:]), "qG", 16)
            on("gpsimd", g_cpb)
            def g_cp(gp):
                wait_all(gp)
                inc(gp.dma_start(out=dc1, in_=centT[:]), "qG", 16)
                inc(gp.dma_start(out=dc2, in_=csneg[:]), "qG", 16)
                inc(gp.dma_start(out=dc3, in_=beffp[:]), "qG", 16)
                inc(gp.dma_start(out=dc4, in_=g1c[:]), "qG", 16)
                inc(gp.dma_start(out=dc5, in_=e1[:]), "qG", 16)
                wait_all(gp, "qG")
            on("gpsimd", g_cp)
            P.barrier()
            return nc, dbg_outs

        P.scope("phaseA")
        xr = sb("xr", [128, 2 * D], st=esA)
        xr2 = sb("xr2", [128, 2 * D], st=esA)
        xnb = sb("xnb", [128, NT * D], BF16, st=esA)
        xn_t = sb("xn_t", [128, 2 * D], st=esA)
        out_t = sb("out_t", [128, 2 * D], st=esA)
        xnT = sb("xnT", [128, 8 * 512], st=esA)
        afffm = sb("afffm", [10, 512], st=esA)
        affs = sb("affs", [128, 16], st=esA)

        atok = {}
        xtok = {}
        def emit_a0_tile(i):
            sl = i % 2
            def s_x(s, i=i, sl=sl, tk0=P.cnt["qS"]):
                if i >= 2:
                    wait(s, "eA", atok[i - 2])
                else:
                    wait(s, "qS", tk0)
                inc(s.dma_start(out=xr[:, D * sl:D * (sl + 1)], in_=x_d[128 * i:128 * (i + 1), :]), "qS", 16)
            on("sync", s_x)
            xtok[i] = P.cnt["qS"]
            def a_st(a, i=i, sl=sl, tok=P.cnt["qS"]):
                wait(a, "qS", tok)
                a.activation(junkA[:], xr[:, D * sl:D * (sl + 1)], AF.Copy,
                             scale=INV_D, accum_out=m0a[:, i:i + 1])
                inc(a.activation(junkA2[:], xr[:, D * sl:D * (sl + 1)], AF.Square,
                                 scale=INV_SQ_D, accum_out=q0a[:, i:i + 1]), "eA")
            on("scalar", a_st)
            atok[i] = P.cnt["eA"]

        tokR0h = {}
        def emit_fin0(h):
            c0, c1 = 16 * h, 16 * (h + 1)
            def v_f0(v, c0=c0, c1=c1, tok=P.cnt["eA"]):
                wait(v, "eA", tok)
                ch(v, "eV", v.tensor_tensor(st1[:, c0:c1], m0a[:, c0:c1], m0a[:, c0:c1], ALU.mult))
                inc(v.tensor_tensor(st2[:, c0:c1], q0a[:, c0:c1], st1[:, c0:c1], ALU.subtract), "eV")
            on("vector", v_f0)
            def a_f0(a, c0=c0, c1=c1, tok=P.cnt["eV"]):
                wait(a, "eV", tok)
                inc(a.activation(st1[:, c0:c1], st2[:, c0:c1], AF.Sqrt, bias=epscol[:, 0:1]), "eA")
            on("scalar", a_f0)
            def v_f0b(v, c0=c0, c1=c1, tok=P.cnt["eA"]):
                wait(v, "eA", tok)
                inc(v.reciprocal(r0a[:, c0:c1], st1[:, c0:c1]), "eV")
            on("vector", v_f0b)
            tokR0h[h] = P.cnt["eV"]

        for i in range(16):
            emit_a0_tile(i)
        emit_fin0(0)

        if upto == "a0":
            for ii in range(16, NT):
                emit_a0_tile(ii)
            emit_fin0(1)
            dm0 = dout("d_m0", [128, NT]); dq0 = dout("d_q0", [128, NT]); dr0 = dout("d_r0", [128, NT])
            def g_cp0(gp):
                wait_all(gp)
                inc(gp.dma_start(out=dm0, in_=m0a[:]), "qG", 16)
                inc(gp.dma_start(out=dq0, in_=q0a[:]), "qG", 16)
                inc(gp.dma_start(out=dr0, in_=r0a[:]), "qG", 16)
                wait_all(gp, "qG")
            on("gpsimd", g_cp0)
            P.barrier()
            esA.close()
            return nc, dbg_outs

        qg_tok = {}
        a1_tok = {}
        def emit_a1_group(g):
            for j in range(4):
                i = 4 * g + j
                sl = i % 2
                xs = xn_t[:, D * sl:D * (sl + 1)]
                os_ = out_t[:, D * sl:D * (sl + 1)]
                xv = xr2[:, D * sl:D * (sl + 1)]
                def s_x1(s, i=i, sl=sl):
                    if i >= 2:
                        wait(s, "eA", a1_tok[i - 2])
                    inc(s.dma_start(out=xr2[:, D * sl:D * (sl + 1)],
                                    in_=x_d[128 * i:128 * (i + 1), :]), "qS", 16)
                on("sync", s_x1)
                tok_x = P.cnt["qS"]
                def v_ap(v, i=i, xs=xs, os_=os_, xv=xv, tok_x=tok_x):
                    if i in (0, 16):
                        wait(v, "eV", tokR0h[i // 16])
                    wait(v, "qS", tok_x)
                    if i >= 2:
                        wait(v, "qG", qg_tok[i - 2])
                    v.tensor_scalar(xs, xv, m0a[:, i:i + 1], r0a[:, i:i + 1],
                                    ALU.subtract, ALU.mult)
                    v.tensor_tensor(os_, xv, xs, ALU.add)
                    inc(v.tensor_copy(xnb[:, D * i:D * (i + 1)], xs), "eV")
                on("vector", v_ap)
                tokXN = P.cnt["eV"]
                def g_st(gp, i=i, xs=xs, os_=os_, tok=tokXN):
                    wait(gp, "eV", tok)
                    inc(gp.dma_start(out=out_d[128 * i:128 * (i + 1), :], in_=os_), "qG", 16)
                    inc(gp.dma_start(out=y_d[128 * i:128 * (i + 1), :], in_=xs), "qG", 16)
                on("gpsimd", g_st)
                qg_tok[i] = P.cnt["qG"]
                def a_s1(a, i=i, xs=xs, tok=tokXN):
                    wait(a, "eV", tok)
                    a.activation(junkA[:], xs, AF.Copy, scale=INV_D, accum_out=m1a[:, i:i + 1])
                    inc(a.activation(junkA2[:], xs, AF.Square, scale=INV_SQ_D,
                                     accum_out=q1a[:, i:i + 1]), "eA")
                on("scalar", a_s1)
                a1_tok[i] = P.cnt["eA"]
                for c in range(8):
                    bk = c % 4
                    def t_tp(t, c=c, bk=bk, xs=xs, tok=tokXN):
                        if c == 0:
                            wait(t, "eV", tok)
                        P.bank_wait(t, f"pb{bk}")
                        inc(t.transpose(pb[bk][:, 0:128], xs[:, 128 * c:128 * (c + 1)], idn[:]), "eT")
                    on("tensor", t_tp)
                    def v_ev(v, c=c, bk=bk, j=j, tok=P.cnt["eT"]):
                        wait(v, "eT", tok)
                        inc(v.tensor_copy(xnT[:, 512 * c + 128 * j:512 * c + 128 * (j + 1)],
                                          pb[bk][:, 0:128]), "eV")
                    on("vector", v_ev)
                    P.bank_set(f"pb{bk}", "eV")
            tokGrp = P.cnt["eV"]
            if noaff:
                return
            def t_aff(t, tok=tokGrp):
                wait(t, "eV", tok)
                P.bank_wait(t, "pb4")
                for c in range(8):
                    ins = t.matmul(pb[4][0:10, 0:512], centT[:, 10 * c:10 * (c + 1)],
                                   xnT[:, 512 * c:512 * (c + 1)], start=(c == 0), stop=(c == 7))
                inc(ins, "eT")
            on("tensor", t_aff)
            def v_affev(v, tok=P.cnt["eT"]):
                wait(v, "eT", tok)
                inc(v.tensor_copy(afffm[:], pb[4][0:10, 0:512]), "eV")
            on("vector", v_affev)
            P.bank_set("pb4", "eV")
            for j in range(4):
                bk = 4 + j
                def t_at(t, j=j, bk=bk, tok=P.cnt["eV"]):
                    if j == 0:
                        wait(t, "eV", tok)
                    P.bank_wait(t, f"pb{bk}")
                    inc(t.transpose(pb[bk][0:128, 0:10], afffm[0:10, 128 * j:128 * (j + 1)],
                                    idn[0:10, 0:10]), "eT")
                on("tensor", t_at)
            def v_ep4(v, g=g, tok=P.cnt["eT"]):
                wait(v, "eT", tok)
                for j in range(4):
                    v.tensor_copy(e2[:, 10 * j:10 * j + 10], pb[4 + j][0:128, 0:10])
                for j in range(4):
                    i = 4 * g + j
                    v.tensor_copy(potraw[:, 2 * i:2 * i + 2], e2[:, 10 * j + 8:10 * j + 10])
                for j in range(4):
                    i = 4 * g + j
                    v.reduce_max(amax_a[:, i:i + 1], e2[:, 10 * j:10 * j + 8], axis=AX.X)
                for j in range(4):
                    i = 4 * g + j
                    v.tensor_scalar(e1[:, 8 * j:8 * (j + 1)], e2[:, 10 * j:10 * j + 8],
                                    amax_a[:, i:i + 1], None, ALU.is_equal)
                for j in range(4):
                    v.tensor_scalar(e1[:, 8 * j:8 * (j + 1)], e1[:, 8 * j:8 * (j + 1)],
                                    -1e9, 1e9, ALU.mult, ALU.add)
                for j in range(4):
                    v.tensor_tensor(e1[:, 8 * j:8 * (j + 1)], e1[:, 8 * j:8 * (j + 1)],
                                    iota8[:], ALU.add)
                ins = None
                for j in range(4):
                    i = 4 * g + j
                    ins = v.tensor_reduce(eid_a[:, i:i + 1], e1[:, 8 * j:8 * (j + 1)],
                                          axis=AX.X, op=ALU.min)
                inc(ins, "eV")
            on("vector", v_ep4)
            for j in range(4):
                P.bank_set(f"pb{4 + j}", "eV")

        for g in range(4):
            for ii in range(16 + 4 * g, 20 + 4 * g):
                emit_a0_tile(ii)
            emit_a1_group(g)
        emit_fin0(1)
        for g in range(4, 8):
            emit_a1_group(g)

        def a_al(a, tok=P.cnt["eV"]):
            wait(a, "eV", tok)
            inc(a.activation(alpha_a[:], amax_a[:], AF.Sigmoid), "eA")
        on("scalar", a_al)

        def v_f1(v, tok=P.cnt["eA"]):
            wait(v, "eA", tok)
            ch(v, "eV", v.tensor_tensor(st1[:], m1a[:], m1a[:], ALU.mult))
            inc(v.tensor_tensor(st2[:], q1a[:], st1[:], ALU.subtract), "eV")
        on("vector", v_f1)
        def a_f1(a, tok=P.cnt["eV"]):
            wait(a, "eV", tok)
            inc(a.activation(st1[:], st2[:], AF.Sqrt, bias=epscol[:, 0:1]), "eA")
        on("scalar", a_f1)
        def v_f1b(v, tok=P.cnt["eA"]):
            wait(v, "eA", tok)
            ch(v, "eV", v.reciprocal(r1a[:], st1[:]))
            inc(v.tensor_tensor(m1r1a[:], m1a[:], r1a[:], ALU.mult), "eV")
        on("vector", v_f1b)

        def v_pot(v, tok=P.cnt["eV"]):
            wait(v, "eV", tok)
            for cpos, dst in ((0, V_a), (1, e3)):
                raw = potraw.rearrange("p (i c) -> p c i", c=2)[:, cpos, :]
                ch(v, "eV", v.tensor_tensor(e1[:, 0:NT], raw, r1a[:], ALU.mult))
                ch(v, "eV", v.scalar_tensor_tensor(e2[:, 0:NT], m1r1a[:], csneg[:, cpos:cpos + 1],
                                                   e1[:, 0:NT], ALU.mult, ALU.add))
                inc(v.tensor_scalar(dst[:, 0:NT], e2[:, 0:NT], beffp[:, cpos:cpos + 1],
                                    None, ALU.add), "eV")
        on("vector", v_pot)
        def a_g1(a, tok=P.cnt["eV"]):
            wait(a, "eV", tok)
            ch(a, "eA", a.activation(e1[:, 0:NT], e3[:, 0:NT], AF.Abs))
            ch(a, "eA", a.activation(e1[:, 0:NT], e1[:, 0:NT], AF.Exp, scale=-1.0))
            inc(a.activation(e1[:, 0:NT], e1[:, 0:NT], AF.Ln, bias=1.0), "eA")
        on("scalar", a_g1)
        def v_g2(v, tok=P.cnt["eA"]):
            wait(v, "eA", tok)
            ch(v, "eV", v.tensor_scalar(e2[:, 0:NT], e3[:, 0:NT], 0.0, None, ALU.max))
            ch(v, "eV", v.tensor_tensor(gam_a[:], e1[:, 0:NT], e2[:, 0:NT], ALU.add))
            inc(v.tensor_scalar(gam_a[:], gam_a[:], GAMMA_FLOOR, None, ALU.add), "eV")
        on("vector", v_g2)

        if upto == "a1":
            deid = dout("d_eid", [128, NT]); dalpha = dout("d_alpha", [128, NT])
            dgam = dout("d_gam", [128, NT]); dV = dout("d_V", [128, NT])
            dxnT = dout("d_xnT", [128, 8 * 512])
            dxnb = dout("d_xnb", [128, 4 * D], BF16)
            dafffm = dout("d_afffm", [128, 512])
            dpraw = dout("d_praw", [128, 2 * NT])
            daffs = dout("d_affs", [128, 16])
            def g_cp1(gp):
                wait_all(gp)
                inc(gp.dma_start(out=dxnT, in_=xnT[:]), "qG", 16)
                inc(gp.dma_start(out=dxnb, in_=xnb[:, 28 * D:32 * D]), "qG", 16)
                inc(gp.dma_start(out=dpraw, in_=potraw[:]), "qG", 16)
                inc(gp.dma_start(out=daffs, in_=affs[:]), "qG", 16)
                def v_c1(v):
                    v.memset(junkV[:, 0:512], 0.0)
                    inc(v.tensor_copy(junkV[0:10, 0:512], afffm[:]), "eV")
                on("vector", v_c1)
                wait_all(gp, "eV")
                inc(gp.dma_start(out=dafffm, in_=junkV[:, 0:512]), "qG", 16)
                inc(gp.dma_start(out=deid, in_=eid_a[:]), "qG", 16)
                inc(gp.dma_start(out=dalpha, in_=alpha_a[:]), "qG", 16)
                inc(gp.dma_start(out=dgam, in_=gam_a[:]), "qG", 16)
                inc(gp.dma_start(out=dV, in_=V_a[:]), "qG", 16)
                wait_all(gp, "qG")
            on("gpsimd", g_cp1)
            P.barrier()
            esA.close()
            return nc, dbg_outs

        P.scope("rank")
        esR = contextlib.ExitStack()
        masks8 = sb("masks8", [8, T], st=esR)
        cum8 = sb("cum8", [8, T], st=esR)
        s8b = sb("s8b", [8, T], st=esR)
        slotrow = rowA

        def t_eT(t, tok=P.cnt["eV"]):
            wait(t, "eV", tok)
            P.bank_wait(t, "pb0")
            inc(t.transpose(pb[0][0:32, 0:128], eid_a[:], idn[:]), "eT")
        on("tensor", t_eT)
        def v_eT(v, tok=P.cnt["eT"]):
            wait(v, "eT", tok)
            inc(v.tensor_copy(vsm32[:], pb[0][0:32, 0:128]), "eV")
        on("vector", v_eT)
        P.bank_set("pb0", "eV")
        def g_eT(gp, tok=P.cnt["eV"]):
            wait(gp, "eV", tok)
            inc(gp.dma_start(out=row_d[0, :].rearrange("(a b) -> a b", a=32), in_=vsm32[:]), "qG", 16)
        on("gpsimd", g_eT)
        def s_er(s, tok=P.cnt["qG"]):
            wait(s, "qG", tok)
            inc(s.dma_start(out=rowA[:], in_=row_d[0:1, :]), "qS", 16)
        on("sync", s_er)
        tok_eidrow = P.cnt["qS"]
        for n in range(8):
            bk = n % 2
            def t_b8(t, n=n, bk=bk):
                if n == 0:
                    wait(t, "qS", tok_eidrow)
                P.bank_wait(t, f"pb{bk}")
                inc(t.matmul(pb[bk][0:8, 0:512], ones1f[0:1, 0:8],
                             rowA[0:1, 512 * n:512 * (n + 1)], start=True, stop=True), "eT")
            on("tensor", t_b8)
            def v_m8(v, n=n, bk=bk, tok=P.cnt["eT"]):
                wait(v, "eT", tok)
                inc(v.tensor_scalar(masks8[:, 512 * n:512 * (n + 1)], pb[bk][0:8, 0:512],
                                    colpack[0:8, 0:1], None, ALU.is_equal), "eV")
            on("vector", v_m8)
            P.bank_set(f"pb{bk}", "eV")
        def v_rk(v, tok=P.cnt["eV"]):
            wait(v, "eV", tok)
            ch(v, "eV", v.tensor_tensor_scan(cum8[:], masks8[:], masks8[:], 0.0, ALU.add, ALU.bypass))
            ch(v, "eV", v.tensor_scalar(s8b[:], cum8[:], CAP + 0.5, None, ALU.is_gt))
            ch(v, "eV", v.tensor_scalar(cum8[:], cum8[:], colpack[0:8, 1:2], -1.0, ALU.add, ALU.add))
            ch(v, "eV", v.scalar_tensor_tensor(s8b[:], s8b[:], 1e7, cum8[:], ALU.mult, ALU.add))
            inc(v.tensor_tensor(masks8[:], s8b[:], masks8[:], ALU.mult), "eV")
        on("vector", v_rk)
        for n in range(8):
            bk = n % 2
            def t_sl(t, n=n, bk=bk, tok=P.cnt["eV"]):
                if n == 0:
                    wait(t, "eV", tok)
                P.bank_wait(t, f"pb{bk}")
                inc(t.matmul(pb[bk][0:1, 0:512], ones8c[:], masks8[:, 512 * n:512 * (n + 1)],
                             start=True, stop=True), "eT")
            on("tensor", t_sl)
            def v_sl(v, n=n, bk=bk, tok=P.cnt["eT"]):
                wait(v, "eT", tok)
                inc(v.tensor_copy(slotrow[0:1, 512 * n:512 * (n + 1)], pb[bk][0:1, 0:512]), "eV")
            on("vector", v_sl)
            P.bank_set(f"pb{bk}", "eV")
        for i in range(NT):
            bk = 2 + (i % 2)
            def t_st(t, i=i, bk=bk, tok=P.cnt["eV"]):
                if i == 0:
                    wait(t, "eV", tok)
                P.bank_wait(t, f"pb{bk}")
                inc(t.transpose(pb[bk][0:128, 0:1], slotrow[0:1, 128 * i:128 * (i + 1)],
                                idn[0:1, 0:1]), "eT")
            on("tensor", t_st)
            def v_su(v, i=i, bk=bk, tok=P.cnt["eT"]):
                wait(v, "eT", tok)
                inc(v.tensor_copy(slotu[:, i:i + 1], pb[bk][0:128, 0:1]), "eV")
            on("vector", v_su)
            P.bank_set(f"pb{bk}", "eV")
        tokSlots = P.cnt["eV"]

        if upto == "a2a":
            dslot = dout("d_slot", [128, NT], U32)
            def g_cp2(gp):
                wait_all(gp)
                inc(gp.dma_start(out=dslot, in_=slotu[:]), "qG", 16)
                wait_all(gp, "qG")
            on("gpsimd", g_cp2)
            P.barrier()
            esR.close()
            esA.close()
            return nc, dbg_outs

        xlnr = xn_t[:].bitcast(BF16)[:, 0:2 * D]
        sc_tok = {}
        for i in range(NT):
            sl = i % 2
            def v_xl(v, i=i, sl=sl):
                if i == 0:
                    wait(v, "eV", tokSlots)
                    wait(v, "qG", qg_tok[NT - 1])
                    wait(v, "eA", a1_tok[NT - 1])
                if i >= 2:
                    wait(v, "qG", sc_tok[i - 2])
                inc(v.tensor_scalar(xlnr[:, D * sl:D * (sl + 1)], xnb[:, D * i:D * (i + 1)],
                                    m1a[:, i:i + 1], r1a[:, i:i + 1],
                                    ALU.subtract, ALU.mult), "eV")
            on("vector", v_xl)
            def g_sc(gp, i=i, sl=sl, tok=P.cnt["eV"]):
                wait(gp, "eV", tok)
                inc(gp.indirect_dma_start(
                    out=xs_d, out_offset=bass.IndirectOffsetOnAxis(ap=slotu[:, i:i + 1], axis=0),
                    in_=xlnr[:, D * sl:D * (sl + 1)], in_offset=None,
                    bounds_check=BREG[0], oob_is_err=False), "qG", 16)
            on("gpsimd", g_sc)
            sc_tok[i] = P.cnt["qG"]

        if dbg >= 1:
            deid = dout("d_eid", [128, NT])
            dalpha = dout("d_alpha", [128, NT])
            dgam = dout("d_gam", [128, NT])
            dV = dout("d_V", [128, NT])
            dslot = dout("d_slot", [128, NT], U32)
            dxs = dout("d_xs", [NS, D], BF16)
            dxnb = dout("d_xnb", [T, D], BF16)
            def g_d3(gp):
                wait_all(gp)
                inc(gp.dma_start(out=deid, in_=eid_a[:]), "qG", 16)
                inc(gp.dma_start(out=dalpha, in_=alpha_a[:]), "qG", 16)
                inc(gp.dma_start(out=dgam, in_=gam_a[:]), "qG", 16)
                inc(gp.dma_start(out=dV, in_=V_a[:]), "qG", 16)
                inc(gp.dma_start(out=dslot, in_=slotu[:]), "qG", 16)
                inc(gp.dma_start(out=dxs, in_=xs_d), "qG", 16)
                for i in range(NT):
                    inc(gp.dma_start(out=dxnb[128 * i:128 * (i + 1), :],
                                     in_=xnb[:, D * i:D * (i + 1)]), "qG", 16)
                wait_all(gp, "qG")
            on("gpsimd", g_d3)
            P.barrier()
            esR.close()
            esA.close()
            return nc, dbg_outs


        P.scope("scan")
        P.barrier()
        esR.close()
        esA.close()
        esB = contextlib.ExitStack()
        esBH = contextlib.ExitStack()
        esB2 = contextlib.ExitStack()
        esB1 = contextlib.ExitStack()

        R_DRE, R_DIM, R_DRER, R_DIMR, R_T1, R_T2, R_T3, R_T4 = 1, 2, 3, 4, 5, 6, 7, 8
        R_GR, R_GI, R_EM, R_EP, R_GC = 9, 10, 11, 12, 13

        gfm = sb("gfm", [2, T], BF16, st=esB)
        gch = sb("gch", [32, 128], st=esB)
        ghat = sb("ghat", [32, 128], st=esB)
        epx = sb("epx", [32, 128], st=esB)
        emx = sb("emx", [32, 128], st=esB)
        epT = sb("epT", [128, 32], st=esB)
        emT = sb("emT", [128, 32], st=esB)
        gcbc = sb("gcbc", [128, 32], st=esB)
        m2a = sb("m2a", [128, NT], st=esB); q2a = sb("q2a", [128, NT], st=esB)
        r2a = sb("r2a", [128, NT], st=esB)
        m3a = sb("m3a", [128, NT], st=esB); q3a = sb("q3a", [128, NT], st=esB)
        r3a = sb("r3a", [128, NT], st=esB)
        hebfm = sb("hebfm", [128, 4 * T], BF16, st=esBH)
        wbkb = sb("wbkb", [2, D], BF16, st=esB2)
        bbk_bc = sb("bbk_bc", [128, D], st=esB2)
        bbk_row = sb("bbk_row", [1, D], st=esB2)
        embc = sb("embc", [128, T], BF16, st=esB2)
        epbc = sb("epbc", [128, T], BF16, st=esB2)
        yring = sb("yring", [128, 3 * D], st=esB2)

        dreF = sb("dreF", [128, 32], st=esB1); dimF = sb("dimF", [128, 32], st=esB1)
        ngiF = sb("ngiF", [128, 32], st=esB1)
        dreB = sb("dreB", [128, 32], st=esB1); dimB = sb("dimB", [128, 32], st=esB1)
        ngiB = sb("ngiB", [128, 32], st=esB1)
        s_re = [sb(f"s_re{k}", [128, 2], st=esB1) for k in range(3)]
        s_im = [sb(f"s_im{k}", [128, 2], st=esB1) for k in range(3)]
        c_re = [sb(f"c_re{k}", [128, 2], st=esB1) for k in range(3)]
        c_im = [sb(f"c_im{k}", [128, 2], st=esB1) for k in range(3)]
        tcomp = sb("tcomp", [128, 8], st=esB1)
        sb8 = sb("sb8", [16, 64], st=esB1); sb8n = sb("sb8n", [16, 64], st=esB1)
        rst = [sb(f"rst{k}", [16, 8], st=esB1) for k in range(2)]
        srow = sb("srow", [1, 128], st=esB1); srown = sb("srown", [1, 128], st=esB1)
        psup = sb("psup", [1, 64], st=esB1)
        ps16 = [sb(f"ps16_{k}", [16, 4], st=esB1) for k in range(2)]
        pblk = sb("pblk", [16, 32], st=esB1)
        pb128 = sb("pb128", [128, 4], st=esB1)
        NRf = sb("NRf", [128, 33], st=esB1); NIf = sb("NIf", [128, 33], st=esB1)
        DRf = sb("DRf", [128, 33], st=esB1); DIf = sb("DIf", [128, 33], st=esB1)
        NRb = sb("NRb", [128, 33], st=esB1); NIb = sb("NIb", [128, 33], st=esB1)
        DRb = sb("DRb", [128, 33], st=esB1); DIb = sb("DIb", [128, 33], st=esB1)
        bre = sb("bre", [128, 32], st=esB1); bim = sb("bim", [128, 32], st=esB1)
        bre2 = sb("bre2", [128, 32], st=esB1); bim2 = sb("bim2", [128, 32], st=esB1)
        gsc = [sb(f"gsc{k}", [128, 32], st=esB1) for k in range(6)]
        tinyc = sb("tinyc", [128, 1], st=esB1)
        rowB = sb("rowB", [1, T], st=esB1)
        s_reB = [sb(f"s_reB{k}", [128, 2], st=esB1) for k in range(2)]
        s_imB = [sb(f"s_imB{k}", [128, 2], st=esB1) for k in range(2)]
        c_reB = [sb(f"c_reB{k}", [128, 2], st=esB1) for k in range(2)]
        c_imB = [sb(f"c_imB{k}", [128, 2], st=esB1) for k in range(2)]
        tcompB = sb("tcompB", [128, 8], st=esB1)
        sb8B = sb("sb8B", [16, 64], st=esB1); sb8nB = sb("sb8nB", [16, 64], st=esB1)
        rstB = [sb(f"rstB{k}", [16, 8], st=esB1) for k in range(2)]
        srowB = sb("srowB", [1, 128], st=esB1); srownB = sb("srownB", [1, 128], st=esB1)
        psupB = sb("psupB", [1, 64], st=esB1)
        ps16B = [sb(f"ps16B_{k}", [16, 4], st=esB1) for k in range(2)]
        pblkB = sb("pblkB", [16, 32], st=esB1)
        pb128B = sb("pb128B", [128, 4], st=esB1)

        def v_db(v):
            wait_all(v, "qG", "eA")
            ch(v, "eV", v.tensor_scalar(e1[:, 0:NT], V_a[:], 2.0, None, ALU.add))
            ch(v, "eV", v.tensor_scalar(e2[:, 0:NT], gam_a[:], -1.0, None, ALU.mult))
        on("vector", v_db)
        for (til, row) in ((e1, R_DRE), (e2, R_DIM)):
            def t_tr(t, til=til, tok=P.cnt["eV"]):
                wait(t, "eV", tok)
                P.bank_wait(t, "pb7")
                inc(t.transpose(pb[7][0:32, 0:128], til[:, 0:NT], idn[:]), "eT")
            on("tensor", t_tr)
            def v_tr(v, tok=P.cnt["eT"], qg=P.cnt["qG"]):
                wait(v, "eT", tok)
                wait(v, "qG", qg)
                inc(v.tensor_copy(vsm32[:], pb[7][0:32, 0:128]), "eV")
            on("vector", v_tr)
            P.bank_set("pb7", "eV")
            def g_tr(gp, row=row, tok=P.cnt["eV"]):
                wait(gp, "eV", tok)
                inc(gp.dma_start(out=row_d[row, :].rearrange("(a b) -> a b", a=32), in_=vsm32[:]), "qG", 16)
            on("gpsimd", g_tr)
        for (row, rrow) in ((R_DRE, R_DRER), (R_DIM, R_DIMR)):
            def s_rv(s, row=row, tok=P.cnt["qG"]):
                wait(s, "qG", tok)
                inc(s.dma_start(out=rowA[:], in_=row_d[row:row + 1, :]), "qS", 16)
            on("sync", s_rv)
            def v_rv(v, tok=P.cnt["qS"]):
                wait(v, "qS", tok)
                inc(v.tensor_copy(rowB[:], rowA[0:1, ::-1]), "eV")
            on("vector", v_rv)
            def g_rv(gp, rrow=rrow, tok=P.cnt["eV"]):
                wait(gp, "eV", tok)
                inc(gp.dma_start(out=row_d[rrow:rrow + 1, :], in_=rowB[:]), "qG", 16)
            on("gpsimd", g_rv)
        def s_bl(s, tok=P.cnt["qG"]):
            wait(s, "qG", tok)
            inc(s.dma_start(out=dreF[:], in_=row_d[R_DRE, :].rearrange("(b j) -> b j", b=128)), "qS", 16)
            inc(s.dma_start(out=dimF[:], in_=row_d[R_DIM, :].rearrange("(b j) -> b j", b=128)), "qS", 16)
            inc(s.dma_start(out=dreB[:], in_=row_d[R_DRER, :].rearrange("(b j) -> b j", b=128)), "qS", 16)
            inc(s.dma_start(out=dimB[:], in_=row_d[R_DIMR, :].rearrange("(b j) -> b j", b=128)), "qS", 16)
            inc(s.dma_start(out=gch[:], in_=row_d[R_DIM, :].rearrange("(c p) -> c p", c=32)), "qS", 16)
        on("sync", s_bl)
        def v_ng(v, tok=P.cnt["qS"]):
            wait(v, "qS", tok)
            ch(v, "eV", v.tensor_scalar(ngiF[:], dimF[:], -1.0, None, ALU.mult))
            ch(v, "eV", v.tensor_scalar(ngiB[:], dimB[:], -1.0, None, ALU.mult))
            ch(v, "eV", v.tensor_scalar(gch[:], gch[:], -DTC, None, ALU.mult))
        on("vector", v_ng)

        def cf_scan2(cfgs):
            def v_i0(v):
                wait_all(v, "eV")
                for cf in cfgs:
                    for k in range(2):
                        v.memset(cf["s_re"][k][:], 0.0); v.memset(cf["s_im"][k][:], 0.0)
                        v.memset(cf["c_re"][k][:], 0.0); v.memset(cf["c_im"][k][:], 0.0)
                    v.memset(cf["s_re"][0][:, 0:1], 1.0)
                    ins = v.memset(cf["c_re"][0][:, 1:2], 1.0)
                inc(ins, "eV")
            on("vector", v_i0)
            ab = 0
            for k in range(32):
                fr = 1 - ab
                def v_st(v, k=k, ab=ab, fr=fr):
                    for cf in cfgs:
                        s_re, s_im = cf["s_re"], cf["s_im"]
                        dcol = cf["dre"][:, k:k + 1]
                        v.scalar_tensor_tensor(s_re[fr][:], s_re[ab][:], dcol, cf["c_re"][ab][:], ALU.mult, ALU.subtract)
                        ins = v.scalar_tensor_tensor(s_im[fr][:], s_im[ab][:], dcol, cf["c_im"][ab][:], ALU.mult, ALU.subtract)
                    for cf in cfgs:
                        s_re, s_im = cf["s_re"], cf["s_im"]
                        icol = cf["dim"][:, k:k + 1]; ncol = cf["ngi"][:, k:k + 1]
                        v.scalar_tensor_tensor(s_re[fr][:], s_im[ab][:], ncol, s_re[fr][:], ALU.mult, ALU.add)
                        ins = v.scalar_tensor_tensor(s_im[fr][:], s_re[ab][:], icol, s_im[fr][:], ALU.mult, ALU.add)
                    for cf in cfgs:
                        v.tensor_copy(cf["c_re"][fr][:], cf["s_re"][ab][:])
                        ins = v.tensor_copy(cf["c_im"][fr][:], cf["s_im"][ab][:])
                    if k % 8 == 7:
                        for ci, cf in enumerate(cfgs):
                            e1c = e1[:, 4 * ci:4 * ci + 4]
                            v.tensor_tensor(e1c[:, 0:1], cf["s_re"][fr][:, 0:1], cf["s_re"][fr][:, 0:1], ALU.mult)
                            ins = v.tensor_tensor(e1c[:, 1:2], cf["s_im"][fr][:, 0:1], cf["s_im"][fr][:, 0:1], ALU.mult)
                        for ci, cf in enumerate(cfgs):
                            e1c = e1[:, 4 * ci:4 * ci + 4]
                            ins = v.scalar_tensor_tensor(e1c[:, 1:2], e1c[:, 1:2], 1e-30, e1c[:, 0:1], ALU.add, ALU.add)
                        for ci, cf in enumerate(cfgs):
                            e1c = e1[:, 4 * ci:4 * ci + 4]
                            ins = v.reciprocal(e1c[:, 2:3], e1c[:, 1:2])
                        for ci, cf in enumerate(cfgs):
                            e1c = e1[:, 4 * ci:4 * ci + 4]
                            for z in (cf["s_re"][fr], cf["s_im"][fr], cf["c_re"][fr], cf["c_im"][fr]):
                                ins = v.tensor_scalar(z[:], z[:], e1c[:, 2:3], None, ALU.mult)
                on("vector", v_st)
                ab = fr
            def v_tc(v, ab=ab):
                wait(v, "eV", P.cnt["eV"])
                for cf in cfgs:
                    v.tensor_copy(cf["tcomp"][:, 0:2], cf["s_re"][ab][:])
                    v.tensor_copy(cf["tcomp"][:, 2:4], cf["s_im"][ab][:])
                    v.tensor_copy(cf["tcomp"][:, 4:6], cf["c_re"][ab][:])
                    ins = v.tensor_copy(cf["tcomp"][:, 6:8], cf["c_im"][ab][:])
                inc(ins, "eV")
            on("vector", v_tc)
            def g_tc(gp, tok=P.cnt["eV"]):
                wait(gp, "eV", tok)
                for ci, cf in enumerate(cfgs):
                    inc(gp.dma_start(out=row_d[R_T1 + ci, 0:1024].rearrange("(a b) -> a b", a=128),
                                     in_=cf["tcomp"][:]), "qG", 16)
            on("gpsimd", g_tc)
            def s_tc(s, tok=P.cnt["qG"]):
                wait(s, "qG", tok)
                for ci, cf in enumerate(cfgs):
                    inc(s.dma_start(out=cf["sb8"][:], in_=row_d[R_T1 + ci, 0:1024].rearrange("(s k) -> s k", s=16)), "qS", 16)
            on("sync", s_tc)
            def v_n8(v, tok=P.cnt["qS"]):
                wait(v, "qS", tok)
                for cf in cfgs:
                    ins = v.tensor_scalar(cf["sb8n"][:], cf["sb8"][:], -1.0, None, ALU.mult)
                for cf in cfgs:
                    v.memset(cf["rst"][0][:], 0.0)
                    v.memset(cf["rst"][0][:, 0:1], 1.0)
                    ins = v.memset(cf["rst"][0][:, 5:6], 1.0)
                inc(ins, "eV")
            on("vector", v_n8)
            cur = 0
            for k in range(8):
                def v_l2(v, k=k, cur=cur):
                    o = 8 * k
                    for slot in range(4):
                        for cf in cfgs:
                            sb8_, sb8n_ = cf["sb8"], cf["sb8n"]
                            R = cf["rst"][cur]; O = cf["rst"][1 - cur]
                            a_r = sb8_[:, o + 0:o + 1]; b_r = sb8_[:, o + 1:o + 2]
                            a_i = sb8_[:, o + 2:o + 3]; b_i = sb8_[:, o + 3:o + 4]
                            c_r = sb8_[:, o + 4:o + 5]; d_r = sb8_[:, o + 5:o + 6]
                            c_i = sb8_[:, o + 6:o + 7]; d_i = sb8_[:, o + 7:o + 8]
                            na_r = sb8n_[:, o + 2:o + 3]; nb_r = sb8n_[:, o + 3:o + 4]
                            nc_r = sb8n_[:, o + 6:o + 7]; nd_r = sb8n_[:, o + 7:o + 8]
                            for sub in range(4):
                                oc = (0, 1, 4, 5)[sub]
                                m1r, m1i, n1i = ((a_r, a_i, na_r), (a_r, a_i, na_r),
                                                 (c_r, c_i, nc_r), (c_r, c_i, nc_r))[sub]
                                m2r, m2i, n2i = ((b_r, b_i, nb_r), (b_r, b_i, nb_r),
                                                 (d_r, d_i, nd_r), (d_r, d_i, nd_r))[sub]
                                x_r, x_i = ((R[:, 0:1], R[:, 2:3]), (R[:, 1:2], R[:, 3:4]),
                                            (R[:, 0:1], R[:, 2:3]), (R[:, 1:2], R[:, 3:4]))[sub]
                                y_r, y_i = ((R[:, 4:5], R[:, 6:7]), (R[:, 5:6], R[:, 7:8]),
                                            (R[:, 4:5], R[:, 6:7]), (R[:, 5:6], R[:, 7:8]))[sub]
                                dst = O[:, oc:oc + 1]; dsti = O[:, oc + 2:oc + 3]
                                if slot == 0:
                                    v.tensor_scalar(dst, x_r, m1r, None, ALU.mult)
                                    ins = v.tensor_scalar(dsti, x_i, m1r, None, ALU.mult)
                                elif slot == 1:
                                    v.scalar_tensor_tensor(dst, x_i, n1i, dst, ALU.mult, ALU.add)
                                    ins = v.scalar_tensor_tensor(dsti, x_r, m1i, dsti, ALU.mult, ALU.add)
                                elif slot == 2:
                                    v.scalar_tensor_tensor(dst, y_r, m2r, dst, ALU.mult, ALU.add)
                                    ins = v.scalar_tensor_tensor(dsti, y_i, m2r, dsti, ALU.mult, ALU.add)
                                else:
                                    v.scalar_tensor_tensor(dst, y_i, n2i, dst, ALU.mult, ALU.add)
                                    ins = v.scalar_tensor_tensor(dsti, y_r, m2i, dsti, ALU.mult, ALU.add)
                    for ci, cf in enumerate(cfgs):
                        O = cf["rst"][1 - cur]
                        e1c = e1[0:16, 8 + 4 * ci:12 + 4 * ci]
                        v.tensor_tensor(e1c[:, 0:1], O[:, 0:1], O[:, 0:1], ALU.mult)
                        ins = v.tensor_tensor(e1c[:, 1:2], O[:, 2:3], O[:, 2:3], ALU.mult)
                    for ci, cf in enumerate(cfgs):
                        e1c = e1[0:16, 8 + 4 * ci:12 + 4 * ci]
                        ins = v.scalar_tensor_tensor(e1c[:, 1:2], e1c[:, 1:2], 1e-30, e1c[:, 0:1], ALU.add, ALU.add)
                    for ci, cf in enumerate(cfgs):
                        e1c = e1[0:16, 8 + 4 * ci:12 + 4 * ci]
                        ins = v.reciprocal(e1c[:, 2:3], e1c[:, 1:2])
                    for ci, cf in enumerate(cfgs):
                        O = cf["rst"][1 - cur]
                        e1c = e1[0:16, 8 + 4 * ci:12 + 4 * ci]
                        ins = v.tensor_scalar(O[:], O[:], e1c[:, 2:3], None, ALU.mult)
                    if k == 7:
                        inc(ins, "eV")
                on("vector", v_l2)
                cur = 1 - cur
            def g_sc(gp, cur=cur, tok=P.cnt["eV"]):
                wait(gp, "eV", tok)
                for ci, cf in enumerate(cfgs):
                    inc(gp.dma_start(out=row_d[R_T1 + ci, 0:128].rearrange("(a b) -> a b", a=16),
                                     in_=cf["rst"][cur][:]), "qG", 16)
            on("gpsimd", g_sc)
            def s_sc(s, tok=P.cnt["qG"]):
                wait(s, "qG", tok)
                for ci, cf in enumerate(cfgs):
                    inc(s.dma_start(out=cf["srow"][:], in_=row_d[R_T1 + ci:R_T1 + ci + 1, 0:128]), "qS", 16)
            on("sync", s_sc)
            def v_sn(v, tok=P.cnt["qS"]):
                wait(v, "qS", tok)
                for cf in cfgs:
                    ins = v.tensor_scalar(cf["srown"][:], cf["srow"][:], -1.0, None, ALU.mult)
                for ci in range(len(cfgs)):
                    v.memset(e2[0:1, 12 * ci:12 * ci + 12], 0.0)
                    ins = v.memset(e2[0:1, 12 * ci:12 * ci + 1], 1.0)
                inc(ins, "eV")
            on("vector", v_sn)
            for s_ in range(16):
                def v_ap2(v, s_=s_):
                    o = 8 * s_
                    for slot in range(4):
                        for ci, cf in enumerate(cfgs):
                            srow_, srown_ = cf["srow"], cf["srown"]
                            st = e2[0:1, 12 * ci:12 * ci + 4]; nw = e2[0:1, 12 * ci + 4:12 * ci + 8]
                            a_r = srow_[0:1, o + 0:o + 1]; b_r = srow_[0:1, o + 1:o + 2]
                            a_i = srow_[0:1, o + 2:o + 3]; b_i = srow_[0:1, o + 3:o + 4]
                            c_r = srow_[0:1, o + 4:o + 5]; d_r = srow_[0:1, o + 5:o + 6]
                            c_i = srow_[0:1, o + 6:o + 7]; d_i = srow_[0:1, o + 7:o + 8]
                            na_i = srown_[0:1, o + 2:o + 3]; nb_i = srown_[0:1, o + 3:o + 4]
                            nc_i = srown_[0:1, o + 6:o + 7]; nd_i = srown_[0:1, o + 7:o + 8]
                            n_r, n_i, d_rr, d_ii = st[:, 0:1], st[:, 1:2], st[:, 2:3], st[:, 3:4]
                            if slot == 0:
                                v.tensor_copy(cf["psup"][0:1, 4 * s_:4 * s_ + 4], st)
                                v.tensor_scalar(nw[:, 0:1], n_r, a_r, None, ALU.mult)
                                v.tensor_scalar(nw[:, 1:2], n_i, a_r, None, ALU.mult)
                                v.tensor_scalar(nw[:, 2:3], n_r, c_r, None, ALU.mult)
                                ins = v.tensor_scalar(nw[:, 3:4], n_i, c_r, None, ALU.mult)
                            elif slot == 1:
                                v.scalar_tensor_tensor(nw[:, 0:1], n_i, na_i, nw[:, 0:1], ALU.mult, ALU.add)
                                v.scalar_tensor_tensor(nw[:, 1:2], n_r, a_i, nw[:, 1:2], ALU.mult, ALU.add)
                                v.scalar_tensor_tensor(nw[:, 2:3], n_i, nc_i, nw[:, 2:3], ALU.mult, ALU.add)
                                ins = v.scalar_tensor_tensor(nw[:, 3:4], n_r, c_i, nw[:, 3:4], ALU.mult, ALU.add)
                            elif slot == 2:
                                v.scalar_tensor_tensor(nw[:, 0:1], d_rr, b_r, nw[:, 0:1], ALU.mult, ALU.add)
                                v.scalar_tensor_tensor(nw[:, 1:2], d_ii, b_r, nw[:, 1:2], ALU.mult, ALU.add)
                                v.scalar_tensor_tensor(nw[:, 2:3], d_rr, d_r, nw[:, 2:3], ALU.mult, ALU.add)
                                ins = v.scalar_tensor_tensor(nw[:, 3:4], d_ii, d_r, nw[:, 3:4], ALU.mult, ALU.add)
                            else:
                                v.scalar_tensor_tensor(nw[:, 0:1], d_ii, nb_i, nw[:, 0:1], ALU.mult, ALU.add)
                                v.scalar_tensor_tensor(nw[:, 1:2], d_rr, b_i, nw[:, 1:2], ALU.mult, ALU.add)
                                v.scalar_tensor_tensor(nw[:, 2:3], d_ii, nd_i, nw[:, 2:3], ALU.mult, ALU.add)
                                ins = v.scalar_tensor_tensor(nw[:, 3:4], d_rr, d_i, nw[:, 3:4], ALU.mult, ALU.add)
                    for ci, cf in enumerate(cfgs):
                        nw = e2[0:1, 12 * ci + 4:12 * ci + 8]; sc_ = e2[0:1, 12 * ci + 8:12 * ci + 10]
                        v.tensor_tensor(sc_[:, 0:1], nw[:, 0:1], nw[:, 0:1], ALU.mult)
                        ins = v.tensor_tensor(sc_[:, 1:2], nw[:, 1:2], nw[:, 1:2], ALU.mult)
                    for ci, cf in enumerate(cfgs):
                        sc_ = e2[0:1, 12 * ci + 8:12 * ci + 10]
                        ins = v.scalar_tensor_tensor(sc_[:, 0:1], sc_[:, 0:1], 1e-30, sc_[:, 1:2], ALU.add, ALU.add)
                    for ci, cf in enumerate(cfgs):
                        sc_ = e2[0:1, 12 * ci + 8:12 * ci + 10]
                        ins = v.reciprocal(sc_[:, 1:2], sc_[:, 0:1])
                    for ci, cf in enumerate(cfgs):
                        st = e2[0:1, 12 * ci:12 * ci + 4]; nw = e2[0:1, 12 * ci + 4:12 * ci + 8]
                        sc_ = e2[0:1, 12 * ci + 8:12 * ci + 10]
                        ins = v.tensor_scalar(st, nw[:], sc_[:, 1:2], None, ALU.mult)
                    if s_ == 15:
                        inc(ins, "eV")
                on("vector", v_ap2)
            def g_ps(gp, tok=P.cnt["eV"]):
                wait(gp, "eV", tok)
                for ci, cf in enumerate(cfgs):
                    inc(gp.dma_start(out=row_d[R_T1 + ci, 0:64].rearrange("(a b) -> a b", a=1),
                                     in_=cf["psup"][:]), "qG", 16)
            on("gpsimd", g_ps)
            def s_ps(s, tok=P.cnt["qG"]):
                wait(s, "qG", tok)
                for ci, cf in enumerate(cfgs):
                    inc(s.dma_start(out=cf["ps16"][0][:], in_=row_d[R_T1 + ci, 0:64].rearrange("(s c) -> s c", s=16)), "qS", 16)
            on("sync", s_ps)
            cur = 0
            for k in range(8):
                def v_l2c(v, k=k, cur=cur):
                    if k == 0:
                        wait(v, "qS", P.cnt["qS"])
                    o = 8 * k
                    for slot in range(4):
                        for ci, cf in enumerate(cfgs):
                            sb8_, sb8n_ = cf["sb8"], cf["sb8n"]
                            St = cf["ps16"][cur]; Nw = cf["ps16"][1 - cur]
                            a_r = sb8_[:, o + 0:o + 1]; b_r = sb8_[:, o + 1:o + 2]
                            a_i = sb8_[:, o + 2:o + 3]; b_i = sb8_[:, o + 3:o + 4]
                            c_r = sb8_[:, o + 4:o + 5]; d_r = sb8_[:, o + 5:o + 6]
                            c_i = sb8_[:, o + 6:o + 7]; d_i = sb8_[:, o + 7:o + 8]
                            na_i = sb8n_[:, o + 2:o + 3]; nb_i = sb8n_[:, o + 3:o + 4]
                            nc_i = sb8n_[:, o + 6:o + 7]; nd_i = sb8n_[:, o + 7:o + 8]
                            n_r, n_i, d_rr, d_ii = St[:, 0:1], St[:, 1:2], St[:, 2:3], St[:, 3:4]
                            if slot == 0:
                                v.tensor_copy(cf["pblk"][:, 4 * k:4 * k + 4], St[:])
                            for (oc, m_r, m_i, nm_i, q_r, q_i, nq_i) in (
                                (0, a_r, a_i, na_i, b_r, b_i, nb_i),
                                (2, c_r, c_i, nc_i, d_r, d_i, nd_i),
                            ):
                                dst = Nw[:, oc:oc + 1]; dsti = Nw[:, oc + 1:oc + 2]
                                if slot == 0:
                                    v.tensor_scalar(dst, n_r, m_r, None, ALU.mult)
                                    ins = v.tensor_scalar(dsti, n_i, m_r, None, ALU.mult)
                                elif slot == 1:
                                    v.scalar_tensor_tensor(dst, n_i, nm_i, dst, ALU.mult, ALU.add)
                                    ins = v.scalar_tensor_tensor(dsti, n_r, m_i, dsti, ALU.mult, ALU.add)
                                elif slot == 2:
                                    v.scalar_tensor_tensor(dst, d_rr, q_r, dst, ALU.mult, ALU.add)
                                    ins = v.scalar_tensor_tensor(dsti, d_ii, q_r, dsti, ALU.mult, ALU.add)
                                else:
                                    v.scalar_tensor_tensor(dst, d_ii, nq_i, dst, ALU.mult, ALU.add)
                                    ins = v.scalar_tensor_tensor(dsti, d_rr, q_i, dsti, ALU.mult, ALU.add)
                    for ci, cf in enumerate(cfgs):
                        Nw = cf["ps16"][1 - cur]
                        e1c = e1[0:16, 16 + 4 * ci:20 + 4 * ci]
                        v.tensor_tensor(e1c[:, 0:1], Nw[:, 0:1], Nw[:, 0:1], ALU.mult)
                        ins = v.tensor_tensor(e1c[:, 1:2], Nw[:, 1:2], Nw[:, 1:2], ALU.mult)
                    for ci, cf in enumerate(cfgs):
                        e1c = e1[0:16, 16 + 4 * ci:20 + 4 * ci]
                        ins = v.scalar_tensor_tensor(e1c[:, 0:1], e1c[:, 0:1], 1e-30, e1c[:, 1:2], ALU.add, ALU.add)
                    for ci, cf in enumerate(cfgs):
                        e1c = e1[0:16, 16 + 4 * ci:20 + 4 * ci]
                        ins = v.reciprocal(e1c[:, 2:3], e1c[:, 0:1])
                    for ci, cf in enumerate(cfgs):
                        Nw = cf["ps16"][1 - cur]
                        e1c = e1[0:16, 16 + 4 * ci:20 + 4 * ci]
                        ins = v.tensor_scalar(Nw[:], Nw[:], e1c[:, 2:3], None, ALU.mult)
                    if k == 7:
                        inc(ins, "eV")
                on("vector", v_l2c)
                cur = 1 - cur
            def g_pb(gp, tok=P.cnt["eV"]):
                wait(gp, "eV", tok)
                for ci, cf in enumerate(cfgs):
                    inc(gp.dma_start(out=row_d[R_T1 + ci, 0:512].rearrange("(a b) -> a b", a=16),
                                     in_=cf["pblk"][:]), "qG", 16)
            on("gpsimd", g_pb)
            def s_pb(s, tok=P.cnt["qG"]):
                wait(s, "qG", tok)
                for ci, cf in enumerate(cfgs):
                    inc(s.dma_start(out=cf["pb128"][:], in_=row_d[R_T1 + ci, 0:512].rearrange("(b c) -> b c", b=128)), "qS", 16)
            on("sync", s_pb)
            def v_l3i(v, tok=P.cnt["qS"]):
                wait(v, "qS", tok)
                for cf in cfgs:
                    v.tensor_copy(cf["NR"][:, 0:1], cf["pb128"][:, 0:1])
                    v.tensor_copy(cf["NI"][:, 0:1], cf["pb128"][:, 1:2])
                    v.tensor_copy(cf["DR"][:, 0:1], cf["pb128"][:, 2:3])
                    ins = v.tensor_copy(cf["DI"][:, 0:1], cf["pb128"][:, 3:4])
                inc(ins, "eV")
            on("vector", v_l3i)
            for k in range(32):
                def v_l3(v, k=k):
                    for cf in cfgs:
                        NR, NI, DR, DI = cf["NR"], cf["NI"], cf["DR"], cf["DI"]
                        dcol = cf["dre"][:, k:k + 1]
                        v.scalar_tensor_tensor(NR[:, k + 1:k + 2], NR[:, k:k + 1], dcol, DR[:, k:k + 1], ALU.mult, ALU.subtract)
                        ins = v.scalar_tensor_tensor(NI[:, k + 1:k + 2], NI[:, k:k + 1], dcol, DI[:, k:k + 1], ALU.mult, ALU.subtract)
                    for cf in cfgs:
                        NR, NI = cf["NR"], cf["NI"]
                        icol = cf["dim"][:, k:k + 1]; ncol = cf["ngi"][:, k:k + 1]
                        v.scalar_tensor_tensor(NR[:, k + 1:k + 2], NI[:, k:k + 1], ncol, NR[:, k + 1:k + 2], ALU.mult, ALU.add)
                        ins = v.scalar_tensor_tensor(NI[:, k + 1:k + 2], NR[:, k:k + 1], icol, NI[:, k + 1:k + 2], ALU.mult, ALU.add)
                    for cf in cfgs:
                        v.tensor_copy(cf["DR"][:, k + 1:k + 2], cf["NR"][:, k:k + 1])
                        ins = v.tensor_copy(cf["DI"][:, k + 1:k + 2], cf["NI"][:, k:k + 1])
                    if k % 8 == 7:
                        for ci, cf in enumerate(cfgs):
                            NR, NI = cf["NR"], cf["NI"]
                            e1c = e1[:, 16 + 4 * ci:20 + 4 * ci]
                            v.tensor_tensor(e1c[:, 0:1], NR[:, k + 1:k + 2], NR[:, k + 1:k + 2], ALU.mult)
                            ins = v.tensor_tensor(e1c[:, 1:2], NI[:, k + 1:k + 2], NI[:, k + 1:k + 2], ALU.mult)
                        for ci, cf in enumerate(cfgs):
                            e1c = e1[:, 16 + 4 * ci:20 + 4 * ci]
                            ins = v.scalar_tensor_tensor(e1c[:, 0:1], e1c[:, 0:1], 1e-30, e1c[:, 1:2], ALU.add, ALU.add)
                        for ci, cf in enumerate(cfgs):
                            e1c = e1[:, 16 + 4 * ci:20 + 4 * ci]
                            ins = v.reciprocal(e1c[:, 2:3], e1c[:, 0:1])
                        for ci, cf in enumerate(cfgs):
                            e1c = e1[:, 16 + 4 * ci:20 + 4 * ci]
                            for z in (cf["NR"], cf["NI"], cf["DR"], cf["DI"]):
                                ins = v.tensor_scalar(z[:, k + 1:k + 2], z[:, k + 1:k + 2], e1c[:, 2:3], None, ALU.mult)
                        if k == 31:
                            inc(ins, "eV")
                on("vector", v_l3)

        wcv = sb("wcv", [128, 4 * 4096], BF16, st=esB1)
        cv_tok = {}
        for j in range(16):
            slc = j % 4
            def g_cv(gp, j=j, slc=slc):
                if j >= 4:
                    wait(gp, "qD", cv_tok[j - 4])
                if j < 8:
                    inc(gp.dma_start(out=wcv[:, 4096 * slc:4096 * (slc + 1)],
                                     in_=wff1_d[128 * j:128 * (j + 1), :]), "qC", 16)
                else:
                    jj = j - 8
                    inc(gp.dma_start(out=wcv.rearrange("p (s a b) -> p s a b", s=4, a=4)[:, slc],
                                     in_=wff2_d.rearrange("(a p) b -> p a b", p=128)[:, 4 * jj:4 * (jj + 1), :]), "qC", 16)
            on("gpsimd", g_cv)
            def s_cv(s, j=j, slc=slc, tok=P.cnt["qC"]):
                wait(s, "qC", tok)
                if j < 8:
                    inc(s.dma_start(out=wff1b_d[128 * j:128 * (j + 1), :],
                                    in_=wcv[:, 4096 * slc:4096 * (slc + 1)]), "qD", 16)
                else:
                    jj = j - 8
                    inc(s.dma_start(out=wff2b_d.rearrange("(a p) b -> p a b", p=128)[:, 4 * jj:4 * (jj + 1), :],
                                    in_=wcv.rearrange("p (s a b) -> p s a b", s=4, a=4)[:, slc]), "qD", 16)
            on("sync", s_cv)
            cv_tok[j] = P.cnt["qD"]

        if upto == "wcv1":
            dw1 = dout("d_w1b", [128, FF], BF16)
            dw2 = dout("d_w2b", [128, D], BF16)
            def g_dw1(gp):
                wait_all(gp)
                inc(gp.dma_start(out=dw1, in_=wff1b_d[0:128, :]), "qG", 16)
                inc(gp.dma_start(out=dw2, in_=wff2b_d[256:384, :]), "qG", 16)
                wait_all(gp, "qG")
            on("gpsimd", g_dw1)
            P.barrier()
            esB1.close()
            esB2.close()
            esBH.close()
            esB.close()
            return nc, dbg_outs

        cfgF = dict(dre=dreF, dim=dimF, ngi=ngiF, NR=NRf, NI=NIf, DR=DRf, DI=DIf,
                    s_re=s_re, s_im=s_im, c_re=c_re, c_im=c_im, tcomp=tcomp,
                    sb8=sb8, sb8n=sb8n, rst=rst, srow=srow, srown=srown,
                    psup=psup, ps16=ps16, pblk=pblk, pb128=pb128)
        cfgB = dict(dre=dreB, dim=dimB, ngi=ngiB, NR=NRb, NI=NIb, DR=DRb, DI=DIb,
                    s_re=s_reB, s_im=s_imB, c_re=c_reB, c_im=c_imB, tcomp=tcompB,
                    sb8=sb8B, sb8n=sb8nB, rst=rstB, srow=srowB, srown=srownB,
                    psup=psupB, ps16=ps16B, pblk=pblkB, pb128=pb128B)
        cf_scan2([cfgF, cfgB])

        for (til, dst, r1, r2) in ((NRb, bre, R_T1, R_T2), (NIb, bim, R_T3, R_T4),
                                   (DRb, bre2, R_T1, R_T2), (DIb, bim2, R_T3, R_T4)):
            def g_u1(gp, til=til, r1=r1, tok=P.cnt["eV"]):
                wait(gp, "eV", tok)
                wait(gp, "qS", P.cnt["qS"])
                inc(gp.dma_start(out=row_d[r1, :].rearrange("(a b) -> a b", a=128), in_=til[:, 1:33]), "qG", 16)
            on("gpsimd", g_u1)
            def s_u1(s, r1=r1, tok=P.cnt["qG"]):
                wait(s, "qG", tok)
                inc(s.dma_start(out=rowA[:], in_=row_d[r1:r1 + 1, :]), "qS", 16)
            on("sync", s_u1)
            def v_u1(v, tok=P.cnt["qS"]):
                wait(v, "qS", tok)
                inc(v.tensor_copy(rowB[:], rowA[0:1, ::-1]), "eV")
            on("vector", v_u1)
            def g_u2(gp, r2=r2, tok=P.cnt["eV"]):
                wait(gp, "eV", tok)
                inc(gp.dma_start(out=row_d[r2:r2 + 1, :], in_=rowB[:]), "qG", 16)
            on("gpsimd", g_u2)
            def s_u2(s, dst=dst, r2=r2, tok=P.cnt["qG"]):
                wait(s, "qG", tok)
                inc(s.dma_start(out=dst[:], in_=row_d[r2, :].rearrange("(b j) -> b j", b=128)), "qS", 16)
            on("sync", s_u2)

        def v_gc(v):
            wait_all(v, "qS", "eV")
            na_r, na_i = NRf[:, 1:33], NIf[:, 1:33]
            da_r, da_i = DRf[:, 1:33], DIf[:, 1:33]
            nb_r, nb_i = bre[:], bim[:]
            db_r, db_i = bre2[:], bim2[:]
            numr, numi, t1r, t1i, denr, deni = (gsc[0], gsc[1], gsc[2], gsc[3], gsc[4], gsc[5])
            for mi, (pr, pi, qr, qi) in ((4, (na_r, na_i, da_r, da_i)), (5, (nb_r, nb_i, db_r, db_i))):
                ch(v, "eV", v.tensor_tensor(gsc[mi][:], pr, pr, ALU.mult))
                ch(v, "eV", v.tensor_tensor(t1r[:], pi, pi, ALU.mult))
                ch(v, "eV", v.tensor_tensor(gsc[mi][:], gsc[mi][:], t1r[:], ALU.add))
                ch(v, "eV", v.tensor_tensor(t1r[:], qr, qr, ALU.mult))
                ch(v, "eV", v.tensor_tensor(gsc[mi][:], gsc[mi][:], t1r[:], ALU.add))
                ch(v, "eV", v.tensor_tensor(t1r[:], qi, qi, ALU.mult))
                ch(v, "eV", v.tensor_tensor(gsc[mi][:], gsc[mi][:], t1r[:], ALU.add))
                inc(v.tensor_scalar(gsc[mi][:], gsc[mi][:], 1e-30, None, ALU.add), "eV")
            inc(v.memset(tinyc[:], 0.0), "eV")
        on("vector", v_gc)
        def a_gcs(a, tok=P.cnt["eV"]):
            wait(a, "eA", P.cnt["eA"])
            wait(a, "eV", tok)
            ch(a, "eA", a.activation(gsc[4][:], gsc[4][:], AF.Sqrt))
            inc(a.activation(gsc[5][:], gsc[5][:], AF.Sqrt), "eA")
        on("scalar", a_gcs)
        def v_gc2b(v, tok=P.cnt["eA"]):
            wait(v, "eA", tok)
            na_r, na_i = NRf[:, 1:33], NIf[:, 1:33]
            da_r, da_i = DRf[:, 1:33], DIf[:, 1:33]
            nb_r, nb_i = bre[:], bim[:]
            db_r, db_i = bre2[:], bim2[:]
            numr, numi, t1r, t1i, denr, deni = (gsc[0], gsc[1], gsc[2], gsc[3], gsc[4], gsc[5])
            ch(v, "eV", v.reciprocal(gsc[4][:], gsc[4][:]))
            ch(v, "eV", v.reciprocal(gsc[5][:], gsc[5][:]))
            for mi, (pr, pi, qr, qi) in ((4, (na_r, na_i, da_r, da_i)), (5, (nb_r, nb_i, db_r, db_i))):
                for z in (pr, pi, qr, qi):
                    ch(v, "eV", v.tensor_tensor(z, z, gsc[mi][:], ALU.mult))
            ch(v, "eV", v.tensor_tensor(numr[:], da_r, db_r, ALU.mult))
            ch(v, "eV", v.tensor_tensor(t1r[:], da_i, db_i, ALU.mult))
            ch(v, "eV", v.tensor_tensor(numr[:], numr[:], t1r[:], ALU.subtract))
            ch(v, "eV", v.tensor_tensor(numi[:], da_r, db_i, ALU.mult))
            ch(v, "eV", v.tensor_tensor(t1r[:], da_i, db_r, ALU.mult))
            ch(v, "eV", v.tensor_tensor(numi[:], numi[:], t1r[:], ALU.add))
            ch(v, "eV", v.tensor_tensor(denr[:], na_r, db_r, ALU.mult))
            ch(v, "eV", v.tensor_tensor(t1r[:], na_i, db_i, ALU.mult))
            ch(v, "eV", v.tensor_tensor(denr[:], denr[:], t1r[:], ALU.subtract))
            ch(v, "eV", v.tensor_tensor(t1r[:], nb_r, da_r, ALU.mult))
            ch(v, "eV", v.tensor_tensor(denr[:], denr[:], t1r[:], ALU.add))
            ch(v, "eV", v.tensor_tensor(t1r[:], nb_i, da_i, ALU.mult))
            ch(v, "eV", v.tensor_tensor(denr[:], denr[:], t1r[:], ALU.subtract))
            ch(v, "eV", v.tensor_tensor(t1r[:], dreF[:], numr[:], ALU.mult))
            ch(v, "eV", v.tensor_tensor(denr[:], denr[:], t1r[:], ALU.subtract))
            ch(v, "eV", v.tensor_tensor(t1r[:], dimF[:], numi[:], ALU.mult))
            ch(v, "eV", v.tensor_tensor(denr[:], denr[:], t1r[:], ALU.add))
            ch(v, "eV", v.tensor_tensor(deni[:], na_r, db_i, ALU.mult))
            ch(v, "eV", v.tensor_tensor(t1r[:], na_i, db_r, ALU.mult))
            ch(v, "eV", v.tensor_tensor(deni[:], deni[:], t1r[:], ALU.add))
            ch(v, "eV", v.tensor_tensor(t1r[:], nb_r, da_i, ALU.mult))
            ch(v, "eV", v.tensor_tensor(deni[:], deni[:], t1r[:], ALU.add))
            ch(v, "eV", v.tensor_tensor(t1r[:], nb_i, da_r, ALU.mult))
            ch(v, "eV", v.tensor_tensor(deni[:], deni[:], t1r[:], ALU.add))
            ch(v, "eV", v.tensor_tensor(t1r[:], dreF[:], numi[:], ALU.mult))
            ch(v, "eV", v.tensor_tensor(deni[:], deni[:], t1r[:], ALU.subtract))
            ch(v, "eV", v.tensor_tensor(t1r[:], dimF[:], numr[:], ALU.mult))
            ch(v, "eV", v.tensor_tensor(deni[:], deni[:], t1r[:], ALU.subtract))
            ch(v, "eV", v.tensor_tensor(t1i[:], denr[:], denr[:], ALU.mult))
            ch(v, "eV", v.tensor_tensor(t1r[:], deni[:], deni[:], ALU.mult))
            ch(v, "eV", v.tensor_tensor(t1i[:], t1i[:], t1r[:], ALU.add))
            ch(v, "eV", v.reciprocal(t1i[:], t1i[:]))
            ch(v, "eV", v.tensor_tensor(bre[:], numr[:], denr[:], ALU.mult))
            ch(v, "eV", v.tensor_tensor(bim[:], numi[:], deni[:], ALU.mult))
            ch(v, "eV", v.tensor_tensor(bre[:], bre[:], bim[:], ALU.add))
            ch(v, "eV", v.tensor_tensor(bim[:], numi[:], denr[:], ALU.mult))
            ch(v, "eV", v.tensor_tensor(bre2[:], numr[:], deni[:], ALU.mult))
            ch(v, "eV", v.tensor_tensor(bim[:], bim[:], bre2[:], ALU.subtract))
            ch(v, "eV", v.tensor_tensor(gsc[2][:], bre[:], t1i[:], ALU.mult))
            inc(v.tensor_tensor(gsc[3][:], bim[:], t1i[:], ALU.mult), "eV")
        on("vector", v_gc2b)
        def g_g1(gp, tok=P.cnt["eV"]):
            wait(gp, "eV", tok)
            inc(gp.dma_start(out=row_d[R_GR, :].rearrange("(a b) -> a b", a=128), in_=gsc[2][:]), "qG", 16)
            inc(gp.dma_start(out=row_d[R_GI, :].rearrange("(a b) -> a b", a=128), in_=gsc[3][:]), "qG", 16)
        on("gpsimd", g_g1)
        def g_g2(gp, tok=P.cnt["qG"]):
            wait(gp, "qG", tok)
            inc(gp.dma_start(out=gfm[0:1, :], in_=row_d[R_GR:R_GR + 1, :]), "qG", 16)
            inc(gp.dma_start(out=gfm[1:2, :], in_=row_d[R_GI:R_GI + 1, :]), "qG", 16)
        on("gpsimd", g_g2)

        def v_gh(v):
            wait_all(v, "qS", "eV")
            inc(v.tensor_tensor_scan(ghat[:], gch[:], gch[:], 0.0, ALU.add, ALU.bypass), "eV")
        on("vector", v_gh)
        def a_eh(a, tok=P.cnt["eV"]):
            wait(a, "eV", tok)
            ch(a, "eA", a.activation(epx[:], ghat[:], AF.Exp, bias=lnecol[0:32, 0:1]))
            inc(a.activation(emx[:], ghat[:], AF.Exp, scale=-1.0), "eA")
        on("scalar", a_eh)
        for (srcT, dstT) in ((epx, epT), (emx, emT)):
            def t_eT2(t, srcT=srcT, tok=P.cnt["eA"]):
                wait(t, "eA", tok)
                P.bank_wait(t, "pb7")
                inc(t.transpose(pb[7][0:128, 0:32], srcT[:], idn[0:32, 0:32]), "eT")
            on("tensor", t_eT2)
            def v_eT2(v, dstT=dstT, tok=P.cnt["eT"]):
                wait(v, "eT", tok)
                inc(v.tensor_copy(dstT[:], pb[7][0:128, 0:32]), "eV")
            on("vector", v_eT2)
            P.bank_set("pb7", "eV")
        def g_gc(gp, tok=P.cnt["eA"]):
            wait(gp, "eA", tok)
            inc(gp.dma_start(out=row_d[R_GC, 0:32].rearrange("(a b) -> a b", a=32), in_=emx[:, 127:128]), "qG", 16)
        on("gpsimd", g_gc)
        def s_gc(s, tok=P.cnt["qG"]):
            wait(s, "qG", tok)
            inc(s.dma_start(out=e3[0:1, 0:32], in_=row_d[R_GC:R_GC + 1, 0:32]), "qS", 16)
        on("sync", s_gc)
        def t_gc(t, tok=P.cnt["qS"]):
            wait(t, "qS", tok)
            P.bank_wait(t, "pb7")
            inc(t.matmul(pb[7][0:128, 0:32], ones1f[:], e3[0:1, 0:32], start=True, stop=True), "eT")
        on("tensor", t_gc)
        def v_gc2(v, tok=P.cnt["eT"]):
            wait(v, "eT", tok)
            inc(v.tensor_copy(gcbc[:], pb[7][0:128, 0:32]), "eV")
        on("vector", v_gc2)
        P.bank_set("pb7", "eV")
        def g_er(gp):
            wait_all(gp, "eA", "qS")
            inc(gp.dma_start(out=row_d[R_EM, :].rearrange("(c p) -> c p", c=32), in_=emx[:]), "qG", 16)
            inc(gp.dma_start(out=row_d[R_EP, :].rearrange("(c p) -> c p", c=32), in_=epx[:]), "qG", 16)
        on("gpsimd", g_er)
        for (rr, dstb) in ((R_EM, embc), (R_EP, epbc)):
            def s_er2(s, rr=rr, tok=P.cnt["qG"]):
                wait(s, "qG", tok)
                wait(s, "eT", P.cnt["eT"])
                inc(s.dma_start(out=rowA[:], in_=row_d[rr:rr + 1, :]), "qS", 16)
            on("sync", s_er2)
            tok_row = P.cnt["qS"]
            for c in range(32):
                bk = c % 2
                def t_eb(t, c=c, bk=bk, tok=tok_row):
                    if c == 0:
                        wait(t, "qS", tok)
                    P.bank_wait(t, f"pb{bk}")
                    inc(t.matmul(pb[bk][0:128, 0:128], ones1f[:], rowA[0:1, 128 * c:128 * (c + 1)],
                                 start=True, stop=True), "eT")
                on("tensor", t_eb)
                def v_eb(v, c=c, bk=bk, dstb=dstb, tok=P.cnt["eT"]):
                    wait(v, "eT", tok)
                    inc(v.tensor_copy(dstb[:, 128 * c:128 * (c + 1)], pb[bk][0:128, 0:128]), "eV")
                on("vector", v_eb)
                P.bank_set(f"pb{bk}", "eV")
        def g_wbk(gp):
            inc(gp.dma_start(out=wbkb[:], in_=wbk_d), "qG", 16)
            inc(gp.dma_start(out=bbk_row[:], in_=bbk_d), "qG", 16)
        on("gpsimd", g_wbk)
        def t_bbk(t, tok=P.cnt["qG"]):
            wait(t, "qG", tok)
            P.bank_wait(t, "pb6")
            P.bank_wait(t, "pb5")
            t.matmul(pb[6][0:128, 0:512], ones1f[:], bbk_row[0:1, 0:512], start=True, stop=True)
            inc(t.matmul(pb[5][0:128, 0:512],
                         ones1f[:], bbk_row[0:1, 512:1024], start=True, stop=True), "eT")
        on("tensor", t_bbk)
        def v_bbk(v, tok=P.cnt["eT"]):
            wait(v, "eT", tok)
            v.tensor_copy(bbk_bc[:, 0:512], pb[6][0:128, 0:512])
            inc(v.tensor_copy(bbk_bc[:, 512:1024], pb[5][0:128, 0:512]), "eV")
        on("vector", v_bbk)
        P.bank_set("pb6", "eV"); P.bank_set("pb5", "eV")

        if dbg >= 2 or upto == "b1":
            dG = dout("d_G", [128, 64])
            dghat = dout("d_ghat", [32, 128])
            dept = dout("d_epT", [128, 32])
            demt = dout("d_emT", [128, 32])
            dgc = dout("d_gcbc", [128, 32])
            dprs = dout("d_pairs", [128, 4 * 33])
            dprb = dout("d_pairb", [128, 4 * 32])
            dprbr = dout("d_pairbr", [128, 4 * 33])
            def g_db1(gp):
                wait_all(gp)
                for kk, til in enumerate((NRf, NIf, DRf, DIf)):
                    inc(gp.dma_start(out=dprs[:, 33 * kk:33 * (kk + 1)], in_=til[:]), "qG", 16)
                for kk, til in enumerate((NRb, NIb, DRb, DIb)):
                    inc(gp.dma_start(out=dprbr[:, 33 * kk:33 * (kk + 1)], in_=til[:]), "qG", 16)
                for kk, til in enumerate((bre, bim, bre2, bim2)):
                    inc(gp.dma_start(out=dprb[:, 32 * kk:32 * (kk + 1)], in_=til[:]), "qG", 16)
                inc(gp.dma_start(out=dG[:, 0:32], in_=gsc[2][:]), "qG", 16)
                inc(gp.dma_start(out=dG[:, 32:64], in_=gsc[3][:]), "qG", 16)
                inc(gp.dma_start(out=dghat, in_=ghat[:]), "qG", 16)
                inc(gp.dma_start(out=dept, in_=epT[:]), "qG", 16)
                inc(gp.dma_start(out=demt, in_=emT[:]), "qG", 16)
                inc(gp.dma_start(out=dgc, in_=gcbc[:]), "qG", 16)
                wait_all(gp, "qG")
            on("gpsimd", g_db1)
            if upto == "b1":
                P.barrier()
                esB1.close()
                esB2.close()
                esBH.close()
                esB.close()
                return nc, dbg_outs


        P.scope("b2")
        P.barrier()
        esB1.close()

        b2_tok = {}
        for i in range(NT):
            sl = i % 3
            yv = yring[:, D * sl:D * (sl + 1)]
            bk0, bk1 = (6, 7) if i % 2 == 0 else (4, 5)
            def s_y(s, i=i, yv=yv):
                if i >= 3:
                    wait(s, "eA", b2_tok[i - 3][0])
                    wait(s, "qG", b2_tok[i - 3][1])
                inc(s.dma_start(out=yv, in_=y_d[128 * i:128 * (i + 1), :]), "qS", 16)
            on("sync", s_y)
            tok_y = P.cnt["qS"]
            def t_bk(t, i=i, bk0=bk0, bk1=bk1, tok=tok_y):
                P.bank_wait(t, f"pb{bk0}"); P.bank_wait(t, f"pb{bk1}")
                if i == 0:
                    wait(t, "eV", P.cnt["eV"])
                    wait(t, "qG", P.cnt["qG"])
                t.matmul(pb[bk0][0:128, 0:512], gfm[:, 128 * i:128 * (i + 1)], wbkb[:, 0:512],
                         start=True, stop=True)
                inc(t.matmul(pb[bk1][0:128, 0:512], gfm[:, 128 * i:128 * (i + 1)], wbkb[:, 512:1024],
                             start=True, stop=True), "eT")
            on("tensor", t_bk)
            def v_y1(v, i=i, yv=yv, bk0=bk0, bk1=bk1, tok=P.cnt["eT"], toky=tok_y):
                wait(v, "eT", tok)
                wait(v, "qS", toky)
                v.tensor_tensor(junkV[:, 0:512], pb[bk0][0:128, 0:512], bbk_bc[:, 0:512], ALU.add)
                ch(v, "eV", v.tensor_tensor(junkV[:, 512:1024], pb[bk1][0:128, 0:512], bbk_bc[:, 512:1024], ALU.add))
                v.tensor_tensor(yv[:, 0:512], junkV[:, 0:512], yv[:, 0:512], ALU.add)
                inc(v.tensor_tensor(yv[:, 512:1024], junkV[:, 512:1024], yv[:, 512:1024], ALU.add), "eV")
            on("vector", v_y1)
            P.bank_set(f"pb{bk0}", "eV"); P.bank_set(f"pb{bk1}", "eV")
            tok_y1 = P.cnt["eV"]
            def g_y1(gp, i=i, yv=yv, tok=tok_y1):
                wait(gp, "eV", tok)
                inc(gp.dma_start(out=y_d[128 * i:128 * (i + 1), :], in_=yv), "qG", 16)
            on("gpsimd", g_y1)
            def a_s2(a, i=i, yv=yv, tok=tok_y1):
                wait(a, "eV", tok)
                a.activation(junkA[:], yv, AF.Copy, scale=INV_D, accum_out=m2a[:, i:i + 1])
                inc(a.activation(junkA2[:], yv, AF.Square, scale=INV_SQ_D,
                                 accum_out=q2a[:, i:i + 1]), "eA")
            on("scalar", a_s2)
            b2_tok[i] = (P.cnt["eA"], P.cnt["qG"])

        def v_f2(v, tok=P.cnt["eA"]):
            wait(v, "eA", tok)
            ch(v, "eV", v.tensor_tensor(st1[:], m2a[:], m2a[:], ALU.mult))
            inc(v.tensor_tensor(st2[:], q2a[:], st1[:], ALU.subtract), "eV")
        on("vector", v_f2)
        def a_f2(a, tok=P.cnt["eV"]):
            wait(a, "eV", tok)
            inc(a.activation(st1[:], st2[:], AF.Sqrt, bias=epscol[:, 0:1]), "eA")
        on("scalar", a_f2)
        def v_f2b(v, tok=P.cnt["eA"]):
            wait(v, "eA", tok)
            inc(v.reciprocal(r2a[:], st1[:]), "eV")
        on("vector", v_f2b)
        tokR2 = P.cnt["eV"]

        P.scope("qkvprep")
        esB3 = contextlib.ExitStack()
        wq_sb = sb("wq_sb", [128, 8 * 1536], BF16, st=esB3)
        xn2g = sb("xn2g", [128, 8 * 1024], BF16, st=esB3)
        g2c = sb("g2c", [128, 8], st=esB3)
        b2cb = sb("b2cb", [128, 8], BF16, st=esB3)
        beffq = sb("beffq", [1, 1536], st=esB3)
        beffqT = sb("beffqT", [128, 12], st=esB3)
        kvbc = sb("kvbc", [128, 1024], st=esB3)
        qkfm = sb("qkfm", [128, 8 * 1024], BF16, st=esB3)
        ktok = sb("ktok", [128, 8 * 512], BF16, st=esB3)
        vtok = sb("vtok", [128, 8 * 512], BF16, st=esB3)
        qts = sb("qts", [128, 4 * 128], BF16, st=esB3)
        kts = sb("kts", [128, 4 * 128], BF16, st=esB3)
        khat = sb("khat", [128, 512], BF16, st=esB3)
        vhat = sb("vhat", [128, 512], BF16, st=esB3)
        Wst = sb("Wst", [128, 4 * 64], st=esB3)
        Wbf = [sb(f"Wbf{k}", [128, 4 * 64], BF16, st=esB3) for k in range(2)]
        ones1b = sb("ones1b", [1, 128], BF16, st=esB3)
        sdt = sb("sdt", [128, 512], BF16, st=esB3)

        def g_wq(gp):
            for dc in range(8):
                inc(gp.dma_start(out=wq_sb[:, 1536 * dc:1536 * (dc + 1)],
                                 in_=wqkv_d[128 * dc:128 * (dc + 1), :]), "qG", 16)
            inc(gp.dma_start(out=beffq[:], in_=bqkv_d), "qG", 16)
        on("gpsimd", g_wq)
        tok_wq = P.cnt["qG"]
        def s_g2(s):
            wait(s, "eT", P.cnt["eT"])
            inc(s.dma_start(out=vsm[:], in_=ln2g_d), "qS", 16)
        on("sync", s_g2)
        def t_g2(t, tok=P.cnt["qS"]):
            wait(t, "qS", tok)
            P.bank_wait(t, "pb0")
            inc(t.transpose(pb[0][0:128, 0:8], vsm[:], idn[0:8, 0:8]), "eT")
        on("tensor", t_g2)
        def v_g2c(v, tok=P.cnt["eT"]):
            wait(v, "eT", tok)
            v.memset(ones1b[:], 1.0)
            inc(v.tensor_copy(g2c[:], pb[0][0:128, 0:8]), "eV")
        on("vector", v_g2c)
        P.bank_set("pb0", "eV")
        def s_b2(s):
            wait(s, "eT", P.cnt["eT"])
            inc(s.dma_start(out=vsm[:], in_=ln2b_d), "qS", 16)
        on("sync", s_b2)
        def t_b2(t, tok=P.cnt["qS"]):
            wait(t, "qS", tok)
            P.bank_wait(t, "pb1")
            inc(t.transpose(pb[1][0:128, 0:8], vsm[:], idn[0:8, 0:8]), "eT")
        on("tensor", t_b2)
        def v_b2c(v, tok=P.cnt["eT"]):
            wait(v, "eT", tok)
            inc(v.tensor_copy(b2cb[:], pb[1][0:128, 0:8]), "eV")
        on("vector", v_b2c)
        P.bank_set("pb1", "eV")
        for cc3 in range(3):
            def t_bq(t, cc3=cc3, tokw=tok_wq):
                if cc3 == 0:
                    wait(t, "qG", tokw)
                    wait(t, "eV", P.cnt["eV"])
                P.bank_wait(t, f"pb{2 + cc3}")
                for dc in range(8):
                    ins = t.matmul(pb[2 + cc3][0:1, 0:512], b2cb[:, dc:dc + 1],
                                   wq_sb[:, 1536 * dc + 512 * cc3:1536 * dc + 512 * (cc3 + 1)],
                                   start=(dc == 0), stop=(dc == 7))
                inc(ins, "eT")
            on("tensor", t_bq)
            def v_bq(v, cc3=cc3, tok=P.cnt["eT"]):
                wait(v, "eT", tok)
                inc(v.tensor_tensor(beffq[0:1, 512 * cc3:512 * (cc3 + 1)],
                                    pb[2 + cc3][0:1, 0:512],
                                    beffq[0:1, 512 * cc3:512 * (cc3 + 1)], ALU.add), "eV")
            on("vector", v_bq)
            P.bank_set(f"pb{2 + cc3}", "eV")
        wqpre_d = nc.dram_tensor("d_wqpre", [128, 8 * 1536], BF16, kind="ExternalOutput").ap() if upto == "b4" else None
        if upto == "b4":
            dbg_outs["d_wqpre"] = wqpre_d
            def g_pre(gp):
                wait_all(gp)
                inc(gp.dma_start(out=wqpre_d, in_=wq_sb[:]), "qG", 16)
            on("gpsimd", g_pre)
        def v_fold(v, tok=P.cnt["eV"]):
            if upto == "b4":
                wait(v, "qG", P.cnt["qG"])
            wait(v, "eV", tok)
            for dc in range(8):
                ins = v.tensor_scalar(wq_sb[:, 1536 * dc:1536 * (dc + 1)],
                                      wq_sb[:, 1536 * dc:1536 * (dc + 1)],
                                      g2c[:, dc:dc + 1], None, ALU.mult)
            inc(ins, "eV")
        on("vector", v_fold)
        def g_bq(gp, tok=P.cnt["eV"]):
            wait(gp, "eV", tok)
            inc(gp.dma_start(out=row_d[R_T1, 0:1536].rearrange("(a b) -> a b", a=1), in_=beffq[:]), "qG", 16)
        on("gpsimd", g_bq)
        def s_bqT(s, tok=P.cnt["qG"]):
            wait(s, "qG", tok)
            inc(s.dma_start(out=vsm32[0:12, :], in_=row_d[R_T1, 0:1536].rearrange("(a b) -> a b", a=12)), "qS", 16)
        on("sync", s_bqT)
        def t_bqT(t, tok=P.cnt["qS"]):
            wait(t, "qS", tok)
            P.bank_wait(t, "pb5")
            inc(t.transpose(pb[5][0:128, 0:12], vsm32[0:12, :], idn[0:12, 0:12]), "eT")
        on("tensor", t_bqT)
        def v_bqT(v, tok=P.cnt["eT"]):
            wait(v, "eT", tok)
            inc(v.tensor_copy(beffqT[:], pb[5][0:128, 0:12]), "eV")
        on("vector", v_bqT)
        P.bank_set("pb5", "eV")
        def t_kvb(t, tok=P.cnt["eV"]):
            wait(t, "eV", tok)
            P.bank_wait(t, "pb6"); P.bank_wait(t, "pb7")
            t.matmul(pb[6][0:128, 0:512], ones1f[:], beffq[0:1, 512:1024], start=True, stop=True)
            inc(t.matmul(pb[7][0:128, 0:512], ones1f[:], beffq[0:1, 1024:1536], start=True, stop=True), "eT")
        on("tensor", t_kvb)
        def v_kvb(v, tok=P.cnt["eT"]):
            wait(v, "eT", tok)
            v.tensor_copy(kvbc[:, 0:512], pb[6][0:128, 0:512])
            inc(v.tensor_copy(kvbc[:, 512:1024], pb[7][0:128, 0:512]), "eV")
        on("vector", v_kvb)
        P.bank_set("pb6", "eV"); P.bank_set("pb7", "eV")
        def v_wi(v):
            v.memset(Wst[:], 0.0)
            v.memset(Wbf[0][:], 0.0)
            inc(v.memset(Wbf[1][:], 0.0), "eV")
        on("vector", v_wi)
        P.barrier()

        P.scope("hebbian")
        b2b_tok = {}
        for g in range(4):
            t0 = 1024 * g
            for tt in range(8):
                i = 8 * g + tt
                sl = i % 2
                yv = yring[:, D * sl:D * (sl + 1)]
                def s_y2(s, i=i, yv=yv):
                    if i < 2:
                        wait(s, "eA", P.cnt["eA"])
                        wait(s, "qG", P.cnt["qG"])
                    else:
                        wait(s, "eT", b2b_tok[i - 2])
                    inc(s.dma_start(out=yv, in_=y_d[128 * i:128 * (i + 1), :]), "qS", 16)
                on("sync", s_y2)
                tok_y = P.cnt["qS"]
                def v_x2(v, i=i, yv=yv, tok=tok_y):
                    if i == 0:
                        wait(v, "eV", tokR2)
                    wait(v, "qS", tok)
                    if i >= 1:
                        wait(v, "eT", b2b_tok[i - 1])
                    inc(v.tensor_scalar(junkV[:], yv, m2a[:, i:i + 1], r2a[:, i:i + 1],
                                        ALU.subtract, ALU.mult), "eV")
                on("vector", v_x2)
                tokXN2 = P.cnt["eV"]
                for c in range(8):
                    bk = c % 4
                    def t_t2(t, c=c, bk=bk, tok=tokXN2):
                        if c == 0:
                            wait(t, "eV", tok)
                        P.bank_wait(t, f"pb{bk}")
                        inc(t.transpose(pb[bk][:, 0:128], junkV[:, 128 * c:128 * (c + 1)], idn[:]), "eT")
                    on("tensor", t_t2)
                    def v_e2(v, c=c, bk=bk, tt=tt, tok=P.cnt["eT"]):
                        wait(v, "eT", tok)
                        inc(v.tensor_copy(xn2g[:, 1024 * c + 128 * tt:1024 * c + 128 * (tt + 1)],
                                          pb[bk][:, 0:128]), "eV")
                    on("vector", v_e2)
                    P.bank_set(f"pb{bk}", "eV")
                b2b_tok[i] = P.cnt["eT"]
            tok_grp_x = P.cnt["eV"]
            for cc in range(8):
                for h2 in range(2):
                    bkq = h2
                    def t_qf(t, cc=cc, h2=h2, bkq=bkq, tok=tok_grp_x):
                        if cc == 0 and h2 == 0:
                            wait(t, "eV", tok)
                        P.bank_wait(t, f"pb{bkq}")
                        for dc in range(8):
                            ins = t.matmul(pb[bkq][0:128, 0:512],
                                           wq_sb[:, 1536 * dc + 128 * cc:1536 * dc + 128 * (cc + 1)],
                                           xn2g[:, 1024 * dc + 512 * h2:1024 * dc + 512 * (h2 + 1)],
                                           start=(dc == 0), stop=(dc == 7))
                        inc(ins, "eT")
                    on("tensor", t_qf)
                    def a_qf(a, cc=cc, h2=h2, bkq=bkq, tok=P.cnt["eT"]):
                        wait(a, "eT", tok)
                        inc(a.activation(qkfm[:, 1024 * cc + 512 * h2:1024 * cc + 512 * (h2 + 1)],
                                         pb[bkq][0:128, 0:512], AF.Identity,
                                         bias=beffqT[:, cc:cc + 1]), "eA")
                    on("scalar", a_qf)
                    P.bank_set(f"pb{bkq}", "eA")
            for tt in range(8):
                def t_kv(t, tt=tt):
                    P.bank_wait(t, "pb2"); P.bank_wait(t, "pb3")
                    for dc in range(8):
                        t.matmul(pb[2][0:128, 0:512],
                                 xn2g[:, 1024 * dc + 128 * tt:1024 * dc + 128 * (tt + 1)],
                                 wq_sb[:, 1536 * dc + 512:1536 * dc + 1024],
                                 start=(dc == 0), stop=(dc == 7))
                        ins = t.matmul(pb[3][0:128, 0:512],
                                       xn2g[:, 1024 * dc + 128 * tt:1024 * dc + 128 * (tt + 1)],
                                       wq_sb[:, 1536 * dc + 1024:1536 * dc + 1536],
                                       start=(dc == 0), stop=(dc == 7))
                    inc(ins, "eT")
                on("tensor", t_kv)
                def v_kv(v, tt=tt, tok=P.cnt["eT"]):
                    wait(v, "eT", tok)
                    v.tensor_tensor(ktok[:, 512 * tt:512 * (tt + 1)], pb[2][0:128, 0:512],
                                    kvbc[:, 0:512], ALU.add)
                    inc(v.tensor_tensor(vtok[:, 512 * tt:512 * (tt + 1)], pb[3][0:128, 0:512],
                                        kvbc[:, 512:1024], ALU.add), "eV")
                on("vector", v_kv)
                P.bank_set("pb2", "eV"); P.bank_set("pb3", "eV")
            tok_qkv = (P.cnt["eA"], P.cnt["eV"])
            for tt in range(8):
                c = 8 * g + tt
                par = c % 2
                tok_p = {}
                def v_sc(v, c=c, tt=tt, tok=tok_qkv):
                    wait(v, "eA", tok[0]); wait(v, "eV", tok[1])
                    if c >= 1:
                        v.tensor_copy(Wbf[c % 2][:, 192:256], Wst[:, 192:256])
                    for pp in range(4):
                        v.tensor_tensor(qts[:, 128 * pp:128 * (pp + 1)],
                                        qkfm[:, 1024 * pp + 128 * tt:1024 * pp + 128 * (tt + 1)],
                                        embc[:, 128 * c:128 * (c + 1)], ALU.mult)
                        ins = v.tensor_tensor(kts[:, 128 * pp:128 * (pp + 1)],
                                              qkfm[:, 1024 * (4 + pp) + 128 * tt:1024 * (4 + pp) + 128 * (tt + 1)],
                                              epbc[:, 128 * c:128 * (c + 1)], ALU.mult)
                        tok_p[pp] = inc(ins, "eV")
                    v.tensor_scalar(khat[:], ktok[:, 512 * tt:512 * (tt + 1)], epT[:, c:c + 1], None, ALU.mult)
                    inc(v.tensor_scalar(vhat[:], vtok[:, 512 * tt:512 * (tt + 1)], gcbc[:, c:c + 1], None, ALU.mult), "eV")
                on("vector", v_sc)
                tok_sc = P.cnt["eV"]
                for pp in range(4):
                    pm = pp % 2
                    bse, bso = 2 * pm, 2 * pm + 1
                    bU, bO = 4 + pm, 6 + pm
                    so = 256 * pm
                    def t_s(t, pp=pp, bse=bse, bso=bso, tok_p=tok_p):
                        wait(t, "eV", tok_p[pp])
                        P.bank_wait(t, f"pb{bse}"); P.bank_wait(t, f"pb{bso}")
                        t.matmul(pb[bse][0:128, 0:128], kts[0:64, 128 * pp:128 * (pp + 1)],
                                 qts[0:64, 128 * pp:128 * (pp + 1)], start=True, stop=True)
                        inc(t.matmul(pb[bso][0:128, 0:128], kts[64:128, 128 * pp:128 * (pp + 1)],
                                     qts[64:128, 128 * pp:128 * (pp + 1)], start=True, stop=True), "eT")
                    on("tensor", t_s)
                    def v_m(v, bse=bse, bso=bso, so=so, tok=P.cnt["eT"]):
                        wait(v, "eT", tok)
                        v.tensor_tensor(sdt[:, so:so + 128], pb[bse][0:128, 0:128], trilT[:], ALU.mult)
                        inc(v.tensor_tensor(sdt[:, so + 128:so + 256], pb[bso][0:128, 0:128], trilT[:], ALU.mult), "eV")
                    on("vector", v_m)
                    P.bank_set(f"pb{bse}", "eV"); P.bank_set(f"pb{bso}", "eV")
                    tok_sdt = P.cnt["eV"]
                    def t_uo(t, pp=pp, c=c, tt=tt, bU=bU, bO=bO, so=so, tok=tok_sdt):
                        wait(t, "eV", tok)
                        P.bank_wait(t, f"pb{bU}"); P.bank_wait(t, f"pb{bO}")
                        t.matmul(pb[bU][0:64, 0:64], khat[:, 128 * pp:128 * pp + 64],
                                 vhat[:, 128 * pp:128 * pp + 64], start=True, stop=True)
                        inc(t.matmul(pb[bU][64:128, 0:64], khat[:, 128 * pp + 64:128 * (pp + 1)],
                                     vhat[:, 128 * pp + 64:128 * (pp + 1)], start=True, stop=True,
                                     tile_position=(0, 64)), "eT")
                        t.matmul(pb[bO][0:64, 0:128], vtok[:, 512 * tt + 128 * pp:512 * tt + 128 * pp + 64],
                                 sdt[:, so:so + 128], start=True, stop=False)
                        t.matmul(pb[bO][0:64, 0:128], Wbf[c % 2][0:64, 64 * pp:64 * (pp + 1)],
                                 qts[0:64, 128 * pp:128 * (pp + 1)], start=False, stop=True)
                        t.matmul(pb[bO][64:128, 0:128], vtok[:, 512 * tt + 128 * pp + 64:512 * tt + 128 * (pp + 1)],
                                 sdt[:, so + 128:so + 256], start=True, stop=False, tile_position=(0, 64))
                        inc(t.matmul(pb[bO][64:128, 0:128], Wbf[c % 2][64:128, 64 * pp:64 * (pp + 1)],
                                     qts[64:128, 128 * pp:128 * (pp + 1)], start=False, stop=True,
                                     tile_position=(64, 64)), "eT")
                    on("tensor", t_uo)
                    tok_uo = P.cnt["eT"]
                    def a_hb(a, pp=pp, c=c, bO=bO, tok=tok_uo):
                        wait(a, "eT", tok)
                        inc(a.activation(hebfm[:, T * pp + 128 * c:T * pp + 128 * (c + 1)],
                                         pb[bO][0:128, 0:128], AF.Copy), "eA")
                    on("scalar", a_hb)
                    P.bank_set(f"pb{bO}", "eA")
                    def v_wu(v, pp=pp, c=c, bU=bU, tok=tok_uo):
                        wait(v, "eT", tok)
                        v.scalar_tensor_tensor(Wst[0:64, 64 * pp:64 * (pp + 1)],
                                               Wst[0:64, 64 * pp:64 * (pp + 1)],
                                               gcbc[0:64, c:c + 1],
                                               pb[bU][0:64, 0:64], ALU.mult, ALU.add)
                        ins = v.scalar_tensor_tensor(Wst[64:128, 64 * pp:64 * (pp + 1)],
                                                     Wst[64:128, 64 * pp:64 * (pp + 1)],
                                                     gcbc[64:128, c:c + 1],
                                                     pb[bU][64:128, 0:64], ALU.mult, ALU.add)
                        if pp >= 1:
                            ins = v.tensor_copy(Wbf[(c + 1) % 2][:, 64 * (pp - 1):64 * pp],
                                                Wst[:, 64 * (pp - 1):64 * pp])
                        inc(ins, "eV")
                    on("vector", v_wu)
                    P.bank_set(f"pb{bU}", "eV")

        if upto == "b2":
            dxn2 = dout("d_xn2fm", [128, 8 * 512])
            dy1 = dout("d_y1", [T, D])
            def g_db2(gp):
                wait_all(gp)
                for c in range(8):
                    inc(gp.dma_start(out=dxn2[:, 512 * c:512 * (c + 1)],
                                     in_=xn2g[:, 1024 * c + 512:1024 * (c + 1)]), "qG", 16)
                for i in range(NT):
                    inc(gp.dma_start(out=dy1[128 * i:128 * (i + 1), :],
                                     in_=y_d[128 * i:128 * (i + 1), :]), "qG", 16)
                wait_all(gp, "qG")
            on("gpsimd", g_db2)
            P.barrier()
            esB3.close()
            esB2.close()
            esBH.close()
            esB.close()
            return nc, dbg_outs

        if upto == "b4":
            dheb = dout("d_heb", [128, 4 * T], BF16)
            dqkf = dout("d_qkfm", [128, 8 * 1024], BF16)
            dktok = dout("d_ktok", [128, 8 * 512], BF16)
            dvtok = dout("d_vtok", [128, 8 * 512], BF16)
            dqts = dout("d_qts", [128, 512], BF16)
            dkts = dout("d_kts", [128, 512], BF16)
            dsdt = dout("d_sdt", [128, 256], BF16)
            dwst = dout("d_wst", [128, 256])
            dwq = dout("d_wq", [128, 8 * 1536], BF16)
            dg2c = dout("d_g2c", [128, 8])
            def g_db4(gp):
                wait_all(gp)
                inc(gp.dma_start(out=dheb, in_=hebfm[:]), "qG", 16)
                inc(gp.dma_start(out=dqkf, in_=qkfm[:]), "qG", 16)
                inc(gp.dma_start(out=dktok, in_=ktok[:]), "qG", 16)
                inc(gp.dma_start(out=dvtok, in_=vtok[:]), "qG", 16)
                inc(gp.dma_start(out=dqts, in_=qts[:]), "qG", 16)
                inc(gp.dma_start(out=dkts, in_=kts[:]), "qG", 16)
                inc(gp.dma_start(out=dsdt, in_=sdt[:, 0:256]), "qG", 16)
                inc(gp.dma_start(out=dwst, in_=Wst[:]), "qG", 16)
                inc(gp.dma_start(out=dwq, in_=wq_sb[:]), "qG", 16)
                inc(gp.dma_start(out=dg2c, in_=g2c[:]), "qG", 16)
                wait_all(gp, "qG")
            on("gpsimd", g_db4)
            P.barrier()
            esB3.close()
            esB2.close()
            esBH.close()
            esB.close()
            return nc, dbg_outs

        P.scope("b5a")
        P.barrier()
        esB3.close()
        esB5a = contextlib.ExitStack()
        wout_sb = sb("wout_sb", [128, 4 * D], BF16, st=esB5a)
        boutrow = sb("boutrow", [1, D], st=esB5a)
        boutb = sb("boutb", [1, D], BF16, st=esB5a)
        ones5b = sb("ones5b", [1, 128], BF16, st=esB5a)

        def g_wo(gp):
            for pp in range(4):
                inc(gp.dma_start(out=wout_sb[:, D * pp:D * (pp + 1)],
                                 in_=wout_d[128 * pp:128 * (pp + 1), :]), "qG", 16)
        on("gpsimd", g_wo)
        wo_tok = P.cnt["qG"]
        def s_bo(s):
            inc(s.dma_start(out=boutrow[:], in_=bout_d), "qS", 16)
        on("sync", s_bo)
        def v_cb(v, tok=P.cnt["qS"]):
            wait(v, "qS", tok)
            v.memset(ones5b[:], 1.0)
            inc(v.tensor_copy(boutb[:], boutrow[:]), "eV")
        on("vector", v_cb)
        cb_tok = P.cnt["eV"]
        if upto == "wcv":
            dw1 = dout("d_w1b", [128, FF], BF16)
            dw2 = dout("d_w2b", [128, D], BF16)
            def g_dw(gp):
                wait_all(gp)
                inc(gp.dma_start(out=dw1, in_=wff1b_d[0:128, :]), "qG", 16)
                inc(gp.dma_start(out=dw2, in_=wff2b_d[256:384, :]), "qG", 16)
                wait_all(gp, "qG")
            on("gpsimd", g_dw)
            P.barrier()
            esB5a.close()
            esB2.close()
            esBH.close()
            esB.close()
            return nc, dbg_outs
        b5_tok = {}
        for i in range(NT):
            sl = i % 3
            yv = yring[:, D * sl:D * (sl + 1)]
            bk0, bk1 = (6, 7) if i % 2 == 0 else (4, 5)
            def s_y5(s, i=i, yv=yv):
                if i >= 3:
                    wait(s, "eA", b5_tok[i - 3][0])
                    wait(s, "qG", b5_tok[i - 3][1])
                inc(s.dma_start(out=yv, in_=y_d[128 * i:128 * (i + 1), :]), "qS", 16)
            on("sync", s_y5)
            tok_y = P.cnt["qS"]
            def t_wo(t, i=i, bk0=bk0, bk1=bk1, tok=tok_y):
                if i == 0:
                    wait(t, "qG", wo_tok)
                    wait(t, "eV", cb_tok)
                P.bank_wait(t, "pb%d" % bk0); P.bank_wait(t, "pb%d" % bk1)
                for h2 in range(2):
                    bkk = bk0 if h2 == 0 else bk1
                    for pp in range(4):
                        t.matmul(pb[bkk][0:128, 0:512],
                                 hebfm[:, T * pp + 128 * i:T * pp + 128 * (i + 1)],
                                 wout_sb[:, D * pp + 512 * h2:D * pp + 512 * (h2 + 1)],
                                 start=(pp == 0), stop=False)
                    ins = t.matmul(pb[bkk][0:128, 0:512], ones5b[:],
                                   boutb[0:1, 512 * h2:512 * (h2 + 1)], start=False, stop=True)
                inc(ins, "eT")
            on("tensor", t_wo)
            def v_y2(v, i=i, yv=yv, bk0=bk0, bk1=bk1, tok=P.cnt["eT"], toky=tok_y):
                wait(v, "eT", tok)
                wait(v, "qS", toky)
                v.tensor_tensor(yv[:, 0:512], pb[bk0][0:128, 0:512], yv[:, 0:512], ALU.add)
                inc(v.tensor_tensor(yv[:, 512:1024], pb[bk1][0:128, 0:512], yv[:, 512:1024], ALU.add), "eV")
            on("vector", v_y2)
            P.bank_set("pb%d" % bk0, "eV"); P.bank_set("pb%d" % bk1, "eV")
            tok_y2 = P.cnt["eV"]
            def g_y2(gp, i=i, yv=yv, tok=tok_y2):
                wait(gp, "eV", tok)
                inc(gp.dma_start(out=y_d[128 * i:128 * (i + 1), :], in_=yv), "qG", 16)
            on("gpsimd", g_y2)
            def a_s3(a, i=i, yv=yv, tok=tok_y2):
                wait(a, "eV", tok)
                a.activation(junkA[:], yv, AF.Copy, scale=INV_D, accum_out=m3a[:, i:i + 1])
                inc(a.activation(junkA2[:], yv, AF.Square, scale=INV_SQ_D,
                                 accum_out=q3a[:, i:i + 1]), "eA")
            on("scalar", a_s3)
            b5_tok[i] = (P.cnt["eA"], P.cnt["qG"])

        def v_f3(v, tok=P.cnt["eA"]):
            wait(v, "eA", tok)
            ch(v, "eV", v.tensor_tensor(st1[:], m3a[:], m3a[:], ALU.mult))
            inc(v.tensor_tensor(st2[:], q3a[:], st1[:], ALU.subtract), "eV")
        on("vector", v_f3)
        def a_f3(a, tok=P.cnt["eV"]):
            wait(a, "eV", tok)
            inc(a.activation(st1[:], st2[:], AF.Sqrt, bias=epscol[:, 0:1]), "eA")
        on("scalar", a_f3)
        def v_f3b(v, tok=P.cnt["eA"]):
            wait(v, "eA", tok)
            inc(v.reciprocal(r3a[:], st1[:]), "eV")
        on("vector", v_f3b)

        if upto == "b5":
            dy2 = dout("d_y2", [T, D])
            def g_db5(gp):
                wait_all(gp)
                for i in range(NT):
                    inc(gp.dma_start(out=dy2[128 * i:128 * (i + 1), :],
                                     in_=y_d[128 * i:128 * (i + 1), :]), "qG", 16)
                wait_all(gp, "qG")
            on("gpsimd", g_db5)
            P.barrier()
            esB5a.close()
            esB2.close()
            esBH.close()
            esB.close()
            return nc, dbg_outs

        P.barrier()
        esB5a.close()
        esB2.close()
        esBH.close()

        P.scope("moe")
        esT = contextlib.ExitStack()
        gt = sb("gt", [128, 2 * D], st=esT)
        xstg = sb("xstg", [128, 8 * 128], BF16, st=esT)
        yring2 = sb("yring2", [128, 2 * D], st=esT)
        jxb = sb("jxb", [128, D], BF16, st=esT)
        esT2 = contextlib.ExitStack()
        G3bc = sb("G3bc", [128, D], st=esT2)
        B3bc = sb("B3bc", [128, D], st=esT2)
        lrow = sb("lrow", [1, D], st=esT2)

        P.scope("b5b")
        for (gsrc, dstb) in ((ln3g_d, G3bc), (ln3b_d, B3bc)):
            def s_l3(s, gsrc=gsrc):
                wait(s, "eT", P.cnt["eT"])
                inc(s.dma_start(out=vsm[:], in_=gsrc), "qS", 16)
            on("sync", s_l3)
            def g_l3(gp, tok=P.cnt["qS"]):
                wait(gp, "qS", tok)
                wait(gp, "qS", P.cnt["qS"])
                inc(gp.dma_start(out=row_d[R_T1, 0:D].rearrange("(a b) -> a b", a=8), in_=vsm[:]), "qG", 16)
            on("gpsimd", g_l3)
            def s_l3b(s, tok=P.cnt["qG"]):
                wait(s, "qG", tok)
                inc(s.dma_start(out=lrow[:], in_=row_d[R_T1:R_T1 + 1, 0:D]), "qS", 16)
            on("sync", s_l3b)
            def t_l3(t, tok=P.cnt["qS"]):
                wait(t, "qS", tok)
                P.bank_wait(t, "pb0"); P.bank_wait(t, "pb1")
                t.matmul(pb[0][0:128, 0:512], ones1f[:], lrow[0:1, 0:512], start=True, stop=True)
                inc(t.matmul(pb[1][0:128, 0:512], ones1f[:], lrow[0:1, 512:1024], start=True, stop=True), "eT")
            on("tensor", t_l3)
            def v_l3x(v, dstb=dstb, tok=P.cnt["eT"]):
                wait(v, "eT", tok)
                v.tensor_copy(dstb[:, 0:512], pb[0][0:128, 0:512])
                inc(v.tensor_copy(dstb[:, 512:1024], pb[1][0:128, 0:512]), "eV")
            on("vector", v_l3x)
            P.bank_set("pb0", "eV"); P.bank_set("pb1", "eV")

        b5b_tok = {}
        xstg_war = {}
        xn3st_tok = {}
        def emit_b5b_tile(i):
            sl = i % 2
            yv = yring2[:, D * sl:D * (sl + 1)]
            def s_y6(s, i=i, yv=yv):
                if i >= 2:
                    wait(s, "qA", b5b_tok[i - 2])
                inc(s.dma_start(out=yv, in_=y_d[128 * i:128 * (i + 1), :]), "qS", 16)
            on("sync", s_y6)
            tok_y = P.cnt["qS"]
            def v_x3(v, i=i, yv=yv, tok=tok_y, jw=(b5b_tok.get(i - 1))):
                if i == 0:
                    wait(v, "eV", P.cnt["eV"])
                wait(v, "qS", tok)
                if jw is not None:
                    wait(v, "qA", jw)
                ch(v, "eV", v.tensor_scalar(junkV[:], yv, m3a[:, i:i + 1], r3a[:, i:i + 1],
                                            ALU.subtract, ALU.mult))
                ch(v, "eV", v.tensor_tensor(junkA[:], junkV[:], G3bc[:], ALU.mult))
                inc(v.tensor_tensor(jxb[:], junkA[:], B3bc[:], ALU.add), "eV")
            on("vector", v_x3)
            tokXN3 = P.cnt["eV"]
            for c in range(8):
                def a_t3(a, c=c, i=i, tok=tokXN3, war=xstg_war.get(c)):
                    if c == 0:
                        wait(a, "eV", tok)
                    if war is not None:
                        wait(a, "qG", war)
                    inc(a.dma_start_transpose(out=xstg[:, 128 * c:128 * (c + 1)],
                                              in_=jxb[:, 128 * c:128 * (c + 1)]), "qA", 16)
                on("scalar", a_t3)
                def g_x3s(gp, c=c, i=i, tok=P.cnt["qA"]):
                    wait(gp, "qA", tok)
                    inc(gp.dma_start(out=xn3_d[128 * c:128 * (c + 1), 128 * i:128 * (i + 1)],
                                     in_=xstg[:, 128 * c:128 * (c + 1)]), "qG", 16)
                on("gpsimd", g_x3s)
                xstg_war[c] = P.cnt["qG"]
            b5b_tok[i] = P.cnt["qA"]
            xn3st_tok[i] = P.cnt["qG"]

        P.scope("moe")
        esC = contextlib.ExitStack()
        xlnT = sb("xlnT", [128, 2 * 8 * CAP], BF16, st=esC)
        hbuf = sb("hbuf", [128, 32 * CAP], BF16, st=esC)
        w1sl = sb("w1sl", [128, 2 * 4096], BF16, st=esC)
        w2res = sb("w2res", [128, 32 * 512], BF16, st=esC)
        y2r = sb("y2r", [128, 3 * 512], st=esC)
        glnT = sb("glnT", [128, 16], st=esC)
        blnT = sb("blnT", [128, 16], st=esC)
        b1T = sb("b1T", [128, 64], st=esC)
        b2rowb = sb("b2rowb", [1, 2 * D], BF16, st=esC)
        b2row = sb("b2row", [1, D], st=esC)
        onesCb = sb("onesCb", [1, 128], BF16, st=esC)
        def v_oc(v):
            inc(v.memset(onesCb[:], 1.0), "eV")
        on("vector", v_oc)

        W2PREV = [0]
        vsm_free = [(P.cnt["eT"], P.cnt["eV"])]
        vsm_qg = [P.cnt["qG"]]
        mm1_par = {0: 0, 1: 0}
        mm1_prev = [None]
        rd_prev = [None]

        def emit_prep(e, par):
            glnTp = glnT[:, 8 * par:8 * (par + 1)]
            blnTp = blnT[:, 8 * par:8 * (par + 1)]
            b1Tp = b1T[:, 32 * par:32 * (par + 1)]
            b2rowbp = b2rowb[:, D * par:D * (par + 1)]
            def s_pe(s, e=e, tok=vsm_free[0], tokg=vsm_qg[0]):
                wait(s, "eT", tok[0]); wait(s, "eV", tok[1])
                if tokg is not None:
                    wait(s, "qG", tokg)
                inc(s.dma_start(out=vsm[:], in_=mlng_d[e]), "qS", 16)
                inc(s.dma_start(out=vsm32[:], in_=mb1_d[e]), "qS", 16)
                inc(s.dma_start(out=b2row[:], in_=mb2_d[e]), "qS", 16)
            on("sync", s_pe)
            def t_pe(t, tok=P.cnt["qS"]):
                wait(t, "qS", tok)
                P.bank_wait(t, "pb0")
                t.transpose(pb[0][0:128, 0:8], vsm[:], idn[0:8, 0:8])
                inc(t.transpose(pb[0][0:128, 16:48], vsm32[:], idn[0:32, 0:32]), "eT")
            on("tensor", t_pe)
            def v_pe(v, tok=P.cnt["eT"]):
                wait(v, "eT", tok)
                v.tensor_copy(glnTp[:], pb[0][0:128, 0:8])
                v.tensor_copy(b1Tp[:], pb[0][0:128, 16:48])
                inc(v.tensor_copy(b2rowbp[:], b2row[:]), "eV")
            on("vector", v_pe)
            P.bank_set("pb0", "eV")
            b2_rd = P.cnt["eV"]
            def s_pe2(s, e=e, tok=P.cnt["eT"]):
                wait(s, "eT", tok)
                inc(s.dma_start(out=vsm[:], in_=mlnb_d[e]), "qS", 16)
            on("sync", s_pe2)
            def t_pe2(t, tok=P.cnt["qS"]):
                wait(t, "qS", tok)
                P.bank_wait(t, "pb2")
                inc(t.transpose(pb[2][0:128, 0:8], vsm[:], idn[0:8, 0:8]), "eT")
            on("tensor", t_pe2)
            vsm_free[0] = (P.cnt["eT"], b2_rd)
            def v_pe2(v, tok=P.cnt["eT"]):
                wait(v, "eT", tok)
                inc(v.tensor_copy(blnTp[:], pb[2][0:128, 0:8]), "eV")
            on("vector", v_pe2)
            P.bank_set("pb2", "eV")
            xbase = 8 * CAP * par
            war = mm1_par[par]
            for dc in range(8):
                def s_xT(s, e=e, dc=dc, war=war):
                    if dc == 0:
                        wait(s, "eT", war)
                    inc(s.dma_start_transpose(
                        out=xlnT[:, xbase + CAP * dc:xbase + CAP * (dc + 1)],
                        in_=xs_d[CAP * e:CAP * (e + 1), 128 * dc:128 * (dc + 1)]), "qS", 16)
                on("sync", s_xT)
            tok_xT = P.cnt["qS"]
            def v_aff(v, tok=tok_xT):
                wait(v, "qS", tok)
                for dc in range(8):
                    ins = v.tensor_scalar(xlnT[:, xbase + CAP * dc:xbase + CAP * (dc + 1)],
                                          xlnT[:, xbase + CAP * dc:xbase + CAP * (dc + 1)],
                                          glnTp[:, dc:dc + 1], blnTp[:, dc:dc + 1],
                                          ALU.mult, ALU.add)
                inc(ins, "eV")
            on("vector", v_aff)
            return P.cnt["eV"]

        def emit_mm1(e, par, tok_xlnT):
            xbase = 8 * CAP * par
            b1Tp = b1T[:, 32 * par:32 * (par + 1)]
            prev = mm1_prev[0]
            w1_tok = {}
            for fsl in range(8):
                slw = fsl % 2
                def g_w1(gp, e=e, fsl=fsl, slw=slw, prev=prev):
                    if fsl >= 2:
                        wait(gp, "eT", w1_tok[fsl - 2])
                    elif prev is not None:
                        wait(gp, "eT", prev[fsl + 6])
                    else:
                        wait(gp, "eT", P.cnt["eT"])
                    inc(gp.dma_start(out=w1sl.rearrange("p (s dc f) -> p s dc f", s=2, dc=8)[:, slw],
                                     in_=mw1_d[e].rearrange("(dc p) f -> p dc f", p=128)[:, :, 512 * fsl:512 * (fsl + 1)],
                                     max_dma_last_dim=8192), "qG", 16)
                on("gpsimd", g_w1)
                tok_w1 = P.cnt["qG"]
                for fs4 in range(4):
                    fs = 4 * fsl + fs4
                    def t_h(t, fs=fs, fs4=fs4, slw=slw, tok=tok_w1, tokx=tok_xlnT):
                        if fs == 0:
                            wait(t, "eV", tokx)
                        wait(t, "qG", tok)
                        P.bank_wait(t, "pb6"); P.bank_wait(t, "pb7")
                        for dc in range(8):
                            t.matmul(pb[6][0:128, 0:CAP // 2],
                                     w1sl[:, 4096 * slw + 512 * dc + 128 * fs4:4096 * slw + 512 * dc + 128 * (fs4 + 1)],
                                     xlnT[:, xbase + CAP * dc + 0:xbase + CAP * dc + CAP // 2],
                                     start=(dc == 0), stop=(dc == 7))
                            ins = t.matmul(pb[7][0:128, 0:CAP // 2],
                                           w1sl[:, 4096 * slw + 512 * dc + 128 * fs4:4096 * slw + 512 * dc + 128 * (fs4 + 1)],
                                           xlnT[:, xbase + CAP * dc + CAP // 2:xbase + CAP * dc + CAP],
                                           start=(dc == 0), stop=(dc == 7))
                        inc(ins, "eT")
                    on("tensor", t_h)
                    w1_tok[fsl] = P.cnt["eT"]
                    if fs % 2 == 0:
                        def v_h(v, fs=fs, tok=P.cnt["eT"]):
                            wait(v, "eT", tok)
                            v.tensor_scalar(hbuf[:, CAP * fs:CAP * fs + CAP // 2],
                                            pb[6][0:128, 0:CAP // 2],
                                            b1Tp[:, fs:fs + 1], 0.0, ALU.add, ALU.max)
                            inc(v.tensor_scalar(hbuf[:, CAP * fs + CAP // 2:CAP * (fs + 1)],
                                                pb[7][0:128, 0:CAP // 2],
                                                b1Tp[:, fs:fs + 1], 0.0, ALU.add, ALU.max), "eV")
                        on("vector", v_h)
                        P.bank_set("pb6", "eV"); P.bank_set("pb7", "eV")
                    else:
                        def a_h(a, fs=fs, tok=P.cnt["eT"]):
                            wait(a, "eT", tok)
                            a.activation(hbuf[:, CAP * fs:CAP * fs + CAP // 2], pb[6][0:128, 0:CAP // 2],
                                         AF.Relu, bias=b1Tp[:, fs:fs + 1])
                            inc(a.activation(hbuf[:, CAP * fs + CAP // 2:CAP * (fs + 1)], pb[7][0:128, 0:CAP // 2],
                                             AF.Relu, bias=b1Tp[:, fs:fs + 1]), "eA")
                        on("scalar", a_h)
                        P.bank_set("pb6", "eA"); P.bank_set("pb7", "eA")
            mm1_prev[0] = w1_tok
            mm1_par[par] = P.cnt["eT"]
            return (P.cnt["eA"], P.cnt["eV"])

        ye_store = [0, 0, 0]
        def emit_mm2(e, par, tok_h):
            b2rowbp = b2rowb[:, D * par:D * (par + 1)]
            rd_new = {}
            for h2 in range(2):
                tokq = {}
                for q_ in range(4):
                    def g_w2(gp, e=e, h2=h2, q_=q_):
                        if h2 == 0:
                            if rd_prev[0] is not None:
                                wait(gp, "eT", rd_prev[0][q_])
                            else:
                                wait(gp, "eT", W2PREV[0])
                        else:
                            wait(gp, "eT", rd_new[(0, q_)])
                        inc(gp.dma_start(
                            out=w2res.rearrange("p (c f) -> p c f", c=32)[:, 8 * q_:8 * (q_ + 1)],
                            in_=mw2_d[e].rearrange("(c p) f -> p c f", p=128)[:, 8 * q_:8 * (q_ + 1),
                                                                             512 * h2:512 * (h2 + 1)],
                            max_dma_last_dim=8192), "qG", 16)
                    on("gpsimd", g_w2)
                    tokq[q_] = P.cnt["qG"]
                for q_ in range(4):
                    def t_y(t, q_=q_, h2=h2, tok=tokq[q_], tokh=tok_h):
                        if q_ == 0 and h2 == 0:
                            wait(t, "eA", tokh[0]); wait(t, "eV", tokh[1])
                        wait(t, "qG", tok)
                        for st_ in range(CAP // 128):
                            if q_ == 0:
                                P.bank_wait(t, f"pb{st_}")
                            for fc in range(8 * q_, 8 * (q_ + 1)):
                                ins = t.matmul(pb[st_][0:128, 0:512],
                                               hbuf[:, CAP * fc + 128 * st_:CAP * fc + 128 * (st_ + 1)],
                                               w2res[:, 512 * fc:512 * (fc + 1)],
                                               start=(fc == 0), stop=False)
                        if q_ == 3:
                            for st_ in range(CAP // 128):
                                ins = t.matmul(pb[st_][0:128, 0:512], onesCb[:],
                                               b2rowbp[0:1, 512 * h2:512 * (h2 + 1)],
                                               start=False, stop=True)
                        inc(ins, "eT")
                    on("tensor", t_y)
                    rd_new[(h2, q_)] = P.cnt["eT"]
                tok_mm2 = P.cnt["eT"]
                for st_ in range(CAP // 128):
                    sl = st_ % 3
                    yslot = y2r[:, 512 * sl:512 * (sl + 1)]
                    if st_ % 2 == 0:
                        def v_ye(v, st_=st_, yslot=yslot, tok=tok_mm2, tokst=ye_store[sl]):
                            wait(v, "eT", tok)
                            wait(v, "qS", tokst)
                            inc(v.tensor_copy(yslot, pb[st_][0:128, 0:512]), "eV")
                        on("vector", v_ye)
                        P.bank_set(f"pb{st_}", "eV")
                        ev_tok = ("eV", P.cnt["eV"])
                    else:
                        def a_ye(a, st_=st_, yslot=yslot, tok=tok_mm2, tokst=ye_store[sl]):
                            wait(a, "eT", tok)
                            wait(a, "qS", tokst)
                            inc(a.activation(yslot, pb[st_][0:128, 0:512], AF.Copy), "eA")
                        on("scalar", a_ye)
                        P.bank_set(f"pb{st_}", "eA")
                        ev_tok = ("eA", P.cnt["eA"])
                    def s_ye(s, e=e, st_=st_, yslot=yslot, h2=h2, tok=ev_tok):
                        wait(s, tok[0], tok[1])
                        inc(s.dma_start(out=y2s_d[CAP * e + 128 * st_:CAP * e + 128 * (st_ + 1),
                                                  512 * h2:512 * (h2 + 1)],
                                        in_=yslot), "qS", 16)
                    on("sync", s_ye)
                    ye_store[sl] = P.cnt["qS"]
            rd_prev[0] = {q_: rd_new[(1, q_)] for q_ in range(4)}
            W2PREV[0] = P.cnt["eT"]

        tok_x = emit_prep(0, 0)
        for e in range(NE):
            par = e % 2
            tok_h = emit_mm1(e, par, tok_x)
            if e + 1 < NE:
                tok_x = emit_prep(e + 1, 1 - par)
            emit_mm2(e, par, tok_h)
        for i5 in range(NT):
            emit_b5b_tile(i5)

        if upto == "c":
            dy2s = dout("d_y2s", [NS, D])
            def g_dbc(gp):
                wait_all(gp)
                for j in range(NS // 128):
                    inc(gp.dma_start(out=dy2s[128 * j:128 * (j + 1), :],
                                     in_=y2s_d[128 * j:128 * (j + 1), :]), "qG", 16)
                wait_all(gp, "qG")
            on("gpsimd", g_dbc)
            P.barrier()
            esC.close()
            esB.close()
            return nc, dbg_outs

        P.barrier()
        esC.close()

        d_tok = {}
        def emit_d_tile(i):
            sl = i % 2
            gv = gt[:, D * sl:D * (sl + 1)]
            def v_gz(v, i=i, gv=gv):
                if i >= 2:
                    wait(v, "qG", d_tok[i - 2])
                inc(v.memset(gv, 0.0), "eV")
            on("vector", v_gz)
            def g_ga(gp, i=i, gv=gv, tok=P.cnt["eV"]):
                wait(gp, "eV", tok)
                inc(gp.indirect_dma_start(
                    out=gv, out_offset=None,
                    in_=y2s_d, in_offset=bass.IndirectOffsetOnAxis(ap=slotu[:, i:i + 1], axis=0),
                    bounds_check=BREG[0], oob_is_err=False), "qG", 16)
            on("gpsimd", g_ga)
            def v_al(v, i=i, gv=gv, tok=P.cnt["qG"]):
                wait(v, "qG", tok)
                inc(v.tensor_scalar(gv, gv, alpha_a[:, i:i + 1], None, ALU.mult), "eV")
            on("vector", v_al)
            def g_ac2(gp, i=i, gv=gv, tok=P.cnt["eV"]):
                wait(gp, "eV", tok)
                inc(gp.dma_start(out=out_d[128 * i:128 * (i + 1), :], in_=gv,
                                 accum_op=ALU.add), "qG", 16)
            on("gpsimd", g_ac2)
            d_tok[i] = P.cnt["qG"]

        P.scope("ff")
        P.barrier()
        esT2.close()
        esB6 = contextlib.ExitStack()
        gelu2 = sb("gelu2", [128, 2 * 32 * 512], BF16, st=esB6)
        wf1sl = sb("wf1sl", [128, 2 * 4096], BF16, st=esB6)
        wf2h = sb("wf2h", [128, 2 * 4 * 1024], BF16, st=esB6)
        xn3buf = sb("xn3buf", [128, 8 * 1024], BF16, st=esB6)
        bff1T = sb("bff1T", [128, 32], st=esB6)
        b2lrow = sb("b2lrow", [1, D], st=esB6)
        b2lrowb = sb("b2lrowb", [1, D], BF16, st=esB6)
        onesFb = sb("onesFb", [1, 128], BF16, st=esB6)

        def s_bf1(s):
            wait(s, "eT", P.cnt["eT"])
            inc(s.dma_start(out=vsm32[:], in_=bff1_d), "qS", 16)
            inc(s.dma_start(out=b2lrow[:], in_=bff2_d), "qS", 16)
        on("sync", s_bf1)
        def t_bf1(t, tok=P.cnt["qS"]):
            wait(t, "qS", tok)
            P.bank_wait(t, "pb0")
            inc(t.transpose(pb[0][0:128, 0:32], vsm32[:], idn[0:32, 0:32]), "eT")
        on("tensor", t_bf1)
        def v_bf1(v, tok=P.cnt["eT"]):
            wait(v, "eT", tok)
            v.tensor_copy(bff1T[:], pb[0][0:128, 0:32])
            v.memset(onesFb[:], 1.0)
            inc(v.tensor_copy(b2lrowb[:], b2lrow[:]), "eV")
        on("vector", v_bf1)
        P.bank_set("pb0", "eV")
        tok_bf1 = P.cnt["eV"]

        F2PREV = [0]
        ACCSL = [{}]
        YRD = [{}]
        b6_acc = {}
        f1_prev = [None]
        x3_rd = [None]
        PF = {}
        def emit_prefetch(pr, gate_x3, gate_w):
            def s_x3(s, pr=pr, prev=gate_x3, st_tok=xn3st_tok[8 * pr + 7]):
                wait(s, "qG", st_tok)
                if prev is not None:
                    wait(s, "eT", prev)
                inc(s.dma_start(out=xn3buf.rearrange("p (dc t) -> p dc t", dc=8),
                                in_=xn3_d.rearrange("(dc p) t -> p dc t", p=128)[:, :, 1024 * pr:1024 * (pr + 1)],
                                max_dma_last_dim=8192), "qS", 16)
            on("sync", s_x3)
            tok_x3 = P.cnt["qS"]
            tw = {}
            for fsl in (0, 1):
                def s_wf1(s, fsl=fsl, gate_w=gate_w):
                    if gate_w is not None:
                        wait(s, "eT", gate_w[fsl + 6])
                    else:
                        wait(s, "eA", P.cnt["eA"])
                    inc(s.dma_start(out=wf1sl.rearrange("p (s dc f) -> p s dc f", s=2, dc=8)[:, fsl],
                                    in_=wff1b_d.rearrange("(dc p) f -> p dc f", p=128)[:, :, 512 * fsl:512 * (fsl + 1)],
                                    max_dma_last_dim=8192), "qS", 16)
                on("sync", s_wf1)
                tw[fsl] = P.cnt["qS"]
            PF[pr] = (tok_x3, tw)

        emit_prefetch(0, None, None)
        for pr in range(4):
            tok_x3, tw01 = PF[pr]
            f1_tok = {}
            for fsl in range(8):
                slw = fsl % 2
                if fsl < 2:
                    tok_w = tw01[fsl]
                else:
                    def s_wf1(s, fsl=fsl, slw=slw):
                        wait(s, "eT", f1_tok[fsl - 2])
                        inc(s.dma_start(out=wf1sl.rearrange("p (s dc f) -> p s dc f", s=2, dc=8)[:, slw],
                                        in_=wff1b_d.rearrange("(dc p) f -> p dc f", p=128)[:, :, 512 * fsl:512 * (fsl + 1)],
                                        max_dma_last_dim=8192), "qS", 16)
                    on("sync", s_wf1)
                    tok_w = P.cnt["qS"]
                for fs4 in range(4):
                    fs = 4 * fsl + fs4
                    bk0, bk1 = (6, 7) if fs % 2 == 0 else (4, 5)
                    def t_f1(t, fs=fs, fs4=fs4, slw=slw, bk0=bk0, bk1=bk1, tok=tok_w, tokx=tok_x3):
                        if fs == 0:
                            wait(t, "qS", tokx)
                        wait(t, "qS", tok)
                        P.bank_wait(t, f"pb{bk0}"); P.bank_wait(t, f"pb{bk1}")
                        for dc in range(8):
                            wsl = wf1sl[:, 4096 * slw + 512 * dc + 128 * fs4:4096 * slw + 512 * dc + 128 * (fs4 + 1)]
                            t.matmul(pb[bk0][0:128, 0:512], wsl,
                                     xn3buf[:, 1024 * dc:1024 * dc + 512],
                                     start=(dc == 0), stop=(dc == 7))
                            ins = t.matmul(pb[bk1][0:128, 0:512], wsl,
                                           xn3buf[:, 1024 * dc + 512:1024 * (dc + 1)],
                                           start=(dc == 0), stop=(dc == 7))
                        inc(ins, "eT")
                    on("tensor", t_f1)
                    f1_tok[fsl] = P.cnt["eT"]
                    def a_g(a, fs=fs, bk0=bk0, bk1=bk1, tok=P.cnt["eT"]):
                        wait(a, "eT", tok)
                        a.activation(gelu2[:, 512 * fs:512 * (fs + 1)], pb[bk0][0:128, 0:512],
                                     AF.Gelu_apprx_tanh, bias=bff1T[:, fs:fs + 1])
                        inc(a.activation(gelu2[:, 16384 + 512 * fs:16384 + 512 * (fs + 1)], pb[bk1][0:128, 0:512],
                                         AF.Gelu_apprx_tanh, bias=bff1T[:, fs:fs + 1]), "eA")
                    on("scalar", a_g)
                    P.bank_set(f"pb{bk0}", "eA"); P.bank_set(f"pb{bk1}", "eA")
            f1_prev[0] = f1_tok
            x3_rd[0] = P.cnt["eT"]
            tok_g = P.cnt["eA"]
            if pr + 1 < 4:
                emit_prefetch(pr + 1, x3_rd[0], f1_tok)
            for half in range(2):
                tc = 2 * pr + half
                gbase = 16384 * half
                tokq = {}
                f2_rd = {}
                def emit_wf2(q_, gate):
                    slq = q_ % 2
                    def g_wf2(gp, q_=q_, slq=slq, gate=gate):
                        wait(gp, "eT", gate)
                        inc(gp.dma_start(
                            out=wf2h.rearrange("p (s c f) -> p s c f", s=2, c=4)[:, slq],
                            in_=wff2b_d.rearrange("(c p) f -> p c f", p=128)[:, 4 * q_:4 * (q_ + 1), :],
                            max_dma_last_dim=8192), "qG", 16)
                    on("gpsimd", g_wf2)
                    tokq[q_] = P.cnt["qG"]
                for q_ in (0, 1):
                    emit_wf2(q_, F2PREV[0][q_ + 6] if isinstance(F2PREV[0], dict) else F2PREV[0])
                for q_ in range(8):
                    slq = q_ % 2
                    def t_f2(t, q_=q_, slq=slq, gbase=gbase, tok=tokq[q_], tokg=tok_g, tokb=tok_bf1):
                        if q_ == 0:
                            wait(t, "eA", tokg)
                            wait(t, "eV", tokb)
                        wait(t, "qG", tok)
                        for tt in range(4):
                            if q_ == 0:
                                P.bank_wait(t, f"pb{2 * tt}"); P.bank_wait(t, f"pb{2 * tt + 1}")
                            for fc in range(4 * q_, 4 * (q_ + 1)):
                                wof = 4096 * slq + 1024 * (fc - 4 * q_)
                                t.matmul(pb[2 * tt][0:128, 0:512],
                                         gelu2[:, gbase + 512 * fc + 128 * tt:gbase + 512 * fc + 128 * (tt + 1)],
                                         wf2h[:, wof:wof + 512],
                                         start=(fc == 0), stop=False)
                                ins = t.matmul(pb[2 * tt + 1][0:128, 0:512],
                                               gelu2[:, gbase + 512 * fc + 128 * tt:gbase + 512 * fc + 128 * (tt + 1)],
                                               wf2h[:, wof + 512:wof + 1024],
                                               start=(fc == 0), stop=False)
                        if q_ == 7:
                            for tt in range(4):
                                t.matmul(pb[2 * tt][0:128, 0:512], onesFb[:],
                                         b2lrowb[0:1, 0:512], start=False, stop=True)
                                ins = t.matmul(pb[2 * tt + 1][0:128, 0:512], onesFb[:],
                                               b2lrowb[0:1, 512:1024], start=False, stop=True)
                        inc(ins, "eT")
                    on("tensor", t_f2)
                    f2_rd[q_] = P.cnt["eT"]
                    if q_ + 2 < 8:
                        emit_wf2(q_ + 2, f2_rd[q_])
                tok_f2 = P.cnt["eT"]
                for tt in range(4):
                    i = 4 * tc + tt
                    yv = yring2[:, D * (tt % 2):D * (tt % 2 + 1)]
                    ytok = YRD[0].get(tt % 2)
                    def s_y7b(s, i=i, yv=yv, ytok=ytok):
                        if ytok is not None:
                            wait(s, "eV", ytok)
                        inc(s.dma_start(out=yv, in_=y_d[128 * i:128 * (i + 1), :]), "qS", 16)
                    on("sync", s_y7b)
                    tok_y = P.cnt["qS"]
                    yf = junkA[:] if tt % 2 == 0 else junkV[:]
                    awar = ACCSL[0].get(tt % 2)
                    def v_yf(v, tt=tt, yv=yv, yf=yf, tok=tok_f2, toky=tok_y, war=awar):
                        wait(v, "eT", tok)
                        wait(v, "qS", toky)
                        if war is not None:
                            wait(v, "qG", war)
                        v.tensor_tensor(yf[:, 0:512], pb[2 * tt][0:128, 0:512], yv[:, 0:512], ALU.add)
                        inc(v.tensor_tensor(yf[:, 512:1024], pb[2 * tt + 1][0:128, 0:512],
                                            yv[:, 512:1024], ALU.add), "eV")
                    on("vector", v_yf)
                    P.bank_set(f"pb{2 * tt}", "eV"); P.bank_set(f"pb{2 * tt + 1}", "eV")
                    YRD[0][tt % 2] = P.cnt["eV"]
                    def g_acc(gp, i=i, yf=yf, tok=P.cnt["eV"]):
                        wait(gp, "eV", tok)
                        inc(gp.dma_start(out=out_d[128 * i:128 * (i + 1), :],
                                         in_=yf, accum_op=ALU.add), "qG", 16)
                    on("gpsimd", g_acc)
                    ACCSL[0][tt % 2] = P.cnt["qG"]
                    b6_acc[i] = P.cnt["qG"]
                F2PREV[0] = {q_: f2_rd[q_] for q_ in range(8)}
                for i5 in range(4 * tc, 4 * (tc + 1)):
                    emit_d_tile(i5)

        if upto == "f6":
            dgel = dout("d_gelu", [128, 32 * 512], BF16)
            dxn3 = dout("d_xn3", [128, 8 * 512], BF16)
            def g_df6(gp):
                wait_all(gp)
                inc(gp.dma_start(out=dgel, in_=gelu2[:, 16384:32768]), "qG", 16)
                for c in range(8):
                    inc(gp.dma_start(out=dxn3[:, 512 * c:512 * (c + 1)],
                                     in_=xn3buf[:, 1024 * c + 512:1024 * (c + 1)]), "qG", 16)
                wait_all(gp, "qG")
            on("gpsimd", g_df6)
            P.barrier()
            esB6.close()
            esT.close()
            esB.close()
            return nc, dbg_outs

        P.barrier()
        esB6.close()
        esT.close()
        esB.close()

        P.scope_end()
        def g_fin(gp):
            wait_all(gp)
        on("gpsimd", g_fin)
        def s_fin(s):
            wait_all(s)
        on("sync", s_fin)
    return nc, dbg_outs


def _in_map(inputs, b):
    c = _host_consts()
    m = {
        "x": np.ascontiguousarray(inputs["L0_attn_x"][b]),
        "ln1_g": inputs["L0_attn_ln1_g"].reshape(8, 128),
        "ln1_b": inputs["L0_attn_ln1_b"].reshape(8, 128),
        "ln2_g": inputs["L0_attn_ln2_g"].reshape(8, 128),
        "ln2_b": inputs["L0_attn_ln2_b"].reshape(8, 128),
        "ln3_g": inputs["L0_attn_ln3_g"].reshape(8, 128),
        "ln3_b": inputs["L0_attn_ln3_b"].reshape(8, 128),
        "w_potT": np.ascontiguousarray(inputs["L0_attn_w_pot"].T),
        "b_pot": inputs["L0_attn_b_pot"].reshape(1, 2),
        "w_bk": inputs["L0_attn_w_bk"],
        "b_bk": inputs["L0_attn_b_bk"].reshape(1, D),
        "w_qkv": inputs["L0_attn_w_qkv"],
        "b_qkv": inputs["L0_attn_b_qkv"].reshape(1, -1),
        "w_out": inputs["L0_attn_w_out"],
        "b_out": inputs["L0_attn_b_out"].reshape(1, D),
        "w_ff1": inputs["L0_attn_w_ff1"],
        "b_ff1": inputs["L0_attn_b_ff1"].reshape(32, 128),
        "w_ff2": inputs["L0_attn_w_ff2"],
        "b_ff2": inputs["L0_attn_b_ff2"].reshape(1, D),
        "cent": inputs["L0_ffn_centroids"],
        "m_ln_g": inputs["L0_ffn_ln_g"].reshape(NE, 8, 128),
        "m_ln_b": inputs["L0_ffn_ln_b"].reshape(NE, 8, 128),
        "m_w1": inputs["L0_ffn_w1"],
        "m_b1": inputs["L0_ffn_b1"].reshape(NE, 32, 128),
        "m_w2": inputs["L0_ffn_w2"],
        "m_b2": inputs["L0_ffn_b2"].reshape(NE, 1, D),
    }
    m.update(c)
    for k, v in list(m.items()):
        if isinstance(v, np.ndarray) and v.dtype == np.float64:
            m[k] = v.astype(np.float32)
    return m


def kernel(**inputs):
    nc, _ = build(dbg=0)
    in_maps = [_in_map(inputs, b) for b in range(8)]
    res = run_bass_kernel_spmd(nc, in_maps, core_ids=list(range(8)))
    return np.stack([r["out"] for r in res.results], axis=0)


# version 26
# speedup vs baseline: 1.2519x; 1.0350x over previous
import contextlib
import numpy as np
import concourse.bass as bass
import concourse.mybir as mybir
from concourse.bass_utils import run_bass_kernel_spmd

F32 = mybir.dt.float32
BF16 = mybir.dt.bfloat16
U32 = mybir.dt.uint32
ALU = mybir.AluOpType
AF = mybir.ActivationFunctionType
AX = mybir.AxisListType

T = 4096
D = 1024
NT = 32
CAP = 640
NE = 8
NS = NE * CAP
H, HD = 8, 64
FF = 4096
EPS = 1e-5
ETA = 0.1
DTC = 0.01
GAMMA_FLOOR = 0.1
INV_D = 1.0 / D
INV_SQ_D = float(1.0 / np.sqrt(D))
LN_ETA = float(np.log(ETA))


def _host_consts():
    import ml_dtypes
    c = {}
    c["c_idn"] = np.eye(128, dtype=np.float32)
    c["c_idnb"] = np.eye(128, dtype=np.float32).astype(ml_dtypes.bfloat16)
    tril = (np.arange(128)[:, None] <= np.arange(128)[None, :]).astype(np.float32)
    c["c_trilT"] = tril
    c["c_iota8"] = np.tile(np.arange(8, dtype=np.float32), (128, 1))
    colpack = np.zeros((128, 2), np.float32)
    for b in (0, 32, 64):
        colpack[b:b + 8, 0] = np.arange(8)
        colpack[b:b + 8, 1] = np.arange(8) * CAP
    c["c_colpack"] = colpack
    return c




class Prog:
    def __init__(self, nc, es):
        self.nc = nc
        self.blk = es.enter_context(nc.Block())
        self.sem = {}
        self.cnt = {}
        for s in ("qS", "qG", "qA", "qC", "qD", "eT", "eV", "eA"):
            self.sem[s] = es.enter_context(nc.semaphore(s))
            self.cnt[s] = 0
        self.bank_free = {}

    def on(self, eng, f):
        getattr(self.blk, eng)(f)

    def inc(self, instr, s, n=1):
        instr.then_inc(self.sem[s], n)
        self.cnt[s] += n
        return self.cnt[s]

    def wait(self, eng, s, v):
        eng.wait_ge(self.sem[s], v)

    def wait_all(self, eng, *ss):
        if not ss:
            ss = tuple(self.sem)
        for s in ss:
            eng.wait_ge(self.sem[s], self.cnt[s])

    def ch(self, eng, s, instr):
        tok = self.inc(instr, s)
        eng.wait_ge(self.sem[s], tok)
        return tok

    def bank_wait(self, eng, name):
        t = self.bank_free.get(name)
        if t is not None:
            self.wait(eng, t[0], t[1])

    def bank_set(self, name, s):
        self.bank_free[name] = (s, self.cnt[s])

    def scope(self, name):
        if getattr(self, "_scope", None) is not None:
            self.nc.leave_named_scope(self._scope[0], self._scope[1], False)
        sid, _ = self.nc.enter_named_scope(name, False)
        self._scope = (name, sid)

    def scope_end(self):
        if getattr(self, "_scope", None) is not None:
            self.nc.leave_named_scope(self._scope[0], self._scope[1], False)
            self._scope = None

    def barrier(self):
        snap = dict(self.cnt)
        for ename in ("sync", "gpsimd", "tensor", "vector", "scalar"):
            def w(e, snap=snap):
                for s, v in snap.items():
                    e.wait_ge(self.sem[s], v)
            self.on(ename, w)


def build(dbg=0, upto=None, noaff=False):
    nc = bass.Bass("TRN2", target_bir_lowering=False)

    def di(name, shape, dt=F32):
        return nc.dram_tensor(name, list(shape), dt, kind="ExternalInput").ap()

    x_d = di("x", [T, D])
    ln1g_d = di("ln1_g", [8, 128]); ln1b_d = di("ln1_b", [8, 128])
    ln2g_d = di("ln2_g", [8, 128]); ln2b_d = di("ln2_b", [8, 128])
    ln3g_d = di("ln3_g", [8, 128]); ln3b_d = di("ln3_b", [8, 128])
    wpotT_d = di("w_potT", [2, D])
    bpot_d = di("b_pot", [1, 2])
    wbk_d = di("w_bk", [2, D])
    bbk_d = di("b_bk", [1, D])
    wqkv_d = di("w_qkv", [D, 3 * H * HD])
    bqkv_d = di("b_qkv", [1, 3 * H * HD])
    wout_d = di("w_out", [H * HD, D])
    bout_d = di("b_out", [1, D])
    wff1_d = di("w_ff1", [D, FF])
    bff1_d = di("b_ff1", [32, 128])
    wff2_d = di("w_ff2", [FF, D])
    bff2_d = di("b_ff2", [1, D])
    cent_d = di("cent", [NE, D])
    mlng_d = di("m_ln_g", [NE, 8, 128])
    mlnb_d = di("m_ln_b", [NE, 8, 128])
    mw1_d = di("m_w1", [NE, D, FF])
    mb1_d = di("m_b1", [NE, 32, 128])
    mw2_d = di("m_w2", [NE, FF, D])
    mb2_d = di("m_b2", [NE, 1, D])
    idn_d = di("c_idn", [128, 128])
    idnb_d = di("c_idnb", [128, 128], BF16)
    trilT_d = di("c_trilT", [128, 128])
    iota8_d = di("c_iota8", [128, 8])
    colpack_d = di("c_colpack", [128, 2])

    out_d = nc.dram_tensor("out", [T, D], F32, kind="ExternalOutput").ap()
    y_d = nc.dram_tensor("ybuf", [T, D], F32, kind="Internal").ap()
    xs_d = nc.dram_tensor("xln_sorted", [NS, D], BF16, kind="Internal").ap()
    y2s_d = nc.dram_tensor("y2_sorted", [NS, D], F32, kind="Internal").ap()
    xn3_d = nc.dram_tensor("xn3fm", [D, T], BF16, kind="Internal").ap()
    row_d = nc.dram_tensor("rowscratch", [48, T], F32, kind="Internal").ap()
    wff1b_d = nc.dram_tensor("wff1b", [D, FF], BF16, kind="Internal").ap()
    wff2b_d = nc.dram_tensor("wff2b", [FF, D], BF16, kind="Internal").ap()

    dbg_outs = {}
    def dout(name, shape, dt=F32):
        tns = nc.dram_tensor(name, list(shape), dt, kind="ExternalOutput").ap()
        dbg_outs[name] = tns
        return tns

    es = contextlib.ExitStack()
    with es:
        def sb(name, shape, dt=F32, st=None):
            return (st or es).enter_context(nc.sbuf_tensor(name, list(shape), dt))

        P = Prog(nc, es)
        on, inc, wait, wait_all, ch = P.on, P.inc, P.wait, P.wait_all, P.ch
        pb = [es.enter_context(nc.psum_tensor(f"pb{k}", [128, 512], F32)) for k in range(8)]

        m0a = sb("m0a", [128, NT]); q0a = sb("q0a", [128, NT]); r0a = sb("r0a", [128, NT])
        m1a = sb("m1a", [128, NT]); q1a = sb("q1a", [128, NT]); r1a = sb("r1a", [128, NT])
        m1r1a = sb("m1r1a", [128, NT])
        alpha_a = sb("alpha_a", [128, NT]); amax_a = sb("amax_a", [128, NT])
        eid_a = sb("eid_a", [128, NT])
        potraw = sb("potraw", [128, 2 * NT])
        V_a = sb("V_a", [128, NT]); gam_a = sb("gam_a", [128, NT])
        slotu = sb("slotu", [128, NT], U32)
        st1 = sb("st1", [128, NT]); st2 = sb("st2", [128, NT])
        junkA = sb("junkA", [128, D]); junkA2 = sb("junkA2", [128, D])
        junkV = sb("junkV", [128, D])
        idn = sb("idn", [128, 128]); idnb = sb("idnb", [128, 128], BF16)
        trilT = sb("trilT", [128, 128])
        iota8 = sb("iota8", [128, 8]); colpack = sb("colpack", [128, 2])
        ones1f = sb("ones1f", [1, 128]); onescol = sb("onescol", [128, 1])
        ones8c = sb("ones8c", [8, 1])
        epscol = sb("epscol", [128, 1])
        lnecol = sb("lnecol", [128, 1])
        centT = sb("centT", [128, 8 * 10])
        csneg = sb("csneg", [128, 2]); beffp = sb("beffp", [128, 2])
        beff_row = sb("beff_row", [1, 2])
        g1c = sb("g1c", [128, 8])
        vsm = sb("vsm", [8, 128]); vsm32 = sb("vsm32", [32, 128])
        rowA = sb("rowA", [1, T])
        e1 = sb("e1", [128, 40]); e2 = sb("e2", [128, 40]); e3 = sb("e3", [128, 40])

        esA = contextlib.ExitStack()
        cent8 = sb("cent8", [8, D], st=esA)
        wpotT = sb("wpotT", [2, D], st=esA)

        P.scope("prep")
        def s_prep(s):
            for dst, src in ((idn, idn_d), (idnb, idnb_d), (trilT, trilT_d),
                             (iota8, iota8_d), (colpack, colpack_d),
                             (cent8, cent_d), (wpotT, wpotT_d), (vsm, ln1g_d),
                             (beff_row, bpot_d)):
                inc(s.dma_start(out=dst[:], in_=src), "qS", 16)
        on("sync", s_prep)
        tok_ld0 = P.cnt["qS"]
        BREG = [None]
        def g_breg(gp):
            BREG[0] = gp.alloc_register(name="breg")
            gp.reg_mov(BREG[0], NS - 1)
        on("gpsimd", g_breg)

        def v_prep(v):
            wait(v, "qS", tok_ld0)
            v.memset(ones1f[:], 1.0)
            v.memset(onescol[:], 1.0)
            v.memset(epscol[:], EPS)
            v.memset(lnecol[:], LN_ETA)
            inc(v.memset(ones8c[:], 1.0), "eV")
        on("vector", v_prep)

        def t_g1(t, tok=P.cnt["eV"]):
            wait(t, "eV", tok)
            inc(t.transpose(pb[0][0:128, 0:8], vsm[:], idn[0:8, 0:8]), "eT")
        on("tensor", t_g1)
        def v_g1(v, tok=P.cnt["eT"]):
            wait(v, "eT", tok)
            inc(v.tensor_copy(g1c[:], pb[0][0:128, 0:8]), "eV")
        on("vector", v_g1)
        P.bank_set("pb0", "eV")

        for c in range(8):
            bk = c % 2
            def t_c(t, c=c, bk=bk):
                P.bank_wait(t, f"pb{bk}")
                t.transpose(pb[bk][0:128, 0:8], cent8[:, 128 * c:128 * (c + 1)], idn[0:8, 0:8])
                inc(t.transpose(pb[bk][0:128, 8:10], wpotT[:, 128 * c:128 * (c + 1)], idn[0:2, 0:2]), "eT")
            on("tensor", t_c)
            def v_c(v, c=c, bk=bk, tok=P.cnt["eT"]):
                wait(v, "eT", tok)
                v.tensor_copy(centT[:, 10 * c:10 * c + 8], pb[bk][0:128, 0:8])
                inc(v.tensor_scalar(centT[:, 10 * c + 8:10 * c + 10], pb[bk][0:128, 8:10],
                                    g1c[:, c:c + 1], None, ALU.mult), "eV")
            on("vector", v_c)
            P.bank_set(f"pb{bk}", "eV")

        def t_cs(t):
            wait_all(t, "eV")
            P.bank_wait(t, "pb2")
            for c in range(8):
                ins = t.matmul(pb[2][0:2, 0:1], centT[:, 10 * c + 8:10 * c + 10], onescol[:],
                               start=(c == 0), stop=(c == 7))
            inc(ins, "eT")
        on("tensor", t_cs)
        def v_cs(v, tok=P.cnt["eT"]):
            wait(v, "eT", tok)
            inc(v.tensor_scalar(e1[0:2, 0:1], pb[2][0:2, 0:1], -1.0, None, ALU.mult), "eV")
        on("vector", v_cs)
        P.bank_set("pb2", "eV")
        def t_cs2(t, tok=P.cnt["eV"]):
            wait(t, "eV", tok)
            P.bank_wait(t, "pb3")
            inc(t.transpose(pb[3][0:1, 0:2], e1[0:2, 0:1], idn[0:2, 0:2]), "eT")
        on("tensor", t_cs2)
        def v_cs2(v, tok=P.cnt["eT"]):
            wait(v, "eT", tok)
            inc(v.tensor_copy(e1[0:1, 4:6], pb[3][0:1, 0:2]), "eV")
        on("vector", v_cs2)
        P.bank_set("pb3", "eV")

        def s_lb(s):
            wait(s, "eT", P.cnt["eT"])
            inc(s.dma_start(out=vsm[:], in_=ln1b_d), "qS", 16)
        on("sync", s_lb)
        def t_lb(t, tok=P.cnt["qS"]):
            wait(t, "qS", tok)
            P.bank_wait(t, "pb0")
            inc(t.transpose(pb[0][0:128, 16:24], vsm[:], idn[0:8, 0:8]), "eT")
        on("tensor", t_lb)
        def v_lb(v, tok=P.cnt["eT"]):
            wait(v, "eT", tok)
            inc(v.tensor_copy(e2[:, 0:8], pb[0][0:128, 16:24]), "eV")
        on("vector", v_lb)
        P.bank_set("pb0", "eV")
        for c in range(8):
            bk = c % 2
            def t_wp(t, c=c, bk=bk):
                if c < 2:
                    wait_all(t, "eV")
                P.bank_wait(t, f"pb{bk}")
                inc(t.transpose(pb[bk][0:128, 12:14], wpotT[:, 128 * c:128 * (c + 1)], idn[0:2, 0:2]), "eT")
            on("tensor", t_wp)
            def v_wp(v, c=c, bk=bk, tok=P.cnt["eT"]):
                wait(v, "eT", tok)
                inc(v.tensor_copy(e3[:, 2 * (c % 4):2 * (c % 4) + 2], pb[bk][0:128, 12:14]), "eV")
            on("vector", v_wp)
            P.bank_set(f"pb{bk}", "eV")
            def t_be(t, c=c, tok=P.cnt["eV"]):
                wait(t, "eV", tok)
                if c == 0:
                    P.bank_wait(t, "pb4")
                inc(t.matmul(pb[4][0:1, 0:2], e2[:, c:c + 1], e3[:, 2 * (c % 4):2 * (c % 4) + 2],
                             start=(c == 0), stop=(c == 7)), "eT")
            on("tensor", t_be)
        def v_be(v, tok=P.cnt["eT"]):
            wait(v, "eT", tok)
            inc(v.tensor_tensor(beff_row[:], pb[4][0:1, 0:2], beff_row[:], ALU.add), "eV")
        on("vector", v_be)
        P.bank_set("pb4", "eV")
        def t_bc(t, tok=P.cnt["eV"]):
            wait(t, "eV", tok)
            P.bank_wait(t, "pb5")
            t.matmul(pb[5][0:128, 0:2], ones1f[:], e1[0:1, 4:6], start=True, stop=True)
            inc(t.matmul(pb[5][0:128, 2:4], ones1f[:], beff_row[:], start=True, stop=True), "eT")
        on("tensor", t_bc)
        def v_bc(v, tok=P.cnt["eT"]):
            wait(v, "eT", tok)
            v.tensor_copy(csneg[:], pb[5][0:128, 0:2])
            inc(v.tensor_copy(beffp[:], pb[5][0:128, 2:4]), "eV")
        on("vector", v_bc)
        P.bank_set("pb5", "eV")

        if upto == "prep":
            dc1 = dout("d_centT", [128, 80])
            dc2 = dout("d_csneg", [128, 2])
            dc3 = dout("d_beffp", [128, 2])
            dc4 = dout("d_g1c", [128, 8])
            dc5 = dout("d_e1row", [128, 40])
            dc6 = dout("d_beffrow", [128, 2])
            def g_cpb(gp):
                wait_all(gp)
                inc(gp.dma_start(out=dc6[0:1, :], in_=beff_row[:]), "qG", 16)
            on("gpsimd", g_cpb)
            def g_cp(gp):
                wait_all(gp)
                inc(gp.dma_start(out=dc1, in_=centT[:]), "qG", 16)
                inc(gp.dma_start(out=dc2, in_=csneg[:]), "qG", 16)
                inc(gp.dma_start(out=dc3, in_=beffp[:]), "qG", 16)
                inc(gp.dma_start(out=dc4, in_=g1c[:]), "qG", 16)
                inc(gp.dma_start(out=dc5, in_=e1[:]), "qG", 16)
                wait_all(gp, "qG")
            on("gpsimd", g_cp)
            P.barrier()
            return nc, dbg_outs

        P.scope("phaseA")
        xr = sb("xr", [128, 2 * D], st=esA)
        xr2 = sb("xr2", [128, 2 * D], st=esA)
        xnb = sb("xnb", [128, NT * D], BF16, st=esA)
        xn_t = sb("xn_t", [128, 2 * D], st=esA)
        out_t = sb("out_t", [128, 2 * D], st=esA)
        xnT = sb("xnT", [128, 8 * 512], st=esA)
        afffm = sb("afffm", [10, 512], st=esA)
        affs = sb("affs", [128, 16], st=esA)

        atok = {}
        xtok = {}
        def emit_a0_tile(i):
            sl = i % 2
            def s_x(s, i=i, sl=sl, tk0=P.cnt["qS"]):
                if i >= 2:
                    wait(s, "eA", atok[i - 2])
                else:
                    wait(s, "qS", tk0)
                inc(s.dma_start(out=xr[:, D * sl:D * (sl + 1)], in_=x_d[128 * i:128 * (i + 1), :]), "qS", 16)
            on("sync", s_x)
            xtok[i] = P.cnt["qS"]
            def a_st(a, i=i, sl=sl, tok=P.cnt["qS"]):
                wait(a, "qS", tok)
                a.activation(junkA[:], xr[:, D * sl:D * (sl + 1)], AF.Copy,
                             scale=INV_D, accum_out=m0a[:, i:i + 1])
                inc(a.activation(junkA2[:], xr[:, D * sl:D * (sl + 1)], AF.Square,
                                 scale=INV_SQ_D, accum_out=q0a[:, i:i + 1]), "eA")
            on("scalar", a_st)
            atok[i] = P.cnt["eA"]

        tokR0h = {}
        def emit_fin0(h):
            c0, c1 = 16 * h, 16 * (h + 1)
            def v_f0(v, c0=c0, c1=c1, tok=P.cnt["eA"]):
                wait(v, "eA", tok)
                ch(v, "eV", v.tensor_tensor(st1[:, c0:c1], m0a[:, c0:c1], m0a[:, c0:c1], ALU.mult))
                inc(v.tensor_tensor(st2[:, c0:c1], q0a[:, c0:c1], st1[:, c0:c1], ALU.subtract), "eV")
            on("vector", v_f0)
            def a_f0(a, c0=c0, c1=c1, tok=P.cnt["eV"]):
                wait(a, "eV", tok)
                inc(a.activation(st1[:, c0:c1], st2[:, c0:c1], AF.Sqrt, bias=epscol[:, 0:1]), "eA")
            on("scalar", a_f0)
            def v_f0b(v, c0=c0, c1=c1, tok=P.cnt["eA"]):
                wait(v, "eA", tok)
                inc(v.reciprocal(r0a[:, c0:c1], st1[:, c0:c1]), "eV")
            on("vector", v_f0b)
            tokR0h[h] = P.cnt["eV"]

        for i in range(16):
            emit_a0_tile(i)
        emit_fin0(0)

        if upto == "a0":
            for ii in range(16, NT):
                emit_a0_tile(ii)
            emit_fin0(1)
            dm0 = dout("d_m0", [128, NT]); dq0 = dout("d_q0", [128, NT]); dr0 = dout("d_r0", [128, NT])
            def g_cp0(gp):
                wait_all(gp)
                inc(gp.dma_start(out=dm0, in_=m0a[:]), "qG", 16)
                inc(gp.dma_start(out=dq0, in_=q0a[:]), "qG", 16)
                inc(gp.dma_start(out=dr0, in_=r0a[:]), "qG", 16)
                wait_all(gp, "qG")
            on("gpsimd", g_cp0)
            P.barrier()
            esA.close()
            return nc, dbg_outs

        qg_tok = {}
        a1_tok = {}
        def emit_a1_group(g):
            for j in range(4):
                i = 4 * g + j
                sl = i % 2
                xs = xn_t[:, D * sl:D * (sl + 1)]
                os_ = out_t[:, D * sl:D * (sl + 1)]
                xv = xr2[:, D * sl:D * (sl + 1)]
                def s_x1(s, i=i, sl=sl):
                    if i >= 2:
                        wait(s, "eA", a1_tok[i - 2])
                    inc(s.dma_start(out=xr2[:, D * sl:D * (sl + 1)],
                                    in_=x_d[128 * i:128 * (i + 1), :]), "qS", 16)
                on("sync", s_x1)
                tok_x = P.cnt["qS"]
                def v_ap(v, i=i, xs=xs, os_=os_, xv=xv, tok_x=tok_x):
                    if i in (0, 16):
                        wait(v, "eV", tokR0h[i // 16])
                    wait(v, "qS", tok_x)
                    if i >= 2:
                        wait(v, "qG", qg_tok[i - 2])
                    v.tensor_scalar(xs, xv, m0a[:, i:i + 1], r0a[:, i:i + 1],
                                    ALU.subtract, ALU.mult)
                    v.tensor_tensor(os_, xv, xs, ALU.add)
                    inc(v.tensor_copy(xnb[:, D * i:D * (i + 1)], xs), "eV")
                on("vector", v_ap)
                tokXN = P.cnt["eV"]
                def g_st(gp, i=i, xs=xs, os_=os_, tok=tokXN):
                    wait(gp, "eV", tok)
                    inc(gp.dma_start(out=out_d[128 * i:128 * (i + 1), :], in_=os_), "qG", 16)
                    inc(gp.dma_start(out=y_d[128 * i:128 * (i + 1), :], in_=xs), "qG", 16)
                on("gpsimd", g_st)
                qg_tok[i] = P.cnt["qG"]
                def a_s1(a, i=i, xs=xs, tok=tokXN):
                    wait(a, "eV", tok)
                    a.activation(junkA[:], xs, AF.Copy, scale=INV_D, accum_out=m1a[:, i:i + 1])
                    inc(a.activation(junkA2[:], xs, AF.Square, scale=INV_SQ_D,
                                     accum_out=q1a[:, i:i + 1]), "eA")
                on("scalar", a_s1)
                a1_tok[i] = P.cnt["eA"]
                for c in range(8):
                    bk = c % 4
                    def t_tp(t, c=c, bk=bk, xs=xs, tok=tokXN):
                        if c == 0:
                            wait(t, "eV", tok)
                        P.bank_wait(t, f"pb{bk}")
                        inc(t.transpose(pb[bk][:, 0:128], xs[:, 128 * c:128 * (c + 1)], idn[:]), "eT")
                    on("tensor", t_tp)
                    def v_ev(v, c=c, bk=bk, j=j, tok=P.cnt["eT"]):
                        wait(v, "eT", tok)
                        inc(v.tensor_copy(xnT[:, 512 * c + 128 * j:512 * c + 128 * (j + 1)],
                                          pb[bk][:, 0:128]), "eV")
                    on("vector", v_ev)
                    P.bank_set(f"pb{bk}", "eV")
            tokGrp = P.cnt["eV"]
            if noaff:
                return
            def t_aff(t, tok=tokGrp):
                wait(t, "eV", tok)
                P.bank_wait(t, "pb4")
                for c in range(8):
                    ins = t.matmul(pb[4][0:10, 0:512], centT[:, 10 * c:10 * (c + 1)],
                                   xnT[:, 512 * c:512 * (c + 1)], start=(c == 0), stop=(c == 7))
                inc(ins, "eT")
            on("tensor", t_aff)
            def v_affev(v, tok=P.cnt["eT"]):
                wait(v, "eT", tok)
                inc(v.tensor_copy(afffm[:], pb[4][0:10, 0:512]), "eV")
            on("vector", v_affev)
            P.bank_set("pb4", "eV")
            for j in range(4):
                bk = 4 + j
                def t_at(t, j=j, bk=bk, tok=P.cnt["eV"]):
                    if j == 0:
                        wait(t, "eV", tok)
                    P.bank_wait(t, f"pb{bk}")
                    inc(t.transpose(pb[bk][0:128, 0:10], afffm[0:10, 128 * j:128 * (j + 1)],
                                    idn[0:10, 0:10]), "eT")
                on("tensor", t_at)
            def v_ep4(v, g=g, tok=P.cnt["eT"]):
                wait(v, "eT", tok)
                for j in range(4):
                    v.tensor_copy(e2[:, 10 * j:10 * j + 10], pb[4 + j][0:128, 0:10])
                for j in range(4):
                    i = 4 * g + j
                    v.tensor_copy(potraw[:, 2 * i:2 * i + 2], e2[:, 10 * j + 8:10 * j + 10])
                for j in range(4):
                    i = 4 * g + j
                    v.reduce_max(amax_a[:, i:i + 1], e2[:, 10 * j:10 * j + 8], axis=AX.X)
                for j in range(4):
                    i = 4 * g + j
                    v.tensor_scalar(e1[:, 8 * j:8 * (j + 1)], e2[:, 10 * j:10 * j + 8],
                                    amax_a[:, i:i + 1], None, ALU.is_equal)
                for j in range(4):
                    v.tensor_scalar(e1[:, 8 * j:8 * (j + 1)], e1[:, 8 * j:8 * (j + 1)],
                                    -1e9, 1e9, ALU.mult, ALU.add)
                for j in range(4):
                    v.tensor_tensor(e1[:, 8 * j:8 * (j + 1)], e1[:, 8 * j:8 * (j + 1)],
                                    iota8[:], ALU.add)
                ins = None
                for j in range(4):
                    i = 4 * g + j
                    ins = v.tensor_reduce(eid_a[:, i:i + 1], e1[:, 8 * j:8 * (j + 1)],
                                          axis=AX.X, op=ALU.min)
                inc(ins, "eV")
            on("vector", v_ep4)
            for j in range(4):
                P.bank_set(f"pb{4 + j}", "eV")

        for g in range(4):
            for ii in range(16 + 4 * g, 20 + 4 * g):
                emit_a0_tile(ii)
            emit_a1_group(g)
        emit_fin0(1)
        for g in range(4, 8):
            emit_a1_group(g)

        def a_al(a, tok=P.cnt["eV"]):
            wait(a, "eV", tok)
            inc(a.activation(alpha_a[:], amax_a[:], AF.Sigmoid), "eA")
        on("scalar", a_al)

        def v_f1(v, tok=P.cnt["eA"]):
            wait(v, "eA", tok)
            ch(v, "eV", v.tensor_tensor(st1[:], m1a[:], m1a[:], ALU.mult))
            inc(v.tensor_tensor(st2[:], q1a[:], st1[:], ALU.subtract), "eV")
        on("vector", v_f1)
        def a_f1(a, tok=P.cnt["eV"]):
            wait(a, "eV", tok)
            inc(a.activation(st1[:], st2[:], AF.Sqrt, bias=epscol[:, 0:1]), "eA")
        on("scalar", a_f1)
        def v_f1b(v, tok=P.cnt["eA"]):
            wait(v, "eA", tok)
            ch(v, "eV", v.reciprocal(r1a[:], st1[:]))
            inc(v.tensor_tensor(m1r1a[:], m1a[:], r1a[:], ALU.mult), "eV")
        on("vector", v_f1b)

        def v_pot(v, tok=P.cnt["eV"]):
            wait(v, "eV", tok)
            for cpos, dst in ((0, V_a), (1, e3)):
                raw = potraw.rearrange("p (i c) -> p c i", c=2)[:, cpos, :]
                ch(v, "eV", v.tensor_tensor(e1[:, 0:NT], raw, r1a[:], ALU.mult))
                ch(v, "eV", v.scalar_tensor_tensor(e2[:, 0:NT], m1r1a[:], csneg[:, cpos:cpos + 1],
                                                   e1[:, 0:NT], ALU.mult, ALU.add))
                inc(v.tensor_scalar(dst[:, 0:NT], e2[:, 0:NT], beffp[:, cpos:cpos + 1],
                                    None, ALU.add), "eV")
        on("vector", v_pot)
        def a_g1(a, tok=P.cnt["eV"]):
            wait(a, "eV", tok)
            ch(a, "eA", a.activation(e1[:, 0:NT], e3[:, 0:NT], AF.Abs))
            ch(a, "eA", a.activation(e1[:, 0:NT], e1[:, 0:NT], AF.Exp, scale=-1.0))
            inc(a.activation(e1[:, 0:NT], e1[:, 0:NT], AF.Ln, bias=1.0), "eA")
        on("scalar", a_g1)
        def v_g2(v, tok=P.cnt["eA"]):
            wait(v, "eA", tok)
            ch(v, "eV", v.tensor_scalar(e2[:, 0:NT], e3[:, 0:NT], 0.0, None, ALU.max))
            ch(v, "eV", v.tensor_tensor(gam_a[:], e1[:, 0:NT], e2[:, 0:NT], ALU.add))
            inc(v.tensor_scalar(gam_a[:], gam_a[:], GAMMA_FLOOR, None, ALU.add), "eV")
        on("vector", v_g2)

        if upto == "a1":
            deid = dout("d_eid", [128, NT]); dalpha = dout("d_alpha", [128, NT])
            dgam = dout("d_gam", [128, NT]); dV = dout("d_V", [128, NT])
            dxnT = dout("d_xnT", [128, 8 * 512])
            dxnb = dout("d_xnb", [128, 4 * D], BF16)
            dafffm = dout("d_afffm", [128, 512])
            dpraw = dout("d_praw", [128, 2 * NT])
            daffs = dout("d_affs", [128, 16])
            def g_cp1(gp):
                wait_all(gp)
                inc(gp.dma_start(out=dxnT, in_=xnT[:]), "qG", 16)
                inc(gp.dma_start(out=dxnb, in_=xnb[:, 28 * D:32 * D]), "qG", 16)
                inc(gp.dma_start(out=dpraw, in_=potraw[:]), "qG", 16)
                inc(gp.dma_start(out=daffs, in_=affs[:]), "qG", 16)
                def v_c1(v):
                    v.memset(junkV[:, 0:512], 0.0)
                    inc(v.tensor_copy(junkV[0:10, 0:512], afffm[:]), "eV")
                on("vector", v_c1)
                wait_all(gp, "eV")
                inc(gp.dma_start(out=dafffm, in_=junkV[:, 0:512]), "qG", 16)
                inc(gp.dma_start(out=deid, in_=eid_a[:]), "qG", 16)
                inc(gp.dma_start(out=dalpha, in_=alpha_a[:]), "qG", 16)
                inc(gp.dma_start(out=dgam, in_=gam_a[:]), "qG", 16)
                inc(gp.dma_start(out=dV, in_=V_a[:]), "qG", 16)
                wait_all(gp, "qG")
            on("gpsimd", g_cp1)
            P.barrier()
            esA.close()
            return nc, dbg_outs

        P.scope("rank")
        esR = contextlib.ExitStack()
        masks8 = sb("masks8", [8, T], st=esR)
        cum8 = sb("cum8", [8, T], st=esR)
        s8b = sb("s8b", [8, T], st=esR)
        slotrow = rowA

        def t_eT(t, tok=P.cnt["eV"]):
            wait(t, "eV", tok)
            P.bank_wait(t, "pb0")
            inc(t.transpose(pb[0][0:32, 0:128], eid_a[:], idn[:]), "eT")
        on("tensor", t_eT)
        def v_eT(v, tok=P.cnt["eT"]):
            wait(v, "eT", tok)
            inc(v.tensor_copy(vsm32[:], pb[0][0:32, 0:128]), "eV")
        on("vector", v_eT)
        P.bank_set("pb0", "eV")
        def g_eT(gp, tok=P.cnt["eV"]):
            wait(gp, "eV", tok)
            inc(gp.dma_start(out=row_d[0, :].rearrange("(a b) -> a b", a=32), in_=vsm32[:]), "qG", 16)
        on("gpsimd", g_eT)
        def s_er(s, tok=P.cnt["qG"]):
            wait(s, "qG", tok)
            inc(s.dma_start(out=rowA[:], in_=row_d[0:1, :]), "qS", 16)
        on("sync", s_er)
        tok_eidrow = P.cnt["qS"]
        for n in range(8):
            bk = n % 2
            def t_b8(t, n=n, bk=bk):
                if n == 0:
                    wait(t, "qS", tok_eidrow)
                P.bank_wait(t, f"pb{bk}")
                inc(t.matmul(pb[bk][0:8, 0:512], ones1f[0:1, 0:8],
                             rowA[0:1, 512 * n:512 * (n + 1)], start=True, stop=True), "eT")
            on("tensor", t_b8)
            def v_m8(v, n=n, bk=bk, tok=P.cnt["eT"]):
                wait(v, "eT", tok)
                inc(v.tensor_scalar(masks8[:, 512 * n:512 * (n + 1)], pb[bk][0:8, 0:512],
                                    colpack[0:8, 0:1], None, ALU.is_equal), "eV")
            on("vector", v_m8)
            P.bank_set(f"pb{bk}", "eV")
        def v_rk(v, tok=P.cnt["eV"]):
            wait(v, "eV", tok)
            ch(v, "eV", v.tensor_tensor_scan(cum8[:], masks8[:], masks8[:], 0.0, ALU.add, ALU.bypass))
            ch(v, "eV", v.tensor_scalar(s8b[:], cum8[:], CAP + 0.5, None, ALU.is_gt))
            ch(v, "eV", v.tensor_scalar(cum8[:], cum8[:], colpack[0:8, 1:2], -1.0, ALU.add, ALU.add))
            ch(v, "eV", v.scalar_tensor_tensor(s8b[:], s8b[:], 1e7, cum8[:], ALU.mult, ALU.add))
            inc(v.tensor_tensor(masks8[:], s8b[:], masks8[:], ALU.mult), "eV")
        on("vector", v_rk)
        for n in range(8):
            bk = n % 2
            def t_sl(t, n=n, bk=bk, tok=P.cnt["eV"]):
                if n == 0:
                    wait(t, "eV", tok)
                P.bank_wait(t, f"pb{bk}")
                inc(t.matmul(pb[bk][0:1, 0:512], ones8c[:], masks8[:, 512 * n:512 * (n + 1)],
                             start=True, stop=True), "eT")
            on("tensor", t_sl)
            def v_sl(v, n=n, bk=bk, tok=P.cnt["eT"]):
                wait(v, "eT", tok)
                inc(v.tensor_copy(slotrow[0:1, 512 * n:512 * (n + 1)], pb[bk][0:1, 0:512]), "eV")
            on("vector", v_sl)
            P.bank_set(f"pb{bk}", "eV")
        for i in range(NT):
            bk = 2 + (i % 2)
            def t_st(t, i=i, bk=bk, tok=P.cnt["eV"]):
                if i == 0:
                    wait(t, "eV", tok)
                P.bank_wait(t, f"pb{bk}")
                inc(t.transpose(pb[bk][0:128, 0:1], slotrow[0:1, 128 * i:128 * (i + 1)],
                                idn[0:1, 0:1]), "eT")
            on("tensor", t_st)
            def v_su(v, i=i, bk=bk, tok=P.cnt["eT"]):
                wait(v, "eT", tok)
                inc(v.tensor_copy(slotu[:, i:i + 1], pb[bk][0:128, 0:1]), "eV")
            on("vector", v_su)
            P.bank_set(f"pb{bk}", "eV")
        tokSlots = P.cnt["eV"]

        if upto == "a2a":
            dslot = dout("d_slot", [128, NT], U32)
            def g_cp2(gp):
                wait_all(gp)
                inc(gp.dma_start(out=dslot, in_=slotu[:]), "qG", 16)
                wait_all(gp, "qG")
            on("gpsimd", g_cp2)
            P.barrier()
            esR.close()
            esA.close()
            return nc, dbg_outs

        xlnr = xn_t[:].bitcast(BF16)[:, 0:2 * D]
        sc_tok = {}
        for i in range(NT):
            sl = i % 2
            def v_xl(v, i=i, sl=sl):
                if i == 0:
                    wait(v, "eV", tokSlots)
                    wait(v, "qG", qg_tok[NT - 1])
                    wait(v, "eA", a1_tok[NT - 1])
                if i >= 2:
                    wait(v, "qG", sc_tok[i - 2])
                inc(v.tensor_scalar(xlnr[:, D * sl:D * (sl + 1)], xnb[:, D * i:D * (i + 1)],
                                    m1a[:, i:i + 1], r1a[:, i:i + 1],
                                    ALU.subtract, ALU.mult), "eV")
            on("vector", v_xl)
            def g_sc(gp, i=i, sl=sl, tok=P.cnt["eV"]):
                wait(gp, "eV", tok)
                inc(gp.indirect_dma_start(
                    out=xs_d, out_offset=bass.IndirectOffsetOnAxis(ap=slotu[:, i:i + 1], axis=0),
                    in_=xlnr[:, D * sl:D * (sl + 1)], in_offset=None,
                    bounds_check=BREG[0], oob_is_err=False), "qG", 16)
            on("gpsimd", g_sc)
            sc_tok[i] = P.cnt["qG"]

        if dbg >= 1:
            deid = dout("d_eid", [128, NT])
            dalpha = dout("d_alpha", [128, NT])
            dgam = dout("d_gam", [128, NT])
            dV = dout("d_V", [128, NT])
            dslot = dout("d_slot", [128, NT], U32)
            dxs = dout("d_xs", [NS, D], BF16)
            dxnb = dout("d_xnb", [T, D], BF16)
            def g_d3(gp):
                wait_all(gp)
                inc(gp.dma_start(out=deid, in_=eid_a[:]), "qG", 16)
                inc(gp.dma_start(out=dalpha, in_=alpha_a[:]), "qG", 16)
                inc(gp.dma_start(out=dgam, in_=gam_a[:]), "qG", 16)
                inc(gp.dma_start(out=dV, in_=V_a[:]), "qG", 16)
                inc(gp.dma_start(out=dslot, in_=slotu[:]), "qG", 16)
                inc(gp.dma_start(out=dxs, in_=xs_d), "qG", 16)
                for i in range(NT):
                    inc(gp.dma_start(out=dxnb[128 * i:128 * (i + 1), :],
                                     in_=xnb[:, D * i:D * (i + 1)]), "qG", 16)
                wait_all(gp, "qG")
            on("gpsimd", g_d3)
            P.barrier()
            esR.close()
            esA.close()
            return nc, dbg_outs


        P.scope("scan")
        P.barrier()
        esR.close()
        esA.close()
        esB = contextlib.ExitStack()
        esBH = contextlib.ExitStack()
        esB2 = contextlib.ExitStack()
        esB1 = contextlib.ExitStack()

        R_DRE, R_DIM, R_DRER, R_DIMR, R_T1, R_T2, R_T3, R_T4 = 1, 2, 3, 4, 5, 6, 7, 8
        R_GR, R_GI, R_EM, R_EP, R_GC = 9, 10, 11, 12, 13

        gfm = sb("gfm", [2, T], BF16, st=esB)
        gch = sb("gch", [32, 128], st=esB)
        ghat = sb("ghat", [32, 128], st=esB)
        epx = sb("epx", [32, 128], st=esB)
        emx = sb("emx", [32, 128], st=esB)
        epT = sb("epT", [128, 32], st=esB)
        emT = sb("emT", [128, 32], st=esB)
        gcbc = sb("gcbc", [128, 32], st=esB)
        m2a = sb("m2a", [128, NT], st=esB); q2a = sb("q2a", [128, NT], st=esB)
        r2a = sb("r2a", [128, NT], st=esB)
        m3a = sb("m3a", [128, NT], st=esB); q3a = sb("q3a", [128, NT], st=esB)
        r3a = sb("r3a", [128, NT], st=esB)
        hebfm = sb("hebfm", [128, 4 * T], BF16, st=esBH)
        wbkb = sb("wbkb", [2, D], BF16, st=esB2)
        bbk_bc = sb("bbk_bc", [128, D], st=esB2)
        bbk_row = sb("bbk_row", [1, D], st=esB2)
        embc = sb("embc", [128, T], BF16, st=esB2)
        epbc = sb("epbc", [128, T], BF16, st=esB2)
        yring = sb("yring", [128, 3 * D], st=esB2)

        dreF = sb("dreF", [128, 32], st=esB1); dimF = sb("dimF", [128, 32], st=esB1)
        ngiF = sb("ngiF", [128, 32], st=esB1)
        dreB = sb("dreB", [128, 32], st=esB1); dimB = sb("dimB", [128, 32], st=esB1)
        ngiB = sb("ngiB", [128, 32], st=esB1)
        s_re = [sb(f"s_re{k}", [128, 2], st=esB1) for k in range(3)]
        s_im = [sb(f"s_im{k}", [128, 2], st=esB1) for k in range(3)]
        c_re = [sb(f"c_re{k}", [128, 2], st=esB1) for k in range(3)]
        c_im = [sb(f"c_im{k}", [128, 2], st=esB1) for k in range(3)]
        tcomp = sb("tcomp", [128, 8], st=esB1)
        sb8 = sb("sb8", [16, 64], st=esB1); sb8n = sb("sb8n", [16, 64], st=esB1)
        rst = [sb(f"rst{k}", [16, 8], st=esB1) for k in range(2)]
        srow = sb("srow", [1, 128], st=esB1); srown = sb("srown", [1, 128], st=esB1)
        psup = sb("psup", [1, 64], st=esB1)
        ps16 = [sb(f"ps16_{k}", [16, 4], st=esB1) for k in range(2)]
        pblk = sb("pblk", [16, 32], st=esB1)
        pb128 = sb("pb128", [128, 4], st=esB1)
        NRf = sb("NRf", [128, 33], st=esB1); NIf = sb("NIf", [128, 33], st=esB1)
        DRf = sb("DRf", [128, 33], st=esB1); DIf = sb("DIf", [128, 33], st=esB1)
        NRb = sb("NRb", [128, 33], st=esB1); NIb = sb("NIb", [128, 33], st=esB1)
        DRb = sb("DRb", [128, 33], st=esB1); DIb = sb("DIb", [128, 33], st=esB1)
        bre = sb("bre", [128, 32], st=esB1); bim = sb("bim", [128, 32], st=esB1)
        bre2 = sb("bre2", [128, 32], st=esB1); bim2 = sb("bim2", [128, 32], st=esB1)
        gsc = [sb(f"gsc{k}", [128, 32], st=esB1) for k in range(6)]
        tinyc = sb("tinyc", [128, 1], st=esB1)
        rowB = sb("rowB", [1, T], st=esB1)
        s_reB = [sb(f"s_reB{k}", [128, 2], st=esB1) for k in range(2)]
        s_imB = [sb(f"s_imB{k}", [128, 2], st=esB1) for k in range(2)]
        c_reB = [sb(f"c_reB{k}", [128, 2], st=esB1) for k in range(2)]
        c_imB = [sb(f"c_imB{k}", [128, 2], st=esB1) for k in range(2)]
        tcompB = sb("tcompB", [128, 8], st=esB1)
        sb8B = sb("sb8B", [16, 64], st=esB1); sb8nB = sb("sb8nB", [16, 64], st=esB1)
        rstB = [sb(f"rstB{k}", [16, 8], st=esB1) for k in range(2)]
        srowB = sb("srowB", [1, 128], st=esB1); srownB = sb("srownB", [1, 128], st=esB1)
        psupB = sb("psupB", [1, 64], st=esB1)
        ps16B = [sb(f"ps16B_{k}", [16, 4], st=esB1) for k in range(2)]
        pblkB = sb("pblkB", [16, 32], st=esB1)
        pb128B = sb("pb128B", [128, 4], st=esB1)

        def v_db(v):
            wait_all(v, "qG", "eA")
            ch(v, "eV", v.tensor_scalar(e1[:, 0:NT], V_a[:], 2.0, None, ALU.add))
            ch(v, "eV", v.tensor_scalar(e2[:, 0:NT], gam_a[:], -1.0, None, ALU.mult))
        on("vector", v_db)
        for (til, row) in ((e1, R_DRE), (e2, R_DIM)):
            def t_tr(t, til=til, tok=P.cnt["eV"]):
                wait(t, "eV", tok)
                P.bank_wait(t, "pb7")
                inc(t.transpose(pb[7][0:32, 0:128], til[:, 0:NT], idn[:]), "eT")
            on("tensor", t_tr)
            def v_tr(v, tok=P.cnt["eT"], qg=P.cnt["qG"]):
                wait(v, "eT", tok)
                wait(v, "qG", qg)
                inc(v.tensor_copy(vsm32[:], pb[7][0:32, 0:128]), "eV")
            on("vector", v_tr)
            P.bank_set("pb7", "eV")
            def g_tr(gp, row=row, tok=P.cnt["eV"]):
                wait(gp, "eV", tok)
                inc(gp.dma_start(out=row_d[row, :].rearrange("(a b) -> a b", a=32), in_=vsm32[:]), "qG", 16)
            on("gpsimd", g_tr)
        for (row, rrow) in ((R_DRE, R_DRER), (R_DIM, R_DIMR)):
            def s_rv(s, row=row, tok=P.cnt["qG"]):
                wait(s, "qG", tok)
                inc(s.dma_start(out=rowA[:], in_=row_d[row:row + 1, :]), "qS", 16)
            on("sync", s_rv)
            def v_rv(v, tok=P.cnt["qS"]):
                wait(v, "qS", tok)
                inc(v.tensor_copy(rowB[:], rowA[0:1, ::-1]), "eV")
            on("vector", v_rv)
            def g_rv(gp, rrow=rrow, tok=P.cnt["eV"]):
                wait(gp, "eV", tok)
                inc(gp.dma_start(out=row_d[rrow:rrow + 1, :], in_=rowB[:]), "qG", 16)
            on("gpsimd", g_rv)
        def s_bl(s, tok=P.cnt["qG"]):
            wait(s, "qG", tok)
            inc(s.dma_start(out=dreF[:], in_=row_d[R_DRE, :].rearrange("(b j) -> b j", b=128)), "qS", 16)
            inc(s.dma_start(out=dimF[:], in_=row_d[R_DIM, :].rearrange("(b j) -> b j", b=128)), "qS", 16)
            inc(s.dma_start(out=dreB[:], in_=row_d[R_DRER, :].rearrange("(b j) -> b j", b=128)), "qS", 16)
            inc(s.dma_start(out=dimB[:], in_=row_d[R_DIMR, :].rearrange("(b j) -> b j", b=128)), "qS", 16)
            inc(s.dma_start(out=gch[:], in_=row_d[R_DIM, :].rearrange("(c p) -> c p", c=32)), "qS", 16)
        on("sync", s_bl)
        def v_ng(v, tok=P.cnt["qS"]):
            wait(v, "qS", tok)
            ch(v, "eV", v.tensor_scalar(ngiF[:], dimF[:], -1.0, None, ALU.mult))
            ch(v, "eV", v.tensor_scalar(ngiB[:], dimB[:], -1.0, None, ALU.mult))
            ch(v, "eV", v.tensor_scalar(gch[:], gch[:], -DTC, None, ALU.mult))
        on("vector", v_ng)

        def cf_scan2(cfgs):
            def v_i0(v):
                wait_all(v, "eV")
                for cf in cfgs:
                    for k in range(2):
                        v.memset(cf["s_re"][k][:], 0.0); v.memset(cf["s_im"][k][:], 0.0)
                        v.memset(cf["c_re"][k][:], 0.0); v.memset(cf["c_im"][k][:], 0.0)
                    v.memset(cf["s_re"][0][:, 0:1], 1.0)
                    ins = v.memset(cf["c_re"][0][:, 1:2], 1.0)
                inc(ins, "eV")
            on("vector", v_i0)
            ab = 0
            for k in range(32):
                fr = 1 - ab
                def v_st(v, k=k, ab=ab, fr=fr):
                    for cf in cfgs:
                        s_re, s_im = cf["s_re"], cf["s_im"]
                        dcol = cf["dre"][:, k:k + 1]
                        v.scalar_tensor_tensor(s_re[fr][:], s_re[ab][:], dcol, cf["c_re"][ab][:], ALU.mult, ALU.subtract)
                        ins = v.scalar_tensor_tensor(s_im[fr][:], s_im[ab][:], dcol, cf["c_im"][ab][:], ALU.mult, ALU.subtract)
                    for cf in cfgs:
                        s_re, s_im = cf["s_re"], cf["s_im"]
                        icol = cf["dim"][:, k:k + 1]; ncol = cf["ngi"][:, k:k + 1]
                        v.scalar_tensor_tensor(s_re[fr][:], s_im[ab][:], ncol, s_re[fr][:], ALU.mult, ALU.add)
                        ins = v.scalar_tensor_tensor(s_im[fr][:], s_re[ab][:], icol, s_im[fr][:], ALU.mult, ALU.add)
                    for cf in cfgs:
                        v.tensor_copy(cf["c_re"][fr][:], cf["s_re"][ab][:])
                        ins = v.tensor_copy(cf["c_im"][fr][:], cf["s_im"][ab][:])
                    if k % 8 == 7:
                        for ci, cf in enumerate(cfgs):
                            e1c = e1[:, 4 * ci:4 * ci + 4]
                            v.tensor_tensor(e1c[:, 0:1], cf["s_re"][fr][:, 0:1], cf["s_re"][fr][:, 0:1], ALU.mult)
                            ins = v.tensor_tensor(e1c[:, 1:2], cf["s_im"][fr][:, 0:1], cf["s_im"][fr][:, 0:1], ALU.mult)
                        for ci, cf in enumerate(cfgs):
                            e1c = e1[:, 4 * ci:4 * ci + 4]
                            ins = v.scalar_tensor_tensor(e1c[:, 1:2], e1c[:, 1:2], 1e-30, e1c[:, 0:1], ALU.add, ALU.add)
                        for ci, cf in enumerate(cfgs):
                            e1c = e1[:, 4 * ci:4 * ci + 4]
                            ins = v.reciprocal(e1c[:, 2:3], e1c[:, 1:2])
                        for ci, cf in enumerate(cfgs):
                            e1c = e1[:, 4 * ci:4 * ci + 4]
                            for z in (cf["s_re"][fr], cf["s_im"][fr], cf["c_re"][fr], cf["c_im"][fr]):
                                ins = v.tensor_scalar(z[:], z[:], e1c[:, 2:3], None, ALU.mult)
                on("vector", v_st)
                ab = fr
            def v_tc(v, ab=ab):
                wait(v, "eV", P.cnt["eV"])
                for cf in cfgs:
                    v.tensor_copy(cf["tcomp"][:, 0:2], cf["s_re"][ab][:])
                    v.tensor_copy(cf["tcomp"][:, 2:4], cf["s_im"][ab][:])
                    v.tensor_copy(cf["tcomp"][:, 4:6], cf["c_re"][ab][:])
                    ins = v.tensor_copy(cf["tcomp"][:, 6:8], cf["c_im"][ab][:])
                inc(ins, "eV")
            on("vector", v_tc)
            def g_tc(gp, tok=P.cnt["eV"]):
                wait(gp, "eV", tok)
                for ci, cf in enumerate(cfgs):
                    inc(gp.dma_start(out=row_d[R_T1 + ci, 0:1024].rearrange("(a b) -> a b", a=128),
                                     in_=cf["tcomp"][:]), "qG", 16)
            on("gpsimd", g_tc)
            def s_tc(s, tok=P.cnt["qG"]):
                wait(s, "qG", tok)
                for ci, cf in enumerate(cfgs):
                    inc(s.dma_start(out=cf["sb8"][:], in_=row_d[R_T1 + ci, 0:1024].rearrange("(s k) -> s k", s=16)), "qS", 16)
            on("sync", s_tc)
            def v_n8(v, tok=P.cnt["qS"]):
                wait(v, "qS", tok)
                for cf in cfgs:
                    ins = v.tensor_scalar(cf["sb8n"][:], cf["sb8"][:], -1.0, None, ALU.mult)
                for cf in cfgs:
                    v.memset(cf["rst"][0][:], 0.0)
                    v.memset(cf["rst"][0][:, 0:1], 1.0)
                    ins = v.memset(cf["rst"][0][:, 5:6], 1.0)
                inc(ins, "eV")
            on("vector", v_n8)
            cur = 0
            for k in range(8):
                def v_l2(v, k=k, cur=cur):
                    o = 8 * k
                    for slot in range(4):
                        for cf in cfgs:
                            sb8_, sb8n_ = cf["sb8"], cf["sb8n"]
                            R = cf["rst"][cur]; O = cf["rst"][1 - cur]
                            a_r = sb8_[:, o + 0:o + 1]; b_r = sb8_[:, o + 1:o + 2]
                            a_i = sb8_[:, o + 2:o + 3]; b_i = sb8_[:, o + 3:o + 4]
                            c_r = sb8_[:, o + 4:o + 5]; d_r = sb8_[:, o + 5:o + 6]
                            c_i = sb8_[:, o + 6:o + 7]; d_i = sb8_[:, o + 7:o + 8]
                            na_r = sb8n_[:, o + 2:o + 3]; nb_r = sb8n_[:, o + 3:o + 4]
                            nc_r = sb8n_[:, o + 6:o + 7]; nd_r = sb8n_[:, o + 7:o + 8]
                            for sub in range(4):
                                oc = (0, 1, 4, 5)[sub]
                                m1r, m1i, n1i = ((a_r, a_i, na_r), (a_r, a_i, na_r),
                                                 (c_r, c_i, nc_r), (c_r, c_i, nc_r))[sub]
                                m2r, m2i, n2i = ((b_r, b_i, nb_r), (b_r, b_i, nb_r),
                                                 (d_r, d_i, nd_r), (d_r, d_i, nd_r))[sub]
                                x_r, x_i = ((R[:, 0:1], R[:, 2:3]), (R[:, 1:2], R[:, 3:4]),
                                            (R[:, 0:1], R[:, 2:3]), (R[:, 1:2], R[:, 3:4]))[sub]
                                y_r, y_i = ((R[:, 4:5], R[:, 6:7]), (R[:, 5:6], R[:, 7:8]),
                                            (R[:, 4:5], R[:, 6:7]), (R[:, 5:6], R[:, 7:8]))[sub]
                                dst = O[:, oc:oc + 1]; dsti = O[:, oc + 2:oc + 3]
                                if slot == 0:
                                    v.tensor_scalar(dst, x_r, m1r, None, ALU.mult)
                                    ins = v.tensor_scalar(dsti, x_i, m1r, None, ALU.mult)
                                elif slot == 1:
                                    v.scalar_tensor_tensor(dst, x_i, n1i, dst, ALU.mult, ALU.add)
                                    ins = v.scalar_tensor_tensor(dsti, x_r, m1i, dsti, ALU.mult, ALU.add)
                                elif slot == 2:
                                    v.scalar_tensor_tensor(dst, y_r, m2r, dst, ALU.mult, ALU.add)
                                    ins = v.scalar_tensor_tensor(dsti, y_i, m2r, dsti, ALU.mult, ALU.add)
                                else:
                                    v.scalar_tensor_tensor(dst, y_i, n2i, dst, ALU.mult, ALU.add)
                                    ins = v.scalar_tensor_tensor(dsti, y_r, m2i, dsti, ALU.mult, ALU.add)
                    for ci, cf in enumerate(cfgs):
                        O = cf["rst"][1 - cur]
                        e1c = e1[0:16, 8 + 4 * ci:12 + 4 * ci]
                        v.tensor_tensor(e1c[:, 0:1], O[:, 0:1], O[:, 0:1], ALU.mult)
                        ins = v.tensor_tensor(e1c[:, 1:2], O[:, 2:3], O[:, 2:3], ALU.mult)
                    for ci, cf in enumerate(cfgs):
                        e1c = e1[0:16, 8 + 4 * ci:12 + 4 * ci]
                        ins = v.scalar_tensor_tensor(e1c[:, 1:2], e1c[:, 1:2], 1e-30, e1c[:, 0:1], ALU.add, ALU.add)
                    for ci, cf in enumerate(cfgs):
                        e1c = e1[0:16, 8 + 4 * ci:12 + 4 * ci]
                        ins = v.reciprocal(e1c[:, 2:3], e1c[:, 1:2])
                    for ci, cf in enumerate(cfgs):
                        O = cf["rst"][1 - cur]
                        e1c = e1[0:16, 8 + 4 * ci:12 + 4 * ci]
                        ins = v.tensor_scalar(O[:], O[:], e1c[:, 2:3], None, ALU.mult)
                    if k == 7:
                        inc(ins, "eV")
                on("vector", v_l2)
                cur = 1 - cur
            def g_sc(gp, cur=cur, tok=P.cnt["eV"]):
                wait(gp, "eV", tok)
                for ci, cf in enumerate(cfgs):
                    inc(gp.dma_start(out=row_d[R_T1 + ci, 0:128].rearrange("(a b) -> a b", a=16),
                                     in_=cf["rst"][cur][:]), "qG", 16)
            on("gpsimd", g_sc)
            def s_sc(s, tok=P.cnt["qG"]):
                wait(s, "qG", tok)
                for ci, cf in enumerate(cfgs):
                    inc(s.dma_start(out=cf["srow"][:], in_=row_d[R_T1 + ci:R_T1 + ci + 1, 0:128]), "qS", 16)
            on("sync", s_sc)
            def v_sn(v, tok=P.cnt["qS"]):
                wait(v, "qS", tok)
                for cf in cfgs:
                    ins = v.tensor_scalar(cf["srown"][:], cf["srow"][:], -1.0, None, ALU.mult)
                for ci in range(len(cfgs)):
                    v.memset(e2[0:1, 12 * ci:12 * ci + 12], 0.0)
                    ins = v.memset(e2[0:1, 12 * ci:12 * ci + 1], 1.0)
                inc(ins, "eV")
            on("vector", v_sn)
            for s_ in range(16):
                def v_ap2(v, s_=s_):
                    o = 8 * s_
                    for slot in range(4):
                        for ci, cf in enumerate(cfgs):
                            srow_, srown_ = cf["srow"], cf["srown"]
                            st = e2[0:1, 12 * ci:12 * ci + 4]; nw = e2[0:1, 12 * ci + 4:12 * ci + 8]
                            a_r = srow_[0:1, o + 0:o + 1]; b_r = srow_[0:1, o + 1:o + 2]
                            a_i = srow_[0:1, o + 2:o + 3]; b_i = srow_[0:1, o + 3:o + 4]
                            c_r = srow_[0:1, o + 4:o + 5]; d_r = srow_[0:1, o + 5:o + 6]
                            c_i = srow_[0:1, o + 6:o + 7]; d_i = srow_[0:1, o + 7:o + 8]
                            na_i = srown_[0:1, o + 2:o + 3]; nb_i = srown_[0:1, o + 3:o + 4]
                            nc_i = srown_[0:1, o + 6:o + 7]; nd_i = srown_[0:1, o + 7:o + 8]
                            n_r, n_i, d_rr, d_ii = st[:, 0:1], st[:, 1:2], st[:, 2:3], st[:, 3:4]
                            if slot == 0:
                                v.tensor_copy(cf["psup"][0:1, 4 * s_:4 * s_ + 4], st)
                                v.tensor_scalar(nw[:, 0:1], n_r, a_r, None, ALU.mult)
                                v.tensor_scalar(nw[:, 1:2], n_i, a_r, None, ALU.mult)
                                v.tensor_scalar(nw[:, 2:3], n_r, c_r, None, ALU.mult)
                                ins = v.tensor_scalar(nw[:, 3:4], n_i, c_r, None, ALU.mult)
                            elif slot == 1:
                                v.scalar_tensor_tensor(nw[:, 0:1], n_i, na_i, nw[:, 0:1], ALU.mult, ALU.add)
                                v.scalar_tensor_tensor(nw[:, 1:2], n_r, a_i, nw[:, 1:2], ALU.mult, ALU.add)
                                v.scalar_tensor_tensor(nw[:, 2:3], n_i, nc_i, nw[:, 2:3], ALU.mult, ALU.add)
                                ins = v.scalar_tensor_tensor(nw[:, 3:4], n_r, c_i, nw[:, 3:4], ALU.mult, ALU.add)
                            elif slot == 2:
                                v.scalar_tensor_tensor(nw[:, 0:1], d_rr, b_r, nw[:, 0:1], ALU.mult, ALU.add)
                                v.scalar_tensor_tensor(nw[:, 1:2], d_ii, b_r, nw[:, 1:2], ALU.mult, ALU.add)
                                v.scalar_tensor_tensor(nw[:, 2:3], d_rr, d_r, nw[:, 2:3], ALU.mult, ALU.add)
                                ins = v.scalar_tensor_tensor(nw[:, 3:4], d_ii, d_r, nw[:, 3:4], ALU.mult, ALU.add)
                            else:
                                v.scalar_tensor_tensor(nw[:, 0:1], d_ii, nb_i, nw[:, 0:1], ALU.mult, ALU.add)
                                v.scalar_tensor_tensor(nw[:, 1:2], d_rr, b_i, nw[:, 1:2], ALU.mult, ALU.add)
                                v.scalar_tensor_tensor(nw[:, 2:3], d_ii, nd_i, nw[:, 2:3], ALU.mult, ALU.add)
                                ins = v.scalar_tensor_tensor(nw[:, 3:4], d_rr, d_i, nw[:, 3:4], ALU.mult, ALU.add)
                    for ci, cf in enumerate(cfgs):
                        nw = e2[0:1, 12 * ci + 4:12 * ci + 8]; sc_ = e2[0:1, 12 * ci + 8:12 * ci + 10]
                        v.tensor_tensor(sc_[:, 0:1], nw[:, 0:1], nw[:, 0:1], ALU.mult)
                        ins = v.tensor_tensor(sc_[:, 1:2], nw[:, 1:2], nw[:, 1:2], ALU.mult)
                    for ci, cf in enumerate(cfgs):
                        sc_ = e2[0:1, 12 * ci + 8:12 * ci + 10]
                        ins = v.scalar_tensor_tensor(sc_[:, 0:1], sc_[:, 0:1], 1e-30, sc_[:, 1:2], ALU.add, ALU.add)
                    for ci, cf in enumerate(cfgs):
                        sc_ = e2[0:1, 12 * ci + 8:12 * ci + 10]
                        ins = v.reciprocal(sc_[:, 1:2], sc_[:, 0:1])
                    for ci, cf in enumerate(cfgs):
                        st = e2[0:1, 12 * ci:12 * ci + 4]; nw = e2[0:1, 12 * ci + 4:12 * ci + 8]
                        sc_ = e2[0:1, 12 * ci + 8:12 * ci + 10]
                        ins = v.tensor_scalar(st, nw[:], sc_[:, 1:2], None, ALU.mult)
                    if s_ == 15:
                        inc(ins, "eV")
                on("vector", v_ap2)
            def g_ps(gp, tok=P.cnt["eV"]):
                wait(gp, "eV", tok)
                for ci, cf in enumerate(cfgs):
                    inc(gp.dma_start(out=row_d[R_T1 + ci, 0:64].rearrange("(a b) -> a b", a=1),
                                     in_=cf["psup"][:]), "qG", 16)
            on("gpsimd", g_ps)
            def s_ps(s, tok=P.cnt["qG"]):
                wait(s, "qG", tok)
                for ci, cf in enumerate(cfgs):
                    inc(s.dma_start(out=cf["ps16"][0][:], in_=row_d[R_T1 + ci, 0:64].rearrange("(s c) -> s c", s=16)), "qS", 16)
            on("sync", s_ps)
            cur = 0
            for k in range(8):
                def v_l2c(v, k=k, cur=cur):
                    if k == 0:
                        wait(v, "qS", P.cnt["qS"])
                    o = 8 * k
                    for slot in range(4):
                        for ci, cf in enumerate(cfgs):
                            sb8_, sb8n_ = cf["sb8"], cf["sb8n"]
                            St = cf["ps16"][cur]; Nw = cf["ps16"][1 - cur]
                            a_r = sb8_[:, o + 0:o + 1]; b_r = sb8_[:, o + 1:o + 2]
                            a_i = sb8_[:, o + 2:o + 3]; b_i = sb8_[:, o + 3:o + 4]
                            c_r = sb8_[:, o + 4:o + 5]; d_r = sb8_[:, o + 5:o + 6]
                            c_i = sb8_[:, o + 6:o + 7]; d_i = sb8_[:, o + 7:o + 8]
                            na_i = sb8n_[:, o + 2:o + 3]; nb_i = sb8n_[:, o + 3:o + 4]
                            nc_i = sb8n_[:, o + 6:o + 7]; nd_i = sb8n_[:, o + 7:o + 8]
                            n_r, n_i, d_rr, d_ii = St[:, 0:1], St[:, 1:2], St[:, 2:3], St[:, 3:4]
                            if slot == 0:
                                v.tensor_copy(cf["pblk"][:, 4 * k:4 * k + 4], St[:])
                            for (oc, m_r, m_i, nm_i, q_r, q_i, nq_i) in (
                                (0, a_r, a_i, na_i, b_r, b_i, nb_i),
                                (2, c_r, c_i, nc_i, d_r, d_i, nd_i),
                            ):
                                dst = Nw[:, oc:oc + 1]; dsti = Nw[:, oc + 1:oc + 2]
                                if slot == 0:
                                    v.tensor_scalar(dst, n_r, m_r, None, ALU.mult)
                                    ins = v.tensor_scalar(dsti, n_i, m_r, None, ALU.mult)
                                elif slot == 1:
                                    v.scalar_tensor_tensor(dst, n_i, nm_i, dst, ALU.mult, ALU.add)
                                    ins = v.scalar_tensor_tensor(dsti, n_r, m_i, dsti, ALU.mult, ALU.add)
                                elif slot == 2:
                                    v.scalar_tensor_tensor(dst, d_rr, q_r, dst, ALU.mult, ALU.add)
                                    ins = v.scalar_tensor_tensor(dsti, d_ii, q_r, dsti, ALU.mult, ALU.add)
                                else:
                                    v.scalar_tensor_tensor(dst, d_ii, nq_i, dst, ALU.mult, ALU.add)
                                    ins = v.scalar_tensor_tensor(dsti, d_rr, q_i, dsti, ALU.mult, ALU.add)
                    for ci, cf in enumerate(cfgs):
                        Nw = cf["ps16"][1 - cur]
                        e1c = e1[0:16, 16 + 4 * ci:20 + 4 * ci]
                        v.tensor_tensor(e1c[:, 0:1], Nw[:, 0:1], Nw[:, 0:1], ALU.mult)
                        ins = v.tensor_tensor(e1c[:, 1:2], Nw[:, 1:2], Nw[:, 1:2], ALU.mult)
                    for ci, cf in enumerate(cfgs):
                        e1c = e1[0:16, 16 + 4 * ci:20 + 4 * ci]
                        ins = v.scalar_tensor_tensor(e1c[:, 0:1], e1c[:, 0:1], 1e-30, e1c[:, 1:2], ALU.add, ALU.add)
                    for ci, cf in enumerate(cfgs):
                        e1c = e1[0:16, 16 + 4 * ci:20 + 4 * ci]
                        ins = v.reciprocal(e1c[:, 2:3], e1c[:, 0:1])
                    for ci, cf in enumerate(cfgs):
                        Nw = cf["ps16"][1 - cur]
                        e1c = e1[0:16, 16 + 4 * ci:20 + 4 * ci]
                        ins = v.tensor_scalar(Nw[:], Nw[:], e1c[:, 2:3], None, ALU.mult)
                    if k == 7:
                        inc(ins, "eV")
                on("vector", v_l2c)
                cur = 1 - cur
            def g_pb(gp, tok=P.cnt["eV"]):
                wait(gp, "eV", tok)
                for ci, cf in enumerate(cfgs):
                    inc(gp.dma_start(out=row_d[R_T1 + ci, 0:512].rearrange("(a b) -> a b", a=16),
                                     in_=cf["pblk"][:]), "qG", 16)
            on("gpsimd", g_pb)
            def s_pb(s, tok=P.cnt["qG"]):
                wait(s, "qG", tok)
                for ci, cf in enumerate(cfgs):
                    inc(s.dma_start(out=cf["pb128"][:], in_=row_d[R_T1 + ci, 0:512].rearrange("(b c) -> b c", b=128)), "qS", 16)
            on("sync", s_pb)
            def v_l3i(v, tok=P.cnt["qS"]):
                wait(v, "qS", tok)
                for cf in cfgs:
                    v.tensor_copy(cf["NR"][:, 0:1], cf["pb128"][:, 0:1])
                    v.tensor_copy(cf["NI"][:, 0:1], cf["pb128"][:, 1:2])
                    v.tensor_copy(cf["DR"][:, 0:1], cf["pb128"][:, 2:3])
                    ins = v.tensor_copy(cf["DI"][:, 0:1], cf["pb128"][:, 3:4])
                inc(ins, "eV")
            on("vector", v_l3i)
            for k in range(32):
                def v_l3(v, k=k):
                    for cf in cfgs:
                        NR, NI, DR, DI = cf["NR"], cf["NI"], cf["DR"], cf["DI"]
                        dcol = cf["dre"][:, k:k + 1]
                        v.scalar_tensor_tensor(NR[:, k + 1:k + 2], NR[:, k:k + 1], dcol, DR[:, k:k + 1], ALU.mult, ALU.subtract)
                        ins = v.scalar_tensor_tensor(NI[:, k + 1:k + 2], NI[:, k:k + 1], dcol, DI[:, k:k + 1], ALU.mult, ALU.subtract)
                    for cf in cfgs:
                        NR, NI = cf["NR"], cf["NI"]
                        icol = cf["dim"][:, k:k + 1]; ncol = cf["ngi"][:, k:k + 1]
                        v.scalar_tensor_tensor(NR[:, k + 1:k + 2], NI[:, k:k + 1], ncol, NR[:, k + 1:k + 2], ALU.mult, ALU.add)
                        ins = v.scalar_tensor_tensor(NI[:, k + 1:k + 2], NR[:, k:k + 1], icol, NI[:, k + 1:k + 2], ALU.mult, ALU.add)
                    for cf in cfgs:
                        v.tensor_copy(cf["DR"][:, k + 1:k + 2], cf["NR"][:, k:k + 1])
                        ins = v.tensor_copy(cf["DI"][:, k + 1:k + 2], cf["NI"][:, k:k + 1])
                    if k % 8 == 7:
                        for ci, cf in enumerate(cfgs):
                            NR, NI = cf["NR"], cf["NI"]
                            e1c = e1[:, 16 + 4 * ci:20 + 4 * ci]
                            v.tensor_tensor(e1c[:, 0:1], NR[:, k + 1:k + 2], NR[:, k + 1:k + 2], ALU.mult)
                            ins = v.tensor_tensor(e1c[:, 1:2], NI[:, k + 1:k + 2], NI[:, k + 1:k + 2], ALU.mult)
                        for ci, cf in enumerate(cfgs):
                            e1c = e1[:, 16 + 4 * ci:20 + 4 * ci]
                            ins = v.scalar_tensor_tensor(e1c[:, 0:1], e1c[:, 0:1], 1e-30, e1c[:, 1:2], ALU.add, ALU.add)
                        for ci, cf in enumerate(cfgs):
                            e1c = e1[:, 16 + 4 * ci:20 + 4 * ci]
                            ins = v.reciprocal(e1c[:, 2:3], e1c[:, 0:1])
                        for ci, cf in enumerate(cfgs):
                            e1c = e1[:, 16 + 4 * ci:20 + 4 * ci]
                            for z in (cf["NR"], cf["NI"], cf["DR"], cf["DI"]):
                                ins = v.tensor_scalar(z[:, k + 1:k + 2], z[:, k + 1:k + 2], e1c[:, 2:3], None, ALU.mult)
                        if k == 31:
                            inc(ins, "eV")
                on("vector", v_l3)

        wcv = sb("wcv", [128, 4 * 4096], BF16, st=esB1)
        cv_tok = {}
        for j in range(16):
            slc = j % 4
            def g_cv(gp, j=j, slc=slc):
                if j >= 4:
                    wait(gp, "qD", cv_tok[j - 4])
                if j < 8:
                    inc(gp.dma_start(out=wcv[:, 4096 * slc:4096 * (slc + 1)],
                                     in_=wff1_d[128 * j:128 * (j + 1), :]), "qC", 16)
                else:
                    jj = j - 8
                    inc(gp.dma_start(out=wcv.rearrange("p (s a b) -> p s a b", s=4, a=4)[:, slc],
                                     in_=wff2_d.rearrange("(a p) b -> p a b", p=128)[:, 4 * jj:4 * (jj + 1), :]), "qC", 16)
            on("gpsimd", g_cv)
            def s_cv(s, j=j, slc=slc, tok=P.cnt["qC"]):
                wait(s, "qC", tok)
                if j < 8:
                    inc(s.dma_start(out=wff1b_d[128 * j:128 * (j + 1), :],
                                    in_=wcv[:, 4096 * slc:4096 * (slc + 1)]), "qD", 16)
                else:
                    jj = j - 8
                    inc(s.dma_start(out=wff2b_d.rearrange("(a p) b -> p a b", p=128)[:, 4 * jj:4 * (jj + 1), :],
                                    in_=wcv.rearrange("p (s a b) -> p s a b", s=4, a=4)[:, slc]), "qD", 16)
            on("sync", s_cv)
            cv_tok[j] = P.cnt["qD"]

        if upto == "wcv1":
            dw1 = dout("d_w1b", [128, FF], BF16)
            dw2 = dout("d_w2b", [128, D], BF16)
            def g_dw1(gp):
                wait_all(gp)
                inc(gp.dma_start(out=dw1, in_=wff1b_d[0:128, :]), "qG", 16)
                inc(gp.dma_start(out=dw2, in_=wff2b_d[256:384, :]), "qG", 16)
                wait_all(gp, "qG")
            on("gpsimd", g_dw1)
            P.barrier()
            esB1.close()
            esB2.close()
            esBH.close()
            esB.close()
            return nc, dbg_outs

        cfgF = dict(dre=dreF, dim=dimF, ngi=ngiF, NR=NRf, NI=NIf, DR=DRf, DI=DIf,
                    s_re=s_re, s_im=s_im, c_re=c_re, c_im=c_im, tcomp=tcomp,
                    sb8=sb8, sb8n=sb8n, rst=rst, srow=srow, srown=srown,
                    psup=psup, ps16=ps16, pblk=pblk, pb128=pb128)
        cfgB = dict(dre=dreB, dim=dimB, ngi=ngiB, NR=NRb, NI=NIb, DR=DRb, DI=DIb,
                    s_re=s_reB, s_im=s_imB, c_re=c_reB, c_im=c_imB, tcomp=tcompB,
                    sb8=sb8B, sb8n=sb8nB, rst=rstB, srow=srowB, srown=srownB,
                    psup=psupB, ps16=ps16B, pblk=pblkB, pb128=pb128B)
        cf_scan2([cfgF, cfgB])

        for (til, dst, r1, r2) in ((NRb, bre, R_T1, R_T2), (NIb, bim, R_T3, R_T4),
                                   (DRb, bre2, R_T1, R_T2), (DIb, bim2, R_T3, R_T4)):
            def g_u1(gp, til=til, r1=r1, tok=P.cnt["eV"]):
                wait(gp, "eV", tok)
                wait(gp, "qS", P.cnt["qS"])
                inc(gp.dma_start(out=row_d[r1, :].rearrange("(a b) -> a b", a=128), in_=til[:, 1:33]), "qG", 16)
            on("gpsimd", g_u1)
            def s_u1(s, r1=r1, tok=P.cnt["qG"]):
                wait(s, "qG", tok)
                inc(s.dma_start(out=rowA[:], in_=row_d[r1:r1 + 1, :]), "qS", 16)
            on("sync", s_u1)
            def v_u1(v, tok=P.cnt["qS"]):
                wait(v, "qS", tok)
                inc(v.tensor_copy(rowB[:], rowA[0:1, ::-1]), "eV")
            on("vector", v_u1)
            def g_u2(gp, r2=r2, tok=P.cnt["eV"]):
                wait(gp, "eV", tok)
                inc(gp.dma_start(out=row_d[r2:r2 + 1, :], in_=rowB[:]), "qG", 16)
            on("gpsimd", g_u2)
            def s_u2(s, dst=dst, r2=r2, tok=P.cnt["qG"]):
                wait(s, "qG", tok)
                inc(s.dma_start(out=dst[:], in_=row_d[r2, :].rearrange("(b j) -> b j", b=128)), "qS", 16)
            on("sync", s_u2)

        def v_gc(v):
            wait_all(v, "qS", "eV")
            na_r, na_i = NRf[:, 1:33], NIf[:, 1:33]
            da_r, da_i = DRf[:, 1:33], DIf[:, 1:33]
            nb_r, nb_i = bre[:], bim[:]
            db_r, db_i = bre2[:], bim2[:]
            numr, numi, t1r, t1i, denr, deni = (gsc[0], gsc[1], gsc[2], gsc[3], gsc[4], gsc[5])
            for mi, (pr, pi, qr, qi) in ((4, (na_r, na_i, da_r, da_i)), (5, (nb_r, nb_i, db_r, db_i))):
                ch(v, "eV", v.tensor_tensor(gsc[mi][:], pr, pr, ALU.mult))
                ch(v, "eV", v.tensor_tensor(t1r[:], pi, pi, ALU.mult))
                ch(v, "eV", v.tensor_tensor(gsc[mi][:], gsc[mi][:], t1r[:], ALU.add))
                ch(v, "eV", v.tensor_tensor(t1r[:], qr, qr, ALU.mult))
                ch(v, "eV", v.tensor_tensor(gsc[mi][:], gsc[mi][:], t1r[:], ALU.add))
                ch(v, "eV", v.tensor_tensor(t1r[:], qi, qi, ALU.mult))
                ch(v, "eV", v.tensor_tensor(gsc[mi][:], gsc[mi][:], t1r[:], ALU.add))
                inc(v.tensor_scalar(gsc[mi][:], gsc[mi][:], 1e-30, None, ALU.add), "eV")
            inc(v.memset(tinyc[:], 0.0), "eV")
        on("vector", v_gc)
        def a_gcs(a, tok=P.cnt["eV"]):
            wait(a, "eA", P.cnt["eA"])
            wait(a, "eV", tok)
            ch(a, "eA", a.activation(gsc[4][:], gsc[4][:], AF.Sqrt))
            inc(a.activation(gsc[5][:], gsc[5][:], AF.Sqrt), "eA")
        on("scalar", a_gcs)
        def v_gc2b(v, tok=P.cnt["eA"]):
            wait(v, "eA", tok)
            na_r, na_i = NRf[:, 1:33], NIf[:, 1:33]
            da_r, da_i = DRf[:, 1:33], DIf[:, 1:33]
            nb_r, nb_i = bre[:], bim[:]
            db_r, db_i = bre2[:], bim2[:]
            numr, numi, t1r, t1i, denr, deni = (gsc[0], gsc[1], gsc[2], gsc[3], gsc[4], gsc[5])
            ch(v, "eV", v.reciprocal(gsc[4][:], gsc[4][:]))
            ch(v, "eV", v.reciprocal(gsc[5][:], gsc[5][:]))
            for mi, (pr, pi, qr, qi) in ((4, (na_r, na_i, da_r, da_i)), (5, (nb_r, nb_i, db_r, db_i))):
                for z in (pr, pi, qr, qi):
                    ch(v, "eV", v.tensor_tensor(z, z, gsc[mi][:], ALU.mult))
            ch(v, "eV", v.tensor_tensor(numr[:], da_r, db_r, ALU.mult))
            ch(v, "eV", v.tensor_tensor(t1r[:], da_i, db_i, ALU.mult))
            ch(v, "eV", v.tensor_tensor(numr[:], numr[:], t1r[:], ALU.subtract))
            ch(v, "eV", v.tensor_tensor(numi[:], da_r, db_i, ALU.mult))
            ch(v, "eV", v.tensor_tensor(t1r[:], da_i, db_r, ALU.mult))
            ch(v, "eV", v.tensor_tensor(numi[:], numi[:], t1r[:], ALU.add))
            ch(v, "eV", v.tensor_tensor(denr[:], na_r, db_r, ALU.mult))
            ch(v, "eV", v.tensor_tensor(t1r[:], na_i, db_i, ALU.mult))
            ch(v, "eV", v.tensor_tensor(denr[:], denr[:], t1r[:], ALU.subtract))
            ch(v, "eV", v.tensor_tensor(t1r[:], nb_r, da_r, ALU.mult))
            ch(v, "eV", v.tensor_tensor(denr[:], denr[:], t1r[:], ALU.add))
            ch(v, "eV", v.tensor_tensor(t1r[:], nb_i, da_i, ALU.mult))
            ch(v, "eV", v.tensor_tensor(denr[:], denr[:], t1r[:], ALU.subtract))
            ch(v, "eV", v.tensor_tensor(t1r[:], dreF[:], numr[:], ALU.mult))
            ch(v, "eV", v.tensor_tensor(denr[:], denr[:], t1r[:], ALU.subtract))
            ch(v, "eV", v.tensor_tensor(t1r[:], dimF[:], numi[:], ALU.mult))
            ch(v, "eV", v.tensor_tensor(denr[:], denr[:], t1r[:], ALU.add))
            ch(v, "eV", v.tensor_tensor(deni[:], na_r, db_i, ALU.mult))
            ch(v, "eV", v.tensor_tensor(t1r[:], na_i, db_r, ALU.mult))
            ch(v, "eV", v.tensor_tensor(deni[:], deni[:], t1r[:], ALU.add))
            ch(v, "eV", v.tensor_tensor(t1r[:], nb_r, da_i, ALU.mult))
            ch(v, "eV", v.tensor_tensor(deni[:], deni[:], t1r[:], ALU.add))
            ch(v, "eV", v.tensor_tensor(t1r[:], nb_i, da_r, ALU.mult))
            ch(v, "eV", v.tensor_tensor(deni[:], deni[:], t1r[:], ALU.add))
            ch(v, "eV", v.tensor_tensor(t1r[:], dreF[:], numi[:], ALU.mult))
            ch(v, "eV", v.tensor_tensor(deni[:], deni[:], t1r[:], ALU.subtract))
            ch(v, "eV", v.tensor_tensor(t1r[:], dimF[:], numr[:], ALU.mult))
            ch(v, "eV", v.tensor_tensor(deni[:], deni[:], t1r[:], ALU.subtract))
            ch(v, "eV", v.tensor_tensor(t1i[:], denr[:], denr[:], ALU.mult))
            ch(v, "eV", v.tensor_tensor(t1r[:], deni[:], deni[:], ALU.mult))
            ch(v, "eV", v.tensor_tensor(t1i[:], t1i[:], t1r[:], ALU.add))
            ch(v, "eV", v.reciprocal(t1i[:], t1i[:]))
            ch(v, "eV", v.tensor_tensor(bre[:], numr[:], denr[:], ALU.mult))
            ch(v, "eV", v.tensor_tensor(bim[:], numi[:], deni[:], ALU.mult))
            ch(v, "eV", v.tensor_tensor(bre[:], bre[:], bim[:], ALU.add))
            ch(v, "eV", v.tensor_tensor(bim[:], numi[:], denr[:], ALU.mult))
            ch(v, "eV", v.tensor_tensor(bre2[:], numr[:], deni[:], ALU.mult))
            ch(v, "eV", v.tensor_tensor(bim[:], bim[:], bre2[:], ALU.subtract))
            ch(v, "eV", v.tensor_tensor(gsc[2][:], bre[:], t1i[:], ALU.mult))
            inc(v.tensor_tensor(gsc[3][:], bim[:], t1i[:], ALU.mult), "eV")
        on("vector", v_gc2b)
        def g_g1(gp, tok=P.cnt["eV"]):
            wait(gp, "eV", tok)
            inc(gp.dma_start(out=row_d[R_GR, :].rearrange("(a b) -> a b", a=128), in_=gsc[2][:]), "qG", 16)
            inc(gp.dma_start(out=row_d[R_GI, :].rearrange("(a b) -> a b", a=128), in_=gsc[3][:]), "qG", 16)
        on("gpsimd", g_g1)
        def g_g2(gp, tok=P.cnt["qG"]):
            wait(gp, "qG", tok)
            inc(gp.dma_start(out=gfm[0:1, :], in_=row_d[R_GR:R_GR + 1, :]), "qG", 16)
            inc(gp.dma_start(out=gfm[1:2, :], in_=row_d[R_GI:R_GI + 1, :]), "qG", 16)
        on("gpsimd", g_g2)

        def v_gh(v):
            wait_all(v, "qS", "eV")
            inc(v.tensor_tensor_scan(ghat[:], gch[:], gch[:], 0.0, ALU.add, ALU.bypass), "eV")
        on("vector", v_gh)
        def a_eh(a, tok=P.cnt["eV"]):
            wait(a, "eV", tok)
            ch(a, "eA", a.activation(epx[:], ghat[:], AF.Exp, bias=lnecol[0:32, 0:1]))
            inc(a.activation(emx[:], ghat[:], AF.Exp, scale=-1.0), "eA")
        on("scalar", a_eh)
        for (srcT, dstT) in ((epx, epT), (emx, emT)):
            def t_eT2(t, srcT=srcT, tok=P.cnt["eA"]):
                wait(t, "eA", tok)
                P.bank_wait(t, "pb7")
                inc(t.transpose(pb[7][0:128, 0:32], srcT[:], idn[0:32, 0:32]), "eT")
            on("tensor", t_eT2)
            def v_eT2(v, dstT=dstT, tok=P.cnt["eT"]):
                wait(v, "eT", tok)
                inc(v.tensor_copy(dstT[:], pb[7][0:128, 0:32]), "eV")
            on("vector", v_eT2)
            P.bank_set("pb7", "eV")
        def g_gc(gp, tok=P.cnt["eA"]):
            wait(gp, "eA", tok)
            inc(gp.dma_start(out=row_d[R_GC, 0:32].rearrange("(a b) -> a b", a=32), in_=emx[:, 127:128]), "qG", 16)
        on("gpsimd", g_gc)
        def s_gc(s, tok=P.cnt["qG"]):
            wait(s, "qG", tok)
            inc(s.dma_start(out=e3[0:1, 0:32], in_=row_d[R_GC:R_GC + 1, 0:32]), "qS", 16)
        on("sync", s_gc)
        def t_gc(t, tok=P.cnt["qS"]):
            wait(t, "qS", tok)
            P.bank_wait(t, "pb7")
            inc(t.matmul(pb[7][0:128, 0:32], ones1f[:], e3[0:1, 0:32], start=True, stop=True), "eT")
        on("tensor", t_gc)
        def v_gc2(v, tok=P.cnt["eT"]):
            wait(v, "eT", tok)
            inc(v.tensor_copy(gcbc[:], pb[7][0:128, 0:32]), "eV")
        on("vector", v_gc2)
        P.bank_set("pb7", "eV")
        def g_er(gp):
            wait_all(gp, "eA", "qS")
            inc(gp.dma_start(out=row_d[R_EM, :].rearrange("(c p) -> c p", c=32), in_=emx[:]), "qG", 16)
            inc(gp.dma_start(out=row_d[R_EP, :].rearrange("(c p) -> c p", c=32), in_=epx[:]), "qG", 16)
        on("gpsimd", g_er)
        for (rr, dstb) in ((R_EM, embc), (R_EP, epbc)):
            def s_er2(s, rr=rr, tok=P.cnt["qG"]):
                wait(s, "qG", tok)
                wait(s, "eT", P.cnt["eT"])
                inc(s.dma_start(out=rowA[:], in_=row_d[rr:rr + 1, :]), "qS", 16)
            on("sync", s_er2)
            tok_row = P.cnt["qS"]
            for c in range(32):
                bk = c % 2
                def t_eb(t, c=c, bk=bk, tok=tok_row):
                    if c == 0:
                        wait(t, "qS", tok)
                    P.bank_wait(t, f"pb{bk}")
                    inc(t.matmul(pb[bk][0:128, 0:128], ones1f[:], rowA[0:1, 128 * c:128 * (c + 1)],
                                 start=True, stop=True), "eT")
                on("tensor", t_eb)
                def v_eb(v, c=c, bk=bk, dstb=dstb, tok=P.cnt["eT"]):
                    wait(v, "eT", tok)
                    inc(v.tensor_copy(dstb[:, 128 * c:128 * (c + 1)], pb[bk][0:128, 0:128]), "eV")
                on("vector", v_eb)
                P.bank_set(f"pb{bk}", "eV")
        def g_wbk(gp):
            inc(gp.dma_start(out=wbkb[:], in_=wbk_d), "qG", 16)
            inc(gp.dma_start(out=bbk_row[:], in_=bbk_d), "qG", 16)
        on("gpsimd", g_wbk)
        def t_bbk(t, tok=P.cnt["qG"]):
            wait(t, "qG", tok)
            P.bank_wait(t, "pb6")
            P.bank_wait(t, "pb5")
            t.matmul(pb[6][0:128, 0:512], ones1f[:], bbk_row[0:1, 0:512], start=True, stop=True)
            inc(t.matmul(pb[5][0:128, 0:512],
                         ones1f[:], bbk_row[0:1, 512:1024], start=True, stop=True), "eT")
        on("tensor", t_bbk)
        def v_bbk(v, tok=P.cnt["eT"]):
            wait(v, "eT", tok)
            v.tensor_copy(bbk_bc[:, 0:512], pb[6][0:128, 0:512])
            inc(v.tensor_copy(bbk_bc[:, 512:1024], pb[5][0:128, 0:512]), "eV")
        on("vector", v_bbk)
        P.bank_set("pb6", "eV"); P.bank_set("pb5", "eV")

        if dbg >= 2 or upto == "b1":
            dG = dout("d_G", [128, 64])
            dghat = dout("d_ghat", [32, 128])
            dept = dout("d_epT", [128, 32])
            demt = dout("d_emT", [128, 32])
            dgc = dout("d_gcbc", [128, 32])
            dprs = dout("d_pairs", [128, 4 * 33])
            dprb = dout("d_pairb", [128, 4 * 32])
            dprbr = dout("d_pairbr", [128, 4 * 33])
            def g_db1(gp):
                wait_all(gp)
                for kk, til in enumerate((NRf, NIf, DRf, DIf)):
                    inc(gp.dma_start(out=dprs[:, 33 * kk:33 * (kk + 1)], in_=til[:]), "qG", 16)
                for kk, til in enumerate((NRb, NIb, DRb, DIb)):
                    inc(gp.dma_start(out=dprbr[:, 33 * kk:33 * (kk + 1)], in_=til[:]), "qG", 16)
                for kk, til in enumerate((bre, bim, bre2, bim2)):
                    inc(gp.dma_start(out=dprb[:, 32 * kk:32 * (kk + 1)], in_=til[:]), "qG", 16)
                inc(gp.dma_start(out=dG[:, 0:32], in_=gsc[2][:]), "qG", 16)
                inc(gp.dma_start(out=dG[:, 32:64], in_=gsc[3][:]), "qG", 16)
                inc(gp.dma_start(out=dghat, in_=ghat[:]), "qG", 16)
                inc(gp.dma_start(out=dept, in_=epT[:]), "qG", 16)
                inc(gp.dma_start(out=demt, in_=emT[:]), "qG", 16)
                inc(gp.dma_start(out=dgc, in_=gcbc[:]), "qG", 16)
                wait_all(gp, "qG")
            on("gpsimd", g_db1)
            if upto == "b1":
                P.barrier()
                esB1.close()
                esB2.close()
                esBH.close()
                esB.close()
                return nc, dbg_outs


        P.scope("b2")
        P.barrier()
        esB1.close()

        b2_tok = {}
        for i in range(NT):
            sl = i % 3
            yv = yring[:, D * sl:D * (sl + 1)]
            bk0, bk1 = (6, 7) if i % 2 == 0 else (4, 5)
            def s_y(s, i=i, yv=yv):
                if i >= 3:
                    wait(s, "eA", b2_tok[i - 3][0])
                    wait(s, "qG", b2_tok[i - 3][1])
                inc(s.dma_start(out=yv, in_=y_d[128 * i:128 * (i + 1), :]), "qS", 16)
            on("sync", s_y)
            tok_y = P.cnt["qS"]
            def t_bk(t, i=i, bk0=bk0, bk1=bk1, tok=tok_y):
                P.bank_wait(t, f"pb{bk0}"); P.bank_wait(t, f"pb{bk1}")
                if i == 0:
                    wait(t, "eV", P.cnt["eV"])
                    wait(t, "qG", P.cnt["qG"])
                t.matmul(pb[bk0][0:128, 0:512], gfm[:, 128 * i:128 * (i + 1)], wbkb[:, 0:512],
                         start=True, stop=True)
                inc(t.matmul(pb[bk1][0:128, 0:512], gfm[:, 128 * i:128 * (i + 1)], wbkb[:, 512:1024],
                             start=True, stop=True), "eT")
            on("tensor", t_bk)
            def v_y1(v, i=i, yv=yv, bk0=bk0, bk1=bk1, tok=P.cnt["eT"], toky=tok_y):
                wait(v, "eT", tok)
                wait(v, "qS", toky)
                v.tensor_tensor(junkV[:, 0:512], pb[bk0][0:128, 0:512], bbk_bc[:, 0:512], ALU.add)
                ch(v, "eV", v.tensor_tensor(junkV[:, 512:1024], pb[bk1][0:128, 0:512], bbk_bc[:, 512:1024], ALU.add))
                v.tensor_tensor(yv[:, 0:512], junkV[:, 0:512], yv[:, 0:512], ALU.add)
                inc(v.tensor_tensor(yv[:, 512:1024], junkV[:, 512:1024], yv[:, 512:1024], ALU.add), "eV")
            on("vector", v_y1)
            P.bank_set(f"pb{bk0}", "eV"); P.bank_set(f"pb{bk1}", "eV")
            tok_y1 = P.cnt["eV"]
            def g_y1(gp, i=i, yv=yv, tok=tok_y1):
                wait(gp, "eV", tok)
                inc(gp.dma_start(out=y_d[128 * i:128 * (i + 1), :], in_=yv), "qG", 16)
            on("gpsimd", g_y1)
            def a_s2(a, i=i, yv=yv, tok=tok_y1):
                wait(a, "eV", tok)
                a.activation(junkA[:], yv, AF.Copy, scale=INV_D, accum_out=m2a[:, i:i + 1])
                inc(a.activation(junkA2[:], yv, AF.Square, scale=INV_SQ_D,
                                 accum_out=q2a[:, i:i + 1]), "eA")
            on("scalar", a_s2)
            b2_tok[i] = (P.cnt["eA"], P.cnt["qG"])

        def v_f2(v, tok=P.cnt["eA"]):
            wait(v, "eA", tok)
            ch(v, "eV", v.tensor_tensor(st1[:], m2a[:], m2a[:], ALU.mult))
            inc(v.tensor_tensor(st2[:], q2a[:], st1[:], ALU.subtract), "eV")
        on("vector", v_f2)
        def a_f2(a, tok=P.cnt["eV"]):
            wait(a, "eV", tok)
            inc(a.activation(st1[:], st2[:], AF.Sqrt, bias=epscol[:, 0:1]), "eA")
        on("scalar", a_f2)
        def v_f2b(v, tok=P.cnt["eA"]):
            wait(v, "eA", tok)
            inc(v.reciprocal(r2a[:], st1[:]), "eV")
        on("vector", v_f2b)
        tokR2 = P.cnt["eV"]

        P.scope("qkvprep")
        esB3 = contextlib.ExitStack()
        wq_sb = sb("wq_sb", [128, 8 * 1536], BF16, st=esB3)
        xn2g = sb("xn2g", [128, 8 * 1024], BF16, st=esB3)
        g2c = sb("g2c", [128, 8], st=esB3)
        b2cb = sb("b2cb", [128, 8], BF16, st=esB3)
        beffq = sb("beffq", [1, 1536], st=esB3)
        beffqT = sb("beffqT", [128, 12], st=esB3)
        kvbc = sb("kvbc", [128, 1024], st=esB3)
        qkfm = sb("qkfm", [128, 8 * 1024], BF16, st=esB3)
        ktok = sb("ktok", [128, 8 * 512], BF16, st=esB3)
        vtok = sb("vtok", [128, 8 * 512], BF16, st=esB3)
        qts = sb("qts", [128, 4 * 128], BF16, st=esB3)
        kts = sb("kts", [128, 4 * 128], BF16, st=esB3)
        khat = sb("khat", [128, 512], BF16, st=esB3)
        vhat = sb("vhat", [128, 512], BF16, st=esB3)
        Wst = sb("Wst", [128, 4 * 64], st=esB3)
        Wbf = [sb(f"Wbf{k}", [128, 4 * 64], BF16, st=esB3) for k in range(2)]
        ones1b = sb("ones1b", [1, 128], BF16, st=esB3)
        sdt = sb("sdt", [128, 512], BF16, st=esB3)

        def g_wq(gp):
            for dc in range(8):
                inc(gp.dma_start(out=wq_sb[:, 1536 * dc:1536 * (dc + 1)],
                                 in_=wqkv_d[128 * dc:128 * (dc + 1), :]), "qG", 16)
            inc(gp.dma_start(out=beffq[:], in_=bqkv_d), "qG", 16)
        on("gpsimd", g_wq)
        tok_wq = P.cnt["qG"]
        def s_g2(s):
            wait(s, "eT", P.cnt["eT"])
            inc(s.dma_start(out=vsm[:], in_=ln2g_d), "qS", 16)
        on("sync", s_g2)
        def t_g2(t, tok=P.cnt["qS"]):
            wait(t, "qS", tok)
            P.bank_wait(t, "pb0")
            inc(t.transpose(pb[0][0:128, 0:8], vsm[:], idn[0:8, 0:8]), "eT")
        on("tensor", t_g2)
        def v_g2c(v, tok=P.cnt["eT"]):
            wait(v, "eT", tok)
            v.memset(ones1b[:], 1.0)
            inc(v.tensor_copy(g2c[:], pb[0][0:128, 0:8]), "eV")
        on("vector", v_g2c)
        P.bank_set("pb0", "eV")
        def s_b2(s):
            wait(s, "eT", P.cnt["eT"])
            inc(s.dma_start(out=vsm[:], in_=ln2b_d), "qS", 16)
        on("sync", s_b2)
        def t_b2(t, tok=P.cnt["qS"]):
            wait(t, "qS", tok)
            P.bank_wait(t, "pb1")
            inc(t.transpose(pb[1][0:128, 0:8], vsm[:], idn[0:8, 0:8]), "eT")
        on("tensor", t_b2)
        def v_b2c(v, tok=P.cnt["eT"]):
            wait(v, "eT", tok)
            inc(v.tensor_copy(b2cb[:], pb[1][0:128, 0:8]), "eV")
        on("vector", v_b2c)
        P.bank_set("pb1", "eV")
        for cc3 in range(3):
            def t_bq(t, cc3=cc3, tokw=tok_wq):
                if cc3 == 0:
                    wait(t, "qG", tokw)
                    wait(t, "eV", P.cnt["eV"])
                P.bank_wait(t, f"pb{2 + cc3}")
                for dc in range(8):
                    ins = t.matmul(pb[2 + cc3][0:1, 0:512], b2cb[:, dc:dc + 1],
                                   wq_sb[:, 1536 * dc + 512 * cc3:1536 * dc + 512 * (cc3 + 1)],
                                   start=(dc == 0), stop=(dc == 7))
                inc(ins, "eT")
            on("tensor", t_bq)
            def v_bq(v, cc3=cc3, tok=P.cnt["eT"]):
                wait(v, "eT", tok)
                inc(v.tensor_tensor(beffq[0:1, 512 * cc3:512 * (cc3 + 1)],
                                    pb[2 + cc3][0:1, 0:512],
                                    beffq[0:1, 512 * cc3:512 * (cc3 + 1)], ALU.add), "eV")
            on("vector", v_bq)
            P.bank_set(f"pb{2 + cc3}", "eV")
        wqpre_d = nc.dram_tensor("d_wqpre", [128, 8 * 1536], BF16, kind="ExternalOutput").ap() if upto == "b4" else None
        if upto == "b4":
            dbg_outs["d_wqpre"] = wqpre_d
            def g_pre(gp):
                wait_all(gp)
                inc(gp.dma_start(out=wqpre_d, in_=wq_sb[:]), "qG", 16)
            on("gpsimd", g_pre)
        def v_fold(v, tok=P.cnt["eV"]):
            if upto == "b4":
                wait(v, "qG", P.cnt["qG"])
            wait(v, "eV", tok)
            for dc in range(8):
                ins = v.tensor_scalar(wq_sb[:, 1536 * dc:1536 * (dc + 1)],
                                      wq_sb[:, 1536 * dc:1536 * (dc + 1)],
                                      g2c[:, dc:dc + 1], None, ALU.mult)
            inc(ins, "eV")
        on("vector", v_fold)
        def g_bq(gp, tok=P.cnt["eV"]):
            wait(gp, "eV", tok)
            inc(gp.dma_start(out=row_d[R_T1, 0:1536].rearrange("(a b) -> a b", a=1), in_=beffq[:]), "qG", 16)
        on("gpsimd", g_bq)
        def s_bqT(s, tok=P.cnt["qG"]):
            wait(s, "qG", tok)
            inc(s.dma_start(out=vsm32[0:12, :], in_=row_d[R_T1, 0:1536].rearrange("(a b) -> a b", a=12)), "qS", 16)
        on("sync", s_bqT)
        def t_bqT(t, tok=P.cnt["qS"]):
            wait(t, "qS", tok)
            P.bank_wait(t, "pb5")
            inc(t.transpose(pb[5][0:128, 0:12], vsm32[0:12, :], idn[0:12, 0:12]), "eT")
        on("tensor", t_bqT)
        def v_bqT(v, tok=P.cnt["eT"]):
            wait(v, "eT", tok)
            inc(v.tensor_copy(beffqT[:], pb[5][0:128, 0:12]), "eV")
        on("vector", v_bqT)
        P.bank_set("pb5", "eV")
        def t_kvb(t, tok=P.cnt["eV"]):
            wait(t, "eV", tok)
            P.bank_wait(t, "pb6"); P.bank_wait(t, "pb7")
            t.matmul(pb[6][0:128, 0:512], ones1f[:], beffq[0:1, 512:1024], start=True, stop=True)
            inc(t.matmul(pb[7][0:128, 0:512], ones1f[:], beffq[0:1, 1024:1536], start=True, stop=True), "eT")
        on("tensor", t_kvb)
        def v_kvb(v, tok=P.cnt["eT"]):
            wait(v, "eT", tok)
            v.tensor_copy(kvbc[:, 0:512], pb[6][0:128, 0:512])
            inc(v.tensor_copy(kvbc[:, 512:1024], pb[7][0:128, 0:512]), "eV")
        on("vector", v_kvb)
        P.bank_set("pb6", "eV"); P.bank_set("pb7", "eV")
        def v_wi(v):
            v.memset(Wst[:], 0.0)
            v.memset(Wbf[0][:], 0.0)
            inc(v.memset(Wbf[1][:], 0.0), "eV")
        on("vector", v_wi)
        P.barrier()

        P.scope("hebbian")
        b2b_tok = {}
        for g in range(4):
            t0 = 1024 * g
            for tt in range(8):
                i = 8 * g + tt
                sl = i % 2
                yv = yring[:, D * sl:D * (sl + 1)]
                def s_y2(s, i=i, yv=yv):
                    if i < 2:
                        wait(s, "eA", P.cnt["eA"])
                        wait(s, "qG", P.cnt["qG"])
                    else:
                        wait(s, "eT", b2b_tok[i - 2])
                    inc(s.dma_start(out=yv, in_=y_d[128 * i:128 * (i + 1), :]), "qS", 16)
                on("sync", s_y2)
                tok_y = P.cnt["qS"]
                def v_x2(v, i=i, yv=yv, tok=tok_y):
                    if i == 0:
                        wait(v, "eV", tokR2)
                    wait(v, "qS", tok)
                    if i >= 1:
                        wait(v, "eT", b2b_tok[i - 1])
                    inc(v.tensor_scalar(junkV[:], yv, m2a[:, i:i + 1], r2a[:, i:i + 1],
                                        ALU.subtract, ALU.mult), "eV")
                on("vector", v_x2)
                tokXN2 = P.cnt["eV"]
                for c in range(8):
                    bk = c % 4
                    def t_t2(t, c=c, bk=bk, tok=tokXN2):
                        if c == 0:
                            wait(t, "eV", tok)
                        P.bank_wait(t, f"pb{bk}")
                        inc(t.transpose(pb[bk][:, 0:128], junkV[:, 128 * c:128 * (c + 1)], idn[:]), "eT")
                    on("tensor", t_t2)
                    def v_e2(v, c=c, bk=bk, tt=tt, tok=P.cnt["eT"]):
                        wait(v, "eT", tok)
                        inc(v.tensor_copy(xn2g[:, 1024 * c + 128 * tt:1024 * c + 128 * (tt + 1)],
                                          pb[bk][:, 0:128]), "eV")
                    on("vector", v_e2)
                    P.bank_set(f"pb{bk}", "eV")
                b2b_tok[i] = P.cnt["eT"]
            tok_grp_x = P.cnt["eV"]
            for cc in range(8):
                for h2 in range(2):
                    bkq = h2
                    def t_qf(t, cc=cc, h2=h2, bkq=bkq, tok=tok_grp_x):
                        if cc == 0 and h2 == 0:
                            wait(t, "eV", tok)
                        P.bank_wait(t, f"pb{bkq}")
                        for dc in range(8):
                            ins = t.matmul(pb[bkq][0:128, 0:512],
                                           wq_sb[:, 1536 * dc + 128 * cc:1536 * dc + 128 * (cc + 1)],
                                           xn2g[:, 1024 * dc + 512 * h2:1024 * dc + 512 * (h2 + 1)],
                                           start=(dc == 0), stop=(dc == 7))
                        inc(ins, "eT")
                    on("tensor", t_qf)
                    def a_qf(a, cc=cc, h2=h2, bkq=bkq, tok=P.cnt["eT"]):
                        wait(a, "eT", tok)
                        inc(a.activation(qkfm[:, 1024 * cc + 512 * h2:1024 * cc + 512 * (h2 + 1)],
                                         pb[bkq][0:128, 0:512], AF.Identity,
                                         bias=beffqT[:, cc:cc + 1]), "eA")
                    on("scalar", a_qf)
                    P.bank_set(f"pb{bkq}", "eA")
            for tt in range(8):
                def t_kv(t, tt=tt):
                    P.bank_wait(t, "pb2"); P.bank_wait(t, "pb3")
                    for dc in range(8):
                        t.matmul(pb[2][0:128, 0:512],
                                 xn2g[:, 1024 * dc + 128 * tt:1024 * dc + 128 * (tt + 1)],
                                 wq_sb[:, 1536 * dc + 512:1536 * dc + 1024],
                                 start=(dc == 0), stop=(dc == 7))
                        ins = t.matmul(pb[3][0:128, 0:512],
                                       xn2g[:, 1024 * dc + 128 * tt:1024 * dc + 128 * (tt + 1)],
                                       wq_sb[:, 1536 * dc + 1024:1536 * dc + 1536],
                                       start=(dc == 0), stop=(dc == 7))
                    inc(ins, "eT")
                on("tensor", t_kv)
                def v_kv(v, tt=tt, tok=P.cnt["eT"]):
                    wait(v, "eT", tok)
                    v.tensor_tensor(ktok[:, 512 * tt:512 * (tt + 1)], pb[2][0:128, 0:512],
                                    kvbc[:, 0:512], ALU.add)
                    inc(v.tensor_tensor(vtok[:, 512 * tt:512 * (tt + 1)], pb[3][0:128, 0:512],
                                        kvbc[:, 512:1024], ALU.add), "eV")
                on("vector", v_kv)
                P.bank_set("pb2", "eV"); P.bank_set("pb3", "eV")
            tok_qkv = (P.cnt["eA"], P.cnt["eV"])
            for tt in range(8):
                c = 8 * g + tt
                par = c % 2
                tok_p = {}
                def v_sc(v, c=c, tt=tt, tok=tok_qkv):
                    wait(v, "eA", tok[0]); wait(v, "eV", tok[1])
                    if c >= 1:
                        v.tensor_copy(Wbf[c % 2][:, 192:256], Wst[:, 192:256])
                    for pp in range(4):
                        v.tensor_tensor(qts[:, 128 * pp:128 * (pp + 1)],
                                        qkfm[:, 1024 * pp + 128 * tt:1024 * pp + 128 * (tt + 1)],
                                        embc[:, 128 * c:128 * (c + 1)], ALU.mult)
                        ins = v.tensor_tensor(kts[:, 128 * pp:128 * (pp + 1)],
                                              qkfm[:, 1024 * (4 + pp) + 128 * tt:1024 * (4 + pp) + 128 * (tt + 1)],
                                              epbc[:, 128 * c:128 * (c + 1)], ALU.mult)
                        tok_p[pp] = inc(ins, "eV")
                    v.tensor_scalar(khat[:], ktok[:, 512 * tt:512 * (tt + 1)], epT[:, c:c + 1], None, ALU.mult)
                    inc(v.tensor_scalar(vhat[:], vtok[:, 512 * tt:512 * (tt + 1)], gcbc[:, c:c + 1], None, ALU.mult), "eV")
                on("vector", v_sc)
                tok_sc = P.cnt["eV"]
                for pp in range(4):
                    pm = pp % 2
                    bse, bso = 2 * pm, 2 * pm + 1
                    bU, bO = 4 + pm, 6 + pm
                    so = 256 * pm
                    def t_s(t, pp=pp, bse=bse, bso=bso, tok_p=tok_p):
                        wait(t, "eV", tok_p[pp])
                        P.bank_wait(t, f"pb{bse}"); P.bank_wait(t, f"pb{bso}")
                        t.matmul(pb[bse][0:128, 0:128], kts[0:64, 128 * pp:128 * (pp + 1)],
                                 qts[0:64, 128 * pp:128 * (pp + 1)], start=True, stop=True)
                        inc(t.matmul(pb[bso][0:128, 0:128], kts[64:128, 128 * pp:128 * (pp + 1)],
                                     qts[64:128, 128 * pp:128 * (pp + 1)], start=True, stop=True), "eT")
                    on("tensor", t_s)
                    def v_m(v, bse=bse, bso=bso, so=so, tok=P.cnt["eT"]):
                        wait(v, "eT", tok)
                        v.tensor_tensor(sdt[:, so:so + 128], pb[bse][0:128, 0:128], trilT[:], ALU.mult)
                        inc(v.tensor_tensor(sdt[:, so + 128:so + 256], pb[bso][0:128, 0:128], trilT[:], ALU.mult), "eV")
                    on("vector", v_m)
                    P.bank_set(f"pb{bse}", "eV"); P.bank_set(f"pb{bso}", "eV")
                    tok_sdt = P.cnt["eV"]
                    def t_uo(t, pp=pp, c=c, tt=tt, bU=bU, bO=bO, so=so, tok=tok_sdt):
                        wait(t, "eV", tok)
                        P.bank_wait(t, f"pb{bU}"); P.bank_wait(t, f"pb{bO}")
                        t.matmul(pb[bU][0:64, 0:64], khat[:, 128 * pp:128 * pp + 64],
                                 vhat[:, 128 * pp:128 * pp + 64], start=True, stop=True)
                        inc(t.matmul(pb[bU][64:128, 0:64], khat[:, 128 * pp + 64:128 * (pp + 1)],
                                     vhat[:, 128 * pp + 64:128 * (pp + 1)], start=True, stop=True,
                                     tile_position=(0, 64)), "eT")
                        t.matmul(pb[bO][0:64, 0:128], vtok[:, 512 * tt + 128 * pp:512 * tt + 128 * pp + 64],
                                 sdt[:, so:so + 128], start=True, stop=False)
                        t.matmul(pb[bO][0:64, 0:128], Wbf[c % 2][0:64, 64 * pp:64 * (pp + 1)],
                                 qts[0:64, 128 * pp:128 * (pp + 1)], start=False, stop=True)
                        t.matmul(pb[bO][64:128, 0:128], vtok[:, 512 * tt + 128 * pp + 64:512 * tt + 128 * (pp + 1)],
                                 sdt[:, so + 128:so + 256], start=True, stop=False, tile_position=(0, 64))
                        inc(t.matmul(pb[bO][64:128, 0:128], Wbf[c % 2][64:128, 64 * pp:64 * (pp + 1)],
                                     qts[64:128, 128 * pp:128 * (pp + 1)], start=False, stop=True,
                                     tile_position=(64, 64)), "eT")
                    on("tensor", t_uo)
                    tok_uo = P.cnt["eT"]
                    def a_hb(a, pp=pp, c=c, bO=bO, tok=tok_uo):
                        wait(a, "eT", tok)
                        inc(a.activation(hebfm[:, T * pp + 128 * c:T * pp + 128 * (c + 1)],
                                         pb[bO][0:128, 0:128], AF.Copy), "eA")
                    on("scalar", a_hb)
                    P.bank_set(f"pb{bO}", "eA")
                    def v_wu(v, pp=pp, c=c, bU=bU, tok=tok_uo):
                        wait(v, "eT", tok)
                        v.scalar_tensor_tensor(Wst[0:64, 64 * pp:64 * (pp + 1)],
                                               Wst[0:64, 64 * pp:64 * (pp + 1)],
                                               gcbc[0:64, c:c + 1],
                                               pb[bU][0:64, 0:64], ALU.mult, ALU.add)
                        ins = v.scalar_tensor_tensor(Wst[64:128, 64 * pp:64 * (pp + 1)],
                                                     Wst[64:128, 64 * pp:64 * (pp + 1)],
                                                     gcbc[64:128, c:c + 1],
                                                     pb[bU][64:128, 0:64], ALU.mult, ALU.add)
                        if pp >= 1:
                            ins = v.tensor_copy(Wbf[(c + 1) % 2][:, 64 * (pp - 1):64 * pp],
                                                Wst[:, 64 * (pp - 1):64 * pp])
                        inc(ins, "eV")
                    on("vector", v_wu)
                    P.bank_set(f"pb{bU}", "eV")

        if upto == "b2":
            dxn2 = dout("d_xn2fm", [128, 8 * 512])
            dy1 = dout("d_y1", [T, D])
            def g_db2(gp):
                wait_all(gp)
                for c in range(8):
                    inc(gp.dma_start(out=dxn2[:, 512 * c:512 * (c + 1)],
                                     in_=xn2g[:, 1024 * c + 512:1024 * (c + 1)]), "qG", 16)
                for i in range(NT):
                    inc(gp.dma_start(out=dy1[128 * i:128 * (i + 1), :],
                                     in_=y_d[128 * i:128 * (i + 1), :]), "qG", 16)
                wait_all(gp, "qG")
            on("gpsimd", g_db2)
            P.barrier()
            esB3.close()
            esB2.close()
            esBH.close()
            esB.close()
            return nc, dbg_outs

        if upto == "b4":
            dheb = dout("d_heb", [128, 4 * T], BF16)
            dqkf = dout("d_qkfm", [128, 8 * 1024], BF16)
            dktok = dout("d_ktok", [128, 8 * 512], BF16)
            dvtok = dout("d_vtok", [128, 8 * 512], BF16)
            dqts = dout("d_qts", [128, 512], BF16)
            dkts = dout("d_kts", [128, 512], BF16)
            dsdt = dout("d_sdt", [128, 256], BF16)
            dwst = dout("d_wst", [128, 256])
            dwq = dout("d_wq", [128, 8 * 1536], BF16)
            dg2c = dout("d_g2c", [128, 8])
            def g_db4(gp):
                wait_all(gp)
                inc(gp.dma_start(out=dheb, in_=hebfm[:]), "qG", 16)
                inc(gp.dma_start(out=dqkf, in_=qkfm[:]), "qG", 16)
                inc(gp.dma_start(out=dktok, in_=ktok[:]), "qG", 16)
                inc(gp.dma_start(out=dvtok, in_=vtok[:]), "qG", 16)
                inc(gp.dma_start(out=dqts, in_=qts[:]), "qG", 16)
                inc(gp.dma_start(out=dkts, in_=kts[:]), "qG", 16)
                inc(gp.dma_start(out=dsdt, in_=sdt[:, 0:256]), "qG", 16)
                inc(gp.dma_start(out=dwst, in_=Wst[:]), "qG", 16)
                inc(gp.dma_start(out=dwq, in_=wq_sb[:]), "qG", 16)
                inc(gp.dma_start(out=dg2c, in_=g2c[:]), "qG", 16)
                wait_all(gp, "qG")
            on("gpsimd", g_db4)
            P.barrier()
            esB3.close()
            esB2.close()
            esBH.close()
            esB.close()
            return nc, dbg_outs

        P.scope("b5a")
        P.barrier()
        esB3.close()
        esB5a = contextlib.ExitStack()
        wout_sb = sb("wout_sb", [128, 4 * D], BF16, st=esB5a)
        boutrow = sb("boutrow", [1, D], st=esB5a)
        boutb = sb("boutb", [1, D], BF16, st=esB5a)
        ones5b = sb("ones5b", [1, 128], BF16, st=esB5a)

        def g_wo(gp):
            for pp in range(4):
                inc(gp.dma_start(out=wout_sb[:, D * pp:D * (pp + 1)],
                                 in_=wout_d[128 * pp:128 * (pp + 1), :]), "qG", 16)
        on("gpsimd", g_wo)
        wo_tok = P.cnt["qG"]
        def s_bo(s):
            inc(s.dma_start(out=boutrow[:], in_=bout_d), "qS", 16)
        on("sync", s_bo)
        def v_cb(v, tok=P.cnt["qS"]):
            wait(v, "qS", tok)
            v.memset(ones5b[:], 1.0)
            inc(v.tensor_copy(boutb[:], boutrow[:]), "eV")
        on("vector", v_cb)
        cb_tok = P.cnt["eV"]
        if upto == "wcv":
            dw1 = dout("d_w1b", [128, FF], BF16)
            dw2 = dout("d_w2b", [128, D], BF16)
            def g_dw(gp):
                wait_all(gp)
                inc(gp.dma_start(out=dw1, in_=wff1b_d[0:128, :]), "qG", 16)
                inc(gp.dma_start(out=dw2, in_=wff2b_d[256:384, :]), "qG", 16)
                wait_all(gp, "qG")
            on("gpsimd", g_dw)
            P.barrier()
            esB5a.close()
            esB2.close()
            esBH.close()
            esB.close()
            return nc, dbg_outs
        b5_tok = {}
        for i in range(NT):
            sl = i % 3
            yv = yring[:, D * sl:D * (sl + 1)]
            bk0, bk1 = (6, 7) if i % 2 == 0 else (4, 5)
            def s_y5(s, i=i, yv=yv):
                if i >= 3:
                    wait(s, "eA", b5_tok[i - 3][0])
                    wait(s, "qG", b5_tok[i - 3][1])
                inc(s.dma_start(out=yv, in_=y_d[128 * i:128 * (i + 1), :]), "qS", 16)
            on("sync", s_y5)
            tok_y = P.cnt["qS"]
            def t_wo(t, i=i, bk0=bk0, bk1=bk1, tok=tok_y):
                if i == 0:
                    wait(t, "qG", wo_tok)
                    wait(t, "eV", cb_tok)
                P.bank_wait(t, "pb%d" % bk0); P.bank_wait(t, "pb%d" % bk1)
                for h2 in range(2):
                    bkk = bk0 if h2 == 0 else bk1
                    for pp in range(4):
                        t.matmul(pb[bkk][0:128, 0:512],
                                 hebfm[:, T * pp + 128 * i:T * pp + 128 * (i + 1)],
                                 wout_sb[:, D * pp + 512 * h2:D * pp + 512 * (h2 + 1)],
                                 start=(pp == 0), stop=False)
                    ins = t.matmul(pb[bkk][0:128, 0:512], ones5b[:],
                                   boutb[0:1, 512 * h2:512 * (h2 + 1)], start=False, stop=True)
                inc(ins, "eT")
            on("tensor", t_wo)
            def v_y2(v, i=i, yv=yv, bk0=bk0, bk1=bk1, tok=P.cnt["eT"], toky=tok_y):
                wait(v, "eT", tok)
                wait(v, "qS", toky)
                v.tensor_tensor(yv[:, 0:512], pb[bk0][0:128, 0:512], yv[:, 0:512], ALU.add)
                inc(v.tensor_tensor(yv[:, 512:1024], pb[bk1][0:128, 0:512], yv[:, 512:1024], ALU.add), "eV")
            on("vector", v_y2)
            P.bank_set("pb%d" % bk0, "eV"); P.bank_set("pb%d" % bk1, "eV")
            tok_y2 = P.cnt["eV"]
            def g_y2(gp, i=i, yv=yv, tok=tok_y2):
                wait(gp, "eV", tok)
                inc(gp.dma_start(out=y_d[128 * i:128 * (i + 1), :], in_=yv), "qG", 16)
            on("gpsimd", g_y2)
            def a_s3(a, i=i, yv=yv, tok=tok_y2):
                wait(a, "eV", tok)
                a.activation(junkA[:], yv, AF.Copy, scale=INV_D, accum_out=m3a[:, i:i + 1])
                inc(a.activation(junkA2[:], yv, AF.Square, scale=INV_SQ_D,
                                 accum_out=q3a[:, i:i + 1]), "eA")
            on("scalar", a_s3)
            b5_tok[i] = (P.cnt["eA"], P.cnt["qG"])

        def v_f3(v, tok=P.cnt["eA"]):
            wait(v, "eA", tok)
            ch(v, "eV", v.tensor_tensor(st1[:], m3a[:], m3a[:], ALU.mult))
            inc(v.tensor_tensor(st2[:], q3a[:], st1[:], ALU.subtract), "eV")
        on("vector", v_f3)
        def a_f3(a, tok=P.cnt["eV"]):
            wait(a, "eV", tok)
            inc(a.activation(st1[:], st2[:], AF.Sqrt, bias=epscol[:, 0:1]), "eA")
        on("scalar", a_f3)
        def v_f3b(v, tok=P.cnt["eA"]):
            wait(v, "eA", tok)
            inc(v.reciprocal(r3a[:], st1[:]), "eV")
        on("vector", v_f3b)

        if upto == "b5":
            dy2 = dout("d_y2", [T, D])
            def g_db5(gp):
                wait_all(gp)
                for i in range(NT):
                    inc(gp.dma_start(out=dy2[128 * i:128 * (i + 1), :],
                                     in_=y_d[128 * i:128 * (i + 1), :]), "qG", 16)
                wait_all(gp, "qG")
            on("gpsimd", g_db5)
            P.barrier()
            esB5a.close()
            esB2.close()
            esBH.close()
            esB.close()
            return nc, dbg_outs

        P.barrier()
        esB5a.close()
        esB2.close()
        esBH.close()

        P.scope("moe")
        esT = contextlib.ExitStack()
        gt = sb("gt", [128, 2 * D], st=esT)
        xstg = sb("xstg", [128, 8 * 128], BF16, st=esT)
        yring2 = sb("yring2", [128, 2 * D], st=esT)
        jxb = sb("jxb", [128, D], BF16, st=esT)
        esT2 = contextlib.ExitStack()
        G3bc = sb("G3bc", [128, D], st=esT2)
        B3bc = sb("B3bc", [128, D], st=esT2)
        lrow = sb("lrow", [1, D], st=esT2)

        P.scope("b5b")
        for (gsrc, dstb) in ((ln3g_d, G3bc), (ln3b_d, B3bc)):
            def s_l3(s, gsrc=gsrc):
                wait(s, "eT", P.cnt["eT"])
                inc(s.dma_start(out=vsm[:], in_=gsrc), "qS", 16)
            on("sync", s_l3)
            def g_l3(gp, tok=P.cnt["qS"]):
                wait(gp, "qS", tok)
                wait(gp, "qS", P.cnt["qS"])
                inc(gp.dma_start(out=row_d[R_T1, 0:D].rearrange("(a b) -> a b", a=8), in_=vsm[:]), "qG", 16)
            on("gpsimd", g_l3)
            def s_l3b(s, tok=P.cnt["qG"]):
                wait(s, "qG", tok)
                inc(s.dma_start(out=lrow[:], in_=row_d[R_T1:R_T1 + 1, 0:D]), "qS", 16)
            on("sync", s_l3b)
            def t_l3(t, tok=P.cnt["qS"]):
                wait(t, "qS", tok)
                P.bank_wait(t, "pb0"); P.bank_wait(t, "pb1")
                t.matmul(pb[0][0:128, 0:512], ones1f[:], lrow[0:1, 0:512], start=True, stop=True)
                inc(t.matmul(pb[1][0:128, 0:512], ones1f[:], lrow[0:1, 512:1024], start=True, stop=True), "eT")
            on("tensor", t_l3)
            def v_l3x(v, dstb=dstb, tok=P.cnt["eT"]):
                wait(v, "eT", tok)
                v.tensor_copy(dstb[:, 0:512], pb[0][0:128, 0:512])
                inc(v.tensor_copy(dstb[:, 512:1024], pb[1][0:128, 0:512]), "eV")
            on("vector", v_l3x)
            P.bank_set("pb0", "eV"); P.bank_set("pb1", "eV")

        b5b_tok = {}
        xstg_war = {}
        xn3st_tok = {}
        def emit_b5b_tile(i):
            sl = i % 2
            yv = yring2[:, D * sl:D * (sl + 1)]
            def s_y6(s, i=i, yv=yv):
                if i >= 2:
                    wait(s, "qA", b5b_tok[i - 2])
                inc(s.dma_start(out=yv, in_=y_d[128 * i:128 * (i + 1), :]), "qS", 16)
            on("sync", s_y6)
            tok_y = P.cnt["qS"]
            def v_x3(v, i=i, yv=yv, tok=tok_y, jw=(b5b_tok.get(i - 1))):
                if i == 0:
                    wait(v, "eV", P.cnt["eV"])
                wait(v, "qS", tok)
                if jw is not None:
                    wait(v, "qA", jw)
                ch(v, "eV", v.tensor_scalar(junkV[:], yv, m3a[:, i:i + 1], r3a[:, i:i + 1],
                                            ALU.subtract, ALU.mult))
                ch(v, "eV", v.tensor_tensor(junkA[:], junkV[:], G3bc[:], ALU.mult))
                inc(v.tensor_tensor(jxb[:], junkA[:], B3bc[:], ALU.add), "eV")
            on("vector", v_x3)
            tokXN3 = P.cnt["eV"]
            for c in range(8):
                def a_t3(a, c=c, i=i, tok=tokXN3, war=xstg_war.get(c)):
                    if c == 0:
                        wait(a, "eV", tok)
                    if war is not None:
                        wait(a, "qG", war)
                    inc(a.dma_start_transpose(out=xstg[:, 128 * c:128 * (c + 1)],
                                              in_=jxb[:, 128 * c:128 * (c + 1)]), "qA", 16)
                on("scalar", a_t3)
                def g_x3s(gp, c=c, i=i, tok=P.cnt["qA"]):
                    wait(gp, "qA", tok)
                    inc(gp.dma_start(out=xn3_d[128 * c:128 * (c + 1), 128 * i:128 * (i + 1)],
                                     in_=xstg[:, 128 * c:128 * (c + 1)]), "qG", 16)
                on("gpsimd", g_x3s)
                xstg_war[c] = P.cnt["qG"]
            b5b_tok[i] = P.cnt["qA"]
            xn3st_tok[i] = P.cnt["qG"]

        P.scope("moe")
        esC = contextlib.ExitStack()
        xlnT = sb("xlnT", [128, 2 * 8 * CAP], BF16, st=esC)
        hbuf = sb("hbuf", [128, 32 * CAP], BF16, st=esC)
        w1sl = sb("w1sl", [128, 2 * 4096], BF16, st=esC)
        w2res = sb("w2res", [128, 32 * 512], BF16, st=esC)
        y2r = sb("y2r", [128, 3 * 512], st=esC)
        glnT = sb("glnT", [128, 16], st=esC)
        blnT = sb("blnT", [128, 16], st=esC)
        b1T = sb("b1T", [128, 64], st=esC)
        b2rowb = sb("b2rowb", [1, 2 * D], BF16, st=esC)
        b2row = sb("b2row", [1, D], st=esC)
        onesCb = sb("onesCb", [1, 128], BF16, st=esC)
        def v_oc(v):
            inc(v.memset(onesCb[:], 1.0), "eV")
        on("vector", v_oc)

        W2PREV = [0]
        vsm_free = [(P.cnt["eT"], P.cnt["eV"])]
        vsm_qg = [P.cnt["qG"]]
        mm1_par = {0: 0, 1: 0}
        mm1_prev = [None]
        rd_prev = [None]

        def emit_prep(e, par):
            glnTp = glnT[:, 8 * par:8 * (par + 1)]
            blnTp = blnT[:, 8 * par:8 * (par + 1)]
            b1Tp = b1T[:, 32 * par:32 * (par + 1)]
            b2rowbp = b2rowb[:, D * par:D * (par + 1)]
            def s_pe(s, e=e, tok=vsm_free[0], tokg=vsm_qg[0]):
                wait(s, "eT", tok[0]); wait(s, "eV", tok[1])
                if tokg is not None:
                    wait(s, "qG", tokg)
                inc(s.dma_start(out=vsm[:], in_=mlng_d[e]), "qS", 16)
                inc(s.dma_start(out=vsm32[:], in_=mb1_d[e]), "qS", 16)
                inc(s.dma_start(out=b2row[:], in_=mb2_d[e]), "qS", 16)
            on("sync", s_pe)
            def t_pe(t, tok=P.cnt["qS"]):
                wait(t, "qS", tok)
                P.bank_wait(t, "pb0")
                t.transpose(pb[0][0:128, 0:8], vsm[:], idn[0:8, 0:8])
                inc(t.transpose(pb[0][0:128, 16:48], vsm32[:], idn[0:32, 0:32]), "eT")
            on("tensor", t_pe)
            def v_pe(v, tok=P.cnt["eT"]):
                wait(v, "eT", tok)
                v.tensor_copy(glnTp[:], pb[0][0:128, 0:8])
                v.tensor_copy(b1Tp[:], pb[0][0:128, 16:48])
                inc(v.tensor_copy(b2rowbp[:], b2row[:]), "eV")
            on("vector", v_pe)
            P.bank_set("pb0", "eV")
            b2_rd = P.cnt["eV"]
            def s_pe2(s, e=e, tok=P.cnt["eT"]):
                wait(s, "eT", tok)
                inc(s.dma_start(out=vsm[:], in_=mlnb_d[e]), "qS", 16)
            on("sync", s_pe2)
            def t_pe2(t, tok=P.cnt["qS"]):
                wait(t, "qS", tok)
                P.bank_wait(t, "pb2")
                inc(t.transpose(pb[2][0:128, 0:8], vsm[:], idn[0:8, 0:8]), "eT")
            on("tensor", t_pe2)
            vsm_free[0] = (P.cnt["eT"], b2_rd)
            def v_pe2(v, tok=P.cnt["eT"]):
                wait(v, "eT", tok)
                inc(v.tensor_copy(blnTp[:], pb[2][0:128, 0:8]), "eV")
            on("vector", v_pe2)
            P.bank_set("pb2", "eV")
            xbase = 8 * CAP * par
            war = mm1_par[par]
            for dc in range(8):
                def s_xT(s, e=e, dc=dc, war=war):
                    if dc == 0:
                        wait(s, "eT", war)
                    inc(s.dma_start_transpose(
                        out=xlnT[:, xbase + CAP * dc:xbase + CAP * (dc + 1)],
                        in_=xs_d[CAP * e:CAP * (e + 1), 128 * dc:128 * (dc + 1)]), "qS", 16)
                on("sync", s_xT)
            tok_xT = P.cnt["qS"]
            def v_aff(v, tok=tok_xT):
                wait(v, "qS", tok)
                for dc in range(8):
                    ins = v.tensor_scalar(xlnT[:, xbase + CAP * dc:xbase + CAP * (dc + 1)],
                                          xlnT[:, xbase + CAP * dc:xbase + CAP * (dc + 1)],
                                          glnTp[:, dc:dc + 1], blnTp[:, dc:dc + 1],
                                          ALU.mult, ALU.add)
                inc(ins, "eV")
            on("vector", v_aff)
            return P.cnt["eV"]

        def emit_mm1(e, par, tok_xlnT):
            xbase = 8 * CAP * par
            b1Tp = b1T[:, 32 * par:32 * (par + 1)]
            prev = mm1_prev[0]
            w1_tok = {}
            for fsl in range(8):
                slw = fsl % 2
                def g_w1(gp, e=e, fsl=fsl, slw=slw, prev=prev):
                    if fsl >= 2:
                        wait(gp, "eT", w1_tok[fsl - 2])
                    elif prev is not None:
                        wait(gp, "eT", prev[fsl + 6])
                    else:
                        wait(gp, "eT", P.cnt["eT"])
                    inc(gp.dma_start(out=w1sl.rearrange("p (s dc f) -> p s dc f", s=2, dc=8)[:, slw],
                                     in_=mw1_d[e].rearrange("(dc p) f -> p dc f", p=128)[:, :, 512 * fsl:512 * (fsl + 1)],
                                     max_dma_last_dim=8192), "qG", 16)
                on("gpsimd", g_w1)
                tok_w1 = P.cnt["qG"]
                for fs4 in range(4):
                    fs = 4 * fsl + fs4
                    def t_h(t, fs=fs, fs4=fs4, slw=slw, tok=tok_w1, tokx=tok_xlnT):
                        if fs == 0:
                            wait(t, "eV", tokx)
                        wait(t, "qG", tok)
                        P.bank_wait(t, "pb6"); P.bank_wait(t, "pb7")
                        for dc in range(8):
                            t.matmul(pb[6][0:128, 0:CAP // 2],
                                     w1sl[:, 4096 * slw + 512 * dc + 128 * fs4:4096 * slw + 512 * dc + 128 * (fs4 + 1)],
                                     xlnT[:, xbase + CAP * dc + 0:xbase + CAP * dc + CAP // 2],
                                     start=(dc == 0), stop=(dc == 7))
                            ins = t.matmul(pb[7][0:128, 0:CAP // 2],
                                           w1sl[:, 4096 * slw + 512 * dc + 128 * fs4:4096 * slw + 512 * dc + 128 * (fs4 + 1)],
                                           xlnT[:, xbase + CAP * dc + CAP // 2:xbase + CAP * dc + CAP],
                                           start=(dc == 0), stop=(dc == 7))
                        inc(ins, "eT")
                    on("tensor", t_h)
                    w1_tok[fsl] = P.cnt["eT"]
                    if True:
                        def v_h(v, fs=fs, tok=P.cnt["eT"]):
                            wait(v, "eT", tok)
                            v.tensor_scalar(hbuf[:, CAP * fs:CAP * fs + CAP // 2],
                                            pb[6][0:128, 0:CAP // 2],
                                            b1Tp[:, fs:fs + 1], 0.0, ALU.add, ALU.max)
                            inc(v.tensor_scalar(hbuf[:, CAP * fs + CAP // 2:CAP * (fs + 1)],
                                                pb[7][0:128, 0:CAP // 2],
                                                b1Tp[:, fs:fs + 1], 0.0, ALU.add, ALU.max), "eV")
                        on("vector", v_h)
                        P.bank_set("pb6", "eV"); P.bank_set("pb7", "eV")
                    else:
                        def a_h(a, fs=fs, tok=P.cnt["eT"]):
                            wait(a, "eT", tok)
                            a.activation(hbuf[:, CAP * fs:CAP * fs + CAP // 2], pb[6][0:128, 0:CAP // 2],
                                         AF.Relu, bias=b1Tp[:, fs:fs + 1])
                            inc(a.activation(hbuf[:, CAP * fs + CAP // 2:CAP * (fs + 1)], pb[7][0:128, 0:CAP // 2],
                                             AF.Relu, bias=b1Tp[:, fs:fs + 1]), "eA")
                        on("scalar", a_h)
                        P.bank_set("pb6", "eA"); P.bank_set("pb7", "eA")
            mm1_prev[0] = w1_tok
            mm1_par[par] = P.cnt["eT"]
            return (P.cnt["eA"], P.cnt["eV"])

        ye_store = [0, 0, 0]
        def emit_mm2(e, par, tok_h):
            b2rowbp = b2rowb[:, D * par:D * (par + 1)]
            rd_new = {}
            for h2 in range(2):
                tokq = {}
                for q_ in range(4):
                    def g_w2(gp, e=e, h2=h2, q_=q_):
                        if h2 == 0:
                            if rd_prev[0] is not None:
                                wait(gp, "eT", rd_prev[0][q_])
                            else:
                                wait(gp, "eT", W2PREV[0])
                        else:
                            wait(gp, "eT", rd_new[(0, q_)])
                        inc(gp.dma_start(
                            out=w2res.rearrange("p (c f) -> p c f", c=32)[:, 8 * q_:8 * (q_ + 1)],
                            in_=mw2_d[e].rearrange("(c p) f -> p c f", p=128)[:, 8 * q_:8 * (q_ + 1),
                                                                             512 * h2:512 * (h2 + 1)],
                            max_dma_last_dim=8192), "qG", 16)
                    on("gpsimd", g_w2)
                    tokq[q_] = P.cnt["qG"]
                for q_ in range(4):
                    def t_y(t, q_=q_, h2=h2, tok=tokq[q_], tokh=tok_h):
                        if q_ == 0 and h2 == 0:
                            wait(t, "eA", tokh[0]); wait(t, "eV", tokh[1])
                        wait(t, "qG", tok)
                        for st_ in range(CAP // 128):
                            if q_ == 0:
                                P.bank_wait(t, f"pb{st_}")
                            for fc in range(8 * q_, 8 * (q_ + 1)):
                                ins = t.matmul(pb[st_][0:128, 0:512],
                                               hbuf[:, CAP * fc + 128 * st_:CAP * fc + 128 * (st_ + 1)],
                                               w2res[:, 512 * fc:512 * (fc + 1)],
                                               start=(fc == 0), stop=False)
                        if q_ == 3:
                            for st_ in range(CAP // 128):
                                ins = t.matmul(pb[st_][0:128, 0:512], onesCb[:],
                                               b2rowbp[0:1, 512 * h2:512 * (h2 + 1)],
                                               start=False, stop=True)
                        inc(ins, "eT")
                    on("tensor", t_y)
                    rd_new[(h2, q_)] = P.cnt["eT"]
                tok_mm2 = P.cnt["eT"]
                for st_ in range(CAP // 128):
                    sl = st_ % 3
                    yslot = y2r[:, 512 * sl:512 * (sl + 1)]
                    if st_ % 2 == 0:
                        def v_ye(v, st_=st_, yslot=yslot, tok=tok_mm2, tokst=ye_store[sl]):
                            wait(v, "eT", tok)
                            wait(v, "qS", tokst)
                            inc(v.tensor_copy(yslot, pb[st_][0:128, 0:512]), "eV")
                        on("vector", v_ye)
                        P.bank_set(f"pb{st_}", "eV")
                        ev_tok = ("eV", P.cnt["eV"])
                    else:
                        def a_ye(a, st_=st_, yslot=yslot, tok=tok_mm2, tokst=ye_store[sl]):
                            wait(a, "eT", tok)
                            wait(a, "qS", tokst)
                            inc(a.activation(yslot, pb[st_][0:128, 0:512], AF.Copy), "eA")
                        on("scalar", a_ye)
                        P.bank_set(f"pb{st_}", "eA")
                        ev_tok = ("eA", P.cnt["eA"])
                    def s_ye(s, e=e, st_=st_, yslot=yslot, h2=h2, tok=ev_tok):
                        wait(s, tok[0], tok[1])
                        inc(s.dma_start(out=y2s_d[CAP * e + 128 * st_:CAP * e + 128 * (st_ + 1),
                                                  512 * h2:512 * (h2 + 1)],
                                        in_=yslot), "qS", 16)
                    on("sync", s_ye)
                    ye_store[sl] = P.cnt["qS"]
            rd_prev[0] = {q_: rd_new[(1, q_)] for q_ in range(4)}
            W2PREV[0] = P.cnt["eT"]

        tok_x = emit_prep(0, 0)
        for e in range(NE):
            par = e % 2
            tok_h = emit_mm1(e, par, tok_x)
            if e + 1 < NE:
                tok_x = emit_prep(e + 1, 1 - par)
            emit_mm2(e, par, tok_h)
        for i5 in range(NT):
            emit_b5b_tile(i5)

        if upto == "c":
            dy2s = dout("d_y2s", [NS, D])
            def g_dbc(gp):
                wait_all(gp)
                for j in range(NS // 128):
                    inc(gp.dma_start(out=dy2s[128 * j:128 * (j + 1), :],
                                     in_=y2s_d[128 * j:128 * (j + 1), :]), "qG", 16)
                wait_all(gp, "qG")
            on("gpsimd", g_dbc)
            P.barrier()
            esC.close()
            esB.close()
            return nc, dbg_outs

        P.barrier()
        esC.close()

        d_tok = {}
        def emit_d_tile(i):
            sl = i % 2
            gv = gt[:, D * sl:D * (sl + 1)]
            def v_gz(v, i=i, gv=gv):
                if i >= 2:
                    wait(v, "qG", d_tok[i - 2])
                inc(v.memset(gv, 0.0), "eV")
            on("vector", v_gz)
            def g_ga(gp, i=i, gv=gv, tok=P.cnt["eV"]):
                wait(gp, "eV", tok)
                inc(gp.indirect_dma_start(
                    out=gv, out_offset=None,
                    in_=y2s_d, in_offset=bass.IndirectOffsetOnAxis(ap=slotu[:, i:i + 1], axis=0),
                    bounds_check=BREG[0], oob_is_err=False), "qG", 16)
            on("gpsimd", g_ga)
            def v_al(v, i=i, gv=gv, tok=P.cnt["qG"]):
                wait(v, "qG", tok)
                inc(v.tensor_scalar(gv, gv, alpha_a[:, i:i + 1], None, ALU.mult), "eV")
            on("vector", v_al)
            def g_ac2(gp, i=i, gv=gv, tok=P.cnt["eV"]):
                wait(gp, "eV", tok)
                inc(gp.dma_start(out=out_d[128 * i:128 * (i + 1), :], in_=gv,
                                 accum_op=ALU.add), "qG", 16)
            on("gpsimd", g_ac2)
            d_tok[i] = P.cnt["qG"]

        P.scope("ff")
        P.barrier()
        esT2.close()
        esB6 = contextlib.ExitStack()
        gelu2 = sb("gelu2", [128, 2 * 32 * 512], BF16, st=esB6)
        wf1sl = sb("wf1sl", [128, 2 * 4096], BF16, st=esB6)
        wf2h = sb("wf2h", [128, 2 * 4 * 1024], BF16, st=esB6)
        xn3buf = sb("xn3buf", [128, 8 * 1024], BF16, st=esB6)
        bff1T = sb("bff1T", [128, 32], st=esB6)
        b2lrow = sb("b2lrow", [1, D], st=esB6)
        b2lrowb = sb("b2lrowb", [1, D], BF16, st=esB6)
        onesFb = sb("onesFb", [1, 128], BF16, st=esB6)

        def s_bf1(s):
            wait(s, "eT", P.cnt["eT"])
            inc(s.dma_start(out=vsm32[:], in_=bff1_d), "qS", 16)
            inc(s.dma_start(out=b2lrow[:], in_=bff2_d), "qS", 16)
        on("sync", s_bf1)
        def t_bf1(t, tok=P.cnt["qS"]):
            wait(t, "qS", tok)
            P.bank_wait(t, "pb0")
            inc(t.transpose(pb[0][0:128, 0:32], vsm32[:], idn[0:32, 0:32]), "eT")
        on("tensor", t_bf1)
        def v_bf1(v, tok=P.cnt["eT"]):
            wait(v, "eT", tok)
            v.tensor_copy(bff1T[:], pb[0][0:128, 0:32])
            v.memset(onesFb[:], 1.0)
            inc(v.tensor_copy(b2lrowb[:], b2lrow[:]), "eV")
        on("vector", v_bf1)
        P.bank_set("pb0", "eV")
        tok_bf1 = P.cnt["eV"]

        F2PREV = [0]
        ACCSL = [{}]
        YRD = [{}]
        b6_acc = {}
        f1_prev = [None]
        x3_rd = [None]
        PF = {}
        def emit_prefetch(pr, gate_x3, gate_w):
            def s_x3(s, pr=pr, prev=gate_x3, st_tok=xn3st_tok[8 * pr + 7]):
                wait(s, "qG", st_tok)
                if prev is not None:
                    wait(s, "eT", prev)
                inc(s.dma_start(out=xn3buf.rearrange("p (dc t) -> p dc t", dc=8),
                                in_=xn3_d.rearrange("(dc p) t -> p dc t", p=128)[:, :, 1024 * pr:1024 * (pr + 1)],
                                max_dma_last_dim=8192), "qS", 16)
            on("sync", s_x3)
            tok_x3 = P.cnt["qS"]
            tw = {}
            for fsl in (0, 1):
                def s_wf1(s, fsl=fsl, gate_w=gate_w):
                    if gate_w is not None:
                        wait(s, "eT", gate_w[fsl + 6])
                    else:
                        wait(s, "eA", P.cnt["eA"])
                    inc(s.dma_start(out=wf1sl.rearrange("p (s dc f) -> p s dc f", s=2, dc=8)[:, fsl],
                                    in_=wff1b_d.rearrange("(dc p) f -> p dc f", p=128)[:, :, 512 * fsl:512 * (fsl + 1)],
                                    max_dma_last_dim=8192), "qS", 16)
                on("sync", s_wf1)
                tw[fsl] = P.cnt["qS"]
            PF[pr] = (tok_x3, tw)

        emit_prefetch(0, None, None)
        for pr in range(4):
            tok_x3, tw01 = PF[pr]
            f1_tok = {}
            for fsl in range(8):
                slw = fsl % 2
                if fsl < 2:
                    tok_w = tw01[fsl]
                else:
                    def s_wf1(s, fsl=fsl, slw=slw):
                        wait(s, "eT", f1_tok[fsl - 2])
                        inc(s.dma_start(out=wf1sl.rearrange("p (s dc f) -> p s dc f", s=2, dc=8)[:, slw],
                                        in_=wff1b_d.rearrange("(dc p) f -> p dc f", p=128)[:, :, 512 * fsl:512 * (fsl + 1)],
                                        max_dma_last_dim=8192), "qS", 16)
                    on("sync", s_wf1)
                    tok_w = P.cnt["qS"]
                for fs4 in range(4):
                    fs = 4 * fsl + fs4
                    bk0, bk1 = (6, 7) if fs % 2 == 0 else (4, 5)
                    def t_f1(t, fs=fs, fs4=fs4, slw=slw, bk0=bk0, bk1=bk1, tok=tok_w, tokx=tok_x3):
                        if fs == 0:
                            wait(t, "qS", tokx)
                        wait(t, "qS", tok)
                        P.bank_wait(t, f"pb{bk0}"); P.bank_wait(t, f"pb{bk1}")
                        for dc in range(8):
                            wsl = wf1sl[:, 4096 * slw + 512 * dc + 128 * fs4:4096 * slw + 512 * dc + 128 * (fs4 + 1)]
                            t.matmul(pb[bk0][0:128, 0:512], wsl,
                                     xn3buf[:, 1024 * dc:1024 * dc + 512],
                                     start=(dc == 0), stop=(dc == 7))
                            ins = t.matmul(pb[bk1][0:128, 0:512], wsl,
                                           xn3buf[:, 1024 * dc + 512:1024 * (dc + 1)],
                                           start=(dc == 0), stop=(dc == 7))
                        inc(ins, "eT")
                    on("tensor", t_f1)
                    f1_tok[fsl] = P.cnt["eT"]
                    def a_g(a, fs=fs, bk0=bk0, bk1=bk1, tok=P.cnt["eT"]):
                        wait(a, "eT", tok)
                        a.activation(gelu2[:, 512 * fs:512 * (fs + 1)], pb[bk0][0:128, 0:512],
                                     AF.Gelu_apprx_tanh, bias=bff1T[:, fs:fs + 1])
                        inc(a.activation(gelu2[:, 16384 + 512 * fs:16384 + 512 * (fs + 1)], pb[bk1][0:128, 0:512],
                                         AF.Gelu_apprx_tanh, bias=bff1T[:, fs:fs + 1]), "eA")
                    on("scalar", a_g)
                    P.bank_set(f"pb{bk0}", "eA"); P.bank_set(f"pb{bk1}", "eA")
            f1_prev[0] = f1_tok
            x3_rd[0] = P.cnt["eT"]
            tok_g = P.cnt["eA"]
            if pr + 1 < 4:
                emit_prefetch(pr + 1, x3_rd[0], f1_tok)
            for half in range(2):
                tc = 2 * pr + half
                gbase = 16384 * half
                tokq = {}
                f2_rd = {}
                def emit_wf2(q_, gate):
                    slq = q_ % 2
                    def g_wf2(gp, q_=q_, slq=slq, gate=gate):
                        wait(gp, "eT", gate)
                        inc(gp.dma_start(
                            out=wf2h.rearrange("p (s c f) -> p s c f", s=2, c=4)[:, slq],
                            in_=wff2b_d.rearrange("(c p) f -> p c f", p=128)[:, 4 * q_:4 * (q_ + 1), :],
                            max_dma_last_dim=8192), "qG", 16)
                    on("gpsimd", g_wf2)
                    tokq[q_] = P.cnt["qG"]
                for q_ in (0, 1):
                    emit_wf2(q_, F2PREV[0][q_ + 6] if isinstance(F2PREV[0], dict) else F2PREV[0])
                for q_ in range(8):
                    slq = q_ % 2
                    def t_f2(t, q_=q_, slq=slq, gbase=gbase, tok=tokq[q_], tokg=tok_g, tokb=tok_bf1):
                        if q_ == 0:
                            wait(t, "eA", tokg)
                            wait(t, "eV", tokb)
                        wait(t, "qG", tok)
                        for tt in range(4):
                            if q_ == 0:
                                P.bank_wait(t, f"pb{2 * tt}"); P.bank_wait(t, f"pb{2 * tt + 1}")
                            for fc in range(4 * q_, 4 * (q_ + 1)):
                                wof = 4096 * slq + 1024 * (fc - 4 * q_)
                                t.matmul(pb[2 * tt][0:128, 0:512],
                                         gelu2[:, gbase + 512 * fc + 128 * tt:gbase + 512 * fc + 128 * (tt + 1)],
                                         wf2h[:, wof:wof + 512],
                                         start=(fc == 0), stop=False)
                                ins = t.matmul(pb[2 * tt + 1][0:128, 0:512],
                                               gelu2[:, gbase + 512 * fc + 128 * tt:gbase + 512 * fc + 128 * (tt + 1)],
                                               wf2h[:, wof + 512:wof + 1024],
                                               start=(fc == 0), stop=False)
                        if q_ == 7:
                            for tt in range(4):
                                t.matmul(pb[2 * tt][0:128, 0:512], onesFb[:],
                                         b2lrowb[0:1, 0:512], start=False, stop=True)
                                ins = t.matmul(pb[2 * tt + 1][0:128, 0:512], onesFb[:],
                                               b2lrowb[0:1, 512:1024], start=False, stop=True)
                        inc(ins, "eT")
                    on("tensor", t_f2)
                    f2_rd[q_] = P.cnt["eT"]
                    if q_ + 2 < 8:
                        emit_wf2(q_ + 2, f2_rd[q_])
                tok_f2 = P.cnt["eT"]
                for tt in range(4):
                    i = 4 * tc + tt
                    yv = yring2[:, D * (tt % 2):D * (tt % 2 + 1)]
                    ytok = YRD[0].get(tt % 2)
                    def s_y7b(s, i=i, yv=yv, ytok=ytok):
                        if ytok is not None:
                            wait(s, "eV", ytok)
                        inc(s.dma_start(out=yv, in_=y_d[128 * i:128 * (i + 1), :]), "qS", 16)
                    on("sync", s_y7b)
                    tok_y = P.cnt["qS"]
                    yf = junkA[:] if tt % 2 == 0 else junkV[:]
                    awar = ACCSL[0].get(tt % 2)
                    def v_yf(v, tt=tt, yv=yv, yf=yf, tok=tok_f2, toky=tok_y, war=awar):
                        wait(v, "eT", tok)
                        wait(v, "qS", toky)
                        if war is not None:
                            wait(v, "qG", war)
                        v.tensor_tensor(yf[:, 0:512], pb[2 * tt][0:128, 0:512], yv[:, 0:512], ALU.add)
                        inc(v.tensor_tensor(yf[:, 512:1024], pb[2 * tt + 1][0:128, 0:512],
                                            yv[:, 512:1024], ALU.add), "eV")
                    on("vector", v_yf)
                    P.bank_set(f"pb{2 * tt}", "eV"); P.bank_set(f"pb{2 * tt + 1}", "eV")
                    YRD[0][tt % 2] = P.cnt["eV"]
                    def g_acc(gp, i=i, yf=yf, tok=P.cnt["eV"]):
                        wait(gp, "eV", tok)
                        inc(gp.dma_start(out=out_d[128 * i:128 * (i + 1), :],
                                         in_=yf, accum_op=ALU.add), "qG", 16)
                    on("gpsimd", g_acc)
                    ACCSL[0][tt % 2] = P.cnt["qG"]
                    b6_acc[i] = P.cnt["qG"]
                F2PREV[0] = {q_: f2_rd[q_] for q_ in range(8)}
                for i5 in range(4 * tc, 4 * (tc + 1)):
                    emit_d_tile(i5)

        if upto == "f6":
            dgel = dout("d_gelu", [128, 32 * 512], BF16)
            dxn3 = dout("d_xn3", [128, 8 * 512], BF16)
            def g_df6(gp):
                wait_all(gp)
                inc(gp.dma_start(out=dgel, in_=gelu2[:, 16384:32768]), "qG", 16)
                for c in range(8):
                    inc(gp.dma_start(out=dxn3[:, 512 * c:512 * (c + 1)],
                                     in_=xn3buf[:, 1024 * c + 512:1024 * (c + 1)]), "qG", 16)
                wait_all(gp, "qG")
            on("gpsimd", g_df6)
            P.barrier()
            esB6.close()
            esT.close()
            esB.close()
            return nc, dbg_outs

        P.barrier()
        esB6.close()
        esT.close()
        esB.close()

        P.scope_end()
        def g_fin(gp):
            wait_all(gp)
        on("gpsimd", g_fin)
        def s_fin(s):
            wait_all(s)
        on("sync", s_fin)
    return nc, dbg_outs


def _in_map(inputs, b):
    c = _host_consts()
    m = {
        "x": np.ascontiguousarray(inputs["L0_attn_x"][b]),
        "ln1_g": inputs["L0_attn_ln1_g"].reshape(8, 128),
        "ln1_b": inputs["L0_attn_ln1_b"].reshape(8, 128),
        "ln2_g": inputs["L0_attn_ln2_g"].reshape(8, 128),
        "ln2_b": inputs["L0_attn_ln2_b"].reshape(8, 128),
        "ln3_g": inputs["L0_attn_ln3_g"].reshape(8, 128),
        "ln3_b": inputs["L0_attn_ln3_b"].reshape(8, 128),
        "w_potT": np.ascontiguousarray(inputs["L0_attn_w_pot"].T),
        "b_pot": inputs["L0_attn_b_pot"].reshape(1, 2),
        "w_bk": inputs["L0_attn_w_bk"],
        "b_bk": inputs["L0_attn_b_bk"].reshape(1, D),
        "w_qkv": inputs["L0_attn_w_qkv"],
        "b_qkv": inputs["L0_attn_b_qkv"].reshape(1, -1),
        "w_out": inputs["L0_attn_w_out"],
        "b_out": inputs["L0_attn_b_out"].reshape(1, D),
        "w_ff1": inputs["L0_attn_w_ff1"],
        "b_ff1": inputs["L0_attn_b_ff1"].reshape(32, 128),
        "w_ff2": inputs["L0_attn_w_ff2"],
        "b_ff2": inputs["L0_attn_b_ff2"].reshape(1, D),
        "cent": inputs["L0_ffn_centroids"],
        "m_ln_g": inputs["L0_ffn_ln_g"].reshape(NE, 8, 128),
        "m_ln_b": inputs["L0_ffn_ln_b"].reshape(NE, 8, 128),
        "m_w1": inputs["L0_ffn_w1"],
        "m_b1": inputs["L0_ffn_b1"].reshape(NE, 32, 128),
        "m_w2": inputs["L0_ffn_w2"],
        "m_b2": inputs["L0_ffn_b2"].reshape(NE, 1, D),
    }
    m.update(c)
    for k, v in list(m.items()):
        if isinstance(v, np.ndarray) and v.dtype == np.float64:
            m[k] = v.astype(np.float32)
    return m


def kernel(**inputs):
    nc, _ = build(dbg=0)
    in_maps = [_in_map(inputs, b) for b in range(8)]
    res = run_bass_kernel_spmd(nc, in_maps, core_ids=list(range(8)))
    return np.stack([r["out"] for r in res.results], axis=0)
